# version 07
# baseline (speedup 1.0000x reference)
import numpy as np
import ml_dtypes
from contextlib import ExitStack
import concourse.bass as bass
import concourse.mybir as mybir
from concourse.bass_utils import run_bass_kernel_spmd

F32 = mybir.dt.float32
BF16 = mybir.dt.bfloat16
AF = mybir.ActivationFunctionType
ALU = mybir.AluOpType
BF = ml_dtypes.bfloat16

B, T, D = 2, 8192, 2048
NS = 16
INTER = 8192
NH, DH = 16, 128
NE, FE = 8, 1024
SF = 1024
TC = 2048
LN_EPS = 1e-5
RMS_EPS = 1e-6


_NAME_UNIQ = [0]


def _uniq(name):
    _NAME_UNIQ[0] += 1
    return f"{name}_{_NAME_UNIQ[0]}"


def hadamard(n):
    H = np.ones((1, 1), dtype=np.float32)
    while H.shape[0] < n:
        H = np.block([[H, H], [H, -H]])
    return H


class Ser:

    def __init__(self, nc, g, gd):
        self.nc = nc
        self.g = g
        self.gd = gd
        self.gc = 0
        self.gdc = 0

    def _wait(self, eng):
        if self.gc:
            eng.wait_ge(self.g, self.gc)
        if self.gdc:
            eng.wait_ge(self.gd, self.gdc)

    def step(self, eng, fn):
        self._wait(eng)
        fn(eng).then_inc(self.g, 1)
        self.gc += 1

    def steps(self, eng, fns):
        self._wait(eng)
        ins = None
        for fn in fns:
            ins = fn(eng)
        ins.then_inc(self.g, 1)
        self.gc += 1

    def dma(self, eng, out, in_):
        self._wait(eng)
        eng.dma_start(out=out, in_=in_).then_inc(self.gd, 16)
        self.gdc += 16

    def dmas(self, eng, pairs):
        self._wait(eng)
        for out, in_ in pairs:
            eng.dma_start(out=out, in_=in_).then_inc(self.gd, 16)
            self.gdc += 16

    def finish(self, eng):
        self._wait(eng)


class Pipe:

    ENGS = ("tensor", "vector", "scalar", "gpsimd", "sync")

    def __init__(self, nc, stack):
        self.nc = nc
        self.sem = {}
        self.cnt = {}
        self.seen = {e: {} for e in self.ENGS}
        for e in self.ENGS:
            self.sem[e] = stack.enter_context(nc.semaphore("sem_" + e))
            self.cnt[e] = 0
        for q in ("dsync", "dgp", "dscal"):
            self.sem[q] = stack.enter_context(nc.semaphore("sem_" + q))
            self.cnt[q] = 0

    def _wait(self, ename, eng, waits):
        kn = self.seen[ename]
        for s, v in waits:
            if s == ename or v <= 0:
                continue
            if kn.get(s, 0) >= v:
                continue
            eng.wait_ge(self.sem[s], v)
            kn[s] = v

    SELFSYNC = {"vector": True, "scalar": True, "gpsimd": True,
                "tensor": False, "sync": False}

    def emit(self, ename, fn, waits=(), inc=True, selfsync=None):
        eng = getattr(self.nc, ename)
        if selfsync is None:
            selfsync = self.SELFSYNC[ename]
        if selfsync and self.cnt[ename] > self.seen[ename].get(ename, 0):
            eng.wait_ge(self.sem[ename], self.cnt[ename])
            self.seen[ename][ename] = self.cnt[ename]
        self._wait(ename, eng, waits)
        ins = fn(eng)
        if inc:
            ins.then_inc(self.sem[ename], 1)
            self.cnt[ename] += 1
        return self.cnt[ename]

    def dma_q(self, qname, out, in_, waits=(), inc=True, sem=None):
        ename = {"dsync": "sync", "dgp": "gpsimd", "dscal": "scalar"}[qname]
        eng = getattr(self.nc, ename)
        self._wait(ename, eng, waits)
        ins = eng.dma_start(out=out, in_=in_)
        key = sem if sem is not None else qname
        if inc:
            ins.then_inc(self.sem[key], 16)
            self.cnt[key] += 16
        return self.cnt[key]

    def new_sem(self, name, stack):
        self.sem[name] = stack.enter_context(self.nc.semaphore("sem_" + name))
        self.cnt[name] = 0

    def snap(self):
        return [(k, v) for k, v in self.cnt.items() if v > 0]

    def lazy(self, ename, fn):
        return self.emit(ename, fn, waits=self.snap())

    def ldma(self, qname, out, in_):
        return self.dma_q(qname, out, in_, waits=self.snap())

    def barrier(self, enames=None):
        w = self.snap()
        for ename in (enames or self.ENGS):
            eng = getattr(self.nc, ename)
            self._wait(ename, eng, w)

    def _name_of(self, eng):
        if not hasattr(self, "_byid"):
            self._byid = {id(getattr(self.nc, e)): e for e in self.ENGS}
        return self._byid[id(eng)]

    def step(self, eng, fn):
        return self.lazy(self._name_of(eng), fn)

    def steps(self, eng, fns):
        ename = self._name_of(eng)
        w = self.snap()
        e = getattr(self.nc, ename)
        if self.SELFSYNC[ename] and self.cnt[ename] > self.seen[ename].get(ename, 0):
            e.wait_ge(self.sem[ename], self.cnt[ename])
            self.seen[ename][ename] = self.cnt[ename]
        self._wait(ename, e, w)
        ins = None
        for fn in fns:
            ins = fn(e)
        ins.then_inc(self.sem[ename], 1)
        self.cnt[ename] += 1
        return self.cnt[ename]

    _QOF = {"sync": "dsync", "gpsimd": "dgp", "scalar": "dscal"}

    def dma(self, eng, out, in_):
        return self.dma_q(self._QOF[self._name_of(eng)], out, in_, waits=self.snap())

    def dmas(self, eng, pairs):
        ename = self._name_of(eng)
        qname = self._QOF[ename]
        e = getattr(self.nc, ename)
        self._wait(ename, e, self.snap())
        for out, in_ in pairs:
            e.dma_start(out=out, in_=in_).then_inc(self.sem[qname], 16)
            self.cnt[qname] += 16
        return self.cnt[qname]

    def finish(self, eng):
        self._wait(self._name_of(eng), eng, self.snap())


ACH = 1024


def emit_A(nc, P, t):
    xA = t["xA"]
    xAs = t["xAs"]
    wdtT = t["wdtT"]
    dtb = t["dtb"]
    xpw = t["xpw"]
    xpb = t["xpb"]
    alog = t["alog"]
    ident = t["ident"]
    sel = t["sel"]
    yTs = t["yTs"]

    nch = T // ACH
    with ExitStack() as stack:
        def sb(name, shape, dt=F32):
            return stack.enter_context(nc.sbuf_tensor(_uniq(name), shape, dt))
        s_xtok = sb("s_xtok", [128, 2 * D])
        s_xT = sb("s_xT", [128, 16 * ACH], BF16)
        s_xs = sb("s_xs", [128, 4 * ACH])
        s_wdt = sb("s_wdt", [128, 16 * 512], BF16)
        s_xpw = sb("s_xpw", [128, 16 * 64], BF16)
        s_sel = sb("s_sel", [64, 4096], BF16)
        s_id = sb("s_id", [128, 128])
        s_dtb = sb("s_dtb", [128, 4])
        s_xpb = sb("s_xpb", [64, 1])
        s_A = sb("s_A", [128, 64])
        s_dtT = sb("s_dtT", [128, 2 * 4 * ACH])
        s_dtx = sb("s_dtx", [128, 2 * 4 * ACH])
        s_bc = sb("s_bc", [64, 2 * ACH], BF16)
        s_dA = sb("s_dA", [128, 2 * ACH])
        s_spt = sb("s_spt", [128, 512])
        s_h = sb("s_h", [128, 2 * ACH])
        s_tmp = sb("s_tmp", [128, ACH])
        s_u = s_tmp
        s_y = sb("s_y", [128, 4 * ACH])
        s_ybf = sb("s_ybf", [128, 4 * ACH], BF16)
        s_hst = sb("s_hst", [128, 64])
        p = stack.enter_context(nc.psum_tensor(_uniq("p"), [128, 4096], F32))
        PTP = [p[:, 0:128], p[:, 512:640]]
        PDT = [p[:, 1024:1536], p[:, 1536:2048]]
        PB = p[:, 2048:3072]
        PBC = p[0:64, 1024:1536]
        PC = p[:, 3072:4096]
        ser = P

        ser.dmas(nc.sync, [
            (s_sel[:], sel), (s_id[:], ident), (s_dtb[:], dtb), (s_xpb[:], xpb),
        ])
        ser.dma(nc.sync, s_A[:], alog)
        ser.step(nc.scalar, lambda e: e.activation(s_A[:], s_A[:], AF.Exp))
        ser.step(nc.vector, lambda e: e.tensor_scalar_mul(s_A[:], s_A[:], -1.0))
        ser.dmas(nc.gpsimd, [
            (s_wdt[:, (k * 4 + et) * 128:(k * 4 + et + 1) * 128],
             wdtT[k * 128:(k + 1) * 128, et * 128:(et + 1) * 128])
            for k in range(16) for et in range(4)
        ])
        ser.dmas(nc.gpsimd, [
            (s_xpw[:, k * 64:(k + 1) * 64], xpw[k * 128:(k + 1) * 128, :])
            for k in range(16)
        ])
        P.barrier()

        tp_done = [0, 0]
        ev_s = [0, 0]
        ev_v = [0, 0]
        xld_cnt = [0, 0]
        xdma_free = [0, 0]
        sp_done = [0, 0]
        dtx_free = [0, 0]
        bc_ev = [0, 0]
        dA_cnt = [0, 0]
        dA_free = [0, 0]
        hst_done = [0] * 64
        hstc_free = [0, 0]
        yst_cnt = [0]
        chk = [dict() for _ in range(nch)]

        def prep_units(ch):
            t0 = ch * ACH
            xw = ch % 2
            units = []

            def u_xT(tt):
                tb = tt % 2
                xld_cnt[tb] = P.dma_q(
                    "dsync", s_xtok[:, tb * D: tb * D + D],
                    xA[t0 + tt * 128: t0 + (tt + 1) * 128, :],
                    waits=[("tensor", xdma_free[tb])], sem=f"xld{tb}")
                for k in range(16):
                    j = k % 2
                    w = [(f"xld{tb}", xld_cnt[tb]), ("scalar", ev_s[j]), ("vector", ev_v[j])]
                    tp_done[j] = P.emit("tensor", lambda e, k=k, tb=tb, j=j: e.transpose(
                        PTP[j], s_xtok[:, tb * D + k * 128: tb * D + (k + 1) * 128],
                        s_id[:]), waits=w)
                    ev_s[j] = P.emit("scalar", lambda e, k=k, tt=tt, j=j: e.copy(
                        s_xT[:, k * ACH + tt * 128: k * ACH + (tt + 1) * 128], PTP[j]),
                        waits=[("tensor", tp_done[j])])
                xdma_free[tb] = P.cnt["tensor"]

            def u_xs(tt):
                tb = tt % 2
                xld_cnt[tb] = P.dma_q(
                    "dsync", s_xtok[:, tb * D: tb * D + 512],
                    xAs[t0 + tt * 128: t0 + (tt + 1) * 128, :],
                    waits=[("tensor", xdma_free[tb])], sem=f"xld{tb}")
                for dt4 in range(4):
                    j = dt4 % 2
                    w = [(f"xld{tb}", xld_cnt[tb]), ("scalar", ev_s[j]), ("vector", ev_v[j])]
                    tp_done[j] = P.emit("tensor", lambda e, dt4=dt4, tb=tb, j=j: e.transpose(
                        PTP[j], s_xtok[:, tb * D + dt4 * 128: tb * D + (dt4 + 1) * 128],
                        s_id[:]), waits=w)
                    ev_v[j] = P.emit("vector", lambda e, dt4=dt4, tt=tt, j=j: e.tensor_copy(
                        s_xs[:, dt4 * ACH + tt * 128: dt4 * ACH + (tt + 1) * 128], PTP[j]),
                        waits=[("tensor", tp_done[j])])
                xdma_free[tb] = P.cnt["tensor"]

            def u_dt(g):
                et, tc2 = g // 2, g % 2
                jd = g % 2
                if g == 0:
                    chk[ch]["xTev"] = P.cnt["scalar"]
                for k in range(16):
                    w = []
                    if k == 0:
                        w = [("scalar", max(chk[ch]["xTev"], sp_done[jd])),
                             ("vector", max(bc_ev[0], bc_ev[1]))]
                    P.emit("tensor", lambda e, et=et, tc2=tc2, k=k, jd=jd: e.matmul(
                        PDT[jd],
                        s_wdt[:, (k * 4 + et) * 128:(k * 4 + et + 1) * 128],
                        s_xT[:, k * ACH + tc2 * 512: k * ACH + (tc2 + 1) * 512],
                        start=(k == 0), stop=(k == 15)), waits=w, inc=(k == 15))
                tdt = P.cnt["tensor"]
                P.emit("scalar", lambda e, et=et, jd=jd: e.activation(
                    s_spt[:, 0:512], PDT[jd], AF.Exp, bias=s_dtb[:, et:et + 1]),
                    waits=[("tensor", tdt), ("vector", dtx_free[xw])])
                sp_done[jd] = P.emit("scalar", lambda e, et=et, tc2=tc2, xw=xw: e.activation(
                    s_dtT[:, xw * 4 * ACH + et * ACH + tc2 * 512:
                          xw * 4 * ACH + et * ACH + (tc2 + 1) * 512],
                    s_spt[:, 0:512], AF.Ln, bias=1.0))
                if g == 7:
                    chk[ch]["sp"] = P.cnt["scalar"]

            def u_bc(tc2):
                for k in range(16):
                    w = []
                    if k == 0:
                        w = [("scalar", max(sp_done[0], sp_done[1])),
                             ("vector", max(bc_ev[0], bc_ev[1]))]
                    P.emit("tensor", lambda e, tc2=tc2, k=k: e.matmul(
                        PBC[:, 0:512], s_xpw[:, k * 64:(k + 1) * 64],
                        s_xT[:, k * ACH + tc2 * 512: k * ACH + (tc2 + 1) * 512],
                        start=(k == 0), stop=(k == 15)), waits=w, inc=(k == 15))
                tbc = P.cnt["tensor"]
                bc_ev[xw] = P.emit("vector", lambda e, tc2=tc2, xw=xw: e.tensor_scalar_add(
                    s_bc[:, xw * ACH + tc2 * 512: xw * ACH + (tc2 + 1) * 512],
                    PBC[:, 0:512], s_xpb[:, 0:1]), waits=[("tensor", tbc)])
                if tc2 == 1:
                    chk[ch]["bc"] = P.cnt["vector"]

            for tt in range(8):
                units.append(lambda tt=tt: u_xT(tt))
            for tt in range(8):
                units.append(lambda tt=tt: u_xs(tt))
            for g in range(8):
                units.append(lambda g=g: u_dt(g))
            for tc2 in range(2):
                units.append(lambda tc2=tc2: u_bc(tc2))
            return units

        def emit_scan(ch, next_units):
            t0 = ch * ACH
            xw = ch % 2
            for dt4 in range(4):
                dtx_free[xw] = P.emit("vector", lambda e, dt4=dt4, xw=xw: e.tensor_mul(
                    s_dtx[:, xw * 4 * ACH + dt4 * ACH: xw * 4 * ACH + (dt4 + 1) * ACH],
                    s_dtT[:, xw * 4 * ACH + dt4 * ACH: xw * 4 * ACH + (dt4 + 1) * ACH],
                    s_xs[:, dt4 * ACH:(dt4 + 1) * ACH]),
                    waits=[("scalar", chk[ch]["sp"])])
            ui = [0]

            def run_units(kmax):
                while ui[0] < min(kmax, len(next_units)):
                    next_units[ui[0]]()
                    ui[0] += 1

            u_last = 0
            ym_last = 0
            for n in range(NS):
                for tc2 in range(ACH // 512):
                    w = [("vector", max(u_last, chk[ch]["bc"]))] if tc2 == 0 else []
                    P.emit("tensor", lambda e, n=n, tc2=tc2, xw=xw: e.matmul(
                        PB[:, tc2 * 512:(tc2 + 1) * 512],
                        s_sel[:, n * 128:(n + 1) * 128],
                        s_bc[:, xw * ACH + tc2 * 512: xw * ACH + (tc2 + 1) * 512],
                        start=True, stop=True), waits=w, inc=(tc2 == 1))
                repB = P.cnt["tensor"]
                for tc2 in range(ACH // 512):
                    w = [("vector", ym_last)] if tc2 == 0 else []
                    P.emit("tensor", lambda e, n=n, tc2=tc2, xw=xw: e.matmul(
                        PC[:, tc2 * 512:(tc2 + 1) * 512],
                        s_sel[:, (16 + n) * 128:(17 + n) * 128],
                        s_bc[:, xw * ACH + tc2 * 512: xw * ACH + (tc2 + 1) * 512],
                        start=True, stop=True), waits=w, inc=(tc2 == 1))
                repC = P.cnt["tensor"]
                for dt4 in range(4):
                    col = dt4 * 16 + n
                    j2 = (n * 4 + dt4) % 2
                    dA_cnt[j2] = P.emit("scalar", lambda e, dt4=dt4, col=col, j2=j2, xw=xw: e.activation(
                        s_dA[:, j2 * ACH:(j2 + 1) * ACH],
                        s_dtT[:, xw * 4 * ACH + dt4 * ACH: xw * 4 * ACH + (dt4 + 1) * ACH],
                        AF.Exp, scale=s_A[:, col:col + 1]),
                        waits=[("vector", dA_free[j2]), ("scalar", chk[ch]["sp"])])
                    u_last = P.emit("vector", lambda e, dt4=dt4, xw=xw: e.tensor_mul(
                        s_u[:], s_dtx[:, xw * 4 * ACH + dt4 * ACH: xw * 4 * ACH + (dt4 + 1) * ACH],
                        PB[:]), waits=[("tensor", repB)])
                    if ch == 0:
                        scv = P.emit("vector", lambda e, j2=j2: e.tensor_tensor_scan(
                            s_h[:, j2 * ACH:(j2 + 1) * ACH], s_dA[:, j2 * ACH:(j2 + 1) * ACH],
                            s_u[:], 0.0, ALU.mult, ALU.add),
                            waits=[("scalar", dA_cnt[j2]), ("gpsimd", hstc_free[j2])])
                    else:
                        scv = P.emit("vector", lambda e, col=col, j2=j2: e.tensor_tensor_scan(
                            s_h[:, j2 * ACH:(j2 + 1) * ACH], s_dA[:, j2 * ACH:(j2 + 1) * ACH],
                            s_u[:], s_hst[:, col:col + 1], ALU.mult, ALU.add),
                            waits=[("scalar", dA_cnt[j2]), ("gpsimd", hst_done[col]),
                                   ("gpsimd", hstc_free[j2])])
                    dA_free[j2] = scv
                    hst_done[col] = P.emit("gpsimd", lambda e, col=col, j2=j2: e.tensor_copy(
                        s_hst[:, col:col + 1], s_h[:, (j2 + 1) * ACH - 1:(j2 + 1) * ACH]),
                        waits=[("vector", scv)])
                    hstc_free[j2] = hst_done[col]
                    if n == 0:
                        w = [("tensor", repC)]
                        if ch > 0 and dt4 == 0:
                            w.append(("yst", yst_cnt[0]))
                        ym_last = P.emit("vector", lambda e, dt4=dt4, j2=j2: e.tensor_mul(
                            s_y[:, dt4 * ACH:(dt4 + 1) * ACH],
                            s_h[:, j2 * ACH:(j2 + 1) * ACH], PC[:]), waits=w)
                    else:
                        P.emit("vector", lambda e, j2=j2: e.tensor_mul(
                            s_tmp[:], s_h[:, j2 * ACH:(j2 + 1) * ACH], PC[:]),
                            waits=[("tensor", repC)])
                        ydst = s_ybf if n == NS - 1 else s_y
                        ym_last = P.emit("vector", lambda e, dt4=dt4, ydst=ydst: e.tensor_add(
                            ydst[:, dt4 * ACH:(dt4 + 1) * ACH],
                            s_y[:, dt4 * ACH:(dt4 + 1) * ACH], s_tmp[:]))
                pass
            run_units(len(next_units))
            jblk, c0 = (t0 // 2048), (t0 % 2048)
            for dt4 in range(4):
                r0 = jblk * 512 + dt4 * 128
                yst_cnt[0] = P.dma_q("dsync", yTs[r0:r0 + 128, c0:c0 + ACH],
                                     s_ybf[:, dt4 * ACH:(dt4 + 1) * ACH],
                                     waits=[("vector", ym_last)], sem="yst")

        for u in prep_units(0):
            u()
        for ch in range(nch):
            nxt = prep_units(ch + 1) if ch + 1 < nch else []
            emit_scan(ch, nxt)
        P.barrier()

def host_inputs_A(inp):
    x = np.asarray(inp["L0_attn_x"], dtype=np.float32)
    wdtT = np.ascontiguousarray(np.asarray(inp["L0_attn_dt_proj_w"], np.float32).T)
    dtb = np.asarray(inp["L0_attn_dt_proj_b"], np.float32)
    xpwT = np.ascontiguousarray(np.asarray(inp["L0_attn_x_proj_w"], np.float32).T)
    xpb = np.asarray(inp["L0_attn_x_proj_b"], np.float32)
    alog = np.asarray(inp["L0_attn_A_log"], np.float32)
    ident = np.eye(128, dtype=np.float32)
    selm = np.zeros((64, 4096), dtype=np.float32)
    for n in range(16):
        selm[n, n * 128:(n + 1) * 128] = 1.0
        selm[32 + n, (16 + n) * 128:(17 + n) * 128] = 1.0
    xpw_pad = np.zeros((D, 64), dtype=np.float32)
    xpw_pad[:, 0:16] = xpwT[:, 0:16]
    xpw_pad[:, 32:48] = xpwT[:, 16:32]
    xpb_pad = np.zeros((64, 1), dtype=np.float32)
    xpb_pad[0:16, 0] = xpb[0:16]
    xpb_pad[32:48, 0] = xpb[16:32]
    maps = []
    for c in range(8):
        b, ds = c // 4, c % 4
        d0 = ds * 512
        maps.append({
            "xA": np.ascontiguousarray(x[b]),
            "xAs": np.ascontiguousarray(x[b][:, d0:d0 + 512]),
            "wdtT": np.ascontiguousarray(wdtT[:, d0:d0 + 512]),
            "dtb": np.ascontiguousarray(dtb[d0:d0 + 512].reshape(4, 128).T),
            "xpw": xpw_pad,
            "xpb": xpb_pad,
            "alog": np.ascontiguousarray(
                alog[d0:d0 + 512].reshape(4, 128, 16).transpose(1, 0, 2).reshape(128, 64)),
            "ident": ident,
            "sel": selm.astype(BF),
        })
    return maps


def sp_pack(Wt):
    Kf, Mf = Wt.shape
    Kt, Mt = Kf // 128, Mf // 128
    return np.ascontiguousarray(
        Wt.reshape(Kt, 128, Mt, 128).transpose(2, 0, 1, 3).reshape(Mt * Kt * 128, 128))


def load_sp(ser, eng, s_tile, dram_pack, mt, Kt):
    src = dram_pack[mt * Kt * 128:(mt + 1) * Kt * 128, :].rearrange(
        "(kt p) f -> p kt f", p=128)
    dst = s_tile[:, 0:Kt * 128].rearrange("p (kt f) -> p kt f", f=128)
    ser.dma(eng, dst, src)


def emit_load_xT(ser, nc, x_tm, t0, ntt, s_tok, s_dst, p_tp, s_id, ach, ndk=16):
    for tt in range(ntt):
        ser.dma(nc.sync, s_tok[:, 0:ndk * 128],
                x_tm[t0 + tt * 128: t0 + (tt + 1) * 128, :])
        for k in range(ndk):
            ser.step(nc.tensor, lambda e, k=k: e.transpose(
                p_tp[:], s_tok[:, k * 128:(k + 1) * 128], s_id[:]))
            ser.step(nc.vector, lambda e, k=k, tt=tt: e.tensor_copy(
                s_dst[:, k * ach + tt * 128: k * ach + (tt + 1) * 128], p_tp[:]))


def emit_ln(ser, nc, s_s, s_bf, s_st, s_onc, s_onr, p_s, p_rep, s_out, eps, w=512):
    for k in range(16):
        ser.step(nc.scalar, lambda e, k=k: e.copy(
            s_bf[:, k * w:(k + 1) * w], s_s[:, k * w:(k + 1) * w]))
    for k in range(16):
        ser.step(nc.tensor, lambda e, k=k: e.matmul(
            p_s[0:1, 0:w], s_onc[:, 0:1], s_bf[:, k * w:(k + 1) * w],
            start=(k == 0), stop=(k == 15)))
    for k in range(16):
        ser.step(nc.scalar, lambda e, k=k: e.activation(
            s_bf[:, k * w:(k + 1) * w], s_s[:, k * w:(k + 1) * w], AF.Square))
    for k in range(16):
        ser.step(nc.tensor, lambda e, k=k: e.matmul(
            p_s[0:1, w:2 * w], s_onc[:, 0:1], s_bf[:, k * w:(k + 1) * w],
            start=(k == 0), stop=(k == 15)))
    mu, e2, va, rs, nm = (s_st[0:1, i * w:(i + 1) * w] for i in range(5))
    ser.step(nc.vector, lambda e: e.tensor_scalar_mul(mu, p_s[0:1, 0:w], 1.0 / 2048))
    ser.step(nc.vector, lambda e: e.tensor_scalar_mul(e2, p_s[0:1, w:2 * w], 1.0 / 2048))
    ser.step(nc.vector, lambda e: e.tensor_mul(va, mu, mu))
    ser.step(nc.vector, lambda e: e.tensor_sub(va, e2, va))
    ser.step(nc.vector, lambda e: e.tensor_scalar_add(va, va, float(eps)))
    ser.step(nc.scalar, lambda e: e.activation(va, va, AF.Ln))
    ser.step(nc.vector, lambda e: e.tensor_scalar_mul(va, va, -0.5))
    ser.step(nc.scalar, lambda e: e.activation(rs, va, AF.Exp))
    ser.step(nc.vector, lambda e: e.tensor_mul(nm, mu, rs))
    ser.step(nc.vector, lambda e: e.tensor_scalar_mul(nm, nm, -1.0))
    ser.step(nc.tensor, lambda e: e.matmul(
        p_rep[:, 0:w], s_onr[0:1, :], rs, start=True, stop=True))
    ser.step(nc.tensor, lambda e: e.matmul(
        p_rep[:, w:2 * w], s_onr[0:1, :], nm, start=True, stop=True))
    for k in range(16):
        ser.step(nc.vector, lambda e, k=k: e.tensor_mul(
            s_out[:, k * w:(k + 1) * w], s_s[:, k * w:(k + 1) * w], p_rep[:, 0:w]))
        ser.step(nc.vector, lambda e, k=k: e.tensor_add(
            s_out[:, k * w:(k + 1) * w], s_out[:, k * w:(k + 1) * w],
            p_rep[:, w:2 * w]))


def emit_W(nc, P, t):
    wupT = t["wupT_sl"]
    wgT = t["wgT_sl"]
    wdT = t["wdT_sl"]
    h128 = t["h128"]
    up_t = t["up_t"]
    g_t = t["g_t"]
    dn_t = t["dn_t"]

    with ExitStack() as stack:
        def sb(name, shape, dt=F32):
            return stack.enter_context(nc.sbuf_tensor(_uniq(name), shape, dt))
        s_w = sb("s_w", [128, 16 * 1024])
        s_h = sb("s_h", [128, 128])
        s_t = sb("s_t", [128, 1024])
        s_o = sb("s_o", [128, 16 * 1024], BF16)
        p_m = stack.enter_context(nc.psum_tensor(_uniq("p_m"), [128, 512], F32))
        ser = P
        ser.dma(nc.sync, s_h[:], h128)

        def transform(srcT, dst, ntile, width, kind):
            ser.dmas(nc.sync, [
                (s_w[:, i * width:(i + 1) * width], srcT[i * 128:(i + 1) * 128, :])
                for i in range(ntile)])
            for i in range(ntile):
                for cqq in range(0, width, 512):
                    cw = min(512, width - cqq)
                    ser.step(nc.tensor, lambda e, i=i, cq=cqq, cw=cw: e.matmul(
                        p_m[:, 0:cw], s_h[:], s_w[:, i * width + cq: i * width + cq + cw],
                        start=True, stop=True))
                    ser.step(nc.vector, lambda e, i=i, cq=cqq, cw=cw: e.tensor_copy(
                        s_w[:, i * width + cq: i * width + cq + cw], p_m[:, 0:cw]))
            h = 1
            while h < ntile:
                for base in range(0, ntile, 2 * h):
                    for j in range(h):
                        a = (base + j) * width
                        b2 = (base + j + h) * width
                        ser.step(nc.vector, lambda e, a=a, b2=b2: e.tensor_sub(
                            s_t[:, 0:width], s_w[:, a:a + width], s_w[:, b2:b2 + width]))
                        ser.step(nc.vector, lambda e, a=a, b2=b2: e.tensor_add(
                            s_w[:, a:a + width], s_w[:, a:a + width], s_w[:, b2:b2 + width]))
                        ser.step(nc.vector, lambda e, b2=b2: e.tensor_copy(
                            s_w[:, b2:b2 + width], s_t[:, 0:width]))
                h *= 2
            for i in range(ntile):
                ser.step(nc.vector, lambda e, i=i: e.tensor_copy(
                    s_o[:, i * width:(i + 1) * width], s_w[:, i * width:(i + 1) * width]))
            if kind == "up":
                for ft in range(8):
                    ser.dma(nc.sync,
                            dst[ft * 16 * 128:(ft + 1) * 16 * 128, :].rearrange(
                                "(kt p) f -> p kt f", p=128),
                            s_o.rearrange("p (kt g) -> p kt g", g=1024)[
                                :, :, ft * 128:(ft + 1) * 128])
            else:
                for dt_ in range(2):
                    ser.dma(nc.sync,
                            dst[dt_ * 64 * 128:(dt_ + 1) * 64 * 128, :].rearrange(
                                "(fk p) f -> p fk f", p=128),
                            s_o.rearrange("p (fk g) -> p fk g", g=256)[
                                :, :, dt_ * 128:(dt_ + 1) * 128])

        transform(wupT, up_t, 16, 1024, "up")
        transform(wgT, g_t, 16, 1024, "up")
        transform(wdT, dn_t, 64, 256, "dn")
        for name, rows in PACK_SECTIONS:
            cin = t[f"cv_{name}_in"]
            cout = t[f"cv_{name}_out"]
            n = rows // 8
            csz = 4096 if n % 4096 == 0 else n // 4 if n % 4 == 0 else n
            r = 0
            while r < n:
                c = min(csz, n - r)
                ser.dma(nc.gpsimd, cout[r:r + c, :], cin[r:r + c, :])
                r += c
        P.barrier()


PACK_SECTIONS = (("qkv", 98304), ("pr", 32768), ("ot", 32768),
                 ("g", 147456), ("u", 147456), ("d", 147456))


def host_packs_f32(inp):
    pk = {}
    pk["qkv"] = sp_pack(np.ascontiguousarray(np.asarray(inp["L1_attn_W_qkv"], np.float32).T))
    pk["pr"] = sp_pack(np.ascontiguousarray(np.asarray(inp["L1_attn_W_proj"], np.float32).T))
    pk["ot"] = sp_pack(np.ascontiguousarray(np.asarray(inp["L0_attn_out_proj_w"], np.float32).T))
    wg = np.asarray(inp["L1_ffn_w_gate"], np.float32)
    wu = np.asarray(inp["L1_ffn_w_up"], np.float32)
    wd = np.asarray(inp["L1_ffn_w_down"], np.float32)
    pk["g"] = np.concatenate([sp_pack(wg[e]) for e in range(8)]
                             + [sp_pack(np.asarray(inp["L1_ffn_sw_gate"], np.float32))], axis=0)
    pk["u"] = np.concatenate([sp_pack(wu[e]) for e in range(8)]
                             + [sp_pack(np.asarray(inp["L1_ffn_sw_up"], np.float32))], axis=0)
    pk["d"] = np.concatenate([sp_pack(wd[e]) for e in range(8)]
                             + [sp_pack(np.asarray(inp["L1_ffn_sw_down"], np.float32))], axis=0)
    return pk


def host_inputs_W(inp):
    wupT = np.ascontiguousarray(np.asarray(inp["L0_ffn_W_up"], np.float32).T)
    wgT = np.ascontiguousarray(np.asarray(inp["L0_ffn_W_gate"], np.float32).T)
    wdT = np.ascontiguousarray(np.asarray(inp["L0_ffn_W_down"], np.float32).T)
    h = hadamard(128)
    pk = host_packs_f32(inp)
    maps = []
    for c in range(8):
        m = {
            "wupT_sl": np.ascontiguousarray(wupT[:, c * 1024:(c + 1) * 1024]),
            "wgT_sl": np.ascontiguousarray(wgT[:, c * 1024:(c + 1) * 1024]),
            "wdT_sl": np.ascontiguousarray(wdT[:, c * 256:(c + 1) * 256]),
            "h128": h,
        }
        for name, rows in PACK_SECTIONS:
            n = rows // 8
            m[f"cv_{name}_in"] = np.ascontiguousarray(pk[name][c * n:(c + 1) * n])
        maps.append(m)
    return maps


def assemble_packs_bf16(resW):
    out = {}
    for name, rows in PACK_SECTIONS:
        out[name] = np.concatenate(
            [resW.results[c][f"cv_{name}_out"] for c in range(8)], axis=0)
    return out


def unpack_fold(resW):
    wup_pk = np.concatenate([resW.results[c]["up_t"] for c in range(8)], axis=0)
    wg_pk = np.concatenate([resW.results[c]["g_t"] for c in range(8)], axis=0)
    wdn_pk = np.concatenate([resW.results[c]["dn_t"] for c in range(8)], axis=0)
    return wup_pk, wg_pk, wdn_pk


def emit_B(nc, P, t):
    yT = t["yT"]
    wot_pk = t["wot_pk"]
    otb = t["otb"]
    x_tm = t["x_tm"]
    ident = t["ident"]
    x1T = t["x1T"]

    with ExitStack() as stack:
        def sb(name, shape, dt=F32):
            return stack.enter_context(nc.sbuf_tensor(_uniq(name), shape, dt))
        s_y = sb("s_y", [128, 16 * TC], BF16)
        s_wb = sb("s_wb", [128, 4 * 2048], BF16)
        s_otb = sb("s_otb", [128, 16])
        s_id = sb("s_id", [128, 128])
        s_tok = sb("s_tok", [128, 2 * D])
        s_xr = sb("s_xr", [128, 16 * 512])
        s_s = sb("s_s", [128, 16 * 512])
        s_bf = sb("s_bf", [128, 16 * 512], BF16)
        s_st = sb("s_st", [1, 8 * 512])
        s_onc = sb("s_onc", [128, 1], BF16)
        s_onr = sb("s_onr", [1, 128])
        s_x1 = s_xr
        p = stack.enter_context(nc.psum_tensor(_uniq("p"), [128, 4096], F32))
        PTP = [p[:, 0:128], p[:, 512:640]]
        PM = [p[:, 1024:1536], p[:, 1536:2048]]
        p_s = p[0:1, 2048:3072]
        p_rep = p[:, 3072:4096]
        ser = P

        ser.dmas(nc.sync, [(s_otb[:], otb), (s_id[:], ident)])
        ser.step(nc.vector, lambda e: e.memset(s_onc[:, 0:1], 1.0))
        ser.step(nc.vector, lambda e: e.memset(s_onr[0:1, :], 1.0))
        ser.dmas(nc.sync, [
            (s_y[:, k * TC:(k + 1) * TC], yT[k * 128:(k + 1) * 128, :])
            for k in range(16)])
        P.barrier()

        NLD = 16
        NTC = TC // 512
        li = [0]
        ld_done = [0] * (NLD * NTC)
        ld_cnt = [0] * (NLD * NTC)

        def issue_loads(upto):
            while li[0] < min(upto, NLD * NTC):
                i = li[0]
                mt = i % NLD
                slot = i % 4
                srcv = wot_pk[mt * 16 * 128:(mt + 1) * 16 * 128, :].rearrange(
                    "(kt p) f -> p kt f", p=128)
                dstv = s_wb[:, slot * 2048: slot * 2048 + 2048].rearrange(
                    "p (kt f) -> p kt f", f=128)
                w = []
                if i >= 4:
                    w = [("tensor", ld_done[i - 4])]
                qn = "dgp" if (i % 2 == 0) else "dsync"
                ld_cnt[i] = P.dma_q(qn, dstv, srcv, waits=w, sem=f"slw{i % 4}")
                li[0] += 1

        tp_done = [0, 0]
        ev_v = [0, 0]
        xld_cnt = [0, 0]
        xdma_free = [0, 0]
        ev_cnt = [0, 0]
        for tc in range(NTC):
            t0 = tc * 512
            for tt in range(4):
                tb = tt % 2
                xld_cnt[tb] = P.dma_q(
                    "dsync", s_tok[:, tb * D: tb * D + D],
                    x_tm[t0 + tt * 128: t0 + (tt + 1) * 128, :],
                    waits=[("tensor", xdma_free[tb])], sem=f"xld{tb}")
                for k in range(16):
                    j = k % 2
                    w = [(f"xld{tb}", xld_cnt[tb]), ("vector", ev_v[j])]
                    tp_done[j] = P.emit("tensor", lambda e, k=k, tb=tb, j=j: e.transpose(
                        PTP[j], s_tok[:, tb * D + k * 128: tb * D + (k + 1) * 128],
                        s_id[:]), waits=w)
                    ev_v[j] = P.emit("vector", lambda e, k=k, tt=tt, j=j: e.tensor_copy(
                        s_xr[:, k * 512 + tt * 128: k * 512 + (tt + 1) * 128], PTP[j]),
                        waits=[("tensor", tp_done[j])])
                xdma_free[tb] = P.cnt["tensor"]
            xrv = P.cnt["vector"]
            for mt in range(16):
                gi = tc * NLD + mt
                issue_loads(gi + 4)
                slot = gi % 4
                par = mt % 2
                for k in range(16):
                    w = []
                    if k == 0:
                        w = [(f"slw{gi % 4}", ld_cnt[gi]), ("vector", ev_cnt[par])]
                    P.emit("tensor", lambda e, mt=mt, k=k, slot=slot, par=par: e.matmul(
                        PM[par], s_wb[:, slot * 2048 + k * 128: slot * 2048 + (k + 1) * 128],
                        s_y[:, k * TC + t0: k * TC + t0 + 512],
                        start=(k == 0), stop=(k == 15)),
                        waits=w, inc=(k == 15))
                tm = P.cnt["tensor"]
                ld_done[gi] = tm
                ev_cnt[par] = P.emit("vector", lambda e, mt=mt, par=par: e.scalar_tensor_tensor(
                    s_s[:, mt * 512:(mt + 1) * 512], PM[par], s_otb[:, mt:mt + 1],
                    s_xr[:, mt * 512:(mt + 1) * 512], ALU.add, ALU.add),
                    waits=[("tensor", tm), ("vector", xrv)])
            P.barrier()
            emit_ln(ser, nc, s_s, s_bf, s_st, s_onc, s_onr, p_s, p_rep,
                    s_x1, LN_EPS)
            ser.dmas(nc.sync, [
                (x1T[k * 128:(k + 1) * 128, t0:t0 + 512],
                 s_x1[:, k * 512:(k + 1) * 512]) for k in range(16)])
            P.barrier()
        P.barrier()


def emit_C(nc, P, t):
    x1T = t["x1T"]
    wup_pk = t["wup_pk"]
    wg_pk = t["wg_pk"]
    wdn_pk = t["wdn_pk"]
    su_ug = t["su_ug"]
    wsg = t["wsg"]
    wsu = t["wsu"]
    wsd = t["wsd"]
    x2T = t["x2T"]

    with ExitStack() as stack:
        def sb(name, shape, dt=F32):
            return stack.enter_context(nc.sbuf_tensor(_uniq(name), shape, dt))
        s_x1 = sb("s_x1", [128, 16 * 512])
        s_xs = sb("s_xs", [128, 16 * 512], BF16)
        s_su = sb("s_su", [128, 16])
        s_wsg = sb("s_wsg", [128, 64])
        s_wsu = sb("s_wsu", [128, 64])
        s_wsd = sb("s_wsd", [128, 16])
        s_wb = sb("s_wb", [128, 4 * 2048], BF16)
        s_sg = sb("s_sg", [128, 2 * 512])
        s_z = sb("s_z", [128, 64 * 512], BF16)
        s_s = sb("s_s", [128, 16 * 512])
        s_bf = sb("s_bf", [128, 16 * 512], BF16)
        s_st = sb("s_st", [1, 8 * 512])
        s_onc = sb("s_onc", [128, 1], BF16)
        s_onr = sb("s_onr", [1, 128])
        s_x2 = s_x1
        p = stack.enter_context(nc.psum_tensor(_uniq("p"), [128, 4096], F32))
        PG = [p[:, 0:512], p[:, 512:1024]]
        PU = [p[:, 1024:1536], p[:, 1536:2048]]
        PD = [p[:, 2048:2560], p[:, 2560:3072]]
        p_rep = p[:, 3072:4096]
        p_s = p[0:1, 2048:3072]
        ser = P

        ser.dmas(nc.sync, [(s_su[:], su_ug), (s_wsg[:], wsg), (s_wsu[:], wsu),
                           (s_wsd[:], wsd)])
        ser.step(nc.vector, lambda e: e.memset(s_onc[:, 0:1], 1.0))
        ser.step(nc.vector, lambda e: e.memset(s_onr[0:1, :], 1.0))

        loads = []
        for ft in range(64):
            loads.append((wg_pk, ft * 16 * 128, 16))
            loads.append((wup_pk, ft * 16 * 128, 16))
        for dt_ in range(16):
            for sub in range(4):
                loads.append((wdn_pk, dt_ * 64 * 128 + sub * 16 * 128, 16))
        NLD = len(loads)
        NTC = TC // 512
        li = [0]
        ld_done = [0] * (NLD * NTC)
        ld_cnt = [0] * (NLD * NTC)

        def issue_loads(upto):
            while li[0] < min(upto, NLD * NTC):
                i = li[0]
                pk, row0, Kt = loads[i % NLD]
                slot = i % 4
                srcv = pk[row0:row0 + Kt * 128, :].rearrange("(kt p) f -> p kt f", p=128)
                dstv = s_wb[:, slot * 2048: slot * 2048 + Kt * 128].rearrange(
                    "p (kt f) -> p kt f", f=128)
                w = []
                if i >= 4:
                    w = [("tensor", ld_done[i - 4])]
                qn = "dgp" if (i % 2 == 0) else "dsync"
                ld_cnt[i] = P.dma_q(qn, dstv, srcv, waits=w, sem=f"slw{i % 4}")
                li[0] += 1

        for tc in range(NTC):
            t0 = tc * 512
            ser.dmas(nc.sync, [
                (s_x1[:, k * 512:(k + 1) * 512], x1T[k * 128:(k + 1) * 128, t0:t0 + 512])
                for k in range(16)])
            for k in range(16):
                ser.step(nc.vector, lambda e, k=k: e.tensor_scalar_mul(
                    s_xs[:, k * 512:(k + 1) * 512], s_x1[:, k * 512:(k + 1) * 512],
                    s_su[:, k:k + 1]))
            P.barrier()

            silu_cnt = [0, 0]
            sg_free = [0, 0]
            z_cnt = [0, 0]
            evac_cnt = [0, 0]
            z_done = 0
            gi = tc * NLD
            for ft in range(64):
                par = ft % 2
                issue_loads(gi + 4)
                slot = gi % 4
                for k in range(16):
                    w = []
                    if k == 0:
                        w = [(f"slw{gi % 4}", ld_cnt[gi]), ("scalar", silu_cnt[par])]
                    P.emit("tensor", lambda e, k=k, slot=slot, par=par: e.matmul(
                        PG[par], s_wb[:, slot * 2048 + k * 128: slot * 2048 + (k + 1) * 128],
                        s_xs[:, k * 512:(k + 1) * 512],
                        start=(k == 0), stop=(k == 15)),
                        waits=w, inc=(k == 15))
                tg = P.cnt["tensor"]
                ld_done[gi] = tg
                gi += 1
                issue_loads(gi + 4)
                slot = gi % 4
                for k in range(16):
                    w = []
                    if k == 0:
                        w = [(f"slw{gi % 4}", ld_cnt[gi]), ("vector", z_cnt[par])]
                    P.emit("tensor", lambda e, k=k, slot=slot, par=par: e.matmul(
                        PU[par], s_wb[:, slot * 2048 + k * 128: slot * 2048 + (k + 1) * 128],
                        s_xs[:, k * 512:(k + 1) * 512],
                        start=(k == 0), stop=(k == 15)),
                        waits=w, inc=(k == 15))
                tu = P.cnt["tensor"]
                ld_done[gi] = tu
                gi += 1
                silu_cnt[par] = P.emit("scalar", lambda e, par=par, ft=ft: e.activation(
                    s_sg[:, par * 512: par * 512 + 512], PG[par], AF.Silu,
                    scale=s_wsg[:, ft:ft + 1]),
                    waits=[("tensor", tg), ("vector", sg_free[par])])
                z_cnt[par] = P.emit("vector", lambda e, par=par, ft=ft: e.scalar_tensor_tensor(
                    s_z[:, ft * 512:(ft + 1) * 512], PU[par], s_wsu[:, ft:ft + 1],
                    s_sg[:, par * 512: par * 512 + 512], ALU.mult, ALU.mult),
                    waits=[("tensor", tu), ("scalar", silu_cnt[par])])
                sg_free[par] = z_cnt[par]
                if ft == 63:
                    z_done = z_cnt[par]
            for dt_ in range(16):
                par2 = dt_ % 2
                for fk in range(64):
                    sub = fk // 16
                    if fk % 16 == 0:
                        issue_loads(gi + sub + 4)
                    sl = (gi + sub) % 4
                    fo = fk % 16
                    w = []
                    if fk == 0:
                        w = [(f"slw{sl}", ld_cnt[gi]),
                             ("vector", max(z_done, evac_cnt[par2]))]
                    elif fk % 16 == 0:
                        w = [(f"slw{sl}", ld_cnt[gi + sub])]
                    P.emit("tensor", lambda e, fk=fk, sl=sl, fo=fo, par2=par2: e.matmul(
                        PD[par2], s_wb[:, sl * 2048 + fo * 128: sl * 2048 + (fo + 1) * 128],
                        s_z[:, fk * 512:(fk + 1) * 512],
                        start=(fk == 0), stop=(fk == 63)),
                        waits=w, inc=(fk % 16 == 15))
                    if fk % 16 == 15:
                        ld_done[gi + sub] = P.cnt["tensor"]
                td = P.cnt["tensor"]
                gi += 4
                evac_cnt[par2] = P.emit("vector", lambda e, dt_=dt_, par2=par2: e.scalar_tensor_tensor(
                    s_s[:, dt_ * 512:(dt_ + 1) * 512], PD[par2], s_wsd[:, dt_:dt_ + 1],
                    s_x1[:, dt_ * 512:(dt_ + 1) * 512], ALU.mult, ALU.add),
                    waits=[("tensor", td)])
            P.barrier()
            emit_ln(ser, nc, s_s, s_bf, s_st, s_onc, s_onr, p_s, p_rep,
                    s_x2, LN_EPS)
            ser.dmas(nc.sync, [
                (x2T[k * 128:(k + 1) * 128, t0:t0 + 512],
                 s_x2[:, k * 512:(k + 1) * 512]) for k in range(16)])
        P.barrier()


def host_scales_C(inp):
    wsc = np.asarray(inp["L0_ffn_Wscale_ug"], np.float32)
    sudp = np.asarray(inp["L0_ffn_SU_dp"], np.float32)
    wsg = np.ascontiguousarray((wsc[INTER:] / np.sqrt(D)).reshape(64, 128).T)
    wsu = np.ascontiguousarray((wsc[:INTER] * sudp / np.sqrt(D)).reshape(64, 128).T)
    wsd = np.ascontiguousarray(
        (np.asarray(inp["L0_ffn_Wscale_dp"], np.float32) / np.sqrt(INTER)).reshape(16, 128).T)
    suug = np.ascontiguousarray(np.asarray(inp["L0_ffn_SU_ug"], np.float32).reshape(16, 128).T)
    return suug, wsg, wsu, wsd


def emit_D(nc, P, t, on_kind=None):
    x2T = t["x2T"]
    wqkv_pk = t["wqkv_pk"]
    qkvT = t["qkvT"]

    with ExitStack() as stack:
        def sb(name, shape, dt=F32):
            return stack.enter_context(nc.sbuf_tensor(_uniq(name), shape, dt))
        s_x = sb("s_x", [128, 16 * TC], BF16)
        s_wb = sb("s_wb", [128, 4 * 2048], BF16)
        s_o = sb("s_o", [128, 2 * TC], BF16)
        p = stack.enter_context(nc.psum_tensor(_uniq("p"), [128, 4096], F32))
        PM = [p[:, 0:512], p[:, 512:1024]]
        ser = P
        ser.dmas(nc.gpsimd, [
            (s_x[:, k * TC:(k + 1) * TC], x2T[k * 128:(k + 1) * 128, :])
            for k in range(16)])
        P.barrier()

        li = [0]
        ld_done = [0] * 48
        ld_cnt = [0] * 48

        def issue_loads(upto):
            while li[0] < min(upto, 48):
                i = li[0]
                slot = i % 4
                srcv = wqkv_pk[i * 16 * 128:(i + 1) * 16 * 128, :].rearrange(
                    "(kt p) f -> p kt f", p=128)
                dstv = s_wb[:, slot * 2048: slot * 2048 + 2048].rearrange(
                    "p (kt f) -> p kt f", f=128)
                w = []
                if i >= 4:
                    w = [("tensor", ld_done[i - 4])]
                qn = "dgp" if (i % 2 == 0) else "dsync"
                ld_cnt[i] = P.dma_q(qn, dstv, srcv, waits=w, sem=f"slw{i % 4}")
                li[0] += 1

        ev_cnt = [0, 0]
        ost_cnt = [0, 0]
        for mt in range(48):
            issue_loads(mt + 4)
            slot = mt % 4
            oh = mt % 2
            for tc in range(TC // 512):
                par = tc % 2
                for k in range(16):
                    w = []
                    if k == 0:
                        w = [(f"slw{slot}", ld_cnt[mt]), ("scalar", ev_cnt[par])]
                    P.emit("tensor", lambda e, k=k, tc=tc, slot=slot, par=par: e.matmul(
                        PM[par], s_wb[:, slot * 2048 + k * 128: slot * 2048 + (k + 1) * 128],
                        s_x[:, k * TC + tc * 512: k * TC + (tc + 1) * 512],
                        start=(k == 0), stop=(k == 15)),
                        waits=w, inc=(k == 15))
                tm = P.cnt["tensor"]
                if tc == 3:
                    ld_done[mt] = tm
                w = [("tensor", tm)]
                if tc == 0:
                    w.append((f"ost{oh}", ost_cnt[oh]))
                ev_cnt[par] = P.emit("scalar", lambda e, tc=tc, oh=oh, par=par: e.copy(
                    s_o[:, oh * TC + tc * 512: oh * TC + (tc + 1) * 512], PM[par]),
                    waits=w)
            ost_cnt[oh] = P.dma_q("dsync", qkvT[mt * 128:(mt + 1) * 128, :],
                                  s_o[:, oh * TC:(oh + 1) * TC],
                                  waits=[("scalar", ev_cnt[(TC // 512 - 1) % 2])],
                                  sem=f"ost{oh}")
            if on_kind is not None and mt % 16 == 15:
                on_kind(mt // 16, ost_cnt[0], ost_cnt[1])
        P.barrier()


def emit_E(nc, P, t):
    qkvg = t["qkvg"]
    maskw = t["maskw"]
    ident_bf = t["ident_bf"]
    y_tm = t["y_tm"]

    NKV = T // 128
    with ExitStack() as stack:
        def sb(name, shape, dt=F32):
            return stack.enter_context(nc.sbuf_tensor(_uniq(name), shape, dt))
        s_q = sb("s_q", [128, 4 * T], BF16)
        s_k = sb("s_k", [128, 4 * T], BF16)
        s_v = sb("s_v", [128, NKV * 4 * 129], BF16)
        s_vt = sb("s_vt", [128, 2 * 2048], BF16)
        s_m = sb("s_m", [128, 896], BF16)
        s_p = sb("s_p", [128, 2 * 512], BF16)
        s_sqq = s_p[:, 0:512]
        s_st = sb("s_st", [1, 64])
        s_onc = sb("s_onc", [128, 1], BF16)
        s_onr = sb("s_onr", [1, 128])
        s_scl = sb("s_scl", [128, 1])
        s_mh = sb("s_mh", [128, 1])
        s_r = sb("s_r", [128, 4])
        s_yo = sb("s_yo", [128, 2 * 512], BF16)
        s_id = sb("s_id", [128, 128], BF16)
        p = stack.enter_context(nc.psum_tensor(_uniq("p"), [128, 3072], F32))
        PS = [p[:, 0:512], p[:, 512:1024]]
        PO = [p[:, 1024 + j * 512: 1024 + j * 512 + 129] for j in range(4)]
        p_s1 = p[0:1, 1024:1536]
        p_b = p[:, 1536:1537]
        p_bf = stack.enter_context(nc.psum_tensor(_uniq("p_bf"), [128, 2048], BF16))
        PTB = [p_bf[:, 0:128], p_bf[:, 1024:1152]]
        ser = P

        ser.dmas(nc.sync, [(s_m[:], maskw), (s_id[:], ident_bf)])
        ser.step(nc.vector, lambda e: e.memset(s_onc[:, 0:1], 1.0))
        ser.step(nc.vector, lambda e: e.memset(s_onr[0:1, :], 1.0))
        ser.step(nc.vector, lambda e: e.memset(s_scl[:, 0:1], 0.08838834764831845))
        for h in range(4):
            ser.dmas(nc.sync, [
                (s_q[:, h * T + r * 2048: h * T + (r + 1) * 2048],
                 qkvg[r * 1536 + h * 128: r * 1536 + (h + 1) * 128, :]) for r in range(4)])
            ser.dmas(nc.sync, [
                (s_k[:, h * T + r * 2048: h * T + (r + 1) * 2048],
                 qkvg[r * 1536 + 512 + h * 128: r * 1536 + 512 + (h + 1) * 128, :])
                for r in range(4)])
        tpd = [0, 0]
        evd = [0, 0]
        for h in range(4):
            for rp in range(2):
                ser.dmas(nc.sync, [
                    (s_vt[:, r2 * 2048:(r2 + 1) * 2048],
                     qkvg[(rp * 2 + r2) * 1536 + 1024 + h * 128:
                          (rp * 2 + r2) * 1536 + 1024 + (h + 1) * 128, :])
                    for r2 in range(2)])
                for r2 in range(2):
                    r = rp * 2 + r2
                    for kvl in range(16):
                        j = kvl % 2
                        tpd[j] = P.emit("tensor", lambda e, r2=r2, kvl=kvl, j=j: e.transpose(
                            PTB[j], s_vt[:, r2 * 2048 + kvl * 128: r2 * 2048 + (kvl + 1) * 128],
                            s_id[:]),
                            waits=[("scalar", evd[j]), ("dsync", P.cnt["dsync"])])
                        kvb = r * 16 + kvl
                        evd[j] = P.emit("scalar", lambda e, kvb=kvb, h=h, j=j: e.copy(
                            s_v[:, (kvb * 4 + h) * 129:(kvb * 4 + h) * 129 + 128], PTB[j]),
                            waits=[("tensor", tpd[j])])
        ser.step(nc.vector, lambda e: e.memset(
            s_v.rearrange("p (t c) -> p t c", c=129)[:, :, 128:129], 1.0))
        P.barrier()

        exp_free = [0, 0]
        av_free = [0, 0]
        yo_free = [0, 0]
        for h in range(4):
            for which, buf in ((0, s_q), (1, s_k)):
                for tcq in range(T // 512):
                    ser.step(nc.scalar, lambda e, buf=buf, h=h, tcq=tcq: e.activation(
                        s_sqq[:], buf[:, h * T + tcq * 512: h * T + (tcq + 1) * 512], AF.Square))
                    ser.step(nc.tensor, lambda e: e.matmul(
                        p_s1[0:1, :], s_onc[:, 0:1], s_sqq[:], start=True, stop=True))
                    ser.step(nc.vector, lambda e, w=which: e.reduce_max(
                        s_st[0:1, 8 + w:9 + w], p_s1[0:1, :], mybir.AxisListType.X))
                    if tcq == 0:
                        ser.step(nc.vector, lambda e, w=which: e.tensor_copy(
                            s_st[0:1, w:w + 1], s_st[0:1, 8 + w:9 + w]))
                    else:
                        ser.step(nc.vector, lambda e, w=which: e.tensor_max(
                            s_st[0:1, w:w + 1], s_st[0:1, w:w + 1], s_st[0:1, 8 + w:9 + w]))
            ser.step(nc.vector, lambda e: e.tensor_mul(
                s_st[0:1, 2:3], s_st[0:1, 0:1], s_st[0:1, 1:2]))
            ser.step(nc.scalar, lambda e: e.activation(s_st[0:1, 2:3], s_st[0:1, 2:3], AF.Ln))
            ser.step(nc.vector, lambda e: e.tensor_scalar_mul(s_st[0:1, 2:3], s_st[0:1, 2:3], 0.5))
            ser.step(nc.scalar, lambda e: e.activation(s_st[0:1, 2:3], s_st[0:1, 2:3], AF.Exp))
            ser.step(nc.vector, lambda e: e.tensor_scalar_mul(
                s_st[0:1, 2:3], s_st[0:1, 2:3], -0.08838834764831845))
            ser.step(nc.tensor, lambda e: e.matmul(
                p_b[:, 0:1], s_onr[0:1, :], s_st[0:1, 2:3], start=True, stop=True))
            ser.step(nc.vector, lambda e: e.tensor_copy(s_mh[:, 0:1], p_b[:, 0:1]))
            P.barrier()

            po_free = [0, 0, 0, 0]

            for qc in range(T // 512):
                nkv = 4 * qc + 4
                exp_cnt = [0] * nkv
                gate_cnt = [0] * nkv
                gate_eng = [None] * nkv
                av_done = [0] * nkv
                s_done = [0] * nkv
                for i in range(nkv + 1):
                    if i < nkv:
                        par = i % 2
                        w = [("scalar", exp_free[par])]
                        s_done[i] = P.emit("tensor", lambda e, h=h, qc=qc, i=i, par=par: e.matmul(
                            PS[par], s_k[:, h * T + i * 128: h * T + (i + 1) * 128],
                            s_q[:, h * T + qc * 512: h * T + (qc + 1) * 512],
                            start=True, stop=True), waits=w)
                        wexp = [("tensor", s_done[i]), ("tensor", av_free[par])]
                        exp_cnt[i] = P.emit("scalar", lambda e, par=par: e.activation(
                            s_p[:, par * 512: par * 512 + 512], PS[par], AF.Exp,
                            bias=s_mh[:, 0:1], scale=s_scl[:, 0:1]), waits=wexp)
                        exp_free[par] = exp_cnt[i]
                        o = i * 128 - qc * 512
                        if o >= 0:
                            gate_cnt[i] = P.emit("vector", lambda e, par=par, o=o: e.tensor_mul(
                                s_p[:, par * 512: par * 512 + 512],
                                s_p[:, par * 512: par * 512 + 512],
                                s_m[:, 384 - o: 896 - o]),
                                waits=[("scalar", exp_cnt[i])])
                            gate_eng[i] = "vector"
                        else:
                            gate_cnt[i] = exp_cnt[i]
                            gate_eng[i] = "scalar"
                    if i >= 1:
                        kvb = i - 1
                        par = kvb % 2
                        for j in range(4):
                            w = [(gate_eng[kvb], gate_cnt[kvb])]
                            if kvb == 0:
                                w.append(("vector", po_free[j]))
                            P.emit("tensor", lambda e, j=j, kvb=kvb, h=h, par=par,
                                   nkv=nkv: e.matmul(
                                PO[j], s_p[:, par * 512 + j * 128: par * 512 + (j + 1) * 128],
                                s_v[:, (kvb * 4 + h) * 129:(kvb * 4 + h) * 129 + 129],
                                start=(kvb == 0), stop=(kvb == nkv - 1)),
                                waits=w, inc=(j == 3))
                        av_done[kvb] = P.cnt["tensor"]
                        av_free[kvb % 2] = av_done[kvb]
                yh = qc % 2
                for j in range(4):
                    P.emit("vector", lambda e, j=j: e.reciprocal(
                        s_r[:, j:j + 1], PO[j][:, 128:129]),
                        waits=[("tensor", av_done[nkv - 1]), (f"yst{yh}", yo_free[yh])])
                    po_free[j] = P.emit("vector", lambda e, j=j, yh=yh: e.tensor_scalar_mul(
                        s_yo[:, yh * 512 + j * 128: yh * 512 + (j + 1) * 128],
                        PO[j][:, 0:128], s_r[:, j:j + 1]))
                dst = y_tm[qc * 512:(qc + 1) * 512, h * 128:(h + 1) * 128].rearrange(
                    "(j p) c -> p j c", p=128)
                srcv = s_yo[:, yh * 512:(yh + 1) * 512].rearrange("p (j c) -> p j c", c=128)
                yo_free[yh] = P.dma_q("dsync", dst, srcv,
                                      waits=[("vector", po_free[3])], sem=f"yst{yh}")
        P.barrier()


def emit_E2(nc, P, t):
    ytmg = t["ytmg"]
    wpr_pk = t["wpr_pk"]
    x2T = t["x2T"]
    ident_bf = t["ident_bf"]
    x3T = t["x3T"]

    with ExitStack() as stack:
        def sb(name, shape, dt=F32):
            return stack.enter_context(nc.sbuf_tensor(_uniq(name), shape, dt))
        s_tok = sb("s_tok", [128, 2 * D], BF16)
        s_y = sb("s_y", [128, 16 * 512], BF16)
        s_wb = sb("s_wb", [128, 4 * 2048], BF16)
        s_id = sb("s_id", [128, 128], BF16)
        s_xr = sb("s_xr", [128, 16 * 512])
        s_s = sb("s_s", [128, 16 * 512])
        s_bf = sb("s_bf", [128, 16 * 512], BF16)
        s_st = sb("s_st", [1, 8 * 512])
        s_onc = sb("s_onc", [128, 1], BF16)
        s_onr = sb("s_onr", [1, 128])
        p = stack.enter_context(nc.psum_tensor(_uniq("p"), [128, 3072], F32))
        PM = [p[:, 0:512], p[:, 512:1024]]
        p_s = p[0:1, 1024:2048]
        p_rep = p[:, 2048:3072]
        p_bf = stack.enter_context(nc.psum_tensor(_uniq("p_bf"), [128, 2048], BF16))
        PTP = [p_bf[:, 0:128], p_bf[:, 1024:1152]]
        ser = P
        ser.dmas(nc.sync, [(s_id[:], ident_bf)])
        ser.step(nc.vector, lambda e: e.memset(s_onc[:, 0:1], 1.0))
        ser.step(nc.vector, lambda e: e.memset(s_onr[0:1, :], 1.0))
        P.barrier()

        NLD = 16
        NTC = TC // 512
        li = [0]
        ld_done = [0] * (NLD * NTC)
        ld_cnt = [0] * (NLD * NTC)

        def issue_loads(upto):
            while li[0] < min(upto, NLD * NTC):
                i = li[0]
                mt = i % NLD
                slot = i % 4
                srcv = wpr_pk[mt * 16 * 128:(mt + 1) * 16 * 128, :].rearrange(
                    "(kt p) f -> p kt f", p=128)
                dstv = s_wb[:, slot * 2048: slot * 2048 + 2048].rearrange(
                    "p (kt f) -> p kt f", f=128)
                w = []
                if i >= 4:
                    w = [("tensor", ld_done[i - 4])]
                qn = "dgp" if (i % 2 == 0) else "dsync"
                ld_cnt[i] = P.dma_q(qn, dstv, srcv, waits=w, sem=f"slw{i % 4}")
                li[0] += 1

        tp_done = [0, 0]
        ev_s = [0, 0]
        xld_cnt = [0, 0]
        xdma_free = [0, 0]
        ev_cnt = [0, 0]
        for tc in range(NTC):
            t0 = tc * 512
            ser.dmas(nc.sync, [
                (s_xr[:, k * 512:(k + 1) * 512], x2T[k * 128:(k + 1) * 128, t0:t0 + 512])
                for k in range(16)])
            for tt in range(4):
                tb = tt % 2
                for r in range(4):
                    base = r * 2048 + t0 + tt * 128
                    xld_cnt[tb] = P.dma_q(
                        "dscal", s_tok[:, tb * D + r * 512: tb * D + (r + 1) * 512],
                        ytmg[base: base + 128, :],
                        waits=([("tensor", xdma_free[tb])] if r == 0 else []),
                        sem=f"xld{tb}")
                for k in range(16):
                    j = k % 2
                    w = [(f"xld{tb}", xld_cnt[tb]), ("scalar", ev_s[j])]
                    tp_done[j] = P.emit("tensor", lambda e, k=k, tb=tb, j=j: e.transpose(
                        PTP[j], s_tok[:, tb * D + k * 128: tb * D + (k + 1) * 128],
                        s_id[:]), waits=w)
                    ev_s[j] = P.emit("scalar", lambda e, k=k, tt=tt, j=j: e.copy(
                        s_y[:, k * 512 + tt * 128: k * 512 + (tt + 1) * 128], PTP[j]),
                        waits=[("tensor", tp_done[j])])
                xdma_free[tb] = P.cnt["tensor"]
            yev = P.cnt["scalar"]
            for mt in range(16):
                gi = tc * NLD + mt
                issue_loads(gi + 4)
                slot = gi % 4
                par = mt % 2
                for k in range(16):
                    w = []
                    if k == 0:
                        w = [(f"slw{gi % 4}", ld_cnt[gi]), ("vector", ev_cnt[par]),
                             ("scalar", yev)]
                    P.emit("tensor", lambda e, mt=mt, k=k, slot=slot, par=par: e.matmul(
                        PM[par], s_wb[:, slot * 2048 + k * 128: slot * 2048 + (k + 1) * 128],
                        s_y[:, k * 512:(k + 1) * 512],
                        start=(k == 0), stop=(k == 15)),
                        waits=w, inc=(k == 15))
                tm = P.cnt["tensor"]
                ld_done[gi] = tm
                ev_cnt[par] = P.emit("vector", lambda e, mt=mt, par=par: e.tensor_add(
                    s_s[:, mt * 512:(mt + 1) * 512], PM[par],
                    s_xr[:, mt * 512:(mt + 1) * 512]),
                    waits=[("tensor", tm)])
            P.barrier()
            emit_ln(ser, nc, s_s, s_bf, s_st, s_onc, s_onr, p_s, p_rep, s_xr, LN_EPS)
            ser.dmas(nc.sync, [
                (x3T[k * 128:(k + 1) * 128, t0:t0 + 512],
                 s_xr[:, k * 512:(k + 1) * 512]) for k in range(16)])
            P.barrier()
        P.barrier()


def emit_F(nc, P, t, pipe_mode=True):
    x3T = t["x3T"]
    rmsw = t["rmsw"]
    rt_w = t["rt_w"]
    wg_pk = t["wg_pk"]
    wu_pk = t["wu_pk"]
    wd_pk = t["wd_pk"]
    sel8 = t["sel8"]
    ident = t["ident"]
    out_tm = t["out_tm"]

    with ExitStack() as stack:
        def sb(name, shape, dt=F32):
            return stack.enter_context(nc.sbuf_tensor(_uniq(name), shape, dt))
        s_x3 = sb("s_x3", [128, 16 * 512])
        s_ht = sb("s_ht", [128, 16 * 512], BF16)
        s_rw = sb("s_rw", [128, 16])
        s_rt = sb("s_rt", [128, 16 * 8], BF16)
        s_sel = sb("s_sel", [8, 1024])
        s_id = sb("s_id", [128, 128])
        s_weT = sb("s_weT", [8, 512])
        s_we = sb("s_we", [128, 2 * 512])
        s_sg = sb("s_sg", [128, 2 * 512])
        s_uw = sb("s_uw", [128, 2 * 512])
        s_z = sb("s_z", [128, 2 * 8 * 512], BF16)
        s_acc = sb("s_acc", [128, 16 * 512])
        s_bf = sb("s_bf", [128, 16 * 512], BF16)
        s_st = sb("s_st", [1, 8 * 512])
        s_onc = sb("s_onc", [128, 1], BF16)
        s_onr = sb("s_onr", [1, 128])
        s_lg = sb("s_lg", [128, 48])
        s_wb = sb("s_wb", [128, 4 * 2048], BF16)
        s_tm = sb("s_tm", [128, 16 * 128])
        p = stack.enter_context(nc.psum_tensor(_uniq("p"), [128, 4096], F32))
        PG = [p[:, 0:512], p[:, 512:1024]]
        PU = [p[:, 1024:1536], p[:, 1536:2048]]
        PD = [p[:, 2048:2560], p[:, 2560:3072]]
        PWE = p[:, 3072:3584]
        p_rep = p[:, 3072:4096]
        p_s = p[0:1, 2048:3072]
        p_r = p[:, 0:8]
        p_tp = p[:, 0:128]
        ser = P

        ser.dmas(nc.sync, [(s_rw[:], rmsw), (s_sel[:], sel8), (s_id[:], ident)])
        ser.dmas(nc.gpsimd, [
            (s_rt[:, k * 8:(k + 1) * 8], rt_w[k * 128:(k + 1) * 128, :])
            for k in range(16)])
        ser.step(nc.vector, lambda e: e.memset(s_onc[:, 0:1], 1.0))
        ser.step(nc.vector, lambda e: e.memset(s_onr[0:1, :], 1.0))

        loads = []
        for e_ in range(9):
            for ft in range(8):
                loads.append((wg_pk, e_ * 8 + ft, 16))
                loads.append((wu_pk, e_ * 8 + ft, 16))
            for dt_ in range(16):
                loads.append((wd_pk, e_ * 16 + dt_, 8))
        NLD = len(loads)
        NTC = TC // 512
        li = [0]
        ld_done = [0] * (NLD * NTC)
        ld_cnt = [0] * (NLD * NTC)

        def W_(w):
            return w if pipe_mode else P.snap()

        def issue_loads(upto):
            while li[0] < min(upto, NLD * NTC):
                i = li[0]
                pk, mt, Kt = loads[i % NLD]
                slot = i % 4
                srcv = pk[mt * Kt * 128:(mt + 1) * Kt * 128, :].rearrange(
                    "(kt p) f -> p kt f", p=128)
                dstv = s_wb[:, slot * 2048: slot * 2048 + Kt * 128].rearrange(
                    "p (kt f) -> p kt f", f=128)
                w = []
                if i >= 4:
                    w = [("tensor", ld_done[i - 4])]
                qn = "dgp" if (i % 2 == 0) else "dsync"
                ld_cnt[i] = P.dma_q(qn, dstv, srcv, waits=W_(w), sem=f"slw{i % 4}")
                li[0] += 1

        for tc in range(NTC):
            t0 = tc * 512
            ser.dmas(nc.sync, [
                (s_x3[:, k * 512:(k + 1) * 512], x3T[k * 128:(k + 1) * 128, t0:t0 + 512])
                for k in range(16)])
            for k in range(16):
                ser.step(nc.scalar, lambda e, k=k: e.activation(
                    s_bf[:, k * 512:(k + 1) * 512], s_x3[:, k * 512:(k + 1) * 512], AF.Square))
            for k in range(16):
                ser.step(nc.tensor, lambda e, k=k: e.matmul(
                    p_s[0:1, 0:512], s_onc[:, 0:1], s_bf[:, k * 512:(k + 1) * 512],
                    start=(k == 0), stop=(k == 15)))
            ms = s_st[0:1, 0:512]
            ser.step(nc.vector, lambda e: e.tensor_scalar_mul(ms, p_s[0:1, 0:512], 1.0 / 2048))
            ser.step(nc.vector, lambda e: e.tensor_scalar_add(ms, ms, RMS_EPS))
            ser.step(nc.scalar, lambda e: e.activation(ms, ms, AF.Ln))
            ser.step(nc.vector, lambda e: e.tensor_scalar_mul(ms, ms, -0.5))
            ser.step(nc.scalar, lambda e: e.activation(ms, ms, AF.Exp))
            ser.step(nc.tensor, lambda e: e.matmul(
                p_rep[:, 0:512], s_onr[0:1, :], ms, start=True, stop=True))
            for k in range(16):
                ser.step(nc.vector, lambda e, k=k: e.scalar_tensor_tensor(
                    s_ht[:, k * 512:(k + 1) * 512], s_x3[:, k * 512:(k + 1) * 512],
                    s_rw[:, k:k + 1], p_rep[:, 0:512], ALU.mult, ALU.mult))
            for tt in range(4):
                for k in range(16):
                    ser.step(nc.tensor, lambda e, k=k, tt=tt: e.matmul(
                        p_r, s_ht[:, k * 512 + tt * 128: k * 512 + (tt + 1) * 128],
                        s_rt[:, k * 8:(k + 1) * 8],
                        start=(k == 0), stop=(k == 15)))
                lg = s_lg[:, 0:8]; pe = s_lg[:, 8:16]; m1 = s_lg[:, 16:17]
                sm = s_lg[:, 17:18]; m2 = s_lg[:, 18:19]; kp = s_lg[:, 24:32]
                wi = s_lg[:, 19:20]; we8 = s_lg[:, 32:40]; nm1 = s_lg[:, 20:21]
                ser.step(nc.vector, lambda e: e.tensor_copy(lg, p_r))
                ser.step(nc.vector, lambda e: e.reduce_max(m1, lg, mybir.AxisListType.X))
                ser.step(nc.vector, lambda e: e.tensor_scalar_mul(nm1, m1, -1.0))
                ser.step(nc.scalar, lambda e: e.activation(pe, lg, AF.Exp, bias=nm1[:, 0:1]))
                ser.step(nc.vector, lambda e: e.reduce_sum(sm, pe, mybir.AxisListType.X))
                ser.step(nc.vector, lambda e: e.reciprocal(wi, sm))
                ser.step(nc.vector, lambda e: e.tensor_scalar_mul(pe, pe, wi[:, 0:1]))
                ser.step(nc.vector, lambda e: e.reduce_max(m1, pe, mybir.AxisListType.X))
                ser.step(nc.vector, lambda e: e.tensor_single_scalar(
                    kp, pe, m1[:, 0:1], ALU.is_equal))
                ser.step(nc.vector, lambda e: e.scalar_tensor_tensor(
                    we8, kp, -1e30, pe, ALU.mult, ALU.add))
                ser.step(nc.vector, lambda e: e.reduce_max(m2, we8, mybir.AxisListType.X))
                ser.step(nc.vector, lambda e: e.tensor_single_scalar(
                    kp, pe, m2[:, 0:1], ALU.is_ge))
                ser.step(nc.vector, lambda e: e.tensor_add(sm, m1, m2))
                ser.step(nc.vector, lambda e: e.reciprocal(wi, sm))
                ser.step(nc.vector, lambda e: e.tensor_mul(we8, pe, kp))
                ser.step(nc.vector, lambda e: e.tensor_scalar_mul(we8, we8, wi[:, 0:1]))
                ser.step(nc.tensor, lambda e: e.transpose(p_tp[0:8, 0:128], we8, s_id[:]))
                ser.step(nc.vector, lambda e, tt=tt: e.tensor_copy(
                    s_weT[0:8, tt * 128:(tt + 1) * 128], p_tp[0:8, 0:128]))
            P.barrier()

            silu_cnt = [0, 0]
            sg_free = [0, 0]
            uw_cnt = [0, 0]
            z_done = [0] * 9
            acc_cnt = [0, 0]
            we_evac = [0] * 9
            uw_last_e = [0] * 9
            gi = tc * NLD
            for e_ in range(9):
                zh = (e_ % 2) * 8 * 512
                if e_ < 8:
                    twe = P.emit("tensor", lambda e, e_=e_: e.matmul(
                        PWE, s_sel[:, e_ * 128:(e_ + 1) * 128], s_weT[0:8, :],
                        start=True, stop=True),
                        waits=W_([("scalar", we_evac[e_ - 1] if e_ else 0)]))
                    we_evac[e_] = P.emit("scalar", lambda e, e_=e_: e.copy(
                        s_we[:, (e_ % 2) * 512:(e_ % 2) * 512 + 512], PWE),
                        waits=W_([("tensor", twe),
                               ("vector", uw_last_e[e_ - 2] if e_ >= 2 else 0)]))
                for ft in range(8):
                    par = ft % 2
                    issue_loads(gi + 4)
                    slot = gi % 4
                    for k in range(16):
                        w = []
                        if k == 0:
                            w = [(f"slw{gi % 4}", ld_cnt[gi]), ("scalar", silu_cnt[par])]
                        P.emit("tensor", lambda e, k=k, slot=slot, par=par: e.matmul(
                            PG[par], s_wb[:, slot * 2048 + k * 128: slot * 2048 + (k + 1) * 128],
                            s_ht[:, k * 512:(k + 1) * 512],
                            start=(k == 0), stop=(k == 15)),
                            waits=W_(w), inc=(k == 15))
                    tg = P.cnt["tensor"]
                    ld_done[gi] = tg
                    gi += 1
                    issue_loads(gi + 4)
                    slot = gi % 4
                    for k in range(16):
                        w = []
                        if k == 0:
                            w = [(f"slw{gi % 4}", ld_cnt[gi]), ("vector", uw_cnt[par])]
                        P.emit("tensor", lambda e, k=k, slot=slot, par=par: e.matmul(
                            PU[par], s_wb[:, slot * 2048 + k * 128: slot * 2048 + (k + 1) * 128],
                            s_ht[:, k * 512:(k + 1) * 512],
                            start=(k == 0), stop=(k == 15)),
                            waits=W_(w), inc=(k == 15))
                    tu = P.cnt["tensor"]
                    ld_done[gi] = tu
                    gi += 1
                    silu_cnt[par] = P.emit("scalar", lambda e, par=par: e.activation(
                        s_sg[:, par * 512: par * 512 + 512], PG[par], AF.Silu),
                        waits=W_([("tensor", tg), ("vector", sg_free[par])]))
                    if e_ < 8:
                        uw_cnt[par] = P.emit("vector", lambda e, par=par: e.tensor_mul(
                            s_uw[:, par * 512: par * 512 + 512], PU[par],
                            s_we[:, (e_ % 2) * 512:(e_ % 2) * 512 + 512]),
                            waits=W_([("tensor", tu), ("scalar", we_evac[e_])]))
                        zc = P.emit("vector", lambda e, par=par, ft=ft, zh=zh: e.tensor_mul(
                            s_z[:, zh + ft * 512: zh + (ft + 1) * 512],
                            s_uw[:, par * 512: par * 512 + 512],
                            s_sg[:, par * 512: par * 512 + 512]),
                            waits=W_([("scalar", silu_cnt[par])]))
                    else:
                        zc = P.emit("vector", lambda e, par=par, ft=ft, zh=zh: e.tensor_mul(
                            s_z[:, zh + ft * 512: zh + (ft + 1) * 512],
                            PU[par], s_sg[:, par * 512: par * 512 + 512]),
                            waits=W_([("tensor", tu), ("scalar", silu_cnt[par])]))
                        uw_cnt[par] = zc
                    sg_free[par] = zc
                    uw_last_e[e_] = uw_cnt[par]
                    if ft == 7:
                        z_done[e_] = zc
                for dt_ in range(16):
                    par2 = dt_ % 2
                    issue_loads(gi + 4)
                    slot = gi % 4
                    for fk in range(8):
                        w = []
                        if fk == 0:
                            w = [(f"slw{gi % 4}", ld_cnt[gi]),
                                 ("vector", max(z_done[e_], acc_cnt[par2]))]
                        P.emit("tensor", lambda e, fk=fk, slot=slot, par2=par2, zh=zh: e.matmul(
                            PD[par2], s_wb[:, slot * 2048 + fk * 128: slot * 2048 + (fk + 1) * 128],
                            s_z[:, zh + fk * 512: zh + (fk + 1) * 512],
                            start=(fk == 0), stop=(fk == 7)),
                            waits=W_(w), inc=(fk == 7))
                    td = P.cnt["tensor"]
                    ld_done[gi] = td
                    gi += 1
                    if e_ == 0:
                        acc_cnt[par2] = P.emit("vector", lambda e, dt_=dt_, par2=par2: e.tensor_copy(
                            s_acc[:, dt_ * 512:(dt_ + 1) * 512], PD[par2]),
                            waits=W_([("tensor", td)]))
                    else:
                        acc_cnt[par2] = P.emit("vector", lambda e, dt_=dt_, par2=par2: e.tensor_add(
                            s_acc[:, dt_ * 512:(dt_ + 1) * 512],
                            s_acc[:, dt_ * 512:(dt_ + 1) * 512], PD[par2]),
                            waits=W_([("tensor", td)]))
            P.barrier()

            for k in range(16):
                ser.step(nc.vector, lambda e, k=k: e.tensor_add(
                    s_acc[:, k * 512:(k + 1) * 512], s_acc[:, k * 512:(k + 1) * 512],
                    s_x3[:, k * 512:(k + 1) * 512]))
            emit_ln(ser, nc, s_acc, s_bf, s_st, s_onc, s_onr, p_s, p_rep, s_x3, LN_EPS)
            for tt in range(4):
                for k in range(16):
                    ser.step(nc.tensor, lambda e, k=k, tt=tt: e.transpose(
                        p_tp[:, 0:128], s_x3[:, k * 512 + tt * 128: k * 512 + (tt + 1) * 128],
                        s_id[:]))
                    ser.step(nc.vector, lambda e, k=k: e.tensor_copy(
                        s_tm[:, k * 128:(k + 1) * 128], p_tp[:, 0:128]))
                ser.dma(nc.sync, out_tm[t0 + tt * 128: t0 + (tt + 1) * 128, :], s_tm[:])
        P.barrier()
        P.barrier()


FUSED_SEMS = ["slw0", "slw1", "slw2", "slw3", "xld0", "xld1", "yst",
              "ost0", "ost1", "yst0", "yst1", "cc"]


def build_fused():
    nc = bass.Bass("TRN2", target_bir_lowering=False)

    def EI(name, shape, dt=F32):
        return nc.dram_tensor(name, shape, dt, kind="ExternalInput").ap()

    xA = EI("xA", [T, D])
    xAs = EI("xAs", [T, 512])
    wdtT = EI("wdtT", [D, 512])
    dtb = EI("dtb", [128, 4])
    xpw = EI("xpw", [D, 64])
    xpb = EI("xpb", [64, 1])
    alog = EI("alog", [128, 64])
    ident = EI("ident", [128, 128])
    sel = EI("sel", [64, 4096], BF16)
    wupT_sl = EI("wupT_sl", [D, 1024])
    wgT_sl = EI("wgT_sl", [D, 1024])
    wdT_sl = EI("wdT_sl", [INTER, 256])
    h128 = EI("h128", [128, 128])
    cv_in = {name: EI(f"cv_{name}_in", [rows // 8, 128]) for name, rows in PACK_SECTIONS}
    otb = EI("otb", [128, 16])
    x_tm = EI("x_tm", [TC, D])
    su_ug = EI("su_ug", [128, 16])
    wsg = EI("wsg", [128, 64])
    wsu = EI("wsu", [128, 64])
    wsd = EI("wsd", [128, 16])
    maskw = EI("maskw", [128, 896], BF16)
    ident_bf = EI("ident_bf", [128, 128], BF16)
    rmsw = EI("rmsw", [128, 16])
    rt_w = EI("rt_w", [D, 8])
    sel8 = EI("sel8", [8, 1024])
    out_tm = nc.dram_tensor("out_tm", [TC, D], F32, kind="ExternalOutput").ap()

    up_t_loc = nc.dram_tensor("up_t_loc", [8 * 16 * 128, 128], BF16)
    g_t_loc = nc.dram_tensor("g_t_loc", [8 * 16 * 128, 128], BF16)
    dn_t_loc = nc.dram_tensor("dn_t_loc", [2 * 64 * 128, 128], BF16)
    cv_out = {name: nc.dram_tensor(f"cv_{name}_out", [rows // 8, 128], BF16)
              for name, rows in PACK_SECTIONS}
    wup_pk = nc.dram_tensor("wup_pk", [64 * 16 * 128, 128], BF16)
    wg_pk = nc.dram_tensor("wg_pk", [64 * 16 * 128, 128], BF16)
    wdn_pk = nc.dram_tensor("wdn_pk", [16 * 64 * 128, 128], BF16)
    pk = {name: nc.dram_tensor(f"pk_{name}", [rows, 128], BF16)
          for name, rows in PACK_SECTIONS}
    yTs = nc.dram_tensor("yTs", [2048, 2048], BF16)
    YG = nc.dram_tensor("YG", [8192, 2048], BF16)
    yloc = nc.dram_tensor("yloc", [2048, 2048], BF16)
    qloc = nc.dram_tensor("qloc", [6144, 2048], BF16)
    ytloc = nc.dram_tensor("ytloc", [8192, 512], BF16)
    x1T = nc.dram_tensor("x1T", [D, TC], F32)
    x2T = nc.dram_tensor("x2T", [D, TC], F32)
    x3T = nc.dram_tensor("x3T", [D, TC], F32)
    qkvT = nc.dram_tensor("qkvT", [3 * D, TC], BF16)
    QGs = [nc.dram_tensor(f"QG{s}", [4 * D, TC], BF16) for s in range(3)]
    ytm = nc.dram_tensor("ytm", [T, 512], BF16)
    YAG = nc.dram_tensor("YAG", [4 * T, 512], BF16)

    G8 = [list(range(8))]
    G4 = [[0, 1, 2, 3], [4, 5, 6, 7]]

    with ExitStack() as stack:
        P = Pipe(nc, stack)
        for n in FUSED_SEMS:
            P.new_sem(n, stack)

        def ccol(kind, groups, in_h, out_h):
            i = nc.gpsimd.collective_compute(
                kind, ALU.bypass, replica_groups=groups,
                ins=[in_h.ap().opt()], outs=[out_h.ap().opt()])
            i.then_inc(P.sem["cc"], 1)
            P.cnt["cc"] += 1

        def ccbar():
            for e2 in P.ENGS:
                P._wait(e2, getattr(nc, e2), [("cc", P.cnt["cc"])])

        tW = {"wupT_sl": wupT_sl, "wgT_sl": wgT_sl, "wdT_sl": wdT_sl, "h128": h128,
              "up_t": up_t_loc.ap(), "g_t": g_t_loc.ap(), "dn_t": dn_t_loc.ap()}
        for name, _ in PACK_SECTIONS:
            tW[f"cv_{name}_in"] = cv_in[name]
            tW[f"cv_{name}_out"] = cv_out[name].ap()
        emit_W(nc, P, tW)
        emit_A(nc, P, {"xA": xA, "xAs": xAs, "wdtT": wdtT, "dtb": dtb, "xpw": xpw,
                       "xpb": xpb, "alog": alog, "ident": ident, "sel": sel,
                       "yTs": yTs.ap()})
        P.barrier(["gpsimd"])
        ccol("AllGather", G8, up_t_loc, wup_pk)
        ccol("AllGather", G8, g_t_loc, wg_pk)
        ccol("AllGather", G8, dn_t_loc, wdn_pk)
        for name, _ in PACK_SECTIONS:
            ccol("AllGather", G8, cv_out[name], pk[name])
        ccol("AllGather", G4, yTs, YG)
        ccbar()
        d512 = (nc.sync.partition_id() % 4) * 512
        yg3 = YG.ap().rearrange("(rank x) c -> x rank c", x=2048)
        yl3 = yloc.ap().rearrange("(rank x) c -> x rank c", x=512)
        for i in range(2):
            P.dma_q("dsync", yl3[:, i * 2:(i + 1) * 2, :],
                    yg3[bass.ds(d512, 512), i * 2:(i + 1) * 2, :])
        P.barrier()

        emit_B(nc, P, {"yT": yloc.ap(), "wot_pk": pk["ot"].ap(), "otb": otb,
                       "x_tm": x_tm, "ident": ident, "x1T": x1T.ap()})
        emit_C(nc, P, {"x1T": x1T.ap(), "wup_pk": wup_pk.ap(), "wg_pk": wg_pk.ap(),
                       "wdn_pk": wdn_pk.ap(), "su_ug": su_ug, "wsg": wsg, "wsu": wsu,
                       "wsd": wsd, "x2T": x2T.ap()})
        def on_kind(kind, c0, c1):
            i = nc.gpsimd.collective_compute(
                "AllGather", ALU.bypass, replica_groups=G4,
                ins=[qkvT.ap()[kind * D:(kind + 1) * D, :].opt()],
                outs=[QGs[kind].ap().opt()])
            i.then_inc(P.sem["cc"], 1)
            P.cnt["cc"] += 1

        def on_kind_w(kind, c0, c1):
            P._wait("gpsimd", nc.gpsimd, [("ost0", c0), ("ost1", c1)])
            on_kind(kind, c0, c1)

        emit_D(nc, P, {"x2T": x2T.ap(), "wqkv_pk": pk["qkv"].ap(), "qkvT": qkvT.ap()},
               on_kind=on_kind_w)
        ccbar()
        ql3 = qloc.ap().rearrange("(rs x) c -> x rs c", x=512)
        for s in range(3):
            qg3 = QGs[s].ap().rearrange("(r x) c -> x r c", x=2048)
            P.dma_q("dsync", ql3[:, s::3, :], qg3[bass.ds(d512, 512), :, :])
        P.barrier()

        emit_E(nc, P, {"qkvg": qloc.ap(), "maskw": maskw, "ident_bf": ident_bf,
                       "y_tm": ytm.ap()})
        P.barrier(["gpsimd"])
        ccol("AllGather", G4, ytm, YAG)
        ccbar()
        d2048 = (nc.sync.partition_id() % 4) * 2048
        ya3 = YAG.ap().rearrange("(r x) c -> x r c", x=8192)
        yt3 = ytloc.ap().rearrange("(r x) c -> x r c", x=2048)
        for i in range(4):
            P.dma_q("dsync", yt3[:, i:i + 1, :],
                    ya3[bass.ds(d2048, 2048), i:i + 1, :])
        P.barrier()

        emit_E2(nc, P, {"ytmg": ytloc.ap(), "wpr_pk": pk["pr"].ap(), "x2T": x2T.ap(),
                        "ident_bf": ident_bf, "x3T": x3T.ap()})
        emit_F(nc, P, {"x3T": x3T.ap(), "rmsw": rmsw, "rt_w": rt_w,
                       "wg_pk": pk["g"].ap(), "wu_pk": pk["u"].ap(),
                       "wd_pk": pk["d"].ap(), "sel8": sel8, "ident": ident,
                       "out_tm": out_tm})
        P.barrier()
    return nc


def host_fused_inputs(inp):
    mapsW = host_inputs_W(inp)
    mapsA = host_inputs_A(inp)
    suug, wsg, wsu, wsd = host_scales_C(inp)
    x = np.asarray(inp["L0_attn_x"], np.float32)
    otb = np.ascontiguousarray(
        np.asarray(inp["L0_attn_out_proj_b"], np.float32).reshape(16, 128).T)
    rmsw = np.ascontiguousarray(
        np.asarray(inp["L1_ffn_rms_weight"], np.float32).reshape(16, 128).T)
    rtw = np.ascontiguousarray(np.asarray(inp["L1_ffn_router_w"], np.float32))
    sel8 = np.zeros((8, 1024), dtype=np.float32)
    for e in range(8):
        sel8[e, e * 128:(e + 1) * 128] = 1.0
    maskw = (np.arange(896)[None, :] - 384 >= np.arange(128)[:, None]).astype(BF)
    ident_bf = np.eye(128, dtype=np.float32).astype(BF)
    maps = []
    for c in range(8):
        b, q = c // 4, c % 4
        m = {}
        m.update(mapsW[c])
        m.update(mapsA[c])
        m.update({
            "otb": otb, "x_tm": np.ascontiguousarray(x[b][q * TC:(q + 1) * TC, :]),
            "su_ug": suug, "wsg": wsg, "wsu": wsu, "wsd": wsd,
            "maskw": maskw, "ident_bf": ident_bf,
            "rmsw": rmsw, "rt_w": rtw, "sel8": sel8,
        })
        maps.append(m)
    return maps


def kernel(**inputs):
    inp = {k: np.asarray(v) for k, v in inputs.items()}
    res = run_bass_kernel_spmd(build_fused(), host_fused_inputs(inp),
                               core_ids=list(range(8)))
    out = np.empty((B, T, D), dtype=np.float32)
    for c in range(8):
        b, q = c // 4, c % 4
        out[b, q * TC:(q + 1) * TC, :] = res.results[c]["out_tm"]
    return out


# version 08
# speedup vs baseline: 1.0909x; 1.0909x over previous
import numpy as np
import ml_dtypes
from contextlib import ExitStack
import concourse.bass as bass
import concourse.mybir as mybir
from concourse.bass_utils import run_bass_kernel_spmd

F32 = mybir.dt.float32
BF16 = mybir.dt.bfloat16
AF = mybir.ActivationFunctionType
ALU = mybir.AluOpType
BF = ml_dtypes.bfloat16

B, T, D = 2, 8192, 2048
NS = 16
INTER = 8192
NH, DH = 16, 128
NE, FE = 8, 1024
SF = 1024
TC = 2048
LN_EPS = 1e-5
RMS_EPS = 1e-6


_NAME_UNIQ = [0]


def _uniq(name):
    _NAME_UNIQ[0] += 1
    return f"{name}_{_NAME_UNIQ[0]}"


def hadamard(n):
    H = np.ones((1, 1), dtype=np.float32)
    while H.shape[0] < n:
        H = np.block([[H, H], [H, -H]])
    return H


class Ser:

    def __init__(self, nc, g, gd):
        self.nc = nc
        self.g = g
        self.gd = gd
        self.gc = 0
        self.gdc = 0

    def _wait(self, eng):
        if self.gc:
            eng.wait_ge(self.g, self.gc)
        if self.gdc:
            eng.wait_ge(self.gd, self.gdc)

    def step(self, eng, fn):
        self._wait(eng)
        fn(eng).then_inc(self.g, 1)
        self.gc += 1

    def steps(self, eng, fns):
        self._wait(eng)
        ins = None
        for fn in fns:
            ins = fn(eng)
        ins.then_inc(self.g, 1)
        self.gc += 1

    def dma(self, eng, out, in_):
        self._wait(eng)
        eng.dma_start(out=out, in_=in_).then_inc(self.gd, 16)
        self.gdc += 16

    def dmas(self, eng, pairs):
        self._wait(eng)
        for out, in_ in pairs:
            eng.dma_start(out=out, in_=in_).then_inc(self.gd, 16)
            self.gdc += 16

    def finish(self, eng):
        self._wait(eng)


class Pipe:

    ENGS = ("tensor", "vector", "scalar", "gpsimd", "sync")

    def __init__(self, nc, stack):
        self.nc = nc
        self.sem = {}
        self.cnt = {}
        self.seen = {e: {} for e in self.ENGS}
        for e in self.ENGS:
            self.sem[e] = stack.enter_context(nc.semaphore("sem_" + e))
            self.cnt[e] = 0
        for q in ("dsync", "dgp", "dscal"):
            self.sem[q] = stack.enter_context(nc.semaphore("sem_" + q))
            self.cnt[q] = 0

    def _wait(self, ename, eng, waits):
        kn = self.seen[ename]
        for s, v in waits:
            if s == ename or v <= 0:
                continue
            if kn.get(s, 0) >= v:
                continue
            eng.wait_ge(self.sem[s], v)
            kn[s] = v

    SELFSYNC = {"vector": True, "scalar": True, "gpsimd": True,
                "tensor": False, "sync": False}

    def emit(self, ename, fn, waits=(), inc=True, selfsync=None):
        eng = getattr(self.nc, ename)
        if selfsync is None:
            selfsync = self.SELFSYNC[ename]
        if selfsync and self.cnt[ename] > self.seen[ename].get(ename, 0):
            eng.wait_ge(self.sem[ename], self.cnt[ename])
            self.seen[ename][ename] = self.cnt[ename]
        self._wait(ename, eng, waits)
        ins = fn(eng)
        if inc:
            ins.then_inc(self.sem[ename], 1)
            self.cnt[ename] += 1
        return self.cnt[ename]

    def dma_q(self, qname, out, in_, waits=(), inc=True, sem=None):
        ename = {"dsync": "sync", "dgp": "gpsimd", "dscal": "scalar"}[qname]
        eng = getattr(self.nc, ename)
        self._wait(ename, eng, waits)
        ins = eng.dma_start(out=out, in_=in_)
        key = sem if sem is not None else qname
        if inc:
            ins.then_inc(self.sem[key], 16)
            self.cnt[key] += 16
        return self.cnt[key]

    def new_sem(self, name, stack):
        self.sem[name] = stack.enter_context(self.nc.semaphore("sem_" + name))
        self.cnt[name] = 0

    def snap(self):
        return [(k, v) for k, v in self.cnt.items() if v > 0]

    def lazy(self, ename, fn):
        return self.emit(ename, fn, waits=self.snap())

    def ldma(self, qname, out, in_):
        return self.dma_q(qname, out, in_, waits=self.snap())

    def barrier(self, enames=None):
        w = self.snap()
        for ename in (enames or self.ENGS):
            eng = getattr(self.nc, ename)
            self._wait(ename, eng, w)

    def _name_of(self, eng):
        if not hasattr(self, "_byid"):
            self._byid = {id(getattr(self.nc, e)): e for e in self.ENGS}
        return self._byid[id(eng)]

    def step(self, eng, fn):
        return self.lazy(self._name_of(eng), fn)

    def steps(self, eng, fns):
        ename = self._name_of(eng)
        w = self.snap()
        e = getattr(self.nc, ename)
        if self.SELFSYNC[ename] and self.cnt[ename] > self.seen[ename].get(ename, 0):
            e.wait_ge(self.sem[ename], self.cnt[ename])
            self.seen[ename][ename] = self.cnt[ename]
        self._wait(ename, e, w)
        ins = None
        for fn in fns:
            ins = fn(e)
        ins.then_inc(self.sem[ename], 1)
        self.cnt[ename] += 1
        return self.cnt[ename]

    _QOF = {"sync": "dsync", "gpsimd": "dgp", "scalar": "dscal"}

    def dma(self, eng, out, in_):
        return self.dma_q(self._QOF[self._name_of(eng)], out, in_, waits=self.snap())

    def dmas(self, eng, pairs):
        ename = self._name_of(eng)
        qname = self._QOF[ename]
        e = getattr(self.nc, ename)
        self._wait(ename, e, self.snap())
        for out, in_ in pairs:
            e.dma_start(out=out, in_=in_).then_inc(self.sem[qname], 16)
            self.cnt[qname] += 16
        return self.cnt[qname]

    def finish(self, eng):
        self._wait(self._name_of(eng), eng, self.snap())


ACH = 1024


def emit_A(nc, P, t):
    xA = t["xA"]
    xAs = t["xAs"]
    wdtT = t["wdtT"]
    dtb = t["dtb"]
    xpw = t["xpw"]
    xpb = t["xpb"]
    alog = t["alog"]
    ident = t["ident"]
    sel = t["sel"]
    yTs = t["yTs"]

    nch = T // ACH
    with ExitStack() as stack:
        def sb(name, shape, dt=F32):
            return stack.enter_context(nc.sbuf_tensor(_uniq(name), shape, dt))
        s_xtok = sb("s_xtok", [128, 2 * D])
        s_xT = sb("s_xT", [128, 16 * ACH], BF16)
        s_xs = sb("s_xs", [128, 4 * ACH])
        s_wdt = sb("s_wdt", [128, 16 * 512], BF16)
        s_xpw = sb("s_xpw", [128, 16 * 64], BF16)
        s_sel = sb("s_sel", [64, 4096], BF16)
        s_id = sb("s_id", [128, 128])
        s_dtb = sb("s_dtb", [128, 4])
        s_xpb = sb("s_xpb", [64, 1])
        s_A = sb("s_A", [128, 64])
        s_dtT = sb("s_dtT", [128, 2 * 4 * ACH])
        s_dtx = sb("s_dtx", [128, 2 * 4 * ACH])
        s_bc = sb("s_bc", [64, 2 * ACH], BF16)
        s_dA = sb("s_dA", [128, 2 * ACH])
        s_spt = sb("s_spt", [128, 512])
        s_h = sb("s_h", [128, 2 * ACH])
        s_tmp = sb("s_tmp", [128, ACH])
        s_u = s_tmp
        s_y = sb("s_y", [128, 4 * ACH])
        s_ybf = sb("s_ybf", [128, 4 * ACH], BF16)
        s_hst = sb("s_hst", [128, 64])
        p = stack.enter_context(nc.psum_tensor(_uniq("p"), [128, 4096], F32))
        PTP = [p[:, 0:128], p[:, 512:640]]
        PDT = [p[:, 1024:1536], p[:, 1536:2048]]
        PB = p[:, 2048:3072]
        PBC = p[0:64, 1024:1536]
        PC = p[:, 3072:4096]
        ser = P

        ser.dmas(nc.sync, [
            (s_sel[:], sel), (s_id[:], ident), (s_dtb[:], dtb), (s_xpb[:], xpb),
        ])
        ser.dma(nc.sync, s_A[:], alog)
        ser.step(nc.scalar, lambda e: e.activation(s_A[:], s_A[:], AF.Exp))
        ser.step(nc.vector, lambda e: e.tensor_scalar_mul(s_A[:], s_A[:], -1.0))
        ser.dmas(nc.gpsimd, [
            (s_wdt[:, (k * 4 + et) * 128:(k * 4 + et + 1) * 128],
             wdtT[k * 128:(k + 1) * 128, et * 128:(et + 1) * 128])
            for k in range(16) for et in range(4)
        ])
        ser.dmas(nc.gpsimd, [
            (s_xpw[:, k * 64:(k + 1) * 64], xpw[k * 128:(k + 1) * 128, :])
            for k in range(16)
        ])
        P.barrier()

        tp_done = [0, 0]
        ev_s = [0, 0]
        ev_v = [0, 0]
        xld_cnt = [0, 0]
        xdma_free = [0, 0]
        sp_done = [0, 0]
        dtx_free = [0, 0]
        bc_ev = [0, 0]
        dA_cnt = [0, 0]
        dA_free = [0, 0]
        hst_done = [0] * 64
        hstc_free = [0, 0]
        yst_cnt = [0]
        chk = [dict() for _ in range(nch)]

        def prep_units(ch):
            t0 = ch * ACH
            xw = ch % 2
            units = []

            def u_xT(tt):
                tb = tt % 2
                xld_cnt[tb] = P.dma_q(
                    "dsync", s_xtok[:, tb * D: tb * D + D],
                    xA[t0 + tt * 128: t0 + (tt + 1) * 128, :],
                    waits=[("tensor", xdma_free[tb])], sem=f"xld{tb}")
                for k in range(16):
                    j = k % 2
                    w = [(f"xld{tb}", xld_cnt[tb]), ("scalar", ev_s[j]), ("vector", ev_v[j])]
                    tp_done[j] = P.emit("tensor", lambda e, k=k, tb=tb, j=j: e.transpose(
                        PTP[j], s_xtok[:, tb * D + k * 128: tb * D + (k + 1) * 128],
                        s_id[:]), waits=w)
                    ev_s[j] = P.emit("scalar", lambda e, k=k, tt=tt, j=j: e.copy(
                        s_xT[:, k * ACH + tt * 128: k * ACH + (tt + 1) * 128], PTP[j]),
                        waits=[("tensor", tp_done[j])])
                xdma_free[tb] = P.cnt["tensor"]

            def u_xs(tt):
                tb = tt % 2
                xld_cnt[tb] = P.dma_q(
                    "dsync", s_xtok[:, tb * D: tb * D + 512],
                    xAs[t0 + tt * 128: t0 + (tt + 1) * 128, :],
                    waits=[("tensor", xdma_free[tb])], sem=f"xld{tb}")
                for dt4 in range(4):
                    j = dt4 % 2
                    w = [(f"xld{tb}", xld_cnt[tb]), ("scalar", ev_s[j]), ("vector", ev_v[j])]
                    tp_done[j] = P.emit("tensor", lambda e, dt4=dt4, tb=tb, j=j: e.transpose(
                        PTP[j], s_xtok[:, tb * D + dt4 * 128: tb * D + (dt4 + 1) * 128],
                        s_id[:]), waits=w)
                    ev_v[j] = P.emit("vector", lambda e, dt4=dt4, tt=tt, j=j: e.tensor_copy(
                        s_xs[:, dt4 * ACH + tt * 128: dt4 * ACH + (tt + 1) * 128], PTP[j]),
                        waits=[("tensor", tp_done[j])])
                xdma_free[tb] = P.cnt["tensor"]

            def u_dt(g):
                et, tc2 = g // 2, g % 2
                jd = g % 2
                if g == 0:
                    chk[ch]["xTev"] = P.cnt["scalar"]
                for k in range(16):
                    w = []
                    if k == 0:
                        w = [("scalar", max(chk[ch]["xTev"], sp_done[jd])),
                             ("vector", max(bc_ev[0], bc_ev[1]))]
                    P.emit("tensor", lambda e, et=et, tc2=tc2, k=k, jd=jd: e.matmul(
                        PDT[jd],
                        s_wdt[:, (k * 4 + et) * 128:(k * 4 + et + 1) * 128],
                        s_xT[:, k * ACH + tc2 * 512: k * ACH + (tc2 + 1) * 512],
                        start=(k == 0), stop=(k == 15)), waits=w, inc=(k == 15))
                tdt = P.cnt["tensor"]
                P.emit("scalar", lambda e, et=et, jd=jd: e.activation(
                    s_spt[:, 0:512], PDT[jd], AF.Exp, bias=s_dtb[:, et:et + 1]),
                    waits=[("tensor", tdt), ("vector", dtx_free[xw])])
                sp_done[jd] = P.emit("scalar", lambda e, et=et, tc2=tc2, xw=xw: e.activation(
                    s_dtT[:, xw * 4 * ACH + et * ACH + tc2 * 512:
                          xw * 4 * ACH + et * ACH + (tc2 + 1) * 512],
                    s_spt[:, 0:512], AF.Ln, bias=1.0))
                if g == 7:
                    chk[ch]["sp"] = P.cnt["scalar"]

            def u_bc(tc2):
                for k in range(16):
                    w = []
                    if k == 0:
                        w = [("scalar", max(sp_done[0], sp_done[1])),
                             ("vector", max(bc_ev[0], bc_ev[1]))]
                    P.emit("tensor", lambda e, tc2=tc2, k=k: e.matmul(
                        PBC[:, 0:512], s_xpw[:, k * 64:(k + 1) * 64],
                        s_xT[:, k * ACH + tc2 * 512: k * ACH + (tc2 + 1) * 512],
                        start=(k == 0), stop=(k == 15)), waits=w, inc=(k == 15))
                tbc = P.cnt["tensor"]
                bc_ev[xw] = P.emit("vector", lambda e, tc2=tc2, xw=xw: e.tensor_scalar_add(
                    s_bc[:, xw * ACH + tc2 * 512: xw * ACH + (tc2 + 1) * 512],
                    PBC[:, 0:512], s_xpb[:, 0:1]), waits=[("tensor", tbc)])
                if tc2 == 1:
                    chk[ch]["bc"] = P.cnt["vector"]

            for tt in range(8):
                units.append(lambda tt=tt: u_xT(tt))
            for tt in range(8):
                units.append(lambda tt=tt: u_xs(tt))
            for g in range(8):
                units.append(lambda g=g: u_dt(g))
            for tc2 in range(2):
                units.append(lambda tc2=tc2: u_bc(tc2))
            return units

        def emit_scan(ch, next_units):
            t0 = ch * ACH
            xw = ch % 2
            for dt4 in range(4):
                dtx_free[xw] = P.emit("vector", lambda e, dt4=dt4, xw=xw: e.tensor_mul(
                    s_dtx[:, xw * 4 * ACH + dt4 * ACH: xw * 4 * ACH + (dt4 + 1) * ACH],
                    s_dtT[:, xw * 4 * ACH + dt4 * ACH: xw * 4 * ACH + (dt4 + 1) * ACH],
                    s_xs[:, dt4 * ACH:(dt4 + 1) * ACH]),
                    waits=[("scalar", chk[ch]["sp"])])
            ui = [0]

            def run_units(kmax):
                while ui[0] < min(kmax, len(next_units)):
                    next_units[ui[0]]()
                    ui[0] += 1

            u_last = 0
            ym_last = 0
            for n in range(NS):
                for tc2 in range(ACH // 512):
                    w = [("vector", max(u_last, chk[ch]["bc"]))] if tc2 == 0 else []
                    P.emit("tensor", lambda e, n=n, tc2=tc2, xw=xw: e.matmul(
                        PB[:, tc2 * 512:(tc2 + 1) * 512],
                        s_sel[:, n * 128:(n + 1) * 128],
                        s_bc[:, xw * ACH + tc2 * 512: xw * ACH + (tc2 + 1) * 512],
                        start=True, stop=True), waits=w, inc=(tc2 == 1))
                repB = P.cnt["tensor"]
                for tc2 in range(ACH // 512):
                    w = [("vector", ym_last)] if tc2 == 0 else []
                    P.emit("tensor", lambda e, n=n, tc2=tc2, xw=xw: e.matmul(
                        PC[:, tc2 * 512:(tc2 + 1) * 512],
                        s_sel[:, (16 + n) * 128:(17 + n) * 128],
                        s_bc[:, xw * ACH + tc2 * 512: xw * ACH + (tc2 + 1) * 512],
                        start=True, stop=True), waits=w, inc=(tc2 == 1))
                repC = P.cnt["tensor"]
                for dt4 in range(4):
                    col = dt4 * 16 + n
                    j2 = (n * 4 + dt4) % 2
                    dA_cnt[j2] = P.emit("scalar", lambda e, dt4=dt4, col=col, j2=j2, xw=xw: e.activation(
                        s_dA[:, j2 * ACH:(j2 + 1) * ACH],
                        s_dtT[:, xw * 4 * ACH + dt4 * ACH: xw * 4 * ACH + (dt4 + 1) * ACH],
                        AF.Exp, scale=s_A[:, col:col + 1]),
                        waits=[("vector", dA_free[j2]), ("scalar", chk[ch]["sp"])])
                    u_last = P.emit("vector", lambda e, dt4=dt4, xw=xw: e.tensor_mul(
                        s_u[:], s_dtx[:, xw * 4 * ACH + dt4 * ACH: xw * 4 * ACH + (dt4 + 1) * ACH],
                        PB[:]), waits=[("tensor", repB)])
                    if ch == 0:
                        scv = P.emit("vector", lambda e, j2=j2: e.tensor_tensor_scan(
                            s_h[:, j2 * ACH:(j2 + 1) * ACH], s_dA[:, j2 * ACH:(j2 + 1) * ACH],
                            s_u[:], 0.0, ALU.mult, ALU.add),
                            waits=[("scalar", dA_cnt[j2]), ("gpsimd", hstc_free[j2])])
                    else:
                        scv = P.emit("vector", lambda e, col=col, j2=j2: e.tensor_tensor_scan(
                            s_h[:, j2 * ACH:(j2 + 1) * ACH], s_dA[:, j2 * ACH:(j2 + 1) * ACH],
                            s_u[:], s_hst[:, col:col + 1], ALU.mult, ALU.add),
                            waits=[("scalar", dA_cnt[j2]), ("gpsimd", hst_done[col]),
                                   ("gpsimd", hstc_free[j2])])
                    dA_free[j2] = scv
                    hst_done[col] = P.emit("gpsimd", lambda e, col=col, j2=j2: e.tensor_copy(
                        s_hst[:, col:col + 1], s_h[:, (j2 + 1) * ACH - 1:(j2 + 1) * ACH]),
                        waits=[("vector", scv)])
                    hstc_free[j2] = hst_done[col]
                    if n == 0:
                        w = [("tensor", repC)]
                        if ch > 0 and dt4 == 0:
                            w.append(("yst", yst_cnt[0]))
                        ym_last = P.emit("vector", lambda e, dt4=dt4, j2=j2: e.tensor_mul(
                            s_y[:, dt4 * ACH:(dt4 + 1) * ACH],
                            s_h[:, j2 * ACH:(j2 + 1) * ACH], PC[:]), waits=w)
                    else:
                        P.emit("vector", lambda e, j2=j2: e.tensor_mul(
                            s_tmp[:], s_h[:, j2 * ACH:(j2 + 1) * ACH], PC[:]),
                            waits=[("tensor", repC)])
                        ydst = s_ybf if n == NS - 1 else s_y
                        ym_last = P.emit("vector", lambda e, dt4=dt4, ydst=ydst: e.tensor_add(
                            ydst[:, dt4 * ACH:(dt4 + 1) * ACH],
                            s_y[:, dt4 * ACH:(dt4 + 1) * ACH], s_tmp[:]))
                pass
            run_units(len(next_units))
            jblk, c0 = (t0 // 2048), (t0 % 2048)
            for dt4 in range(4):
                r0 = jblk * 512 + dt4 * 128
                yst_cnt[0] = P.dma_q("dsync", yTs[r0:r0 + 128, c0:c0 + ACH],
                                     s_ybf[:, dt4 * ACH:(dt4 + 1) * ACH],
                                     waits=[("vector", ym_last)], sem="yst")

        for u in prep_units(0):
            u()
        for ch in range(nch):
            nxt = prep_units(ch + 1) if ch + 1 < nch else []
            emit_scan(ch, nxt)
        P.barrier()

def host_inputs_A(inp):
    x = np.asarray(inp["L0_attn_x"], dtype=np.float32)
    wdtT = np.ascontiguousarray(np.asarray(inp["L0_attn_dt_proj_w"], np.float32).T)
    dtb = np.asarray(inp["L0_attn_dt_proj_b"], np.float32)
    xpwT = np.ascontiguousarray(np.asarray(inp["L0_attn_x_proj_w"], np.float32).T)
    xpb = np.asarray(inp["L0_attn_x_proj_b"], np.float32)
    alog = np.asarray(inp["L0_attn_A_log"], np.float32)
    ident = np.eye(128, dtype=np.float32)
    selm = np.zeros((64, 4096), dtype=np.float32)
    for n in range(16):
        selm[n, n * 128:(n + 1) * 128] = 1.0
        selm[32 + n, (16 + n) * 128:(17 + n) * 128] = 1.0
    xpw_pad = np.zeros((D, 64), dtype=np.float32)
    xpw_pad[:, 0:16] = xpwT[:, 0:16]
    xpw_pad[:, 32:48] = xpwT[:, 16:32]
    xpb_pad = np.zeros((64, 1), dtype=np.float32)
    xpb_pad[0:16, 0] = xpb[0:16]
    xpb_pad[32:48, 0] = xpb[16:32]
    maps = []
    for c in range(8):
        b, ds = c // 4, c % 4
        d0 = ds * 512
        maps.append({
            "xA": np.ascontiguousarray(x[b]),
            "xAs": np.ascontiguousarray(x[b][:, d0:d0 + 512]),
            "wdtT": np.ascontiguousarray(wdtT[:, d0:d0 + 512]),
            "dtb": np.ascontiguousarray(dtb[d0:d0 + 512].reshape(4, 128).T),
            "xpw": xpw_pad,
            "xpb": xpb_pad,
            "alog": np.ascontiguousarray(
                alog[d0:d0 + 512].reshape(4, 128, 16).transpose(1, 0, 2).reshape(128, 64)),
            "ident": ident,
            "sel": selm.astype(BF),
        })
    return maps


def sp_pack(Wt):
    Kf, Mf = Wt.shape
    Kt, Mt = Kf // 128, Mf // 128
    return np.ascontiguousarray(
        Wt.reshape(Kt, 128, Mt, 128).transpose(2, 0, 1, 3).reshape(Mt * Kt * 128, 128))


def load_sp(ser, eng, s_tile, dram_pack, mt, Kt):
    src = dram_pack[mt * Kt * 128:(mt + 1) * Kt * 128, :].rearrange(
        "(kt p) f -> p kt f", p=128)
    dst = s_tile[:, 0:Kt * 128].rearrange("p (kt f) -> p kt f", f=128)
    ser.dma(eng, dst, src)


def emit_load_xT(ser, nc, x_tm, t0, ntt, s_tok, s_dst, p_tp, s_id, ach, ndk=16):
    for tt in range(ntt):
        ser.dma(nc.sync, s_tok[:, 0:ndk * 128],
                x_tm[t0 + tt * 128: t0 + (tt + 1) * 128, :])
        for k in range(ndk):
            ser.step(nc.tensor, lambda e, k=k: e.transpose(
                p_tp[:], s_tok[:, k * 128:(k + 1) * 128], s_id[:]))
            ser.step(nc.vector, lambda e, k=k, tt=tt: e.tensor_copy(
                s_dst[:, k * ach + tt * 128: k * ach + (tt + 1) * 128], p_tp[:]))


def emit_ln(ser, nc, s_s, s_bf, s_st, s_onc, s_onr, p_s, p_rep, s_out, eps, w=512):
    for k in range(16):
        ser.step(nc.scalar, lambda e, k=k: e.copy(
            s_bf[:, k * w:(k + 1) * w], s_s[:, k * w:(k + 1) * w]))
    for k in range(16):
        ser.step(nc.tensor, lambda e, k=k: e.matmul(
            p_s[0:1, 0:w], s_onc[:, 0:1], s_bf[:, k * w:(k + 1) * w],
            start=(k == 0), stop=(k == 15)))
    for k in range(16):
        ser.step(nc.scalar, lambda e, k=k: e.activation(
            s_bf[:, k * w:(k + 1) * w], s_s[:, k * w:(k + 1) * w], AF.Square))
    for k in range(16):
        ser.step(nc.tensor, lambda e, k=k: e.matmul(
            p_s[0:1, w:2 * w], s_onc[:, 0:1], s_bf[:, k * w:(k + 1) * w],
            start=(k == 0), stop=(k == 15)))
    mu, e2, va, rs, nm = (s_st[0:1, i * w:(i + 1) * w] for i in range(5))
    ser.step(nc.vector, lambda e: e.tensor_scalar_mul(mu, p_s[0:1, 0:w], 1.0 / 2048))
    ser.step(nc.vector, lambda e: e.tensor_scalar_mul(e2, p_s[0:1, w:2 * w], 1.0 / 2048))
    ser.step(nc.vector, lambda e: e.tensor_mul(va, mu, mu))
    ser.step(nc.vector, lambda e: e.tensor_sub(va, e2, va))
    ser.step(nc.vector, lambda e: e.tensor_scalar_add(va, va, float(eps)))
    ser.step(nc.scalar, lambda e: e.activation(va, va, AF.Ln))
    ser.step(nc.vector, lambda e: e.tensor_scalar_mul(va, va, -0.5))
    ser.step(nc.scalar, lambda e: e.activation(rs, va, AF.Exp))
    ser.step(nc.vector, lambda e: e.tensor_mul(nm, mu, rs))
    ser.step(nc.vector, lambda e: e.tensor_scalar_mul(nm, nm, -1.0))
    ser.step(nc.tensor, lambda e: e.matmul(
        p_rep[:, 0:w], s_onr[0:1, :], rs, start=True, stop=True))
    ser.step(nc.tensor, lambda e: e.matmul(
        p_rep[:, w:2 * w], s_onr[0:1, :], nm, start=True, stop=True))
    for k in range(16):
        ser.step(nc.vector, lambda e, k=k: e.tensor_mul(
            s_out[:, k * w:(k + 1) * w], s_s[:, k * w:(k + 1) * w], p_rep[:, 0:w]))
        ser.step(nc.vector, lambda e, k=k: e.tensor_add(
            s_out[:, k * w:(k + 1) * w], s_out[:, k * w:(k + 1) * w],
            p_rep[:, w:2 * w]))


def emit_W(nc, P, t):
    wupT = t["wupT_sl"]
    wgT = t["wgT_sl"]
    wdT = t["wdT_sl"]
    h128 = t["h128"]
    up_t = t["up_t"]
    g_t = t["g_t"]
    dn_t = t["dn_t"]

    with ExitStack() as stack:
        def sb(name, shape, dt=F32):
            return stack.enter_context(nc.sbuf_tensor(_uniq(name), shape, dt))
        s_w = sb("s_w", [128, 16 * 1024])
        s_h = sb("s_h", [128, 128])
        s_t = sb("s_t", [128, 1024])
        s_o = sb("s_o", [128, 16 * 1024], BF16)
        p_m = stack.enter_context(nc.psum_tensor(_uniq("p_m"), [128, 512], F32))
        ser = P
        ser.dma(nc.sync, s_h[:], h128)

        def transform(srcT, dst, ntile, width, kind):
            ser.dmas(nc.sync, [
                (s_w[:, i * width:(i + 1) * width], srcT[i * 128:(i + 1) * 128, :])
                for i in range(ntile)])
            for i in range(ntile):
                for cqq in range(0, width, 512):
                    cw = min(512, width - cqq)
                    ser.step(nc.tensor, lambda e, i=i, cq=cqq, cw=cw: e.matmul(
                        p_m[:, 0:cw], s_h[:], s_w[:, i * width + cq: i * width + cq + cw],
                        start=True, stop=True))
                    ser.step(nc.vector, lambda e, i=i, cq=cqq, cw=cw: e.tensor_copy(
                        s_w[:, i * width + cq: i * width + cq + cw], p_m[:, 0:cw]))
            h = 1
            while h < ntile:
                for base in range(0, ntile, 2 * h):
                    for j in range(h):
                        a = (base + j) * width
                        b2 = (base + j + h) * width
                        ser.step(nc.vector, lambda e, a=a, b2=b2: e.tensor_sub(
                            s_t[:, 0:width], s_w[:, a:a + width], s_w[:, b2:b2 + width]))
                        ser.step(nc.vector, lambda e, a=a, b2=b2: e.tensor_add(
                            s_w[:, a:a + width], s_w[:, a:a + width], s_w[:, b2:b2 + width]))
                        ser.step(nc.vector, lambda e, b2=b2: e.tensor_copy(
                            s_w[:, b2:b2 + width], s_t[:, 0:width]))
                h *= 2
            for i in range(ntile):
                ser.step(nc.vector, lambda e, i=i: e.tensor_copy(
                    s_o[:, i * width:(i + 1) * width], s_w[:, i * width:(i + 1) * width]))
            if kind == "up":
                for ft in range(8):
                    ser.dma(nc.sync,
                            dst[ft * 16 * 128:(ft + 1) * 16 * 128, :].rearrange(
                                "(kt p) f -> p kt f", p=128),
                            s_o.rearrange("p (kt g) -> p kt g", g=1024)[
                                :, :, ft * 128:(ft + 1) * 128])
            else:
                for dt_ in range(2):
                    ser.dma(nc.sync,
                            dst[dt_ * 64 * 128:(dt_ + 1) * 64 * 128, :].rearrange(
                                "(fk p) f -> p fk f", p=128),
                            s_o.rearrange("p (fk g) -> p fk g", g=256)[
                                :, :, dt_ * 128:(dt_ + 1) * 128])

        transform(wupT, up_t, 16, 1024, "up")
        transform(wgT, g_t, 16, 1024, "up")
        transform(wdT, dn_t, 64, 256, "dn")
        for name, rows in PACK_SECTIONS:
            cin = t[f"cv_{name}_in"]
            cout = t[f"cv_{name}_out"]
            n = rows // 8
            csz = 4096 if n % 4096 == 0 else n // 4 if n % 4 == 0 else n
            r = 0
            while r < n:
                c = min(csz, n - r)
                ser.dma(nc.gpsimd, cout[r:r + c, :], cin[r:r + c, :])
                r += c
        P.barrier()


PACK_SECTIONS = (("qkv", 98304), ("pr", 32768), ("ot", 32768),
                 ("g", 147456), ("u", 147456), ("d", 147456))


def host_packs_f32(inp):
    pk = {}
    pk["qkv"] = sp_pack(np.ascontiguousarray(np.asarray(inp["L1_attn_W_qkv"], np.float32).T))
    pk["pr"] = sp_pack(np.ascontiguousarray(np.asarray(inp["L1_attn_W_proj"], np.float32).T))
    pk["ot"] = sp_pack(np.ascontiguousarray(np.asarray(inp["L0_attn_out_proj_w"], np.float32).T))
    wg = np.asarray(inp["L1_ffn_w_gate"], np.float32)
    wu = np.asarray(inp["L1_ffn_w_up"], np.float32)
    wd = np.asarray(inp["L1_ffn_w_down"], np.float32)
    pk["g"] = np.concatenate([sp_pack(wg[e]) for e in range(8)]
                             + [sp_pack(np.asarray(inp["L1_ffn_sw_gate"], np.float32))], axis=0)
    pk["u"] = np.concatenate([sp_pack(wu[e]) for e in range(8)]
                             + [sp_pack(np.asarray(inp["L1_ffn_sw_up"], np.float32))], axis=0)
    pk["d"] = np.concatenate([sp_pack(wd[e]) for e in range(8)]
                             + [sp_pack(np.asarray(inp["L1_ffn_sw_down"], np.float32))], axis=0)
    return pk


def host_inputs_W(inp):
    wupT = np.ascontiguousarray(np.asarray(inp["L0_ffn_W_up"], np.float32).T)
    wgT = np.ascontiguousarray(np.asarray(inp["L0_ffn_W_gate"], np.float32).T)
    wdT = np.ascontiguousarray(np.asarray(inp["L0_ffn_W_down"], np.float32).T)
    h = hadamard(128)
    pk = host_packs_f32(inp)
    maps = []
    for c in range(8):
        m = {
            "wupT_sl": np.ascontiguousarray(wupT[:, c * 1024:(c + 1) * 1024]),
            "wgT_sl": np.ascontiguousarray(wgT[:, c * 1024:(c + 1) * 1024]),
            "wdT_sl": np.ascontiguousarray(wdT[:, c * 256:(c + 1) * 256]),
            "h128": h,
        }
        for name, rows in PACK_SECTIONS:
            n = rows // 8
            m[f"cv_{name}_in"] = np.ascontiguousarray(pk[name][c * n:(c + 1) * n])
        maps.append(m)
    return maps


def assemble_packs_bf16(resW):
    out = {}
    for name, rows in PACK_SECTIONS:
        out[name] = np.concatenate(
            [resW.results[c][f"cv_{name}_out"] for c in range(8)], axis=0)
    return out


def unpack_fold(resW):
    wup_pk = np.concatenate([resW.results[c]["up_t"] for c in range(8)], axis=0)
    wg_pk = np.concatenate([resW.results[c]["g_t"] for c in range(8)], axis=0)
    wdn_pk = np.concatenate([resW.results[c]["dn_t"] for c in range(8)], axis=0)
    return wup_pk, wg_pk, wdn_pk


def emit_B(nc, P, t):
    yT = t["yT"]
    wot_pk = t["wot_pk"]
    otb = t["otb"]
    x_tm = t["x_tm"]
    ident = t["ident"]
    x1T = t["x1T"]

    with ExitStack() as stack:
        def sb(name, shape, dt=F32):
            return stack.enter_context(nc.sbuf_tensor(_uniq(name), shape, dt))
        s_y = sb("s_y", [128, 16 * TC], BF16)
        s_wb = sb("s_wb", [128, 4 * 2048], BF16)
        s_otb = sb("s_otb", [128, 16])
        s_id = sb("s_id", [128, 128])
        s_tok = sb("s_tok", [128, 2 * D])
        s_xr = sb("s_xr", [128, 16 * 512])
        s_s = sb("s_s", [128, 16 * 512])
        s_bf = sb("s_bf", [128, 16 * 512], BF16)
        s_st = sb("s_st", [1, 8 * 512])
        s_onc = sb("s_onc", [128, 1], BF16)
        s_onr = sb("s_onr", [1, 128])
        s_x1 = s_xr
        p = stack.enter_context(nc.psum_tensor(_uniq("p"), [128, 4096], F32))
        PTP = [p[:, 0:128], p[:, 512:640]]
        PM = [p[:, 1024:1536], p[:, 1536:2048]]
        p_s = p[0:1, 2048:3072]
        p_rep = p[:, 3072:4096]
        ser = P

        ser.dmas(nc.sync, [(s_otb[:], otb), (s_id[:], ident)])
        ser.step(nc.vector, lambda e: e.memset(s_onc[:, 0:1], 1.0))
        ser.step(nc.vector, lambda e: e.memset(s_onr[0:1, :], 1.0))
        ser.dmas(nc.sync, [
            (s_y[:, k * TC:(k + 1) * TC], yT[k * 128:(k + 1) * 128, :])
            for k in range(16)])
        P.barrier()

        NLD = 16
        NTC = TC // 512
        li = [0]
        ld_done = [0] * (NLD * NTC)
        ld_cnt = [0] * (NLD * NTC)

        def issue_loads(upto):
            while li[0] < min(upto, NLD * NTC):
                i = li[0]
                mt = i % NLD
                slot = i % 4
                srcv = wot_pk[mt * 16 * 128:(mt + 1) * 16 * 128, :].rearrange(
                    "(kt p) f -> p kt f", p=128)
                dstv = s_wb[:, slot * 2048: slot * 2048 + 2048].rearrange(
                    "p (kt f) -> p kt f", f=128)
                w = []
                if i >= 4:
                    w = [("tensor", ld_done[i - 4])]
                qn = "dgp" if (i % 2 == 0) else "dsync"
                ld_cnt[i] = P.dma_q(qn, dstv, srcv, waits=w, sem=f"slw{i % 4}")
                li[0] += 1

        tp_done = [0, 0]
        ev_v = [0, 0]
        xld_cnt = [0, 0]
        xdma_free = [0, 0]
        ev_cnt = [0, 0]
        for tc in range(NTC):
            t0 = tc * 512
            for tt in range(4):
                tb = tt % 2
                xld_cnt[tb] = P.dma_q(
                    "dsync", s_tok[:, tb * D: tb * D + D],
                    x_tm[t0 + tt * 128: t0 + (tt + 1) * 128, :],
                    waits=[("tensor", xdma_free[tb])], sem=f"xld{tb}")
                for k in range(16):
                    j = k % 2
                    w = [(f"xld{tb}", xld_cnt[tb]), ("vector", ev_v[j])]
                    tp_done[j] = P.emit("tensor", lambda e, k=k, tb=tb, j=j: e.transpose(
                        PTP[j], s_tok[:, tb * D + k * 128: tb * D + (k + 1) * 128],
                        s_id[:]), waits=w)
                    ev_v[j] = P.emit("vector", lambda e, k=k, tt=tt, j=j: e.tensor_copy(
                        s_xr[:, k * 512 + tt * 128: k * 512 + (tt + 1) * 128], PTP[j]),
                        waits=[("tensor", tp_done[j])])
                xdma_free[tb] = P.cnt["tensor"]
            xrv = P.cnt["vector"]
            for mt in range(16):
                gi = tc * NLD + mt
                issue_loads(gi + 4)
                slot = gi % 4
                par = mt % 2
                for k in range(16):
                    w = []
                    if k == 0:
                        w = [(f"slw{gi % 4}", ld_cnt[gi]), ("vector", ev_cnt[par])]
                    P.emit("tensor", lambda e, mt=mt, k=k, slot=slot, par=par: e.matmul(
                        PM[par], s_wb[:, slot * 2048 + k * 128: slot * 2048 + (k + 1) * 128],
                        s_y[:, k * TC + t0: k * TC + t0 + 512],
                        start=(k == 0), stop=(k == 15)),
                        waits=w, inc=(k == 15))
                tm = P.cnt["tensor"]
                ld_done[gi] = tm
                ev_cnt[par] = P.emit("vector", lambda e, mt=mt, par=par: e.scalar_tensor_tensor(
                    s_s[:, mt * 512:(mt + 1) * 512], PM[par], s_otb[:, mt:mt + 1],
                    s_xr[:, mt * 512:(mt + 1) * 512], ALU.add, ALU.add),
                    waits=[("tensor", tm), ("vector", xrv)])
            P.barrier()
            emit_ln(ser, nc, s_s, s_bf, s_st, s_onc, s_onr, p_s, p_rep,
                    s_x1, LN_EPS)
            ser.dmas(nc.sync, [
                (x1T[k * 128:(k + 1) * 128, t0:t0 + 512],
                 s_x1[:, k * 512:(k + 1) * 512]) for k in range(16)])
            P.barrier()
        P.barrier()


def emit_C(nc, P, t):
    x1T = t["x1T"]
    wup_pk = t["wup_pk"]
    wg_pk = t["wg_pk"]
    wdn_pk = t["wdn_pk"]
    su_ug = t["su_ug"]
    wsg = t["wsg"]
    wsu = t["wsu"]
    wsd = t["wsd"]
    x2T = t["x2T"]

    with ExitStack() as stack:
        def sb(name, shape, dt=F32):
            return stack.enter_context(nc.sbuf_tensor(_uniq(name), shape, dt))
        s_x1 = sb("s_x1", [128, 16 * 512])
        s_xs = sb("s_xs", [128, 16 * 512], BF16)
        s_su = sb("s_su", [128, 16])
        s_wsg = sb("s_wsg", [128, 64])
        s_wsu = sb("s_wsu", [128, 64])
        s_wsd = sb("s_wsd", [128, 16])
        s_wb = sb("s_wb", [128, 4 * 2048], BF16)
        s_sg = sb("s_sg", [128, 2 * 512])
        s_z = sb("s_z", [128, 64 * 512], BF16)
        s_s = sb("s_s", [128, 16 * 512])
        s_bf = sb("s_bf", [128, 16 * 512], BF16)
        s_st = sb("s_st", [1, 8 * 512])
        s_onc = sb("s_onc", [128, 1], BF16)
        s_onr = sb("s_onr", [1, 128])
        s_x2 = s_x1
        p = stack.enter_context(nc.psum_tensor(_uniq("p"), [128, 4096], F32))
        PG = [p[:, 0:512], p[:, 512:1024]]
        PU = [p[:, 1024:1536], p[:, 1536:2048]]
        PD = [p[:, 2048:2560], p[:, 2560:3072]]
        p_rep = p[:, 3072:4096]
        p_s = p[0:1, 2048:3072]
        ser = P

        ser.dmas(nc.sync, [(s_su[:], su_ug), (s_wsg[:], wsg), (s_wsu[:], wsu),
                           (s_wsd[:], wsd)])
        ser.step(nc.vector, lambda e: e.memset(s_onc[:, 0:1], 1.0))
        ser.step(nc.vector, lambda e: e.memset(s_onr[0:1, :], 1.0))

        loads = []
        for ft in range(64):
            loads.append((wg_pk, ft * 16 * 128, 16))
            loads.append((wup_pk, ft * 16 * 128, 16))
        for dt_ in range(16):
            for sub in range(4):
                loads.append((wdn_pk, dt_ * 64 * 128 + sub * 16 * 128, 16))
        NLD = len(loads)
        NTC = TC // 512
        li = [0]
        ld_done = [0] * (NLD * NTC)
        ld_cnt = [0] * (NLD * NTC)

        def issue_loads(upto):
            while li[0] < min(upto, NLD * NTC):
                i = li[0]
                pk, row0, Kt = loads[i % NLD]
                slot = i % 4
                srcv = pk[row0:row0 + Kt * 128, :].rearrange("(kt p) f -> p kt f", p=128)
                dstv = s_wb[:, slot * 2048: slot * 2048 + Kt * 128].rearrange(
                    "p (kt f) -> p kt f", f=128)
                w = []
                if i >= 4:
                    w = [("tensor", ld_done[i - 4])]
                qn = "dgp" if (i % 2 == 0) else "dsync"
                ld_cnt[i] = P.dma_q(qn, dstv, srcv, waits=w, sem=f"slw{i % 4}")
                li[0] += 1

        for tc in range(NTC):
            t0 = tc * 512
            ser.dmas(nc.sync, [
                (s_x1[:, k * 512:(k + 1) * 512], x1T[k * 128:(k + 1) * 128, t0:t0 + 512])
                for k in range(16)])
            for k in range(16):
                ser.step(nc.vector, lambda e, k=k: e.tensor_scalar_mul(
                    s_xs[:, k * 512:(k + 1) * 512], s_x1[:, k * 512:(k + 1) * 512],
                    s_su[:, k:k + 1]))
            P.barrier()

            silu_cnt = [0, 0]
            sg_free = [0, 0]
            z_cnt = [0, 0]
            evac_cnt = [0, 0]
            z_done = 0
            gi = tc * NLD
            for ft in range(64):
                par = ft % 2
                issue_loads(gi + 4)
                slot = gi % 4
                for k in range(16):
                    w = []
                    if k == 0:
                        w = [(f"slw{gi % 4}", ld_cnt[gi]), ("scalar", silu_cnt[par])]
                    P.emit("tensor", lambda e, k=k, slot=slot, par=par: e.matmul(
                        PG[par], s_wb[:, slot * 2048 + k * 128: slot * 2048 + (k + 1) * 128],
                        s_xs[:, k * 512:(k + 1) * 512],
                        start=(k == 0), stop=(k == 15)),
                        waits=w, inc=(k == 15))
                tg = P.cnt["tensor"]
                ld_done[gi] = tg
                gi += 1
                issue_loads(gi + 4)
                slot = gi % 4
                for k in range(16):
                    w = []
                    if k == 0:
                        w = [(f"slw{gi % 4}", ld_cnt[gi]), ("vector", z_cnt[par])]
                    P.emit("tensor", lambda e, k=k, slot=slot, par=par: e.matmul(
                        PU[par], s_wb[:, slot * 2048 + k * 128: slot * 2048 + (k + 1) * 128],
                        s_xs[:, k * 512:(k + 1) * 512],
                        start=(k == 0), stop=(k == 15)),
                        waits=w, inc=(k == 15))
                tu = P.cnt["tensor"]
                ld_done[gi] = tu
                gi += 1
                silu_cnt[par] = P.emit("scalar", lambda e, par=par, ft=ft: e.activation(
                    s_sg[:, par * 512: par * 512 + 512], PG[par], AF.Silu,
                    scale=s_wsg[:, ft:ft + 1]),
                    waits=[("tensor", tg), ("vector", sg_free[par])])
                z_cnt[par] = P.emit("vector", lambda e, par=par, ft=ft: e.scalar_tensor_tensor(
                    s_z[:, ft * 512:(ft + 1) * 512], PU[par], s_wsu[:, ft:ft + 1],
                    s_sg[:, par * 512: par * 512 + 512], ALU.mult, ALU.mult),
                    waits=[("tensor", tu), ("scalar", silu_cnt[par])])
                sg_free[par] = z_cnt[par]
                if ft == 63:
                    z_done = z_cnt[par]
            for dt_ in range(16):
                par2 = dt_ % 2
                for fk in range(64):
                    sub = fk // 16
                    if fk % 16 == 0:
                        issue_loads(gi + sub + 4)
                    sl = (gi + sub) % 4
                    fo = fk % 16
                    w = []
                    if fk == 0:
                        w = [(f"slw{sl}", ld_cnt[gi]),
                             ("vector", max(z_done, evac_cnt[par2]))]
                    elif fk % 16 == 0:
                        w = [(f"slw{sl}", ld_cnt[gi + sub])]
                    P.emit("tensor", lambda e, fk=fk, sl=sl, fo=fo, par2=par2: e.matmul(
                        PD[par2], s_wb[:, sl * 2048 + fo * 128: sl * 2048 + (fo + 1) * 128],
                        s_z[:, fk * 512:(fk + 1) * 512],
                        start=(fk == 0), stop=(fk == 63)),
                        waits=w, inc=(fk % 16 == 15))
                    if fk % 16 == 15:
                        ld_done[gi + sub] = P.cnt["tensor"]
                td = P.cnt["tensor"]
                gi += 4
                evac_cnt[par2] = P.emit("vector", lambda e, dt_=dt_, par2=par2: e.scalar_tensor_tensor(
                    s_s[:, dt_ * 512:(dt_ + 1) * 512], PD[par2], s_wsd[:, dt_:dt_ + 1],
                    s_x1[:, dt_ * 512:(dt_ + 1) * 512], ALU.mult, ALU.add),
                    waits=[("tensor", td)])
            P.barrier()
            emit_ln(ser, nc, s_s, s_bf, s_st, s_onc, s_onr, p_s, p_rep,
                    s_x2, LN_EPS)
            ser.dmas(nc.sync, [
                (x2T[k * 128:(k + 1) * 128, t0:t0 + 512],
                 s_x2[:, k * 512:(k + 1) * 512]) for k in range(16)])
        P.barrier()


def host_scales_C(inp):
    wsc = np.asarray(inp["L0_ffn_Wscale_ug"], np.float32)
    sudp = np.asarray(inp["L0_ffn_SU_dp"], np.float32)
    wsg = np.ascontiguousarray((wsc[INTER:] / np.sqrt(D)).reshape(64, 128).T)
    wsu = np.ascontiguousarray((wsc[:INTER] * sudp / np.sqrt(D)).reshape(64, 128).T)
    wsd = np.ascontiguousarray(
        (np.asarray(inp["L0_ffn_Wscale_dp"], np.float32) / np.sqrt(INTER)).reshape(16, 128).T)
    suug = np.ascontiguousarray(np.asarray(inp["L0_ffn_SU_ug"], np.float32).reshape(16, 128).T)
    return suug, wsg, wsu, wsd


def emit_D(nc, P, t, on_kind=None):
    x2T = t["x2T"]
    wqkv_pk = t["wqkv_pk"]
    qkvT = t["qkvT"]

    with ExitStack() as stack:
        def sb(name, shape, dt=F32):
            return stack.enter_context(nc.sbuf_tensor(_uniq(name), shape, dt))
        s_x = sb("s_x", [128, 16 * TC], BF16)
        s_wb = sb("s_wb", [128, 4 * 2048], BF16)
        s_o = sb("s_o", [128, 2 * TC], BF16)
        p = stack.enter_context(nc.psum_tensor(_uniq("p"), [128, 4096], F32))
        PM = [p[:, 0:512], p[:, 512:1024]]
        ser = P
        ser.dmas(nc.gpsimd, [
            (s_x[:, k * TC:(k + 1) * TC], x2T[k * 128:(k + 1) * 128, :])
            for k in range(16)])
        P.barrier()

        li = [0]
        ld_done = [0] * 48
        ld_cnt = [0] * 48

        def issue_loads(upto):
            while li[0] < min(upto, 48):
                i = li[0]
                slot = i % 4
                srcv = wqkv_pk[i * 16 * 128:(i + 1) * 16 * 128, :].rearrange(
                    "(kt p) f -> p kt f", p=128)
                dstv = s_wb[:, slot * 2048: slot * 2048 + 2048].rearrange(
                    "p (kt f) -> p kt f", f=128)
                w = []
                if i >= 4:
                    w = [("tensor", ld_done[i - 4])]
                qn = "dgp" if (i % 2 == 0) else "dsync"
                ld_cnt[i] = P.dma_q(qn, dstv, srcv, waits=w, sem=f"slw{i % 4}")
                li[0] += 1

        ev_cnt = [0, 0]
        ost_cnt = [0, 0]
        for mt in range(48):
            issue_loads(mt + 4)
            slot = mt % 4
            oh = mt % 2
            for tc in range(TC // 512):
                par = tc % 2
                for k in range(16):
                    w = []
                    if k == 0:
                        w = [(f"slw{slot}", ld_cnt[mt]), ("scalar", ev_cnt[par])]
                    P.emit("tensor", lambda e, k=k, tc=tc, slot=slot, par=par: e.matmul(
                        PM[par], s_wb[:, slot * 2048 + k * 128: slot * 2048 + (k + 1) * 128],
                        s_x[:, k * TC + tc * 512: k * TC + (tc + 1) * 512],
                        start=(k == 0), stop=(k == 15)),
                        waits=w, inc=(k == 15))
                tm = P.cnt["tensor"]
                if tc == 3:
                    ld_done[mt] = tm
                w = [("tensor", tm)]
                if tc == 0:
                    w.append((f"ost{oh}", ost_cnt[oh]))
                ev_cnt[par] = P.emit("scalar", lambda e, tc=tc, oh=oh, par=par: e.copy(
                    s_o[:, oh * TC + tc * 512: oh * TC + (tc + 1) * 512], PM[par]),
                    waits=w)
            ost_cnt[oh] = P.dma_q("dsync", qkvT[mt * 128:(mt + 1) * 128, :],
                                  s_o[:, oh * TC:(oh + 1) * TC],
                                  waits=[("scalar", ev_cnt[(TC // 512 - 1) % 2])],
                                  sem=f"ost{oh}")
            if on_kind is not None and mt % 16 == 15:
                on_kind(mt // 16, ost_cnt[0], ost_cnt[1])
        P.barrier()


def emit_E(nc, P, t):
    qkvg = t["qkvg"]
    maskw = t["maskw"]
    ident_bf = t["ident_bf"]
    y_tm = t["y_tm"]

    NKV = T // 128
    with ExitStack() as stack:
        def sb(name, shape, dt=F32):
            return stack.enter_context(nc.sbuf_tensor(_uniq(name), shape, dt))
        s_q = sb("s_q", [128, 4 * T], BF16)
        s_k = sb("s_k", [128, 4 * T], BF16)
        s_v = sb("s_v", [128, NKV * 4 * 129], BF16)
        s_vt = sb("s_vt", [128, 2 * 2048], BF16)
        s_m = sb("s_m", [128, 896], BF16)
        s_p = sb("s_p", [128, 2 * 512], BF16)
        s_sqq = s_p[:, 0:512]
        s_st = sb("s_st", [1, 64])
        s_onc = sb("s_onc", [128, 1], BF16)
        s_onr = sb("s_onr", [1, 128])
        s_scl = sb("s_scl", [128, 1])
        s_mh = sb("s_mh", [128, 1])
        s_r = sb("s_r", [128, 4])
        s_yo = sb("s_yo", [128, 2 * 512], BF16)
        s_id = sb("s_id", [128, 128], BF16)
        p = stack.enter_context(nc.psum_tensor(_uniq("p"), [128, 3072], F32))
        PS = [p[:, 0:512], p[:, 512:1024]]
        PO = [p[:, 1024 + j * 512: 1024 + j * 512 + 129] for j in range(4)]
        p_s1 = p[0:1, 1024:1536]
        p_b = p[:, 1536:1537]
        p_bf = stack.enter_context(nc.psum_tensor(_uniq("p_bf"), [128, 2048], BF16))
        PTB = [p_bf[:, 0:128], p_bf[:, 1024:1152]]
        ser = P

        ser.dmas(nc.sync, [(s_m[:], maskw), (s_id[:], ident_bf)])
        ser.step(nc.vector, lambda e: e.memset(s_onc[:, 0:1], 1.0))
        ser.step(nc.vector, lambda e: e.memset(s_onr[0:1, :], 1.0))
        ser.step(nc.vector, lambda e: e.memset(s_scl[:, 0:1], 0.08838834764831845))
        for h in range(4):
            ser.dmas(nc.sync, [
                (s_q[:, h * T + r * 2048: h * T + (r + 1) * 2048],
                 qkvg[r * 1536 + h * 128: r * 1536 + (h + 1) * 128, :]) for r in range(4)])
            ser.dmas(nc.sync, [
                (s_k[:, h * T + r * 2048: h * T + (r + 1) * 2048],
                 qkvg[r * 1536 + 512 + h * 128: r * 1536 + 512 + (h + 1) * 128, :])
                for r in range(4)])
        tpd = [0, 0]
        evd = [0, 0]
        for h in range(4):
            for rp in range(2):
                ser.dmas(nc.sync, [
                    (s_vt[:, r2 * 2048:(r2 + 1) * 2048],
                     qkvg[(rp * 2 + r2) * 1536 + 1024 + h * 128:
                          (rp * 2 + r2) * 1536 + 1024 + (h + 1) * 128, :])
                    for r2 in range(2)])
                for r2 in range(2):
                    r = rp * 2 + r2
                    for kvl in range(16):
                        j = kvl % 2
                        tpd[j] = P.emit("tensor", lambda e, r2=r2, kvl=kvl, j=j: e.transpose(
                            PTB[j], s_vt[:, r2 * 2048 + kvl * 128: r2 * 2048 + (kvl + 1) * 128],
                            s_id[:]),
                            waits=[("scalar", evd[j]), ("dsync", P.cnt["dsync"])])
                        kvb = r * 16 + kvl
                        evd[j] = P.emit("scalar", lambda e, kvb=kvb, h=h, j=j: e.copy(
                            s_v[:, (kvb * 4 + h) * 129:(kvb * 4 + h) * 129 + 128], PTB[j]),
                            waits=[("tensor", tpd[j])])
        ser.step(nc.vector, lambda e: e.memset(
            s_v.rearrange("p (t c) -> p t c", c=129)[:, :, 128:129], 1.0))
        P.barrier()

        exp_free = [0, 0]
        av_free = [0, 0]
        yo_free = [0, 0]
        for h in range(4):
            for which, buf in ((0, s_q), (1, s_k)):
                for tcq in range(T // 512):
                    ser.step(nc.scalar, lambda e, buf=buf, h=h, tcq=tcq: e.activation(
                        s_sqq[:], buf[:, h * T + tcq * 512: h * T + (tcq + 1) * 512], AF.Square))
                    ser.step(nc.tensor, lambda e: e.matmul(
                        p_s1[0:1, :], s_onc[:, 0:1], s_sqq[:], start=True, stop=True))
                    ser.step(nc.vector, lambda e, w=which: e.reduce_max(
                        s_st[0:1, 8 + w:9 + w], p_s1[0:1, :], mybir.AxisListType.X))
                    if tcq == 0:
                        ser.step(nc.vector, lambda e, w=which: e.tensor_copy(
                            s_st[0:1, w:w + 1], s_st[0:1, 8 + w:9 + w]))
                    else:
                        ser.step(nc.vector, lambda e, w=which: e.tensor_max(
                            s_st[0:1, w:w + 1], s_st[0:1, w:w + 1], s_st[0:1, 8 + w:9 + w]))
            ser.step(nc.vector, lambda e: e.tensor_mul(
                s_st[0:1, 2:3], s_st[0:1, 0:1], s_st[0:1, 1:2]))
            ser.step(nc.scalar, lambda e: e.activation(s_st[0:1, 2:3], s_st[0:1, 2:3], AF.Ln))
            ser.step(nc.vector, lambda e: e.tensor_scalar_mul(s_st[0:1, 2:3], s_st[0:1, 2:3], 0.5))
            ser.step(nc.scalar, lambda e: e.activation(s_st[0:1, 2:3], s_st[0:1, 2:3], AF.Exp))
            ser.step(nc.vector, lambda e: e.tensor_scalar_mul(
                s_st[0:1, 2:3], s_st[0:1, 2:3], -0.08838834764831845))
            ser.step(nc.tensor, lambda e: e.matmul(
                p_b[:, 0:1], s_onr[0:1, :], s_st[0:1, 2:3], start=True, stop=True))
            ser.step(nc.vector, lambda e: e.tensor_copy(s_mh[:, 0:1], p_b[:, 0:1]))
            P.barrier()

            po_free = [0, 0, 0, 0]

            for qc in range(T // 512):
                nkv = 4 * qc + 4
                exp_cnt = [0] * nkv
                gate_cnt = [0] * nkv
                gate_eng = [None] * nkv
                av_done = [0] * nkv
                s_done = [0] * nkv
                for i in range(nkv + 1):
                    if i < nkv:
                        par = i % 2
                        w = [("scalar", exp_free[par])]
                        s_done[i] = P.emit("tensor", lambda e, h=h, qc=qc, i=i, par=par: e.matmul(
                            PS[par], s_k[:, h * T + i * 128: h * T + (i + 1) * 128],
                            s_q[:, h * T + qc * 512: h * T + (qc + 1) * 512],
                            start=True, stop=True), waits=w)
                        wexp = [("tensor", s_done[i]), ("tensor", av_free[par])]
                        exp_cnt[i] = P.emit("scalar", lambda e, par=par: e.activation(
                            s_p[:, par * 512: par * 512 + 512], PS[par], AF.Exp,
                            bias=s_mh[:, 0:1], scale=s_scl[:, 0:1]), waits=wexp)
                        exp_free[par] = exp_cnt[i]
                        o = i * 128 - qc * 512
                        if o >= 0:
                            gate_cnt[i] = P.emit("vector", lambda e, par=par, o=o: e.tensor_mul(
                                s_p[:, par * 512: par * 512 + 512],
                                s_p[:, par * 512: par * 512 + 512],
                                s_m[:, 384 - o: 896 - o]),
                                waits=[("scalar", exp_cnt[i])])
                            gate_eng[i] = "vector"
                        else:
                            gate_cnt[i] = exp_cnt[i]
                            gate_eng[i] = "scalar"
                    if i >= 1:
                        kvb = i - 1
                        par = kvb % 2
                        for j in range(4):
                            w = [(gate_eng[kvb], gate_cnt[kvb])]
                            if kvb == 0:
                                w.append(("vector", po_free[j]))
                            P.emit("tensor", lambda e, j=j, kvb=kvb, h=h, par=par,
                                   nkv=nkv: e.matmul(
                                PO[j], s_p[:, par * 512 + j * 128: par * 512 + (j + 1) * 128],
                                s_v[:, (kvb * 4 + h) * 129:(kvb * 4 + h) * 129 + 129],
                                start=(kvb == 0), stop=(kvb == nkv - 1)),
                                waits=w, inc=(j == 3))
                        av_done[kvb] = P.cnt["tensor"]
                        av_free[kvb % 2] = av_done[kvb]
                yh = qc % 2
                for j in range(4):
                    P.emit("vector", lambda e, j=j: e.reciprocal(
                        s_r[:, j:j + 1], PO[j][:, 128:129]),
                        waits=[("tensor", av_done[nkv - 1]), (f"yst{yh}", yo_free[yh])])
                    po_free[j] = P.emit("vector", lambda e, j=j, yh=yh: e.tensor_scalar_mul(
                        s_yo[:, yh * 512 + j * 128: yh * 512 + (j + 1) * 128],
                        PO[j][:, 0:128], s_r[:, j:j + 1]))
                dst = y_tm[qc * 512:(qc + 1) * 512, h * 128:(h + 1) * 128].rearrange(
                    "(j p) c -> p j c", p=128)
                srcv = s_yo[:, yh * 512:(yh + 1) * 512].rearrange("p (j c) -> p j c", c=128)
                yo_free[yh] = P.dma_q("dsync", dst, srcv,
                                      waits=[("vector", po_free[3])], sem=f"yst{yh}")
        P.barrier()


def emit_E2(nc, P, t):
    ytmg = t["ytmg"]
    wpr_pk = t["wpr_pk"]
    x2T = t["x2T"]
    ident_bf = t["ident_bf"]
    x3T = t["x3T"]

    with ExitStack() as stack:
        def sb(name, shape, dt=F32):
            return stack.enter_context(nc.sbuf_tensor(_uniq(name), shape, dt))
        s_tok = sb("s_tok", [128, 2 * D], BF16)
        s_y = sb("s_y", [128, 16 * 512], BF16)
        s_wb = sb("s_wb", [128, 4 * 2048], BF16)
        s_id = sb("s_id", [128, 128], BF16)
        s_xr = sb("s_xr", [128, 16 * 512])
        s_s = sb("s_s", [128, 16 * 512])
        s_bf = sb("s_bf", [128, 16 * 512], BF16)
        s_st = sb("s_st", [1, 8 * 512])
        s_onc = sb("s_onc", [128, 1], BF16)
        s_onr = sb("s_onr", [1, 128])
        p = stack.enter_context(nc.psum_tensor(_uniq("p"), [128, 3072], F32))
        PM = [p[:, 0:512], p[:, 512:1024]]
        p_s = p[0:1, 1024:2048]
        p_rep = p[:, 2048:3072]
        p_bf = stack.enter_context(nc.psum_tensor(_uniq("p_bf"), [128, 2048], BF16))
        PTP = [p_bf[:, 0:128], p_bf[:, 1024:1152]]
        ser = P
        ser.dmas(nc.sync, [(s_id[:], ident_bf)])
        ser.step(nc.vector, lambda e: e.memset(s_onc[:, 0:1], 1.0))
        ser.step(nc.vector, lambda e: e.memset(s_onr[0:1, :], 1.0))
        P.barrier()

        NLD = 16
        NTC = TC // 512
        li = [0]
        ld_done = [0] * (NLD * NTC)
        ld_cnt = [0] * (NLD * NTC)

        def issue_loads(upto):
            while li[0] < min(upto, NLD * NTC):
                i = li[0]
                mt = i % NLD
                slot = i % 4
                srcv = wpr_pk[mt * 16 * 128:(mt + 1) * 16 * 128, :].rearrange(
                    "(kt p) f -> p kt f", p=128)
                dstv = s_wb[:, slot * 2048: slot * 2048 + 2048].rearrange(
                    "p (kt f) -> p kt f", f=128)
                w = []
                if i >= 4:
                    w = [("tensor", ld_done[i - 4])]
                qn = "dgp" if (i % 2 == 0) else "dsync"
                ld_cnt[i] = P.dma_q(qn, dstv, srcv, waits=w, sem=f"slw{i % 4}")
                li[0] += 1

        tp_done = [0, 0]
        ev_s = [0, 0]
        xld_cnt = [0, 0]
        xdma_free = [0, 0]
        ev_cnt = [0, 0]
        for tc in range(NTC):
            t0 = tc * 512
            ser.dmas(nc.sync, [
                (s_xr[:, k * 512:(k + 1) * 512], x2T[k * 128:(k + 1) * 128, t0:t0 + 512])
                for k in range(16)])
            for tt in range(4):
                tb = tt % 2
                for r in range(4):
                    base = r * 2048 + t0 + tt * 128
                    xld_cnt[tb] = P.dma_q(
                        "dscal", s_tok[:, tb * D + r * 512: tb * D + (r + 1) * 512],
                        ytmg[base: base + 128, :],
                        waits=([("tensor", xdma_free[tb])] if r == 0 else []),
                        sem=f"xld{tb}")
                for k in range(16):
                    j = k % 2
                    w = [(f"xld{tb}", xld_cnt[tb]), ("scalar", ev_s[j])]
                    tp_done[j] = P.emit("tensor", lambda e, k=k, tb=tb, j=j: e.transpose(
                        PTP[j], s_tok[:, tb * D + k * 128: tb * D + (k + 1) * 128],
                        s_id[:]), waits=w)
                    ev_s[j] = P.emit("scalar", lambda e, k=k, tt=tt, j=j: e.copy(
                        s_y[:, k * 512 + tt * 128: k * 512 + (tt + 1) * 128], PTP[j]),
                        waits=[("tensor", tp_done[j])])
                xdma_free[tb] = P.cnt["tensor"]
            yev = P.cnt["scalar"]
            for mt in range(16):
                gi = tc * NLD + mt
                issue_loads(gi + 4)
                slot = gi % 4
                par = mt % 2
                for k in range(16):
                    w = []
                    if k == 0:
                        w = [(f"slw{gi % 4}", ld_cnt[gi]), ("vector", ev_cnt[par]),
                             ("scalar", yev)]
                    P.emit("tensor", lambda e, mt=mt, k=k, slot=slot, par=par: e.matmul(
                        PM[par], s_wb[:, slot * 2048 + k * 128: slot * 2048 + (k + 1) * 128],
                        s_y[:, k * 512:(k + 1) * 512],
                        start=(k == 0), stop=(k == 15)),
                        waits=w, inc=(k == 15))
                tm = P.cnt["tensor"]
                ld_done[gi] = tm
                ev_cnt[par] = P.emit("vector", lambda e, mt=mt, par=par: e.tensor_add(
                    s_s[:, mt * 512:(mt + 1) * 512], PM[par],
                    s_xr[:, mt * 512:(mt + 1) * 512]),
                    waits=[("tensor", tm)])
            P.barrier()
            emit_ln(ser, nc, s_s, s_bf, s_st, s_onc, s_onr, p_s, p_rep, s_xr, LN_EPS)
            ser.dmas(nc.sync, [
                (x3T[k * 128:(k + 1) * 128, t0:t0 + 512],
                 s_xr[:, k * 512:(k + 1) * 512]) for k in range(16)])
            P.barrier()
        P.barrier()


def emit_F(nc, P, t, pipe_mode=True):
    x3T = t["x3T"]
    rmsw = t["rmsw"]
    rt_w = t["rt_w"]
    wg_pk = t["wg_pk"]
    wu_pk = t["wu_pk"]
    wd_pk = t["wd_pk"]
    sel8 = t["sel8"]
    ident = t["ident"]
    out_tm = t["out_tm"]

    with ExitStack() as stack:
        def sb(name, shape, dt=F32):
            return stack.enter_context(nc.sbuf_tensor(_uniq(name), shape, dt))
        s_x3 = sb("s_x3", [128, 16 * 512])
        s_ht = sb("s_ht", [128, 16 * 512], BF16)
        s_rw = sb("s_rw", [128, 16])
        s_rt = sb("s_rt", [128, 16 * 8], BF16)
        s_sel = sb("s_sel", [8, 1024])
        s_id = sb("s_id", [128, 128])
        s_weT = sb("s_weT", [8, 512])
        s_we = sb("s_we", [128, 2 * 512])
        s_sg = sb("s_sg", [128, 2 * 512])
        s_uw = sb("s_uw", [128, 2 * 512])
        s_z = sb("s_z", [128, 2 * 8 * 512], BF16)
        s_acc = sb("s_acc", [128, 16 * 512])
        s_bf = sb("s_bf", [128, 16 * 512], BF16)
        s_st = sb("s_st", [1, 8 * 512])
        s_onc = sb("s_onc", [128, 1], BF16)
        s_onr = sb("s_onr", [1, 128])
        s_lg = sb("s_lg", [128, 48])
        s_wb = sb("s_wb", [128, 4 * 2048], BF16)
        s_tm = sb("s_tm", [128, 16 * 128])
        p = stack.enter_context(nc.psum_tensor(_uniq("p"), [128, 4096], F32))
        PG = [p[:, 0:512], p[:, 512:1024]]
        PU = [p[:, 1024:1536], p[:, 1536:2048]]
        PD = [p[:, 2048:2560], p[:, 2560:3072]]
        PWE = p[:, 3072:3584]
        p_rep = p[:, 3072:4096]
        p_s = p[0:1, 2048:3072]
        p_r = p[:, 0:8]
        p_tp = p[:, 0:128]
        ser = P

        ser.dmas(nc.sync, [(s_rw[:], rmsw), (s_sel[:], sel8), (s_id[:], ident)])
        ser.dmas(nc.gpsimd, [
            (s_rt[:, k * 8:(k + 1) * 8], rt_w[k * 128:(k + 1) * 128, :])
            for k in range(16)])
        ser.step(nc.vector, lambda e: e.memset(s_onc[:, 0:1], 1.0))
        ser.step(nc.vector, lambda e: e.memset(s_onr[0:1, :], 1.0))

        loads = []
        for e_ in range(9):
            for ft in range(8):
                loads.append((wg_pk, e_ * 8 + ft, 16))
                loads.append((wu_pk, e_ * 8 + ft, 16))
            for dt_ in range(16):
                loads.append((wd_pk, e_ * 16 + dt_, 8))
        NLD = len(loads)
        NTC = TC // 512
        li = [0]
        ld_done = [0] * (NLD * NTC)
        ld_cnt = [0] * (NLD * NTC)

        def W_(w):
            return w if pipe_mode else P.snap()

        def issue_loads(upto):
            while li[0] < min(upto, NLD * NTC):
                i = li[0]
                pk, mt, Kt = loads[i % NLD]
                slot = i % 4
                srcv = pk[mt * Kt * 128:(mt + 1) * Kt * 128, :].rearrange(
                    "(kt p) f -> p kt f", p=128)
                dstv = s_wb[:, slot * 2048: slot * 2048 + Kt * 128].rearrange(
                    "p (kt f) -> p kt f", f=128)
                w = []
                if i >= 4:
                    w = [("tensor", ld_done[i - 4])]
                qn = "dgp" if (i % 2 == 0) else "dsync"
                ld_cnt[i] = P.dma_q(qn, dstv, srcv, waits=W_(w), sem=f"slw{i % 4}")
                li[0] += 1

        for tc in range(NTC):
            t0 = tc * 512
            ser.dmas(nc.sync, [
                (s_x3[:, k * 512:(k + 1) * 512], x3T[k * 128:(k + 1) * 128, t0:t0 + 512])
                for k in range(16)])
            for k in range(16):
                ser.step(nc.scalar, lambda e, k=k: e.activation(
                    s_bf[:, k * 512:(k + 1) * 512], s_x3[:, k * 512:(k + 1) * 512], AF.Square))
            for k in range(16):
                ser.step(nc.tensor, lambda e, k=k: e.matmul(
                    p_s[0:1, 0:512], s_onc[:, 0:1], s_bf[:, k * 512:(k + 1) * 512],
                    start=(k == 0), stop=(k == 15)))
            ms = s_st[0:1, 0:512]
            ser.step(nc.vector, lambda e: e.tensor_scalar_mul(ms, p_s[0:1, 0:512], 1.0 / 2048))
            ser.step(nc.vector, lambda e: e.tensor_scalar_add(ms, ms, RMS_EPS))
            ser.step(nc.scalar, lambda e: e.activation(ms, ms, AF.Ln))
            ser.step(nc.vector, lambda e: e.tensor_scalar_mul(ms, ms, -0.5))
            ser.step(nc.scalar, lambda e: e.activation(ms, ms, AF.Exp))
            ser.step(nc.tensor, lambda e: e.matmul(
                p_rep[:, 0:512], s_onr[0:1, :], ms, start=True, stop=True))
            for k in range(16):
                ser.step(nc.vector, lambda e, k=k: e.scalar_tensor_tensor(
                    s_ht[:, k * 512:(k + 1) * 512], s_x3[:, k * 512:(k + 1) * 512],
                    s_rw[:, k:k + 1], p_rep[:, 0:512], ALU.mult, ALU.mult))
            for tt in range(4):
                for k in range(16):
                    ser.step(nc.tensor, lambda e, k=k, tt=tt: e.matmul(
                        p_r, s_ht[:, k * 512 + tt * 128: k * 512 + (tt + 1) * 128],
                        s_rt[:, k * 8:(k + 1) * 8],
                        start=(k == 0), stop=(k == 15)))
                lg = s_lg[:, 0:8]; pe = s_lg[:, 8:16]; m1 = s_lg[:, 16:17]
                sm = s_lg[:, 17:18]; m2 = s_lg[:, 18:19]; kp = s_lg[:, 24:32]
                wi = s_lg[:, 19:20]; we8 = s_lg[:, 32:40]; nm1 = s_lg[:, 20:21]
                ser.step(nc.vector, lambda e: e.tensor_copy(lg, p_r))
                ser.step(nc.vector, lambda e: e.reduce_max(m1, lg, mybir.AxisListType.X))
                ser.step(nc.vector, lambda e: e.tensor_scalar_mul(nm1, m1, -1.0))
                ser.step(nc.scalar, lambda e: e.activation(pe, lg, AF.Exp, bias=nm1[:, 0:1]))
                ser.step(nc.vector, lambda e: e.reduce_sum(sm, pe, mybir.AxisListType.X))
                ser.step(nc.vector, lambda e: e.reciprocal(wi, sm))
                ser.step(nc.vector, lambda e: e.tensor_scalar_mul(pe, pe, wi[:, 0:1]))
                ser.step(nc.vector, lambda e: e.reduce_max(m1, pe, mybir.AxisListType.X))
                ser.step(nc.vector, lambda e: e.tensor_single_scalar(
                    kp, pe, m1[:, 0:1], ALU.is_equal))
                ser.step(nc.vector, lambda e: e.scalar_tensor_tensor(
                    we8, kp, -1e30, pe, ALU.mult, ALU.add))
                ser.step(nc.vector, lambda e: e.reduce_max(m2, we8, mybir.AxisListType.X))
                ser.step(nc.vector, lambda e: e.tensor_single_scalar(
                    kp, pe, m2[:, 0:1], ALU.is_ge))
                ser.step(nc.vector, lambda e: e.tensor_add(sm, m1, m2))
                ser.step(nc.vector, lambda e: e.reciprocal(wi, sm))
                ser.step(nc.vector, lambda e: e.tensor_mul(we8, pe, kp))
                ser.step(nc.vector, lambda e: e.tensor_scalar_mul(we8, we8, wi[:, 0:1]))
                ser.step(nc.tensor, lambda e: e.transpose(p_tp[0:8, 0:128], we8, s_id[:]))
                ser.step(nc.vector, lambda e, tt=tt: e.tensor_copy(
                    s_weT[0:8, tt * 128:(tt + 1) * 128], p_tp[0:8, 0:128]))
            P.barrier()

            silu_cnt = [0, 0]
            sg_free = [0, 0]
            uw_cnt = [0, 0]
            z_done = [0] * 9
            acc_cnt = [0, 0]
            we_evac = [0] * 9
            uw_last_e = [0] * 9
            gi = tc * NLD
            for e_ in range(9):
                zh = (e_ % 2) * 8 * 512
                if e_ < 8:
                    twe = P.emit("tensor", lambda e, e_=e_: e.matmul(
                        PWE, s_sel[:, e_ * 128:(e_ + 1) * 128], s_weT[0:8, :],
                        start=True, stop=True),
                        waits=W_([("scalar", we_evac[e_ - 1] if e_ else 0)]))
                    we_evac[e_] = P.emit("scalar", lambda e, e_=e_: e.copy(
                        s_we[:, (e_ % 2) * 512:(e_ % 2) * 512 + 512], PWE),
                        waits=W_([("tensor", twe),
                               ("vector", uw_last_e[e_ - 2] if e_ >= 2 else 0)]))
                for ft in range(8):
                    par = ft % 2
                    issue_loads(gi + 4)
                    slot = gi % 4
                    for k in range(16):
                        w = []
                        if k == 0:
                            w = [(f"slw{gi % 4}", ld_cnt[gi]), ("scalar", silu_cnt[par])]
                        P.emit("tensor", lambda e, k=k, slot=slot, par=par: e.matmul(
                            PG[par], s_wb[:, slot * 2048 + k * 128: slot * 2048 + (k + 1) * 128],
                            s_ht[:, k * 512:(k + 1) * 512],
                            start=(k == 0), stop=(k == 15)),
                            waits=W_(w), inc=(k == 15))
                    tg = P.cnt["tensor"]
                    ld_done[gi] = tg
                    gi += 1
                    issue_loads(gi + 4)
                    slot = gi % 4
                    for k in range(16):
                        w = []
                        if k == 0:
                            w = [(f"slw{gi % 4}", ld_cnt[gi]), ("vector", uw_cnt[par])]
                        P.emit("tensor", lambda e, k=k, slot=slot, par=par: e.matmul(
                            PU[par], s_wb[:, slot * 2048 + k * 128: slot * 2048 + (k + 1) * 128],
                            s_ht[:, k * 512:(k + 1) * 512],
                            start=(k == 0), stop=(k == 15)),
                            waits=W_(w), inc=(k == 15))
                    tu = P.cnt["tensor"]
                    ld_done[gi] = tu
                    gi += 1
                    silu_cnt[par] = P.emit("scalar", lambda e, par=par: e.activation(
                        s_sg[:, par * 512: par * 512 + 512], PG[par], AF.Silu),
                        waits=W_([("tensor", tg), ("vector", sg_free[par])]))
                    if e_ < 8:
                        uw_cnt[par] = P.emit("vector", lambda e, par=par: e.tensor_mul(
                            s_uw[:, par * 512: par * 512 + 512], PU[par],
                            s_we[:, (e_ % 2) * 512:(e_ % 2) * 512 + 512]),
                            waits=W_([("tensor", tu), ("scalar", we_evac[e_])]))
                        zc = P.emit("vector", lambda e, par=par, ft=ft, zh=zh: e.tensor_mul(
                            s_z[:, zh + ft * 512: zh + (ft + 1) * 512],
                            s_uw[:, par * 512: par * 512 + 512],
                            s_sg[:, par * 512: par * 512 + 512]),
                            waits=W_([("scalar", silu_cnt[par])]))
                    else:
                        zc = P.emit("vector", lambda e, par=par, ft=ft, zh=zh: e.tensor_mul(
                            s_z[:, zh + ft * 512: zh + (ft + 1) * 512],
                            PU[par], s_sg[:, par * 512: par * 512 + 512]),
                            waits=W_([("tensor", tu), ("scalar", silu_cnt[par])]))
                        uw_cnt[par] = zc
                    sg_free[par] = zc
                    uw_last_e[e_] = uw_cnt[par]
                    if ft == 7:
                        z_done[e_] = zc
                for dt_ in range(16):
                    par2 = dt_ % 2
                    issue_loads(gi + 4)
                    slot = gi % 4
                    for fk in range(8):
                        w = []
                        if fk == 0:
                            w = [(f"slw{gi % 4}", ld_cnt[gi]),
                                 ("vector", max(z_done[e_], acc_cnt[par2]))]
                        P.emit("tensor", lambda e, fk=fk, slot=slot, par2=par2, zh=zh: e.matmul(
                            PD[par2], s_wb[:, slot * 2048 + fk * 128: slot * 2048 + (fk + 1) * 128],
                            s_z[:, zh + fk * 512: zh + (fk + 1) * 512],
                            start=(fk == 0), stop=(fk == 7)),
                            waits=W_(w), inc=(fk == 7))
                    td = P.cnt["tensor"]
                    ld_done[gi] = td
                    gi += 1
                    if e_ == 0:
                        acc_cnt[par2] = P.emit("vector", lambda e, dt_=dt_, par2=par2: e.tensor_copy(
                            s_acc[:, dt_ * 512:(dt_ + 1) * 512], PD[par2]),
                            waits=W_([("tensor", td)]))
                    else:
                        acc_cnt[par2] = P.emit("vector", lambda e, dt_=dt_, par2=par2: e.tensor_add(
                            s_acc[:, dt_ * 512:(dt_ + 1) * 512],
                            s_acc[:, dt_ * 512:(dt_ + 1) * 512], PD[par2]),
                            waits=W_([("tensor", td)]))
            P.barrier()

            for k in range(16):
                ser.step(nc.vector, lambda e, k=k: e.tensor_add(
                    s_acc[:, k * 512:(k + 1) * 512], s_acc[:, k * 512:(k + 1) * 512],
                    s_x3[:, k * 512:(k + 1) * 512]))
            emit_ln(ser, nc, s_acc, s_bf, s_st, s_onc, s_onr, p_s, p_rep, s_x3, LN_EPS)
            for tt in range(4):
                for k in range(16):
                    ser.step(nc.tensor, lambda e, k=k, tt=tt: e.transpose(
                        p_tp[:, 0:128], s_x3[:, k * 512 + tt * 128: k * 512 + (tt + 1) * 128],
                        s_id[:]))
                    ser.step(nc.vector, lambda e, k=k: e.tensor_copy(
                        s_tm[:, k * 128:(k + 1) * 128], p_tp[:, 0:128]))
                ser.dma(nc.sync, out_tm[t0 + tt * 128: t0 + (tt + 1) * 128, :], s_tm[:])
        P.barrier()
        P.barrier()


FUSED_SEMS = ["slw0", "slw1", "slw2", "slw3", "xld0", "xld1", "yst",
              "ost0", "ost1", "yst0", "yst1", "cc"]


def build_fused():
    nc = bass.Bass("TRN2", target_bir_lowering=False)

    def EI(name, shape, dt=F32):
        return nc.dram_tensor(name, shape, dt, kind="ExternalInput").ap()

    xA = EI("xA", [T, D])
    xAs = EI("xAs", [T, 512])
    wdtT = EI("wdtT", [D, 512])
    dtb = EI("dtb", [128, 4])
    xpw = EI("xpw", [D, 64])
    xpb = EI("xpb", [64, 1])
    alog = EI("alog", [128, 64])
    ident = EI("ident", [128, 128])
    sel = EI("sel", [64, 4096], BF16)
    wupT_sl = EI("wupT_sl", [D, 1024])
    wgT_sl = EI("wgT_sl", [D, 1024])
    wdT_sl = EI("wdT_sl", [INTER, 256])
    h128 = EI("h128", [128, 128])
    cv_in = {name: EI(f"cv_{name}_in", [rows // 8, 128]) for name, rows in PACK_SECTIONS}
    otb = EI("otb", [128, 16])
    x_tm = EI("x_tm", [TC, D])
    su_ug = EI("su_ug", [128, 16])
    wsg = EI("wsg", [128, 64])
    wsu = EI("wsu", [128, 64])
    wsd = EI("wsd", [128, 16])
    maskw = EI("maskw", [128, 896], BF16)
    ident_bf = EI("ident_bf", [128, 128], BF16)
    rmsw = EI("rmsw", [128, 16])
    rt_w = EI("rt_w", [D, 8])
    sel8 = EI("sel8", [8, 1024])
    out_tm = nc.dram_tensor("out_tm", [TC, D], F32, kind="ExternalOutput").ap()

    up_t_loc = nc.dram_tensor("up_t_loc", [8 * 16 * 128, 128], BF16)
    g_t_loc = nc.dram_tensor("g_t_loc", [8 * 16 * 128, 128], BF16)
    dn_t_loc = nc.dram_tensor("dn_t_loc", [2 * 64 * 128, 128], BF16)
    cv_out = {name: nc.dram_tensor(f"cv_{name}_out", [rows // 8, 128], BF16)
              for name, rows in PACK_SECTIONS}
    wup_pk = nc.dram_tensor("wup_pk", [64 * 16 * 128, 128], BF16)
    wg_pk = nc.dram_tensor("wg_pk", [64 * 16 * 128, 128], BF16)
    wdn_pk = nc.dram_tensor("wdn_pk", [16 * 64 * 128, 128], BF16)
    pk = {name: nc.dram_tensor(f"pk_{name}", [rows, 128], BF16)
          for name, rows in PACK_SECTIONS}
    yTs = nc.dram_tensor("yTs", [2048, 2048], BF16)
    YG = nc.dram_tensor("YG", [8192, 2048], BF16)
    yloc = nc.dram_tensor("yloc", [2048, 2048], BF16)
    qloc = nc.dram_tensor("qloc", [6144, 2048], BF16)
    ytloc = nc.dram_tensor("ytloc", [8192, 512], BF16)
    x1T = nc.dram_tensor("x1T", [D, TC], F32)
    x2T = nc.dram_tensor("x2T", [D, TC], F32)
    x3T = nc.dram_tensor("x3T", [D, TC], F32)
    qkvT = nc.dram_tensor("qkvT", [3 * D, TC], BF16)
    QGs = [nc.dram_tensor(f"QG{s}", [4 * D, TC], BF16) for s in range(3)]
    ytm = nc.dram_tensor("ytm", [T, 512], BF16)
    YAG = nc.dram_tensor("YAG", [4 * T, 512], BF16)

    G8 = [list(range(8))]
    G4 = [[0, 1, 2, 3], [4, 5, 6, 7]]

    with ExitStack() as stack:
        P = Pipe(nc, stack)
        for n in FUSED_SEMS:
            P.new_sem(n, stack)

        def ccol(kind, groups, in_h, out_h):
            i = nc.gpsimd.collective_compute(
                kind, ALU.bypass, replica_groups=groups,
                ins=[in_h.ap().opt()], outs=[out_h.ap().opt()])
            i.then_inc(P.sem["cc"], 1)
            P.cnt["cc"] += 1

        def ccbar():
            for e2 in P.ENGS:
                P._wait(e2, getattr(nc, e2), [("cc", P.cnt["cc"])])

        tW = {"wupT_sl": wupT_sl, "wgT_sl": wgT_sl, "wdT_sl": wdT_sl, "h128": h128,
              "up_t": up_t_loc.ap(), "g_t": g_t_loc.ap(), "dn_t": dn_t_loc.ap()}
        for name, _ in PACK_SECTIONS:
            tW[f"cv_{name}_in"] = cv_in[name]
            tW[f"cv_{name}_out"] = cv_out[name].ap()
        emit_W(nc, P, tW)
        emit_A(nc, P, {"xA": xA, "xAs": xAs, "wdtT": wdtT, "dtb": dtb, "xpw": xpw,
                       "xpb": xpb, "alog": alog, "ident": ident, "sel": sel,
                       "yTs": yTs.ap()})
        P.barrier(["gpsimd"])
        ccol("AllGather", G8, up_t_loc, wup_pk)
        ccol("AllGather", G8, g_t_loc, wg_pk)
        ccol("AllGather", G8, dn_t_loc, wdn_pk)
        for name, _ in PACK_SECTIONS:
            ccol("AllGather", G8, cv_out[name], pk[name])
        ccol("AllGather", G4, yTs, YG)
        ccbar()
        d512 = (nc.sync.partition_id() % 4) * 512
        yg3 = YG.ap().rearrange("(rank x) c -> x rank c", x=2048)
        yl3 = yloc.ap().rearrange("(rank x) c -> x rank c", x=512)
        for i in range(2):
            P.dma_q("dsync", yl3[:, i * 2:(i + 1) * 2, :],
                    yg3[bass.ds(d512, 512), i * 2:(i + 1) * 2, :])

        emit_B(nc, P, {"yT": yloc.ap(), "wot_pk": pk["ot"].ap(), "otb": otb,
                       "x_tm": x_tm, "ident": ident, "x1T": x1T.ap()})
        emit_C(nc, P, {"x1T": x1T.ap(), "wup_pk": wup_pk.ap(), "wg_pk": wg_pk.ap(),
                       "wdn_pk": wdn_pk.ap(), "su_ug": su_ug, "wsg": wsg, "wsu": wsu,
                       "wsd": wsd, "x2T": x2T.ap()})
        def on_kind(kind, c0, c1):
            i = nc.gpsimd.collective_compute(
                "AllGather", ALU.bypass, replica_groups=G4,
                ins=[qkvT.ap()[kind * D:(kind + 1) * D, :].opt()],
                outs=[QGs[kind].ap().opt()])
            i.then_inc(P.sem["cc"], 1)
            P.cnt["cc"] += 1

        def on_kind_w(kind, c0, c1):
            P._wait("gpsimd", nc.gpsimd, [("ost0", c0), ("ost1", c1)])
            on_kind(kind, c0, c1)

        emit_D(nc, P, {"x2T": x2T.ap(), "wqkv_pk": pk["qkv"].ap(), "qkvT": qkvT.ap()},
               on_kind=on_kind_w)
        ccbar()
        ql3 = qloc.ap().rearrange("(rs x) c -> x rs c", x=512)
        for s in range(3):
            qg3 = QGs[s].ap().rearrange("(r x) c -> x r c", x=2048)
            P.dma_q("dsync", ql3[:, s::3, :], qg3[bass.ds(d512, 512), :, :])

        emit_E(nc, P, {"qkvg": qloc.ap(), "maskw": maskw, "ident_bf": ident_bf,
                       "y_tm": ytm.ap()})
        P.barrier(["gpsimd"])
        ccol("AllGather", G4, ytm, YAG)
        ccbar()
        d2048 = (nc.sync.partition_id() % 4) * 2048
        ya3 = YAG.ap().rearrange("(r x) c -> x r c", x=8192)
        yt3 = ytloc.ap().rearrange("(r x) c -> x r c", x=2048)
        for i in range(4):
            P.dma_q("dsync", yt3[:, i:i + 1, :],
                    ya3[bass.ds(d2048, 2048), i:i + 1, :])
        P.barrier()

        emit_E2(nc, P, {"ytmg": ytloc.ap(), "wpr_pk": pk["pr"].ap(), "x2T": x2T.ap(),
                        "ident_bf": ident_bf, "x3T": x3T.ap()})
        emit_F(nc, P, {"x3T": x3T.ap(), "rmsw": rmsw, "rt_w": rt_w,
                       "wg_pk": pk["g"].ap(), "wu_pk": pk["u"].ap(),
                       "wd_pk": pk["d"].ap(), "sel8": sel8, "ident": ident,
                       "out_tm": out_tm})
        P.barrier()
    return nc


def host_fused_inputs(inp):
    mapsW = host_inputs_W(inp)
    mapsA = host_inputs_A(inp)
    suug, wsg, wsu, wsd = host_scales_C(inp)
    x = np.asarray(inp["L0_attn_x"], np.float32)
    otb = np.ascontiguousarray(
        np.asarray(inp["L0_attn_out_proj_b"], np.float32).reshape(16, 128).T)
    rmsw = np.ascontiguousarray(
        np.asarray(inp["L1_ffn_rms_weight"], np.float32).reshape(16, 128).T)
    rtw = np.ascontiguousarray(np.asarray(inp["L1_ffn_router_w"], np.float32))
    sel8 = np.zeros((8, 1024), dtype=np.float32)
    for e in range(8):
        sel8[e, e * 128:(e + 1) * 128] = 1.0
    maskw = (np.arange(896)[None, :] - 384 >= np.arange(128)[:, None]).astype(BF)
    ident_bf = np.eye(128, dtype=np.float32).astype(BF)
    maps = []
    for c in range(8):
        b, q = c // 4, c % 4
        m = {}
        m.update(mapsW[c])
        m.update(mapsA[c])
        m.update({
            "otb": otb, "x_tm": np.ascontiguousarray(x[b][q * TC:(q + 1) * TC, :]),
            "su_ug": suug, "wsg": wsg, "wsu": wsu, "wsd": wsd,
            "maskw": maskw, "ident_bf": ident_bf,
            "rmsw": rmsw, "rt_w": rtw, "sel8": sel8,
        })
        maps.append(m)
    return maps


def kernel(**inputs):
    inp = {k: np.asarray(v) for k, v in inputs.items()}
    res = run_bass_kernel_spmd(build_fused(), host_fused_inputs(inp),
                               core_ids=list(range(8)))
    out = np.empty((B, T, D), dtype=np.float32)
    for c in range(8):
        b, q = c // 4, c % 4
        out[b, q * TC:(q + 1) * TC, :] = res.results[c]["out_tm"]
    return out


# version 09
# speedup vs baseline: 1.1024x; 1.0105x over previous
import numpy as np
import ml_dtypes
from contextlib import ExitStack
import concourse.bass as bass
import concourse.mybir as mybir
from concourse.bass_utils import run_bass_kernel_spmd

F32 = mybir.dt.float32
BF16 = mybir.dt.bfloat16
AF = mybir.ActivationFunctionType
ALU = mybir.AluOpType
BF = ml_dtypes.bfloat16

B, T, D = 2, 8192, 2048
NS = 16
INTER = 8192
NH, DH = 16, 128
NE, FE = 8, 1024
SF = 1024
TC = 2048
LN_EPS = 1e-5
RMS_EPS = 1e-6


_NAME_UNIQ = [0]


def _uniq(name):
    _NAME_UNIQ[0] += 1
    return f"{name}_{_NAME_UNIQ[0]}"


def hadamard(n):
    H = np.ones((1, 1), dtype=np.float32)
    while H.shape[0] < n:
        H = np.block([[H, H], [H, -H]])
    return H


class Ser:

    def __init__(self, nc, g, gd):
        self.nc = nc
        self.g = g
        self.gd = gd
        self.gc = 0
        self.gdc = 0

    def _wait(self, eng):
        if self.gc:
            eng.wait_ge(self.g, self.gc)
        if self.gdc:
            eng.wait_ge(self.gd, self.gdc)

    def step(self, eng, fn):
        self._wait(eng)
        fn(eng).then_inc(self.g, 1)
        self.gc += 1

    def steps(self, eng, fns):
        self._wait(eng)
        ins = None
        for fn in fns:
            ins = fn(eng)
        ins.then_inc(self.g, 1)
        self.gc += 1

    def dma(self, eng, out, in_):
        self._wait(eng)
        eng.dma_start(out=out, in_=in_).then_inc(self.gd, 16)
        self.gdc += 16

    def dmas(self, eng, pairs):
        self._wait(eng)
        for out, in_ in pairs:
            eng.dma_start(out=out, in_=in_).then_inc(self.gd, 16)
            self.gdc += 16

    def finish(self, eng):
        self._wait(eng)


class Pipe:

    ENGS = ("tensor", "vector", "scalar", "gpsimd", "sync")

    def __init__(self, nc, stack):
        self.nc = nc
        self.sem = {}
        self.cnt = {}
        self.seen = {e: {} for e in self.ENGS}
        for e in self.ENGS:
            self.sem[e] = stack.enter_context(nc.semaphore("sem_" + e))
            self.cnt[e] = 0
        for q in ("dsync", "dgp", "dscal"):
            self.sem[q] = stack.enter_context(nc.semaphore("sem_" + q))
            self.cnt[q] = 0

    def _wait(self, ename, eng, waits):
        kn = self.seen[ename]
        for s, v in waits:
            if s == ename or v <= 0:
                continue
            if kn.get(s, 0) >= v:
                continue
            eng.wait_ge(self.sem[s], v)
            kn[s] = v

    SELFSYNC = {"vector": True, "scalar": True, "gpsimd": True,
                "tensor": False, "sync": False}

    def emit(self, ename, fn, waits=(), inc=True, selfsync=None):
        eng = getattr(self.nc, ename)
        if selfsync is None:
            selfsync = self.SELFSYNC[ename]
        if selfsync and self.cnt[ename] > self.seen[ename].get(ename, 0):
            eng.wait_ge(self.sem[ename], self.cnt[ename])
            self.seen[ename][ename] = self.cnt[ename]
        self._wait(ename, eng, waits)
        ins = fn(eng)
        if inc:
            ins.then_inc(self.sem[ename], 1)
            self.cnt[ename] += 1
        return self.cnt[ename]

    def dma_q(self, qname, out, in_, waits=(), inc=True, sem=None):
        ename = {"dsync": "sync", "dgp": "gpsimd", "dscal": "scalar"}[qname]
        eng = getattr(self.nc, ename)
        self._wait(ename, eng, waits)
        ins = eng.dma_start(out=out, in_=in_)
        key = sem if sem is not None else qname
        if inc:
            ins.then_inc(self.sem[key], 16)
            self.cnt[key] += 16
        return self.cnt[key]

    def new_sem(self, name, stack):
        self.sem[name] = stack.enter_context(self.nc.semaphore("sem_" + name))
        self.cnt[name] = 0

    def snap(self):
        return [(k, v) for k, v in self.cnt.items() if v > 0]

    def lazy(self, ename, fn):
        return self.emit(ename, fn, waits=self.snap())

    def ldma(self, qname, out, in_):
        return self.dma_q(qname, out, in_, waits=self.snap())

    def barrier(self, enames=None):
        w = self.snap()
        for ename in (enames or self.ENGS):
            eng = getattr(self.nc, ename)
            self._wait(ename, eng, w)

    def _name_of(self, eng):
        if not hasattr(self, "_byid"):
            self._byid = {id(getattr(self.nc, e)): e for e in self.ENGS}
        return self._byid[id(eng)]

    def step(self, eng, fn):
        return self.lazy(self._name_of(eng), fn)

    def steps(self, eng, fns):
        ename = self._name_of(eng)
        w = self.snap()
        e = getattr(self.nc, ename)
        if self.SELFSYNC[ename] and self.cnt[ename] > self.seen[ename].get(ename, 0):
            e.wait_ge(self.sem[ename], self.cnt[ename])
            self.seen[ename][ename] = self.cnt[ename]
        self._wait(ename, e, w)
        ins = None
        for fn in fns:
            ins = fn(e)
        ins.then_inc(self.sem[ename], 1)
        self.cnt[ename] += 1
        return self.cnt[ename]

    _QOF = {"sync": "dsync", "gpsimd": "dgp", "scalar": "dscal"}

    def dma(self, eng, out, in_):
        return self.dma_q(self._QOF[self._name_of(eng)], out, in_, waits=self.snap())

    def dmas(self, eng, pairs):
        ename = self._name_of(eng)
        qname = self._QOF[ename]
        e = getattr(self.nc, ename)
        self._wait(ename, e, self.snap())
        for out, in_ in pairs:
            e.dma_start(out=out, in_=in_).then_inc(self.sem[qname], 16)
            self.cnt[qname] += 16
        return self.cnt[qname]

    def finish(self, eng):
        self._wait(self._name_of(eng), eng, self.snap())


ACH = 1024


def emit_A(nc, P, t):
    xA = t["xA"]
    xAs = t["xAs"]
    wdtT = t["wdtT"]
    dtb = t["dtb"]
    xpw = t["xpw"]
    xpb = t["xpb"]
    alog = t["alog"]
    ident = t["ident"]
    sel = t["sel"]
    yTs = t["yTs"]

    nch = T // ACH
    with ExitStack() as stack:
        def sb(name, shape, dt=F32):
            return stack.enter_context(nc.sbuf_tensor(_uniq(name), shape, dt))
        s_xtok = sb("s_xtok", [128, 2 * D])
        s_xT = sb("s_xT", [128, 16 * ACH], BF16)
        s_xs = sb("s_xs", [128, 4 * ACH], BF16)
        s_wdt = sb("s_wdt", [128, 16 * 512], BF16)
        s_xpw = sb("s_xpw", [128, 16 * 64], BF16)
        s_sel = sb("s_sel", [64, 4096], BF16)
        s_id = sb("s_id", [128, 128])
        s_dtb = sb("s_dtb", [128, 4])
        s_xpb = sb("s_xpb", [64, 1])
        s_A = sb("s_A", [128, 64])
        s_dtT = sb("s_dtT", [128, 2 * 4 * ACH])
        s_dtx = sb("s_dtx", [128, 2 * 4 * ACH])
        s_bc = sb("s_bc", [64, 2 * ACH], BF16)
        s_dA = sb("s_dA", [128, 2 * ACH])
        s_spt = sb("s_spt", [128, 512])
        s_h = sb("s_h", [128, 2 * ACH])
        s_tmp = sb("s_tmp", [128, 2 * ACH])
        s_u = sb("s_u", [128, ACH])
        s_y = sb("s_y", [128, 4 * ACH])
        s_ybf = sb("s_ybf", [128, 4 * ACH], BF16)
        s_hst = sb("s_hst", [128, 64])
        p = stack.enter_context(nc.psum_tensor(_uniq("p"), [128, 4096], F32))
        PTP = [p[:, 0:128], p[:, 512:640]]
        PDT = [p[:, 1024:1536], p[:, 1536:2048]]
        PB = p[:, 2048:3072]
        PBC = p[0:64, 1024:1536]
        PC = p[:, 3072:4096]
        ser = P

        ser.dmas(nc.sync, [
            (s_sel[:], sel), (s_id[:], ident), (s_dtb[:], dtb), (s_xpb[:], xpb),
        ])
        ser.dma(nc.sync, s_A[:], alog)
        ser.step(nc.scalar, lambda e: e.activation(s_A[:], s_A[:], AF.Exp))
        ser.step(nc.vector, lambda e: e.tensor_scalar_mul(s_A[:], s_A[:], -1.0))
        ser.dmas(nc.gpsimd, [
            (s_wdt[:, (k * 4 + et) * 128:(k * 4 + et + 1) * 128],
             wdtT[k * 128:(k + 1) * 128, et * 128:(et + 1) * 128])
            for k in range(16) for et in range(4)
        ])
        ser.dmas(nc.gpsimd, [
            (s_xpw[:, k * 64:(k + 1) * 64], xpw[k * 128:(k + 1) * 128, :])
            for k in range(16)
        ])
        P.barrier()

        tp_done = [0, 0]
        ev_s = [0, 0]
        ev_v = [0, 0]
        xld_cnt = [0, 0]
        xdma_free = [0, 0]
        sp_done = [0, 0]
        dtx_free = [0, 0]
        bc_ev = [0, 0]
        dA_cnt = [0, 0]
        dA_free = [0, 0]
        hst_done = [0] * 64
        hstc_free = [0, 0]
        yst_cnt = [0]
        chk = [dict() for _ in range(nch)]

        def prep_units(ch):
            t0 = ch * ACH
            xw = ch % 2
            units = []

            def u_xT(tt):
                tb = tt % 2
                xld_cnt[tb] = P.dma_q(
                    "dsync", s_xtok[:, tb * D: tb * D + D],
                    xA[t0 + tt * 128: t0 + (tt + 1) * 128, :],
                    waits=[("tensor", xdma_free[tb])], sem=f"xld{tb}")
                for k in range(16):
                    j = k % 2
                    w = [(f"xld{tb}", xld_cnt[tb]), ("scalar", ev_s[j]), ("vector", ev_v[j])]
                    tp_done[j] = P.emit("tensor", lambda e, k=k, tb=tb, j=j: e.transpose(
                        PTP[j], s_xtok[:, tb * D + k * 128: tb * D + (k + 1) * 128],
                        s_id[:]), waits=w)
                    ev_s[j] = P.emit("scalar", lambda e, k=k, tt=tt, j=j: e.copy(
                        s_xT[:, k * ACH + tt * 128: k * ACH + (tt + 1) * 128], PTP[j]),
                        waits=[("tensor", tp_done[j])])
                xdma_free[tb] = P.cnt["tensor"]

            def u_xs(tt):
                tb = tt % 2
                xld_cnt[tb] = P.dma_q(
                    "dsync", s_xtok[:, tb * D: tb * D + 512],
                    xAs[t0 + tt * 128: t0 + (tt + 1) * 128, :],
                    waits=[("tensor", xdma_free[tb])], sem=f"xld{tb}")
                for dt4 in range(4):
                    j = dt4 % 2
                    w = [(f"xld{tb}", xld_cnt[tb]), ("scalar", ev_s[j]), ("vector", ev_v[j])]
                    tp_done[j] = P.emit("tensor", lambda e, dt4=dt4, tb=tb, j=j: e.transpose(
                        PTP[j], s_xtok[:, tb * D + dt4 * 128: tb * D + (dt4 + 1) * 128],
                        s_id[:]), waits=w)
                    ev_v[j] = P.emit("vector", lambda e, dt4=dt4, tt=tt, j=j: e.tensor_copy(
                        s_xs[:, dt4 * ACH + tt * 128: dt4 * ACH + (tt + 1) * 128], PTP[j]),
                        waits=[("tensor", tp_done[j])])
                xdma_free[tb] = P.cnt["tensor"]

            def u_dt(g):
                et, tc2 = g // 2, g % 2
                jd = g % 2
                if g == 0:
                    chk[ch]["xTev"] = P.cnt["scalar"]
                for k in range(16):
                    w = []
                    if k == 0:
                        w = [("scalar", max(chk[ch]["xTev"], sp_done[jd])),
                             ("vector", max(bc_ev[0], bc_ev[1]))]
                    P.emit("tensor", lambda e, et=et, tc2=tc2, k=k, jd=jd: e.matmul(
                        PDT[jd],
                        s_wdt[:, (k * 4 + et) * 128:(k * 4 + et + 1) * 128],
                        s_xT[:, k * ACH + tc2 * 512: k * ACH + (tc2 + 1) * 512],
                        start=(k == 0), stop=(k == 15)), waits=w, inc=(k == 15))
                tdt = P.cnt["tensor"]
                P.emit("scalar", lambda e, et=et, jd=jd: e.activation(
                    s_spt[:, 0:512], PDT[jd], AF.Exp, bias=s_dtb[:, et:et + 1]),
                    waits=[("tensor", tdt), ("vector", dtx_free[xw])])
                sp_done[jd] = P.emit("scalar", lambda e, et=et, tc2=tc2, xw=xw: e.activation(
                    s_dtT[:, xw * 4 * ACH + et * ACH + tc2 * 512:
                          xw * 4 * ACH + et * ACH + (tc2 + 1) * 512],
                    s_spt[:, 0:512], AF.Ln, bias=1.0))
                if g == 7:
                    chk[ch]["sp"] = P.cnt["scalar"]

            def u_bc(tc2):
                for k in range(16):
                    w = []
                    if k == 0:
                        w = [("scalar", max(sp_done[0], sp_done[1])),
                             ("vector", max(bc_ev[0], bc_ev[1]))]
                    P.emit("tensor", lambda e, tc2=tc2, k=k: e.matmul(
                        PBC[:, 0:512], s_xpw[:, k * 64:(k + 1) * 64],
                        s_xT[:, k * ACH + tc2 * 512: k * ACH + (tc2 + 1) * 512],
                        start=(k == 0), stop=(k == 15)), waits=w, inc=(k == 15))
                tbc = P.cnt["tensor"]
                bc_ev[xw] = P.emit("vector", lambda e, tc2=tc2, xw=xw: e.tensor_scalar_add(
                    s_bc[:, xw * ACH + tc2 * 512: xw * ACH + (tc2 + 1) * 512],
                    PBC[:, 0:512], s_xpb[:, 0:1]), waits=[("tensor", tbc)])
                if tc2 == 1:
                    chk[ch]["bc"] = P.cnt["vector"]

            for tt in range(8):
                units.append(lambda tt=tt: u_xT(tt))
            for tt in range(8):
                units.append(lambda tt=tt: u_xs(tt))
            for g in range(8):
                units.append(lambda g=g: u_dt(g))
            for tc2 in range(2):
                units.append(lambda tc2=tc2: u_bc(tc2))
            return units

        def emit_scan(ch, next_units):
            t0 = ch * ACH
            xw = ch % 2
            for dt4 in range(4):
                dtx_free[xw] = P.emit("vector", lambda e, dt4=dt4, xw=xw: e.tensor_mul(
                    s_dtx[:, xw * 4 * ACH + dt4 * ACH: xw * 4 * ACH + (dt4 + 1) * ACH],
                    s_dtT[:, xw * 4 * ACH + dt4 * ACH: xw * 4 * ACH + (dt4 + 1) * ACH],
                    s_xs[:, dt4 * ACH:(dt4 + 1) * ACH]),
                    waits=[("scalar", chk[ch]["sp"])])
            ui = [0]
            tmp_free = [0, 0]
            yv_last = [0, 0, 0, 0]
            ya_last = 0

            def run_units(kmax):
                while ui[0] < min(kmax, len(next_units)):
                    next_units[ui[0]]()
                    ui[0] += 1

            u_last = 0
            ym_last = 0
            for n in range(NS):
                for tc2 in range(ACH // 512):
                    w = [("vector", max(u_last, chk[ch]["bc"]))] if tc2 == 0 else []
                    P.emit("tensor", lambda e, n=n, tc2=tc2, xw=xw: e.matmul(
                        PB[:, tc2 * 512:(tc2 + 1) * 512],
                        s_sel[:, n * 128:(n + 1) * 128],
                        s_bc[:, xw * ACH + tc2 * 512: xw * ACH + (tc2 + 1) * 512],
                        start=True, stop=True), waits=w, inc=(tc2 == 1))
                repB = P.cnt["tensor"]
                for tc2 in range(ACH // 512):
                    w = [("vector", ym_last)] if tc2 == 0 else []
                    P.emit("tensor", lambda e, n=n, tc2=tc2, xw=xw: e.matmul(
                        PC[:, tc2 * 512:(tc2 + 1) * 512],
                        s_sel[:, (16 + n) * 128:(17 + n) * 128],
                        s_bc[:, xw * ACH + tc2 * 512: xw * ACH + (tc2 + 1) * 512],
                        start=True, stop=True), waits=w, inc=(tc2 == 1))
                repC = P.cnt["tensor"]
                for dt4 in range(4):
                    col = dt4 * 16 + n
                    j2 = (n * 4 + dt4) % 2
                    dA_cnt[j2] = P.emit("scalar", lambda e, dt4=dt4, col=col, j2=j2, xw=xw: e.activation(
                        s_dA[:, j2 * ACH:(j2 + 1) * ACH],
                        s_dtT[:, xw * 4 * ACH + dt4 * ACH: xw * 4 * ACH + (dt4 + 1) * ACH],
                        AF.Exp, scale=s_A[:, col:col + 1]),
                        waits=[("vector", dA_free[j2]), ("scalar", chk[ch]["sp"])])
                    u_last = P.emit("vector", lambda e, dt4=dt4, xw=xw: e.tensor_mul(
                        s_u[:], s_dtx[:, xw * 4 * ACH + dt4 * ACH: xw * 4 * ACH + (dt4 + 1) * ACH],
                        PB[:]), waits=[("tensor", repB)])
                    if ch == 0:
                        scv = P.emit("vector", lambda e, j2=j2: e.tensor_tensor_scan(
                            s_h[:, j2 * ACH:(j2 + 1) * ACH], s_dA[:, j2 * ACH:(j2 + 1) * ACH],
                            s_u[:], 0.0, ALU.mult, ALU.add),
                            waits=[("scalar", dA_cnt[j2]), ("gpsimd", hstc_free[j2])])
                    else:
                        scv = P.emit("vector", lambda e, col=col, j2=j2: e.tensor_tensor_scan(
                            s_h[:, j2 * ACH:(j2 + 1) * ACH], s_dA[:, j2 * ACH:(j2 + 1) * ACH],
                            s_u[:], s_hst[:, col:col + 1], ALU.mult, ALU.add),
                            waits=[("scalar", dA_cnt[j2]), ("gpsimd", hst_done[col]),
                                   ("gpsimd", hstc_free[j2])])
                    dA_free[j2] = scv
                    hst_done[col] = P.emit("gpsimd", lambda e, col=col, j2=j2: e.tensor_copy(
                        s_hst[:, col:col + 1], s_h[:, (j2 + 1) * ACH - 1:(j2 + 1) * ACH]),
                        waits=[("vector", scv)])
                    hstc_free[j2] = hst_done[col]
                    if n == 0:
                        w = [("tensor", repC)]
                        if ch > 0 and dt4 == 0:
                            w.append(("yst", yst_cnt[0]))
                        ym_last = P.emit("vector", lambda e, dt4=dt4, j2=j2: e.tensor_mul(
                            s_y[:, dt4 * ACH:(dt4 + 1) * ACH],
                            s_h[:, j2 * ACH:(j2 + 1) * ACH], PC[:]), waits=w)
                        ya_last = 0
                        yv_last[dt4] = ym_last
                    else:
                        ym_last = P.emit("vector", lambda e, j2=j2: e.tensor_mul(
                            s_tmp[:, j2 * ACH:(j2 + 1) * ACH],
                            s_h[:, j2 * ACH:(j2 + 1) * ACH], PC[:]),
                            waits=[("tensor", repC), ("gpsimd", tmp_free[j2])])
                        ydst = s_ybf if n == NS - 1 else s_y
                        ya_last = P.emit("gpsimd", lambda e, dt4=dt4, j2=j2, ydst=ydst: e.tensor_add(
                            ydst[:, dt4 * ACH:(dt4 + 1) * ACH],
                            s_y[:, dt4 * ACH:(dt4 + 1) * ACH],
                            s_tmp[:, j2 * ACH:(j2 + 1) * ACH]),
                            waits=[("vector", max(ym_last, yv_last[dt4]))])
                        tmp_free[j2] = ya_last
                pass
            run_units(len(next_units))
            jblk, c0 = (t0 // 2048), (t0 % 2048)
            for dt4 in range(4):
                r0 = jblk * 512 + dt4 * 128
                yst_cnt[0] = P.dma_q("dsync", yTs[r0:r0 + 128, c0:c0 + ACH],
                                     s_ybf[:, dt4 * ACH:(dt4 + 1) * ACH],
                                     waits=[("gpsimd", ya_last)], sem="yst")

        for u in prep_units(0):
            u()
        for ch in range(nch):
            nxt = prep_units(ch + 1) if ch + 1 < nch else []
            emit_scan(ch, nxt)
        P.barrier()

def host_inputs_A(inp):
    x = np.asarray(inp["L0_attn_x"], dtype=np.float32)
    wdtT = np.ascontiguousarray(np.asarray(inp["L0_attn_dt_proj_w"], np.float32).T)
    dtb = np.asarray(inp["L0_attn_dt_proj_b"], np.float32)
    xpwT = np.ascontiguousarray(np.asarray(inp["L0_attn_x_proj_w"], np.float32).T)
    xpb = np.asarray(inp["L0_attn_x_proj_b"], np.float32)
    alog = np.asarray(inp["L0_attn_A_log"], np.float32)
    ident = np.eye(128, dtype=np.float32)
    selm = np.zeros((64, 4096), dtype=np.float32)
    for n in range(16):
        selm[n, n * 128:(n + 1) * 128] = 1.0
        selm[32 + n, (16 + n) * 128:(17 + n) * 128] = 1.0
    xpw_pad = np.zeros((D, 64), dtype=np.float32)
    xpw_pad[:, 0:16] = xpwT[:, 0:16]
    xpw_pad[:, 32:48] = xpwT[:, 16:32]
    xpb_pad = np.zeros((64, 1), dtype=np.float32)
    xpb_pad[0:16, 0] = xpb[0:16]
    xpb_pad[32:48, 0] = xpb[16:32]
    maps = []
    for c in range(8):
        b, ds = c // 4, c % 4
        d0 = ds * 512
        maps.append({
            "xA": np.ascontiguousarray(x[b]),
            "xAs": np.ascontiguousarray(x[b][:, d0:d0 + 512]),
            "wdtT": np.ascontiguousarray(wdtT[:, d0:d0 + 512]),
            "dtb": np.ascontiguousarray(dtb[d0:d0 + 512].reshape(4, 128).T),
            "xpw": xpw_pad,
            "xpb": xpb_pad,
            "alog": np.ascontiguousarray(
                alog[d0:d0 + 512].reshape(4, 128, 16).transpose(1, 0, 2).reshape(128, 64)),
            "ident": ident,
            "sel": selm.astype(BF),
        })
    return maps


def sp_pack(Wt):
    Kf, Mf = Wt.shape
    Kt, Mt = Kf // 128, Mf // 128
    return np.ascontiguousarray(
        Wt.reshape(Kt, 128, Mt, 128).transpose(2, 0, 1, 3).reshape(Mt * Kt * 128, 128))


def load_sp(ser, eng, s_tile, dram_pack, mt, Kt):
    src = dram_pack[mt * Kt * 128:(mt + 1) * Kt * 128, :].rearrange(
        "(kt p) f -> p kt f", p=128)
    dst = s_tile[:, 0:Kt * 128].rearrange("p (kt f) -> p kt f", f=128)
    ser.dma(eng, dst, src)


def emit_load_xT(ser, nc, x_tm, t0, ntt, s_tok, s_dst, p_tp, s_id, ach, ndk=16):
    for tt in range(ntt):
        ser.dma(nc.sync, s_tok[:, 0:ndk * 128],
                x_tm[t0 + tt * 128: t0 + (tt + 1) * 128, :])
        for k in range(ndk):
            ser.step(nc.tensor, lambda e, k=k: e.transpose(
                p_tp[:], s_tok[:, k * 128:(k + 1) * 128], s_id[:]))
            ser.step(nc.vector, lambda e, k=k, tt=tt: e.tensor_copy(
                s_dst[:, k * ach + tt * 128: k * ach + (tt + 1) * 128], p_tp[:]))


def emit_ln(ser, nc, s_s, s_bf, s_st, s_onc, s_onr, p_s, p_rep, s_out, eps, w=512):
    for k in range(16):
        ser.step(nc.scalar, lambda e, k=k: e.copy(
            s_bf[:, k * w:(k + 1) * w], s_s[:, k * w:(k + 1) * w]))
    for k in range(16):
        ser.step(nc.tensor, lambda e, k=k: e.matmul(
            p_s[0:1, 0:w], s_onc[:, 0:1], s_bf[:, k * w:(k + 1) * w],
            start=(k == 0), stop=(k == 15)))
    for k in range(16):
        ser.step(nc.scalar, lambda e, k=k: e.activation(
            s_bf[:, k * w:(k + 1) * w], s_s[:, k * w:(k + 1) * w], AF.Square))
    for k in range(16):
        ser.step(nc.tensor, lambda e, k=k: e.matmul(
            p_s[0:1, w:2 * w], s_onc[:, 0:1], s_bf[:, k * w:(k + 1) * w],
            start=(k == 0), stop=(k == 15)))
    mu, e2, va, rs, nm = (s_st[0:1, i * w:(i + 1) * w] for i in range(5))
    ser.step(nc.vector, lambda e: e.tensor_scalar_mul(mu, p_s[0:1, 0:w], 1.0 / 2048))
    ser.step(nc.vector, lambda e: e.tensor_scalar_mul(e2, p_s[0:1, w:2 * w], 1.0 / 2048))
    ser.step(nc.vector, lambda e: e.tensor_mul(va, mu, mu))
    ser.step(nc.vector, lambda e: e.tensor_sub(va, e2, va))
    ser.step(nc.vector, lambda e: e.tensor_scalar_add(va, va, float(eps)))
    ser.step(nc.scalar, lambda e: e.activation(va, va, AF.Ln))
    ser.step(nc.vector, lambda e: e.tensor_scalar_mul(va, va, -0.5))
    ser.step(nc.scalar, lambda e: e.activation(rs, va, AF.Exp))
    ser.step(nc.vector, lambda e: e.tensor_mul(nm, mu, rs))
    ser.step(nc.vector, lambda e: e.tensor_scalar_mul(nm, nm, -1.0))
    ser.step(nc.tensor, lambda e: e.matmul(
        p_rep[:, 0:w], s_onr[0:1, :], rs, start=True, stop=True))
    ser.step(nc.tensor, lambda e: e.matmul(
        p_rep[:, w:2 * w], s_onr[0:1, :], nm, start=True, stop=True))
    for k in range(16):
        ser.step(nc.vector, lambda e, k=k: e.tensor_mul(
            s_out[:, k * w:(k + 1) * w], s_s[:, k * w:(k + 1) * w], p_rep[:, 0:w]))
        ser.step(nc.vector, lambda e, k=k: e.tensor_add(
            s_out[:, k * w:(k + 1) * w], s_out[:, k * w:(k + 1) * w],
            p_rep[:, w:2 * w]))


def emit_W(nc, P, t):
    wupT = t["wupT_sl"]
    wgT = t["wgT_sl"]
    wdT = t["wdT_sl"]
    h128 = t["h128"]
    up_t = t["up_t"]
    g_t = t["g_t"]
    dn_t = t["dn_t"]

    with ExitStack() as stack:
        def sb(name, shape, dt=F32):
            return stack.enter_context(nc.sbuf_tensor(_uniq(name), shape, dt))
        s_w = sb("s_w", [128, 16 * 1024])
        s_h = sb("s_h", [128, 128])
        s_t = sb("s_t", [128, 1024])
        s_o = sb("s_o", [128, 16 * 1024], BF16)
        p_m = stack.enter_context(nc.psum_tensor(_uniq("p_m"), [128, 512], F32))
        ser = P
        ser.dma(nc.sync, s_h[:], h128)

        def transform(srcT, dst, ntile, width, kind):
            ser.dmas(nc.sync, [
                (s_w[:, i * width:(i + 1) * width], srcT[i * 128:(i + 1) * 128, :])
                for i in range(ntile)])
            for i in range(ntile):
                for cqq in range(0, width, 512):
                    cw = min(512, width - cqq)
                    ser.step(nc.tensor, lambda e, i=i, cq=cqq, cw=cw: e.matmul(
                        p_m[:, 0:cw], s_h[:], s_w[:, i * width + cq: i * width + cq + cw],
                        start=True, stop=True))
                    ser.step(nc.vector, lambda e, i=i, cq=cqq, cw=cw: e.tensor_copy(
                        s_w[:, i * width + cq: i * width + cq + cw], p_m[:, 0:cw]))
            h = 1
            while h < ntile:
                for base in range(0, ntile, 2 * h):
                    for j in range(h):
                        a = (base + j) * width
                        b2 = (base + j + h) * width
                        ser.step(nc.vector, lambda e, a=a, b2=b2: e.tensor_sub(
                            s_t[:, 0:width], s_w[:, a:a + width], s_w[:, b2:b2 + width]))
                        ser.step(nc.vector, lambda e, a=a, b2=b2: e.tensor_add(
                            s_w[:, a:a + width], s_w[:, a:a + width], s_w[:, b2:b2 + width]))
                        ser.step(nc.vector, lambda e, b2=b2: e.tensor_copy(
                            s_w[:, b2:b2 + width], s_t[:, 0:width]))
                h *= 2
            for i in range(ntile):
                ser.step(nc.vector, lambda e, i=i: e.tensor_copy(
                    s_o[:, i * width:(i + 1) * width], s_w[:, i * width:(i + 1) * width]))
            if kind == "up":
                for ft in range(8):
                    ser.dma(nc.sync,
                            dst[ft * 16 * 128:(ft + 1) * 16 * 128, :].rearrange(
                                "(kt p) f -> p kt f", p=128),
                            s_o.rearrange("p (kt g) -> p kt g", g=1024)[
                                :, :, ft * 128:(ft + 1) * 128])
            else:
                for dt_ in range(2):
                    ser.dma(nc.sync,
                            dst[dt_ * 64 * 128:(dt_ + 1) * 64 * 128, :].rearrange(
                                "(fk p) f -> p fk f", p=128),
                            s_o.rearrange("p (fk g) -> p fk g", g=256)[
                                :, :, dt_ * 128:(dt_ + 1) * 128])

        transform(wupT, up_t, 16, 1024, "up")
        transform(wgT, g_t, 16, 1024, "up")
        transform(wdT, dn_t, 64, 256, "dn")
        for name, rows in PACK_SECTIONS:
            cin = t[f"cv_{name}_in"]
            cout = t[f"cv_{name}_out"]
            n = rows // 8
            csz = 4096 if n % 4096 == 0 else n // 4 if n % 4 == 0 else n
            r = 0
            while r < n:
                c = min(csz, n - r)
                ser.dma(nc.gpsimd, cout[r:r + c, :], cin[r:r + c, :])
                r += c
        P.barrier()


PACK_SECTIONS = (("qkv", 98304), ("pr", 32768), ("ot", 32768),
                 ("g", 147456), ("u", 147456), ("d", 147456))


def host_packs_f32(inp):
    pk = {}
    pk["qkv"] = sp_pack(np.ascontiguousarray(np.asarray(inp["L1_attn_W_qkv"], np.float32).T))
    pk["pr"] = sp_pack(np.ascontiguousarray(np.asarray(inp["L1_attn_W_proj"], np.float32).T))
    pk["ot"] = sp_pack(np.ascontiguousarray(np.asarray(inp["L0_attn_out_proj_w"], np.float32).T))
    wg = np.asarray(inp["L1_ffn_w_gate"], np.float32)
    wu = np.asarray(inp["L1_ffn_w_up"], np.float32)
    wd = np.asarray(inp["L1_ffn_w_down"], np.float32)
    pk["g"] = np.concatenate([sp_pack(wg[e]) for e in range(8)]
                             + [sp_pack(np.asarray(inp["L1_ffn_sw_gate"], np.float32))], axis=0)
    pk["u"] = np.concatenate([sp_pack(wu[e]) for e in range(8)]
                             + [sp_pack(np.asarray(inp["L1_ffn_sw_up"], np.float32))], axis=0)
    pk["d"] = np.concatenate([sp_pack(wd[e]) for e in range(8)]
                             + [sp_pack(np.asarray(inp["L1_ffn_sw_down"], np.float32))], axis=0)
    return pk


def host_inputs_W(inp):
    wupT = np.ascontiguousarray(np.asarray(inp["L0_ffn_W_up"], np.float32).T)
    wgT = np.ascontiguousarray(np.asarray(inp["L0_ffn_W_gate"], np.float32).T)
    wdT = np.ascontiguousarray(np.asarray(inp["L0_ffn_W_down"], np.float32).T)
    h = hadamard(128)
    pk = host_packs_f32(inp)
    maps = []
    for c in range(8):
        m = {
            "wupT_sl": np.ascontiguousarray(wupT[:, c * 1024:(c + 1) * 1024]),
            "wgT_sl": np.ascontiguousarray(wgT[:, c * 1024:(c + 1) * 1024]),
            "wdT_sl": np.ascontiguousarray(wdT[:, c * 256:(c + 1) * 256]),
            "h128": h,
        }
        for name, rows in PACK_SECTIONS:
            n = rows // 8
            m[f"cv_{name}_in"] = np.ascontiguousarray(pk[name][c * n:(c + 1) * n])
        maps.append(m)
    return maps


def assemble_packs_bf16(resW):
    out = {}
    for name, rows in PACK_SECTIONS:
        out[name] = np.concatenate(
            [resW.results[c][f"cv_{name}_out"] for c in range(8)], axis=0)
    return out


def unpack_fold(resW):
    wup_pk = np.concatenate([resW.results[c]["up_t"] for c in range(8)], axis=0)
    wg_pk = np.concatenate([resW.results[c]["g_t"] for c in range(8)], axis=0)
    wdn_pk = np.concatenate([resW.results[c]["dn_t"] for c in range(8)], axis=0)
    return wup_pk, wg_pk, wdn_pk


def emit_B(nc, P, t):
    yT = t["yT"]
    wot_pk = t["wot_pk"]
    otb = t["otb"]
    x_tm = t["x_tm"]
    ident = t["ident"]
    x1T = t["x1T"]

    with ExitStack() as stack:
        def sb(name, shape, dt=F32):
            return stack.enter_context(nc.sbuf_tensor(_uniq(name), shape, dt))
        s_y = sb("s_y", [128, 16 * TC], BF16)
        s_wb = sb("s_wb", [128, 4 * 2048], BF16)
        s_otb = sb("s_otb", [128, 16])
        s_id = sb("s_id", [128, 128])
        s_tok = sb("s_tok", [128, 2 * D])
        s_xr = sb("s_xr", [128, 16 * 512])
        s_s = sb("s_s", [128, 16 * 512])
        s_bf = sb("s_bf", [128, 16 * 512], BF16)
        s_st = sb("s_st", [1, 8 * 512])
        s_onc = sb("s_onc", [128, 1], BF16)
        s_onr = sb("s_onr", [1, 128])
        s_x1 = s_xr
        p = stack.enter_context(nc.psum_tensor(_uniq("p"), [128, 4096], F32))
        PTP = [p[:, 0:128], p[:, 512:640]]
        PM = [p[:, 1024:1536], p[:, 1536:2048]]
        p_s = p[0:1, 2048:3072]
        p_rep = p[:, 3072:4096]
        ser = P

        ser.dmas(nc.sync, [(s_otb[:], otb), (s_id[:], ident)])
        ser.step(nc.vector, lambda e: e.memset(s_onc[:, 0:1], 1.0))
        ser.step(nc.vector, lambda e: e.memset(s_onr[0:1, :], 1.0))
        ser.dmas(nc.sync, [
            (s_y[:, k * TC:(k + 1) * TC], yT[k * 128:(k + 1) * 128, :])
            for k in range(16)])
        P.barrier()

        NLD = 16
        NTC = TC // 512
        li = [0]
        ld_done = [0] * (NLD * NTC)
        ld_cnt = [0] * (NLD * NTC)

        def issue_loads(upto):
            while li[0] < min(upto, NLD * NTC):
                i = li[0]
                mt = i % NLD
                slot = i % 4
                srcv = wot_pk[mt * 16 * 128:(mt + 1) * 16 * 128, :].rearrange(
                    "(kt p) f -> p kt f", p=128)
                dstv = s_wb[:, slot * 2048: slot * 2048 + 2048].rearrange(
                    "p (kt f) -> p kt f", f=128)
                w = []
                if i >= 4:
                    w = [("tensor", ld_done[i - 4])]
                qn = "dgp" if (i % 2 == 0) else "dsync"
                ld_cnt[i] = P.dma_q(qn, dstv, srcv, waits=w, sem=f"slw{i % 4}")
                li[0] += 1

        tp_done = [0, 0]
        ev_v = [0, 0]
        xld_cnt = [0, 0]
        xdma_free = [0, 0]
        ev_cnt = [0, 0]
        for tc in range(NTC):
            t0 = tc * 512
            for tt in range(4):
                tb = tt % 2
                xld_cnt[tb] = P.dma_q(
                    "dsync", s_tok[:, tb * D: tb * D + D],
                    x_tm[t0 + tt * 128: t0 + (tt + 1) * 128, :],
                    waits=[("tensor", xdma_free[tb])], sem=f"xld{tb}")
                for k in range(16):
                    j = k % 2
                    w = [(f"xld{tb}", xld_cnt[tb]), ("vector", ev_v[j])]
                    tp_done[j] = P.emit("tensor", lambda e, k=k, tb=tb, j=j: e.transpose(
                        PTP[j], s_tok[:, tb * D + k * 128: tb * D + (k + 1) * 128],
                        s_id[:]), waits=w)
                    ev_v[j] = P.emit("vector", lambda e, k=k, tt=tt, j=j: e.tensor_copy(
                        s_xr[:, k * 512 + tt * 128: k * 512 + (tt + 1) * 128], PTP[j]),
                        waits=[("tensor", tp_done[j])])
                xdma_free[tb] = P.cnt["tensor"]
            xrv = P.cnt["vector"]
            for mt in range(16):
                gi = tc * NLD + mt
                issue_loads(gi + 4)
                slot = gi % 4
                par = mt % 2
                for k in range(16):
                    w = []
                    if k == 0:
                        w = [(f"slw{gi % 4}", ld_cnt[gi]), ("vector", ev_cnt[par])]
                    P.emit("tensor", lambda e, mt=mt, k=k, slot=slot, par=par: e.matmul(
                        PM[par], s_wb[:, slot * 2048 + k * 128: slot * 2048 + (k + 1) * 128],
                        s_y[:, k * TC + t0: k * TC + t0 + 512],
                        start=(k == 0), stop=(k == 15)),
                        waits=w, inc=(k == 15))
                tm = P.cnt["tensor"]
                ld_done[gi] = tm
                ev_cnt[par] = P.emit("vector", lambda e, mt=mt, par=par: e.scalar_tensor_tensor(
                    s_s[:, mt * 512:(mt + 1) * 512], PM[par], s_otb[:, mt:mt + 1],
                    s_xr[:, mt * 512:(mt + 1) * 512], ALU.add, ALU.add),
                    waits=[("tensor", tm), ("vector", xrv)])
            P.barrier()
            emit_ln(ser, nc, s_s, s_bf, s_st, s_onc, s_onr, p_s, p_rep,
                    s_x1, LN_EPS)
            ser.dmas(nc.sync, [
                (x1T[k * 128:(k + 1) * 128, t0:t0 + 512],
                 s_x1[:, k * 512:(k + 1) * 512]) for k in range(16)])
            P.barrier()
        P.barrier()


def emit_C(nc, P, t):
    x1T = t["x1T"]
    wup_pk = t["wup_pk"]
    wg_pk = t["wg_pk"]
    wdn_pk = t["wdn_pk"]
    su_ug = t["su_ug"]
    wsg = t["wsg"]
    wsu = t["wsu"]
    wsd = t["wsd"]
    x2T = t["x2T"]

    with ExitStack() as stack:
        def sb(name, shape, dt=F32):
            return stack.enter_context(nc.sbuf_tensor(_uniq(name), shape, dt))
        s_x1 = sb("s_x1", [128, 16 * 512])
        s_xs = sb("s_xs", [128, 16 * 512], BF16)
        s_su = sb("s_su", [128, 16])
        s_wsg = sb("s_wsg", [128, 64])
        s_wsu = sb("s_wsu", [128, 64])
        s_wsd = sb("s_wsd", [128, 16])
        s_wb = sb("s_wb", [128, 4 * 2048], BF16)
        s_sg = sb("s_sg", [128, 2 * 512])
        s_z = sb("s_z", [128, 64 * 512], BF16)
        s_s = sb("s_s", [128, 16 * 512])
        s_bf = sb("s_bf", [128, 16 * 512], BF16)
        s_st = sb("s_st", [1, 8 * 512])
        s_onc = sb("s_onc", [128, 1], BF16)
        s_onr = sb("s_onr", [1, 128])
        s_x2 = s_x1
        p = stack.enter_context(nc.psum_tensor(_uniq("p"), [128, 4096], F32))
        PG = [p[:, 0:512], p[:, 512:1024]]
        PU = [p[:, 1024:1536], p[:, 1536:2048]]
        PD = [p[:, 2048:2560], p[:, 2560:3072]]
        p_rep = p[:, 3072:4096]
        p_s = p[0:1, 2048:3072]
        ser = P

        ser.dmas(nc.sync, [(s_su[:], su_ug), (s_wsg[:], wsg), (s_wsu[:], wsu),
                           (s_wsd[:], wsd)])
        ser.step(nc.vector, lambda e: e.memset(s_onc[:, 0:1], 1.0))
        ser.step(nc.vector, lambda e: e.memset(s_onr[0:1, :], 1.0))

        loads = []
        for ft in range(64):
            loads.append((wg_pk, ft * 16 * 128, 16))
            loads.append((wup_pk, ft * 16 * 128, 16))
        for dt_ in range(16):
            for sub in range(4):
                loads.append((wdn_pk, dt_ * 64 * 128 + sub * 16 * 128, 16))
        NLD = len(loads)
        NTC = TC // 512
        li = [0]
        ld_done = [0] * (NLD * NTC)
        ld_cnt = [0] * (NLD * NTC)

        def issue_loads(upto):
            while li[0] < min(upto, NLD * NTC):
                i = li[0]
                pk, row0, Kt = loads[i % NLD]
                slot = i % 4
                srcv = pk[row0:row0 + Kt * 128, :].rearrange("(kt p) f -> p kt f", p=128)
                dstv = s_wb[:, slot * 2048: slot * 2048 + Kt * 128].rearrange(
                    "p (kt f) -> p kt f", f=128)
                w = []
                if i >= 4:
                    w = [("tensor", ld_done[i - 4])]
                qn = "dgp" if (i % 2 == 0) else "dsync"
                ld_cnt[i] = P.dma_q(qn, dstv, srcv, waits=w, sem=f"slw{i % 4}")
                li[0] += 1

        for tc in range(NTC):
            t0 = tc * 512
            ser.dmas(nc.sync, [
                (s_x1[:, k * 512:(k + 1) * 512], x1T[k * 128:(k + 1) * 128, t0:t0 + 512])
                for k in range(16)])
            for k in range(16):
                ser.step(nc.vector, lambda e, k=k: e.tensor_scalar_mul(
                    s_xs[:, k * 512:(k + 1) * 512], s_x1[:, k * 512:(k + 1) * 512],
                    s_su[:, k:k + 1]))
            P.barrier()

            silu_cnt = [0, 0]
            sg_free = [0, 0]
            z_cnt = [0, 0]
            evac_cnt = [0, 0]
            z_done = 0
            gi = tc * NLD
            for ft in range(64):
                par = ft % 2
                issue_loads(gi + 4)
                slot = gi % 4
                for k in range(16):
                    w = []
                    if k == 0:
                        w = [(f"slw{gi % 4}", ld_cnt[gi]), ("scalar", silu_cnt[par])]
                    P.emit("tensor", lambda e, k=k, slot=slot, par=par: e.matmul(
                        PG[par], s_wb[:, slot * 2048 + k * 128: slot * 2048 + (k + 1) * 128],
                        s_xs[:, k * 512:(k + 1) * 512],
                        start=(k == 0), stop=(k == 15)),
                        waits=w, inc=(k == 15))
                tg = P.cnt["tensor"]
                ld_done[gi] = tg
                gi += 1
                issue_loads(gi + 4)
                slot = gi % 4
                for k in range(16):
                    w = []
                    if k == 0:
                        w = [(f"slw{gi % 4}", ld_cnt[gi]), ("vector", z_cnt[par])]
                    P.emit("tensor", lambda e, k=k, slot=slot, par=par: e.matmul(
                        PU[par], s_wb[:, slot * 2048 + k * 128: slot * 2048 + (k + 1) * 128],
                        s_xs[:, k * 512:(k + 1) * 512],
                        start=(k == 0), stop=(k == 15)),
                        waits=w, inc=(k == 15))
                tu = P.cnt["tensor"]
                ld_done[gi] = tu
                gi += 1
                silu_cnt[par] = P.emit("scalar", lambda e, par=par, ft=ft: e.activation(
                    s_sg[:, par * 512: par * 512 + 512], PG[par], AF.Silu,
                    scale=s_wsg[:, ft:ft + 1]),
                    waits=[("tensor", tg), ("vector", sg_free[par])])
                z_cnt[par] = P.emit("vector", lambda e, par=par, ft=ft: e.scalar_tensor_tensor(
                    s_z[:, ft * 512:(ft + 1) * 512], PU[par], s_wsu[:, ft:ft + 1],
                    s_sg[:, par * 512: par * 512 + 512], ALU.mult, ALU.mult),
                    waits=[("tensor", tu), ("scalar", silu_cnt[par])])
                sg_free[par] = z_cnt[par]
                if ft == 63:
                    z_done = z_cnt[par]
            for dt_ in range(16):
                par2 = dt_ % 2
                for fk in range(64):
                    sub = fk // 16
                    if fk % 16 == 0:
                        issue_loads(gi + sub + 4)
                    sl = (gi + sub) % 4
                    fo = fk % 16
                    w = []
                    if fk == 0:
                        w = [(f"slw{sl}", ld_cnt[gi]),
                             ("vector", max(z_done, evac_cnt[par2]))]
                    elif fk % 16 == 0:
                        w = [(f"slw{sl}", ld_cnt[gi + sub])]
                    P.emit("tensor", lambda e, fk=fk, sl=sl, fo=fo, par2=par2: e.matmul(
                        PD[par2], s_wb[:, sl * 2048 + fo * 128: sl * 2048 + (fo + 1) * 128],
                        s_z[:, fk * 512:(fk + 1) * 512],
                        start=(fk == 0), stop=(fk == 63)),
                        waits=w, inc=(fk % 16 == 15))
                    if fk % 16 == 15:
                        ld_done[gi + sub] = P.cnt["tensor"]
                td = P.cnt["tensor"]
                gi += 4
                evac_cnt[par2] = P.emit("vector", lambda e, dt_=dt_, par2=par2: e.scalar_tensor_tensor(
                    s_s[:, dt_ * 512:(dt_ + 1) * 512], PD[par2], s_wsd[:, dt_:dt_ + 1],
                    s_x1[:, dt_ * 512:(dt_ + 1) * 512], ALU.mult, ALU.add),
                    waits=[("tensor", td)])
            P.barrier()
            emit_ln(ser, nc, s_s, s_bf, s_st, s_onc, s_onr, p_s, p_rep,
                    s_x2, LN_EPS)
            ser.dmas(nc.sync, [
                (x2T[k * 128:(k + 1) * 128, t0:t0 + 512],
                 s_x2[:, k * 512:(k + 1) * 512]) for k in range(16)])
        P.barrier()


def host_scales_C(inp):
    wsc = np.asarray(inp["L0_ffn_Wscale_ug"], np.float32)
    sudp = np.asarray(inp["L0_ffn_SU_dp"], np.float32)
    wsg = np.ascontiguousarray((wsc[INTER:] / np.sqrt(D)).reshape(64, 128).T)
    wsu = np.ascontiguousarray((wsc[:INTER] * sudp / np.sqrt(D)).reshape(64, 128).T)
    wsd = np.ascontiguousarray(
        (np.asarray(inp["L0_ffn_Wscale_dp"], np.float32) / np.sqrt(INTER)).reshape(16, 128).T)
    suug = np.ascontiguousarray(np.asarray(inp["L0_ffn_SU_ug"], np.float32).reshape(16, 128).T)
    return suug, wsg, wsu, wsd


def emit_D(nc, P, t, on_kind=None):
    x2T = t["x2T"]
    wqkv_pk = t["wqkv_pk"]
    qkvT = t["qkvT"]

    with ExitStack() as stack:
        def sb(name, shape, dt=F32):
            return stack.enter_context(nc.sbuf_tensor(_uniq(name), shape, dt))
        s_x = sb("s_x", [128, 16 * TC], BF16)
        s_wb = sb("s_wb", [128, 4 * 2048], BF16)
        s_o = sb("s_o", [128, 2 * TC], BF16)
        p = stack.enter_context(nc.psum_tensor(_uniq("p"), [128, 4096], F32))
        PM = [p[:, 0:512], p[:, 512:1024]]
        ser = P
        ser.dmas(nc.gpsimd, [
            (s_x[:, k * TC:(k + 1) * TC], x2T[k * 128:(k + 1) * 128, :])
            for k in range(16)])
        P.barrier()

        li = [0]
        ld_done = [0] * 48
        ld_cnt = [0] * 48

        def issue_loads(upto):
            while li[0] < min(upto, 48):
                i = li[0]
                slot = i % 4
                srcv = wqkv_pk[i * 16 * 128:(i + 1) * 16 * 128, :].rearrange(
                    "(kt p) f -> p kt f", p=128)
                dstv = s_wb[:, slot * 2048: slot * 2048 + 2048].rearrange(
                    "p (kt f) -> p kt f", f=128)
                w = []
                if i >= 4:
                    w = [("tensor", ld_done[i - 4])]
                qn = "dgp" if (i % 2 == 0) else "dsync"
                ld_cnt[i] = P.dma_q(qn, dstv, srcv, waits=w, sem=f"slw{i % 4}")
                li[0] += 1

        ev_cnt = [0, 0]
        ost_cnt = [0, 0]
        for mt in range(48):
            issue_loads(mt + 4)
            slot = mt % 4
            oh = mt % 2
            for tc in range(TC // 512):
                par = tc % 2
                for k in range(16):
                    w = []
                    if k == 0:
                        w = [(f"slw{slot}", ld_cnt[mt]), ("scalar", ev_cnt[par])]
                    P.emit("tensor", lambda e, k=k, tc=tc, slot=slot, par=par: e.matmul(
                        PM[par], s_wb[:, slot * 2048 + k * 128: slot * 2048 + (k + 1) * 128],
                        s_x[:, k * TC + tc * 512: k * TC + (tc + 1) * 512],
                        start=(k == 0), stop=(k == 15)),
                        waits=w, inc=(k == 15))
                tm = P.cnt["tensor"]
                if tc == 3:
                    ld_done[mt] = tm
                w = [("tensor", tm)]
                if tc == 0:
                    w.append((f"ost{oh}", ost_cnt[oh]))
                ev_cnt[par] = P.emit("scalar", lambda e, tc=tc, oh=oh, par=par: e.copy(
                    s_o[:, oh * TC + tc * 512: oh * TC + (tc + 1) * 512], PM[par]),
                    waits=w)
            ost_cnt[oh] = P.dma_q("dsync", qkvT[mt * 128:(mt + 1) * 128, :],
                                  s_o[:, oh * TC:(oh + 1) * TC],
                                  waits=[("scalar", ev_cnt[(TC // 512 - 1) % 2])],
                                  sem=f"ost{oh}")
            if on_kind is not None and mt % 16 == 15:
                on_kind(mt // 16, ost_cnt[0], ost_cnt[1])
        P.barrier()


def emit_E(nc, P, t):
    qkvg = t["qkvg"]
    maskw = t["maskw"]
    ident_bf = t["ident_bf"]
    y_tm = t["y_tm"]

    NKV = T // 128
    with ExitStack() as stack:
        def sb(name, shape, dt=F32):
            return stack.enter_context(nc.sbuf_tensor(_uniq(name), shape, dt))
        s_q = sb("s_q", [128, 4 * T], BF16)
        s_k = sb("s_k", [128, 4 * T], BF16)
        s_v = sb("s_v", [128, NKV * 4 * 129], BF16)
        s_vt = sb("s_vt", [128, 2 * 2048], BF16)
        s_m = sb("s_m", [128, 896], BF16)
        s_p = sb("s_p", [128, 2 * 512], BF16)
        s_sqq = s_p[:, 0:512]
        s_st = sb("s_st", [1, 64])
        s_onc = sb("s_onc", [128, 1], BF16)
        s_onr = sb("s_onr", [1, 128])
        s_scl = sb("s_scl", [128, 1])
        s_mh = sb("s_mh", [128, 1])
        s_r = sb("s_r", [128, 4])
        s_yo = sb("s_yo", [128, 2 * 512], BF16)
        s_id = sb("s_id", [128, 128], BF16)
        p = stack.enter_context(nc.psum_tensor(_uniq("p"), [128, 3072], F32))
        PS = [p[:, 0:512], p[:, 512:1024]]
        PO = [p[:, 1024 + j * 512: 1024 + j * 512 + 129] for j in range(4)]
        p_s1 = p[0:1, 1024:1536]
        p_b = p[:, 1536:1537]
        p_bf = stack.enter_context(nc.psum_tensor(_uniq("p_bf"), [128, 2048], BF16))
        PTB = [p_bf[:, 0:128], p_bf[:, 1024:1152]]
        ser = P

        ser.dmas(nc.sync, [(s_m[:], maskw), (s_id[:], ident_bf)])
        ser.step(nc.vector, lambda e: e.memset(s_onc[:, 0:1], 1.0))
        ser.step(nc.vector, lambda e: e.memset(s_onr[0:1, :], 1.0))
        ser.step(nc.vector, lambda e: e.memset(s_scl[:, 0:1], 0.08838834764831845))
        for h in range(4):
            ser.dmas(nc.sync, [
                (s_q[:, h * T + r * 2048: h * T + (r + 1) * 2048],
                 qkvg[r * 1536 + h * 128: r * 1536 + (h + 1) * 128, :]) for r in range(4)])
            ser.dmas(nc.sync, [
                (s_k[:, h * T + r * 2048: h * T + (r + 1) * 2048],
                 qkvg[r * 1536 + 512 + h * 128: r * 1536 + 512 + (h + 1) * 128, :])
                for r in range(4)])
        tpd = [0, 0]
        evd = [0, 0]
        for h in range(4):
            for rp in range(2):
                ser.dmas(nc.sync, [
                    (s_vt[:, r2 * 2048:(r2 + 1) * 2048],
                     qkvg[(rp * 2 + r2) * 1536 + 1024 + h * 128:
                          (rp * 2 + r2) * 1536 + 1024 + (h + 1) * 128, :])
                    for r2 in range(2)])
                for r2 in range(2):
                    r = rp * 2 + r2
                    for kvl in range(16):
                        j = kvl % 2
                        tpd[j] = P.emit("tensor", lambda e, r2=r2, kvl=kvl, j=j: e.transpose(
                            PTB[j], s_vt[:, r2 * 2048 + kvl * 128: r2 * 2048 + (kvl + 1) * 128],
                            s_id[:]),
                            waits=[("scalar", evd[j]), ("dsync", P.cnt["dsync"])])
                        kvb = r * 16 + kvl
                        evd[j] = P.emit("scalar", lambda e, kvb=kvb, h=h, j=j: e.copy(
                            s_v[:, (kvb * 4 + h) * 129:(kvb * 4 + h) * 129 + 128], PTB[j]),
                            waits=[("tensor", tpd[j])])
        ser.step(nc.vector, lambda e: e.memset(
            s_v.rearrange("p (t c) -> p t c", c=129)[:, :, 128:129], 1.0))
        P.barrier()

        exp_free = [0, 0]
        av_free = [0, 0]
        yo_free = [0, 0]
        for h in range(4):
            for which, buf in ((0, s_q), (1, s_k)):
                for tcq in range(T // 512):
                    ser.step(nc.scalar, lambda e, buf=buf, h=h, tcq=tcq: e.activation(
                        s_sqq[:], buf[:, h * T + tcq * 512: h * T + (tcq + 1) * 512], AF.Square))
                    ser.step(nc.tensor, lambda e: e.matmul(
                        p_s1[0:1, :], s_onc[:, 0:1], s_sqq[:], start=True, stop=True))
                    ser.step(nc.vector, lambda e, w=which: e.reduce_max(
                        s_st[0:1, 8 + w:9 + w], p_s1[0:1, :], mybir.AxisListType.X))
                    if tcq == 0:
                        ser.step(nc.vector, lambda e, w=which: e.tensor_copy(
                            s_st[0:1, w:w + 1], s_st[0:1, 8 + w:9 + w]))
                    else:
                        ser.step(nc.vector, lambda e, w=which: e.tensor_max(
                            s_st[0:1, w:w + 1], s_st[0:1, w:w + 1], s_st[0:1, 8 + w:9 + w]))
            ser.step(nc.vector, lambda e: e.tensor_mul(
                s_st[0:1, 2:3], s_st[0:1, 0:1], s_st[0:1, 1:2]))
            ser.step(nc.scalar, lambda e: e.activation(s_st[0:1, 2:3], s_st[0:1, 2:3], AF.Ln))
            ser.step(nc.vector, lambda e: e.tensor_scalar_mul(s_st[0:1, 2:3], s_st[0:1, 2:3], 0.5))
            ser.step(nc.scalar, lambda e: e.activation(s_st[0:1, 2:3], s_st[0:1, 2:3], AF.Exp))
            ser.step(nc.vector, lambda e: e.tensor_scalar_mul(
                s_st[0:1, 2:3], s_st[0:1, 2:3], -0.08838834764831845))
            ser.step(nc.tensor, lambda e: e.matmul(
                p_b[:, 0:1], s_onr[0:1, :], s_st[0:1, 2:3], start=True, stop=True))
            ser.step(nc.vector, lambda e: e.tensor_copy(s_mh[:, 0:1], p_b[:, 0:1]))
            P.barrier()

            po_free = [0, 0, 0, 0]

            for qc in range(T // 512):
                nkv = 4 * qc + 4
                exp_cnt = [0] * nkv
                gate_cnt = [0] * nkv
                gate_eng = [None] * nkv
                av_done = [0] * nkv
                s_done = [0] * nkv
                for i in range(nkv + 1):
                    if i < nkv:
                        par = i % 2
                        w = [("scalar", exp_free[par])]
                        s_done[i] = P.emit("tensor", lambda e, h=h, qc=qc, i=i, par=par: e.matmul(
                            PS[par], s_k[:, h * T + i * 128: h * T + (i + 1) * 128],
                            s_q[:, h * T + qc * 512: h * T + (qc + 1) * 512],
                            start=True, stop=True), waits=w)
                        wexp = [("tensor", s_done[i]), ("tensor", av_free[par])]
                        exp_cnt[i] = P.emit("scalar", lambda e, par=par: e.activation(
                            s_p[:, par * 512: par * 512 + 512], PS[par], AF.Exp,
                            bias=s_mh[:, 0:1], scale=s_scl[:, 0:1]), waits=wexp)
                        exp_free[par] = exp_cnt[i]
                        o = i * 128 - qc * 512
                        if o >= 0:
                            gate_cnt[i] = P.emit("vector", lambda e, par=par, o=o: e.tensor_mul(
                                s_p[:, par * 512: par * 512 + 512],
                                s_p[:, par * 512: par * 512 + 512],
                                s_m[:, 384 - o: 896 - o]),
                                waits=[("scalar", exp_cnt[i])])
                            gate_eng[i] = "vector"
                        else:
                            gate_cnt[i] = exp_cnt[i]
                            gate_eng[i] = "scalar"
                    if i >= 1:
                        kvb = i - 1
                        par = kvb % 2
                        for j in range(4):
                            w = [(gate_eng[kvb], gate_cnt[kvb])]
                            if kvb == 0:
                                w.append(("vector", po_free[j]))
                            P.emit("tensor", lambda e, j=j, kvb=kvb, h=h, par=par,
                                   nkv=nkv: e.matmul(
                                PO[j], s_p[:, par * 512 + j * 128: par * 512 + (j + 1) * 128],
                                s_v[:, (kvb * 4 + h) * 129:(kvb * 4 + h) * 129 + 129],
                                start=(kvb == 0), stop=(kvb == nkv - 1)),
                                waits=w, inc=(j == 3))
                        av_done[kvb] = P.cnt["tensor"]
                        av_free[kvb % 2] = av_done[kvb]
                yh = qc % 2
                for j in range(4):
                    P.emit("vector", lambda e, j=j: e.reciprocal(
                        s_r[:, j:j + 1], PO[j][:, 128:129]),
                        waits=[("tensor", av_done[nkv - 1]), (f"yst{yh}", yo_free[yh])])
                    po_free[j] = P.emit("vector", lambda e, j=j, yh=yh: e.tensor_scalar_mul(
                        s_yo[:, yh * 512 + j * 128: yh * 512 + (j + 1) * 128],
                        PO[j][:, 0:128], s_r[:, j:j + 1]))
                dst = y_tm[qc * 512:(qc + 1) * 512, h * 128:(h + 1) * 128].rearrange(
                    "(j p) c -> p j c", p=128)
                srcv = s_yo[:, yh * 512:(yh + 1) * 512].rearrange("p (j c) -> p j c", c=128)
                yo_free[yh] = P.dma_q("dsync", dst, srcv,
                                      waits=[("vector", po_free[3])], sem=f"yst{yh}")
        P.barrier()


def emit_E2(nc, P, t):
    ytmg = t["ytmg"]
    wpr_pk = t["wpr_pk"]
    x2T = t["x2T"]
    ident_bf = t["ident_bf"]
    x3T = t["x3T"]

    with ExitStack() as stack:
        def sb(name, shape, dt=F32):
            return stack.enter_context(nc.sbuf_tensor(_uniq(name), shape, dt))
        s_tok = sb("s_tok", [128, 2 * D], BF16)
        s_y = sb("s_y", [128, 16 * 512], BF16)
        s_wb = sb("s_wb", [128, 4 * 2048], BF16)
        s_id = sb("s_id", [128, 128], BF16)
        s_xr = sb("s_xr", [128, 16 * 512])
        s_s = sb("s_s", [128, 16 * 512])
        s_bf = sb("s_bf", [128, 16 * 512], BF16)
        s_st = sb("s_st", [1, 8 * 512])
        s_onc = sb("s_onc", [128, 1], BF16)
        s_onr = sb("s_onr", [1, 128])
        p = stack.enter_context(nc.psum_tensor(_uniq("p"), [128, 3072], F32))
        PM = [p[:, 0:512], p[:, 512:1024]]
        p_s = p[0:1, 1024:2048]
        p_rep = p[:, 2048:3072]
        p_bf = stack.enter_context(nc.psum_tensor(_uniq("p_bf"), [128, 2048], BF16))
        PTP = [p_bf[:, 0:128], p_bf[:, 1024:1152]]
        ser = P
        ser.dmas(nc.sync, [(s_id[:], ident_bf)])
        ser.step(nc.vector, lambda e: e.memset(s_onc[:, 0:1], 1.0))
        ser.step(nc.vector, lambda e: e.memset(s_onr[0:1, :], 1.0))
        P.barrier()

        NLD = 16
        NTC = TC // 512
        li = [0]
        ld_done = [0] * (NLD * NTC)
        ld_cnt = [0] * (NLD * NTC)

        def issue_loads(upto):
            while li[0] < min(upto, NLD * NTC):
                i = li[0]
                mt = i % NLD
                slot = i % 4
                srcv = wpr_pk[mt * 16 * 128:(mt + 1) * 16 * 128, :].rearrange(
                    "(kt p) f -> p kt f", p=128)
                dstv = s_wb[:, slot * 2048: slot * 2048 + 2048].rearrange(
                    "p (kt f) -> p kt f", f=128)
                w = []
                if i >= 4:
                    w = [("tensor", ld_done[i - 4])]
                qn = "dgp" if (i % 2 == 0) else "dsync"
                ld_cnt[i] = P.dma_q(qn, dstv, srcv, waits=w, sem=f"slw{i % 4}")
                li[0] += 1

        tp_done = [0, 0]
        ev_s = [0, 0]
        xld_cnt = [0, 0]
        xdma_free = [0, 0]
        ev_cnt = [0, 0]
        for tc in range(NTC):
            t0 = tc * 512
            ser.dmas(nc.sync, [
                (s_xr[:, k * 512:(k + 1) * 512], x2T[k * 128:(k + 1) * 128, t0:t0 + 512])
                for k in range(16)])
            for tt in range(4):
                tb = tt % 2
                for r in range(4):
                    base = r * 2048 + t0 + tt * 128
                    xld_cnt[tb] = P.dma_q(
                        "dscal", s_tok[:, tb * D + r * 512: tb * D + (r + 1) * 512],
                        ytmg[base: base + 128, :],
                        waits=([("tensor", xdma_free[tb])] if r == 0 else []),
                        sem=f"xld{tb}")
                for k in range(16):
                    j = k % 2
                    w = [(f"xld{tb}", xld_cnt[tb]), ("scalar", ev_s[j])]
                    tp_done[j] = P.emit("tensor", lambda e, k=k, tb=tb, j=j: e.transpose(
                        PTP[j], s_tok[:, tb * D + k * 128: tb * D + (k + 1) * 128],
                        s_id[:]), waits=w)
                    ev_s[j] = P.emit("scalar", lambda e, k=k, tt=tt, j=j: e.copy(
                        s_y[:, k * 512 + tt * 128: k * 512 + (tt + 1) * 128], PTP[j]),
                        waits=[("tensor", tp_done[j])])
                xdma_free[tb] = P.cnt["tensor"]
            yev = P.cnt["scalar"]
            for mt in range(16):
                gi = tc * NLD + mt
                issue_loads(gi + 4)
                slot = gi % 4
                par = mt % 2
                for k in range(16):
                    w = []
                    if k == 0:
                        w = [(f"slw{gi % 4}", ld_cnt[gi]), ("vector", ev_cnt[par]),
                             ("scalar", yev)]
                    P.emit("tensor", lambda e, mt=mt, k=k, slot=slot, par=par: e.matmul(
                        PM[par], s_wb[:, slot * 2048 + k * 128: slot * 2048 + (k + 1) * 128],
                        s_y[:, k * 512:(k + 1) * 512],
                        start=(k == 0), stop=(k == 15)),
                        waits=w, inc=(k == 15))
                tm = P.cnt["tensor"]
                ld_done[gi] = tm
                ev_cnt[par] = P.emit("vector", lambda e, mt=mt, par=par: e.tensor_add(
                    s_s[:, mt * 512:(mt + 1) * 512], PM[par],
                    s_xr[:, mt * 512:(mt + 1) * 512]),
                    waits=[("tensor", tm)])
            P.barrier()
            emit_ln(ser, nc, s_s, s_bf, s_st, s_onc, s_onr, p_s, p_rep, s_xr, LN_EPS)
            ser.dmas(nc.sync, [
                (x3T[k * 128:(k + 1) * 128, t0:t0 + 512],
                 s_xr[:, k * 512:(k + 1) * 512]) for k in range(16)])
            P.barrier()
        P.barrier()


def emit_F(nc, P, t, pipe_mode=True):
    x3T = t["x3T"]
    rmsw = t["rmsw"]
    rt_w = t["rt_w"]
    wg_pk = t["wg_pk"]
    wu_pk = t["wu_pk"]
    wd_pk = t["wd_pk"]
    sel8 = t["sel8"]
    ident = t["ident"]
    out_tm = t["out_tm"]

    with ExitStack() as stack:
        def sb(name, shape, dt=F32):
            return stack.enter_context(nc.sbuf_tensor(_uniq(name), shape, dt))
        s_x3 = sb("s_x3", [128, 16 * 512])
        s_ht = sb("s_ht", [128, 16 * 512], BF16)
        s_rw = sb("s_rw", [128, 16])
        s_rt = sb("s_rt", [128, 16 * 8], BF16)
        s_sel = sb("s_sel", [8, 1024])
        s_id = sb("s_id", [128, 128])
        s_weT = sb("s_weT", [8, 512])
        s_we = sb("s_we", [128, 2 * 512])
        s_sg = sb("s_sg", [128, 2 * 512])
        s_uw = sb("s_uw", [128, 2 * 512])
        s_z = sb("s_z", [128, 2 * 8 * 512], BF16)
        s_acc = sb("s_acc", [128, 16 * 512])
        s_bf = sb("s_bf", [128, 16 * 512], BF16)
        s_st = sb("s_st", [1, 8 * 512])
        s_onc = sb("s_onc", [128, 1], BF16)
        s_onr = sb("s_onr", [1, 128])
        s_lg = sb("s_lg", [128, 48])
        s_wb = sb("s_wb", [128, 4 * 2048], BF16)
        s_tm = sb("s_tm", [128, 16 * 128])
        p = stack.enter_context(nc.psum_tensor(_uniq("p"), [128, 4096], F32))
        PG = [p[:, 0:512], p[:, 512:1024]]
        PU = [p[:, 1024:1536], p[:, 1536:2048]]
        PD = [p[:, 2048:2560], p[:, 2560:3072]]
        PWE = p[:, 3072:3584]
        p_rep = p[:, 3072:4096]
        p_s = p[0:1, 2048:3072]
        p_r = p[:, 0:8]
        p_tp = p[:, 0:128]
        ser = P

        ser.dmas(nc.sync, [(s_rw[:], rmsw), (s_sel[:], sel8), (s_id[:], ident)])
        ser.dmas(nc.gpsimd, [
            (s_rt[:, k * 8:(k + 1) * 8], rt_w[k * 128:(k + 1) * 128, :])
            for k in range(16)])
        ser.step(nc.vector, lambda e: e.memset(s_onc[:, 0:1], 1.0))
        ser.step(nc.vector, lambda e: e.memset(s_onr[0:1, :], 1.0))

        loads = []
        for e_ in range(9):
            for ft in range(8):
                loads.append((wg_pk, e_ * 8 + ft, 16))
                loads.append((wu_pk, e_ * 8 + ft, 16))
            for dt_ in range(16):
                loads.append((wd_pk, e_ * 16 + dt_, 8))
        NLD = len(loads)
        NTC = TC // 512
        li = [0]
        ld_done = [0] * (NLD * NTC)
        ld_cnt = [0] * (NLD * NTC)

        def W_(w):
            return w if pipe_mode else P.snap()

        def issue_loads(upto):
            while li[0] < min(upto, NLD * NTC):
                i = li[0]
                pk, mt, Kt = loads[i % NLD]
                slot = i % 4
                srcv = pk[mt * Kt * 128:(mt + 1) * Kt * 128, :].rearrange(
                    "(kt p) f -> p kt f", p=128)
                dstv = s_wb[:, slot * 2048: slot * 2048 + Kt * 128].rearrange(
                    "p (kt f) -> p kt f", f=128)
                w = []
                if i >= 4:
                    w = [("tensor", ld_done[i - 4])]
                qn = "dgp" if (i % 2 == 0) else "dsync"
                ld_cnt[i] = P.dma_q(qn, dstv, srcv, waits=W_(w), sem=f"slw{i % 4}")
                li[0] += 1

        for tc in range(NTC):
            t0 = tc * 512
            ser.dmas(nc.sync, [
                (s_x3[:, k * 512:(k + 1) * 512], x3T[k * 128:(k + 1) * 128, t0:t0 + 512])
                for k in range(16)])
            for k in range(16):
                ser.step(nc.scalar, lambda e, k=k: e.activation(
                    s_bf[:, k * 512:(k + 1) * 512], s_x3[:, k * 512:(k + 1) * 512], AF.Square))
            for k in range(16):
                ser.step(nc.tensor, lambda e, k=k: e.matmul(
                    p_s[0:1, 0:512], s_onc[:, 0:1], s_bf[:, k * 512:(k + 1) * 512],
                    start=(k == 0), stop=(k == 15)))
            ms = s_st[0:1, 0:512]
            ser.step(nc.vector, lambda e: e.tensor_scalar_mul(ms, p_s[0:1, 0:512], 1.0 / 2048))
            ser.step(nc.vector, lambda e: e.tensor_scalar_add(ms, ms, RMS_EPS))
            ser.step(nc.scalar, lambda e: e.activation(ms, ms, AF.Ln))
            ser.step(nc.vector, lambda e: e.tensor_scalar_mul(ms, ms, -0.5))
            ser.step(nc.scalar, lambda e: e.activation(ms, ms, AF.Exp))
            ser.step(nc.tensor, lambda e: e.matmul(
                p_rep[:, 0:512], s_onr[0:1, :], ms, start=True, stop=True))
            for k in range(16):
                ser.step(nc.vector, lambda e, k=k: e.scalar_tensor_tensor(
                    s_ht[:, k * 512:(k + 1) * 512], s_x3[:, k * 512:(k + 1) * 512],
                    s_rw[:, k:k + 1], p_rep[:, 0:512], ALU.mult, ALU.mult))
            for tt in range(4):
                for k in range(16):
                    ser.step(nc.tensor, lambda e, k=k, tt=tt: e.matmul(
                        p_r, s_ht[:, k * 512 + tt * 128: k * 512 + (tt + 1) * 128],
                        s_rt[:, k * 8:(k + 1) * 8],
                        start=(k == 0), stop=(k == 15)))
                lg = s_lg[:, 0:8]; pe = s_lg[:, 8:16]; m1 = s_lg[:, 16:17]
                sm = s_lg[:, 17:18]; m2 = s_lg[:, 18:19]; kp = s_lg[:, 24:32]
                wi = s_lg[:, 19:20]; we8 = s_lg[:, 32:40]; nm1 = s_lg[:, 20:21]
                ser.step(nc.vector, lambda e: e.tensor_copy(lg, p_r))
                ser.step(nc.vector, lambda e: e.reduce_max(m1, lg, mybir.AxisListType.X))
                ser.step(nc.vector, lambda e: e.tensor_scalar_mul(nm1, m1, -1.0))
                ser.step(nc.scalar, lambda e: e.activation(pe, lg, AF.Exp, bias=nm1[:, 0:1]))
                ser.step(nc.vector, lambda e: e.reduce_sum(sm, pe, mybir.AxisListType.X))
                ser.step(nc.vector, lambda e: e.reciprocal(wi, sm))
                ser.step(nc.vector, lambda e: e.tensor_scalar_mul(pe, pe, wi[:, 0:1]))
                ser.step(nc.vector, lambda e: e.reduce_max(m1, pe, mybir.AxisListType.X))
                ser.step(nc.vector, lambda e: e.tensor_single_scalar(
                    kp, pe, m1[:, 0:1], ALU.is_equal))
                ser.step(nc.vector, lambda e: e.scalar_tensor_tensor(
                    we8, kp, -1e30, pe, ALU.mult, ALU.add))
                ser.step(nc.vector, lambda e: e.reduce_max(m2, we8, mybir.AxisListType.X))
                ser.step(nc.vector, lambda e: e.tensor_single_scalar(
                    kp, pe, m2[:, 0:1], ALU.is_ge))
                ser.step(nc.vector, lambda e: e.tensor_add(sm, m1, m2))
                ser.step(nc.vector, lambda e: e.reciprocal(wi, sm))
                ser.step(nc.vector, lambda e: e.tensor_mul(we8, pe, kp))
                ser.step(nc.vector, lambda e: e.tensor_scalar_mul(we8, we8, wi[:, 0:1]))
                ser.step(nc.tensor, lambda e: e.transpose(p_tp[0:8, 0:128], we8, s_id[:]))
                ser.step(nc.vector, lambda e, tt=tt: e.tensor_copy(
                    s_weT[0:8, tt * 128:(tt + 1) * 128], p_tp[0:8, 0:128]))
            P.barrier()

            silu_cnt = [0, 0]
            sg_free = [0, 0]
            uw_cnt = [0, 0]
            z_done = [0] * 9
            acc_cnt = [0, 0]
            we_evac = [0] * 9
            uw_last_e = [0] * 9
            gi = tc * NLD
            for e_ in range(9):
                zh = (e_ % 2) * 8 * 512
                if e_ < 8:
                    twe = P.emit("tensor", lambda e, e_=e_: e.matmul(
                        PWE, s_sel[:, e_ * 128:(e_ + 1) * 128], s_weT[0:8, :],
                        start=True, stop=True),
                        waits=W_([("scalar", we_evac[e_ - 1] if e_ else 0)]))
                    we_evac[e_] = P.emit("scalar", lambda e, e_=e_: e.copy(
                        s_we[:, (e_ % 2) * 512:(e_ % 2) * 512 + 512], PWE),
                        waits=W_([("tensor", twe),
                               ("vector", uw_last_e[e_ - 2] if e_ >= 2 else 0)]))
                for ft in range(8):
                    par = ft % 2
                    issue_loads(gi + 4)
                    slot = gi % 4
                    for k in range(16):
                        w = []
                        if k == 0:
                            w = [(f"slw{gi % 4}", ld_cnt[gi]), ("scalar", silu_cnt[par])]
                        P.emit("tensor", lambda e, k=k, slot=slot, par=par: e.matmul(
                            PG[par], s_wb[:, slot * 2048 + k * 128: slot * 2048 + (k + 1) * 128],
                            s_ht[:, k * 512:(k + 1) * 512],
                            start=(k == 0), stop=(k == 15)),
                            waits=W_(w), inc=(k == 15))
                    tg = P.cnt["tensor"]
                    ld_done[gi] = tg
                    gi += 1
                    issue_loads(gi + 4)
                    slot = gi % 4
                    for k in range(16):
                        w = []
                        if k == 0:
                            w = [(f"slw{gi % 4}", ld_cnt[gi]), ("vector", uw_cnt[par])]
                        P.emit("tensor", lambda e, k=k, slot=slot, par=par: e.matmul(
                            PU[par], s_wb[:, slot * 2048 + k * 128: slot * 2048 + (k + 1) * 128],
                            s_ht[:, k * 512:(k + 1) * 512],
                            start=(k == 0), stop=(k == 15)),
                            waits=W_(w), inc=(k == 15))
                    tu = P.cnt["tensor"]
                    ld_done[gi] = tu
                    gi += 1
                    silu_cnt[par] = P.emit("scalar", lambda e, par=par: e.activation(
                        s_sg[:, par * 512: par * 512 + 512], PG[par], AF.Silu),
                        waits=W_([("tensor", tg), ("vector", sg_free[par])]))
                    if e_ < 8:
                        uw_cnt[par] = P.emit("vector", lambda e, par=par: e.tensor_mul(
                            s_uw[:, par * 512: par * 512 + 512], PU[par],
                            s_we[:, (e_ % 2) * 512:(e_ % 2) * 512 + 512]),
                            waits=W_([("tensor", tu), ("scalar", we_evac[e_])]))
                        zc = P.emit("vector", lambda e, par=par, ft=ft, zh=zh: e.tensor_mul(
                            s_z[:, zh + ft * 512: zh + (ft + 1) * 512],
                            s_uw[:, par * 512: par * 512 + 512],
                            s_sg[:, par * 512: par * 512 + 512]),
                            waits=W_([("scalar", silu_cnt[par])]))
                    else:
                        zc = P.emit("vector", lambda e, par=par, ft=ft, zh=zh: e.tensor_mul(
                            s_z[:, zh + ft * 512: zh + (ft + 1) * 512],
                            PU[par], s_sg[:, par * 512: par * 512 + 512]),
                            waits=W_([("tensor", tu), ("scalar", silu_cnt[par])]))
                        uw_cnt[par] = zc
                    sg_free[par] = zc
                    uw_last_e[e_] = uw_cnt[par]
                    if ft == 7:
                        z_done[e_] = zc
                for dt_ in range(16):
                    par2 = dt_ % 2
                    issue_loads(gi + 4)
                    slot = gi % 4
                    for fk in range(8):
                        w = []
                        if fk == 0:
                            w = [(f"slw{gi % 4}", ld_cnt[gi]),
                                 ("vector", max(z_done[e_], acc_cnt[par2]))]
                        P.emit("tensor", lambda e, fk=fk, slot=slot, par2=par2, zh=zh: e.matmul(
                            PD[par2], s_wb[:, slot * 2048 + fk * 128: slot * 2048 + (fk + 1) * 128],
                            s_z[:, zh + fk * 512: zh + (fk + 1) * 512],
                            start=(fk == 0), stop=(fk == 7)),
                            waits=W_(w), inc=(fk == 7))
                    td = P.cnt["tensor"]
                    ld_done[gi] = td
                    gi += 1
                    if e_ == 0:
                        acc_cnt[par2] = P.emit("vector", lambda e, dt_=dt_, par2=par2: e.tensor_copy(
                            s_acc[:, dt_ * 512:(dt_ + 1) * 512], PD[par2]),
                            waits=W_([("tensor", td)]))
                    else:
                        acc_cnt[par2] = P.emit("vector", lambda e, dt_=dt_, par2=par2: e.tensor_add(
                            s_acc[:, dt_ * 512:(dt_ + 1) * 512],
                            s_acc[:, dt_ * 512:(dt_ + 1) * 512], PD[par2]),
                            waits=W_([("tensor", td)]))
            P.barrier()

            for k in range(16):
                ser.step(nc.vector, lambda e, k=k: e.tensor_add(
                    s_acc[:, k * 512:(k + 1) * 512], s_acc[:, k * 512:(k + 1) * 512],
                    s_x3[:, k * 512:(k + 1) * 512]))
            emit_ln(ser, nc, s_acc, s_bf, s_st, s_onc, s_onr, p_s, p_rep, s_x3, LN_EPS)
            for tt in range(4):
                for k in range(16):
                    ser.step(nc.tensor, lambda e, k=k, tt=tt: e.transpose(
                        p_tp[:, 0:128], s_x3[:, k * 512 + tt * 128: k * 512 + (tt + 1) * 128],
                        s_id[:]))
                    ser.step(nc.vector, lambda e, k=k: e.tensor_copy(
                        s_tm[:, k * 128:(k + 1) * 128], p_tp[:, 0:128]))
                ser.dma(nc.sync, out_tm[t0 + tt * 128: t0 + (tt + 1) * 128, :], s_tm[:])
        P.barrier()
        P.barrier()


FUSED_SEMS = ["slw0", "slw1", "slw2", "slw3", "xld0", "xld1", "yst",
              "ost0", "ost1", "yst0", "yst1", "cc"]


def build_fused():
    nc = bass.Bass("TRN2", target_bir_lowering=False)

    def EI(name, shape, dt=F32):
        return nc.dram_tensor(name, shape, dt, kind="ExternalInput").ap()

    xA = EI("xA", [T, D])
    xAs = EI("xAs", [T, 512])
    wdtT = EI("wdtT", [D, 512])
    dtb = EI("dtb", [128, 4])
    xpw = EI("xpw", [D, 64])
    xpb = EI("xpb", [64, 1])
    alog = EI("alog", [128, 64])
    ident = EI("ident", [128, 128])
    sel = EI("sel", [64, 4096], BF16)
    wupT_sl = EI("wupT_sl", [D, 1024])
    wgT_sl = EI("wgT_sl", [D, 1024])
    wdT_sl = EI("wdT_sl", [INTER, 256])
    h128 = EI("h128", [128, 128])
    cv_in = {name: EI(f"cv_{name}_in", [rows // 8, 128]) for name, rows in PACK_SECTIONS}
    otb = EI("otb", [128, 16])
    x_tm = EI("x_tm", [TC, D])
    su_ug = EI("su_ug", [128, 16])
    wsg = EI("wsg", [128, 64])
    wsu = EI("wsu", [128, 64])
    wsd = EI("wsd", [128, 16])
    maskw = EI("maskw", [128, 896], BF16)
    ident_bf = EI("ident_bf", [128, 128], BF16)
    rmsw = EI("rmsw", [128, 16])
    rt_w = EI("rt_w", [D, 8])
    sel8 = EI("sel8", [8, 1024])
    out_tm = nc.dram_tensor("out_tm", [TC, D], F32, kind="ExternalOutput").ap()

    up_t_loc = nc.dram_tensor("up_t_loc", [8 * 16 * 128, 128], BF16)
    g_t_loc = nc.dram_tensor("g_t_loc", [8 * 16 * 128, 128], BF16)
    dn_t_loc = nc.dram_tensor("dn_t_loc", [2 * 64 * 128, 128], BF16)
    cv_out = {name: nc.dram_tensor(f"cv_{name}_out", [rows // 8, 128], BF16)
              for name, rows in PACK_SECTIONS}
    wup_pk = nc.dram_tensor("wup_pk", [64 * 16 * 128, 128], BF16)
    wg_pk = nc.dram_tensor("wg_pk", [64 * 16 * 128, 128], BF16)
    wdn_pk = nc.dram_tensor("wdn_pk", [16 * 64 * 128, 128], BF16)
    pk = {name: nc.dram_tensor(f"pk_{name}", [rows, 128], BF16)
          for name, rows in PACK_SECTIONS}
    yTs = nc.dram_tensor("yTs", [2048, 2048], BF16)
    YG = nc.dram_tensor("YG", [8192, 2048], BF16)
    yloc = nc.dram_tensor("yloc", [2048, 2048], BF16)
    qloc = nc.dram_tensor("qloc", [6144, 2048], BF16)
    ytloc = nc.dram_tensor("ytloc", [8192, 512], BF16)
    x1T = nc.dram_tensor("x1T", [D, TC], F32)
    x2T = nc.dram_tensor("x2T", [D, TC], F32)
    x3T = nc.dram_tensor("x3T", [D, TC], F32)
    qkvT = nc.dram_tensor("qkvT", [3 * D, TC], BF16)
    QGs = [nc.dram_tensor(f"QG{s}", [4 * D, TC], BF16) for s in range(3)]
    ytm = nc.dram_tensor("ytm", [T, 512], BF16)
    YAG = nc.dram_tensor("YAG", [4 * T, 512], BF16)

    G8 = [list(range(8))]
    G4 = [[0, 1, 2, 3], [4, 5, 6, 7]]

    with ExitStack() as stack:
        P = Pipe(nc, stack)
        for n in FUSED_SEMS:
            P.new_sem(n, stack)

        def ccol(kind, groups, in_h, out_h):
            i = nc.gpsimd.collective_compute(
                kind, ALU.bypass, replica_groups=groups,
                ins=[in_h.ap().opt()], outs=[out_h.ap().opt()])
            i.then_inc(P.sem["cc"], 1)
            P.cnt["cc"] += 1

        def ccbar():
            for e2 in P.ENGS:
                P._wait(e2, getattr(nc, e2), [("cc", P.cnt["cc"])])

        tW = {"wupT_sl": wupT_sl, "wgT_sl": wgT_sl, "wdT_sl": wdT_sl, "h128": h128,
              "up_t": up_t_loc.ap(), "g_t": g_t_loc.ap(), "dn_t": dn_t_loc.ap()}
        for name, _ in PACK_SECTIONS:
            tW[f"cv_{name}_in"] = cv_in[name]
            tW[f"cv_{name}_out"] = cv_out[name].ap()
        emit_W(nc, P, tW)
        emit_A(nc, P, {"xA": xA, "xAs": xAs, "wdtT": wdtT, "dtb": dtb, "xpw": xpw,
                       "xpb": xpb, "alog": alog, "ident": ident, "sel": sel,
                       "yTs": yTs.ap()})
        P.barrier(["gpsimd"])
        ccol("AllGather", G8, up_t_loc, wup_pk)
        ccol("AllGather", G8, g_t_loc, wg_pk)
        ccol("AllGather", G8, dn_t_loc, wdn_pk)
        for name, _ in PACK_SECTIONS:
            ccol("AllGather", G8, cv_out[name], pk[name])
        ccol("AllGather", G4, yTs, YG)
        ccbar()
        d512 = (nc.sync.partition_id() % 4) * 512
        yg3 = YG.ap().rearrange("(rank x) c -> x rank c", x=2048)
        yl3 = yloc.ap().rearrange("(rank x) c -> x rank c", x=512)
        for i in range(2):
            P.dma_q("dsync", yl3[:, i * 2:(i + 1) * 2, :],
                    yg3[bass.ds(d512, 512), i * 2:(i + 1) * 2, :])

        emit_B(nc, P, {"yT": yloc.ap(), "wot_pk": pk["ot"].ap(), "otb": otb,
                       "x_tm": x_tm, "ident": ident, "x1T": x1T.ap()})
        emit_C(nc, P, {"x1T": x1T.ap(), "wup_pk": wup_pk.ap(), "wg_pk": wg_pk.ap(),
                       "wdn_pk": wdn_pk.ap(), "su_ug": su_ug, "wsg": wsg, "wsu": wsu,
                       "wsd": wsd, "x2T": x2T.ap()})
        def on_kind(kind, c0, c1):
            i = nc.gpsimd.collective_compute(
                "AllGather", ALU.bypass, replica_groups=G4,
                ins=[qkvT.ap()[kind * D:(kind + 1) * D, :].opt()],
                outs=[QGs[kind].ap().opt()])
            i.then_inc(P.sem["cc"], 1)
            P.cnt["cc"] += 1

        def on_kind_w(kind, c0, c1):
            P._wait("gpsimd", nc.gpsimd, [("ost0", c0), ("ost1", c1)])
            on_kind(kind, c0, c1)

        emit_D(nc, P, {"x2T": x2T.ap(), "wqkv_pk": pk["qkv"].ap(), "qkvT": qkvT.ap()},
               on_kind=on_kind_w)
        ccbar()
        ql3 = qloc.ap().rearrange("(rs x) c -> x rs c", x=512)
        for s in range(3):
            qg3 = QGs[s].ap().rearrange("(r x) c -> x r c", x=2048)
            P.dma_q("dsync", ql3[:, s::3, :], qg3[bass.ds(d512, 512), :, :])

        emit_E(nc, P, {"qkvg": qloc.ap(), "maskw": maskw, "ident_bf": ident_bf,
                       "y_tm": ytm.ap()})
        P.barrier(["gpsimd"])
        ccol("AllGather", G4, ytm, YAG)
        ccbar()
        d2048 = (nc.sync.partition_id() % 4) * 2048
        ya3 = YAG.ap().rearrange("(r x) c -> x r c", x=8192)
        yt3 = ytloc.ap().rearrange("(r x) c -> x r c", x=2048)
        for i in range(4):
            P.dma_q("dsync", yt3[:, i:i + 1, :],
                    ya3[bass.ds(d2048, 2048), i:i + 1, :])
        P.barrier()

        emit_E2(nc, P, {"ytmg": ytloc.ap(), "wpr_pk": pk["pr"].ap(), "x2T": x2T.ap(),
                        "ident_bf": ident_bf, "x3T": x3T.ap()})
        emit_F(nc, P, {"x3T": x3T.ap(), "rmsw": rmsw, "rt_w": rt_w,
                       "wg_pk": pk["g"].ap(), "wu_pk": pk["u"].ap(),
                       "wd_pk": pk["d"].ap(), "sel8": sel8, "ident": ident,
                       "out_tm": out_tm})
        P.barrier()
    return nc


def host_fused_inputs(inp):
    mapsW = host_inputs_W(inp)
    mapsA = host_inputs_A(inp)
    suug, wsg, wsu, wsd = host_scales_C(inp)
    x = np.asarray(inp["L0_attn_x"], np.float32)
    otb = np.ascontiguousarray(
        np.asarray(inp["L0_attn_out_proj_b"], np.float32).reshape(16, 128).T)
    rmsw = np.ascontiguousarray(
        np.asarray(inp["L1_ffn_rms_weight"], np.float32).reshape(16, 128).T)
    rtw = np.ascontiguousarray(np.asarray(inp["L1_ffn_router_w"], np.float32))
    sel8 = np.zeros((8, 1024), dtype=np.float32)
    for e in range(8):
        sel8[e, e * 128:(e + 1) * 128] = 1.0
    maskw = (np.arange(896)[None, :] - 384 >= np.arange(128)[:, None]).astype(BF)
    ident_bf = np.eye(128, dtype=np.float32).astype(BF)
    maps = []
    for c in range(8):
        b, q = c // 4, c % 4
        m = {}
        m.update(mapsW[c])
        m.update(mapsA[c])
        m.update({
            "otb": otb, "x_tm": np.ascontiguousarray(x[b][q * TC:(q + 1) * TC, :]),
            "su_ug": suug, "wsg": wsg, "wsu": wsu, "wsd": wsd,
            "maskw": maskw, "ident_bf": ident_bf,
            "rmsw": rmsw, "rt_w": rtw, "sel8": sel8,
        })
        maps.append(m)
    return maps


def kernel(**inputs):
    inp = {k: np.asarray(v) for k, v in inputs.items()}
    res = run_bass_kernel_spmd(build_fused(), host_fused_inputs(inp),
                               core_ids=list(range(8)))
    out = np.empty((B, T, D), dtype=np.float32)
    for c in range(8):
        b, q = c // 4, c % 4
        out[b, q * TC:(q + 1) * TC, :] = res.results[c]["out_tm"]
    return out


# version 10 (final)
# speedup vs baseline: 1.1221x; 1.0179x over previous
import numpy as np
import ml_dtypes
from contextlib import ExitStack
import concourse.bass as bass
import concourse.mybir as mybir
from concourse.bass_utils import run_bass_kernel_spmd

F32 = mybir.dt.float32
BF16 = mybir.dt.bfloat16
AF = mybir.ActivationFunctionType
ALU = mybir.AluOpType
BF = ml_dtypes.bfloat16

B, T, D = 2, 8192, 2048
NS = 16
INTER = 8192
NH, DH = 16, 128
NE, FE = 8, 1024
SF = 1024
TC = 2048
LN_EPS = 1e-5
RMS_EPS = 1e-6


_NAME_UNIQ = [0]


def _uniq(name):
    _NAME_UNIQ[0] += 1
    return f"{name}_{_NAME_UNIQ[0]}"


def hadamard(n):
    H = np.ones((1, 1), dtype=np.float32)
    while H.shape[0] < n:
        H = np.block([[H, H], [H, -H]])
    return H


class Ser:

    def __init__(self, nc, g, gd):
        self.nc = nc
        self.g = g
        self.gd = gd
        self.gc = 0
        self.gdc = 0

    def _wait(self, eng):
        if self.gc:
            eng.wait_ge(self.g, self.gc)
        if self.gdc:
            eng.wait_ge(self.gd, self.gdc)

    def step(self, eng, fn):
        self._wait(eng)
        fn(eng).then_inc(self.g, 1)
        self.gc += 1

    def steps(self, eng, fns):
        self._wait(eng)
        ins = None
        for fn in fns:
            ins = fn(eng)
        ins.then_inc(self.g, 1)
        self.gc += 1

    def dma(self, eng, out, in_):
        self._wait(eng)
        eng.dma_start(out=out, in_=in_).then_inc(self.gd, 16)
        self.gdc += 16

    def dmas(self, eng, pairs):
        self._wait(eng)
        for out, in_ in pairs:
            eng.dma_start(out=out, in_=in_).then_inc(self.gd, 16)
            self.gdc += 16

    def finish(self, eng):
        self._wait(eng)


class Pipe:

    ENGS = ("tensor", "vector", "scalar", "gpsimd", "sync")

    def __init__(self, nc, stack):
        self.nc = nc
        self.sem = {}
        self.cnt = {}
        self.seen = {e: {} for e in self.ENGS}
        for e in self.ENGS:
            self.sem[e] = stack.enter_context(nc.semaphore("sem_" + e))
            self.cnt[e] = 0
        for q in ("dsync", "dgp", "dscal"):
            self.sem[q] = stack.enter_context(nc.semaphore("sem_" + q))
            self.cnt[q] = 0

    def _wait(self, ename, eng, waits):
        kn = self.seen[ename]
        for s, v in waits:
            if s == ename or v <= 0:
                continue
            if kn.get(s, 0) >= v:
                continue
            eng.wait_ge(self.sem[s], v)
            kn[s] = v

    SELFSYNC = {"vector": True, "scalar": True, "gpsimd": True,
                "tensor": False, "sync": False}

    def emit(self, ename, fn, waits=(), inc=True, selfsync=None):
        eng = getattr(self.nc, ename)
        if selfsync is None:
            selfsync = self.SELFSYNC[ename]
        if selfsync and self.cnt[ename] > self.seen[ename].get(ename, 0):
            eng.wait_ge(self.sem[ename], self.cnt[ename])
            self.seen[ename][ename] = self.cnt[ename]
        self._wait(ename, eng, waits)
        ins = fn(eng)
        if inc:
            ins.then_inc(self.sem[ename], 1)
            self.cnt[ename] += 1
        return self.cnt[ename]

    def dma_q(self, qname, out, in_, waits=(), inc=True, sem=None):
        ename = {"dsync": "sync", "dgp": "gpsimd", "dscal": "scalar"}[qname]
        eng = getattr(self.nc, ename)
        self._wait(ename, eng, waits)
        ins = eng.dma_start(out=out, in_=in_)
        key = sem if sem is not None else qname
        if inc:
            ins.then_inc(self.sem[key], 16)
            self.cnt[key] += 16
        return self.cnt[key]

    def new_sem(self, name, stack):
        self.sem[name] = stack.enter_context(self.nc.semaphore("sem_" + name))
        self.cnt[name] = 0

    def snap(self):
        return [(k, v) for k, v in self.cnt.items() if v > 0]

    def lazy(self, ename, fn):
        return self.emit(ename, fn, waits=self.snap())

    def ldma(self, qname, out, in_):
        return self.dma_q(qname, out, in_, waits=self.snap())

    def barrier(self, enames=None):
        w = self.snap()
        for ename in (enames or self.ENGS):
            eng = getattr(self.nc, ename)
            self._wait(ename, eng, w)

    def _name_of(self, eng):
        if not hasattr(self, "_byid"):
            self._byid = {id(getattr(self.nc, e)): e for e in self.ENGS}
        return self._byid[id(eng)]

    def step(self, eng, fn):
        return self.lazy(self._name_of(eng), fn)

    def steps(self, eng, fns):
        ename = self._name_of(eng)
        w = self.snap()
        e = getattr(self.nc, ename)
        if self.SELFSYNC[ename] and self.cnt[ename] > self.seen[ename].get(ename, 0):
            e.wait_ge(self.sem[ename], self.cnt[ename])
            self.seen[ename][ename] = self.cnt[ename]
        self._wait(ename, e, w)
        ins = None
        for fn in fns:
            ins = fn(e)
        ins.then_inc(self.sem[ename], 1)
        self.cnt[ename] += 1
        return self.cnt[ename]

    _QOF = {"sync": "dsync", "gpsimd": "dgp", "scalar": "dscal"}

    def dma(self, eng, out, in_):
        return self.dma_q(self._QOF[self._name_of(eng)], out, in_, waits=self.snap())

    def dmas(self, eng, pairs):
        ename = self._name_of(eng)
        qname = self._QOF[ename]
        e = getattr(self.nc, ename)
        self._wait(ename, e, self.snap())
        for out, in_ in pairs:
            e.dma_start(out=out, in_=in_).then_inc(self.sem[qname], 16)
            self.cnt[qname] += 16
        return self.cnt[qname]

    def finish(self, eng):
        self._wait(self._name_of(eng), eng, self.snap())


ACH = 1024


def emit_A(nc, P, t):
    xA = t["xA"]
    xAs = t["xAs"]
    wdtT = t["wdtT"]
    dtb = t["dtb"]
    xpw = t["xpw"]
    xpb = t["xpb"]
    alog = t["alog"]
    ident = t["ident"]
    sel = t["sel"]
    yTs = t["yTs"]

    nch = T // ACH
    with ExitStack() as stack:
        def sb(name, shape, dt=F32):
            return stack.enter_context(nc.sbuf_tensor(_uniq(name), shape, dt))
        s_xtok = sb("s_xtok", [128, 2 * D])
        s_xT = sb("s_xT", [128, 16 * ACH], BF16)
        s_xs = sb("s_xs", [128, 4 * ACH], BF16)
        s_wdt = sb("s_wdt", [128, 16 * 512], BF16)
        s_xpw = sb("s_xpw", [128, 16 * 64], BF16)
        s_sel = sb("s_sel", [64, 4096], BF16)
        s_id = sb("s_id", [128, 128])
        s_dtb = sb("s_dtb", [128, 4])
        s_xpb = sb("s_xpb", [64, 1])
        s_A = sb("s_A", [128, 64])
        s_dtT = sb("s_dtT", [128, 2 * 4 * ACH])
        s_dtx = sb("s_dtx", [128, 2 * 4 * ACH])
        s_bc = sb("s_bc", [64, 2 * ACH], BF16)
        s_dA = sb("s_dA", [128, 2 * ACH])
        s_spt = sb("s_spt", [128, 512])
        s_h = sb("s_h", [128, 2 * ACH])
        s_tmp = sb("s_tmp", [128, 2 * ACH])
        s_u = sb("s_u", [128, ACH])
        s_y = sb("s_y", [128, 4 * ACH])
        s_ybf = sb("s_ybf", [128, 4 * ACH], BF16)
        s_hst = sb("s_hst", [128, 64])
        p = stack.enter_context(nc.psum_tensor(_uniq("p"), [128, 4096], F32))
        PTP = [p[:, 0:128], p[:, 512:640]]
        PDT = [p[:, 1024:1536], p[:, 1536:2048]]
        PB = p[:, 2048:3072]
        PBC = p[0:64, 1024:1536]
        PC = p[:, 3072:4096]
        ser = P

        ser.dmas(nc.sync, [
            (s_sel[:], sel), (s_id[:], ident), (s_dtb[:], dtb), (s_xpb[:], xpb),
        ])
        ser.dma(nc.sync, s_A[:], alog)
        ser.step(nc.scalar, lambda e: e.activation(s_A[:], s_A[:], AF.Exp))
        ser.step(nc.vector, lambda e: e.tensor_scalar_mul(s_A[:], s_A[:], -1.0))
        ser.dmas(nc.gpsimd, [
            (s_wdt[:, (k * 4 + et) * 128:(k * 4 + et + 1) * 128],
             wdtT[k * 128:(k + 1) * 128, et * 128:(et + 1) * 128])
            for k in range(16) for et in range(4)
        ])
        ser.dmas(nc.gpsimd, [
            (s_xpw[:, k * 64:(k + 1) * 64], xpw[k * 128:(k + 1) * 128, :])
            for k in range(16)
        ])
        P.barrier()

        tp_done = [0, 0]
        ev_s = [0, 0]
        ev_v = [0, 0]
        xld_cnt = [0, 0]
        xdma_free = [0, 0]
        sp_done = [0, 0]
        dtx_free = [0, 0]
        bc_ev = [0, 0]
        dA_cnt = [0, 0]
        dA_free = [0, 0]
        hst_done = [0] * 64
        hstc_free = [0, 0]
        yst_cnt = [0]
        chk = [dict() for _ in range(nch)]

        def prep_units(ch):
            t0 = ch * ACH
            xw = ch % 2
            units = []

            def u_xT(tt):
                tb = tt % 2
                xld_cnt[tb] = P.dma_q(
                    "dsync", s_xtok[:, tb * D: tb * D + D],
                    xA[t0 + tt * 128: t0 + (tt + 1) * 128, :],
                    waits=[("tensor", xdma_free[tb])], sem=f"xld{tb}")
                for k in range(16):
                    j = k % 2
                    w = [(f"xld{tb}", xld_cnt[tb]), ("scalar", ev_s[j]), ("vector", ev_v[j])]
                    tp_done[j] = P.emit("tensor", lambda e, k=k, tb=tb, j=j: e.transpose(
                        PTP[j], s_xtok[:, tb * D + k * 128: tb * D + (k + 1) * 128],
                        s_id[:]), waits=w)
                    ev_s[j] = P.emit("scalar", lambda e, k=k, tt=tt, j=j: e.copy(
                        s_xT[:, k * ACH + tt * 128: k * ACH + (tt + 1) * 128], PTP[j]),
                        waits=[("tensor", tp_done[j])])
                xdma_free[tb] = P.cnt["tensor"]

            def u_xs(tt):
                tb = tt % 2
                xld_cnt[tb] = P.dma_q(
                    "dsync", s_xtok[:, tb * D: tb * D + 512],
                    xAs[t0 + tt * 128: t0 + (tt + 1) * 128, :],
                    waits=[("tensor", xdma_free[tb])], sem=f"xld{tb}")
                for dt4 in range(4):
                    j = dt4 % 2
                    w = [(f"xld{tb}", xld_cnt[tb]), ("scalar", ev_s[j]), ("vector", ev_v[j])]
                    tp_done[j] = P.emit("tensor", lambda e, dt4=dt4, tb=tb, j=j: e.transpose(
                        PTP[j], s_xtok[:, tb * D + dt4 * 128: tb * D + (dt4 + 1) * 128],
                        s_id[:]), waits=w)
                    ev_v[j] = P.emit("vector", lambda e, dt4=dt4, tt=tt, j=j: e.tensor_copy(
                        s_xs[:, dt4 * ACH + tt * 128: dt4 * ACH + (tt + 1) * 128], PTP[j]),
                        waits=[("tensor", tp_done[j])])
                xdma_free[tb] = P.cnt["tensor"]

            def u_dt(g):
                et, tc2 = g // 2, g % 2
                jd = g % 2
                if g == 0:
                    chk[ch]["xTev"] = P.cnt["scalar"]
                for k in range(16):
                    w = []
                    if k == 0:
                        w = [("scalar", max(chk[ch]["xTev"], sp_done[jd])),
                             ("vector", max(bc_ev[0], bc_ev[1]))]
                    P.emit("tensor", lambda e, et=et, tc2=tc2, k=k, jd=jd: e.matmul(
                        PDT[jd],
                        s_wdt[:, (k * 4 + et) * 128:(k * 4 + et + 1) * 128],
                        s_xT[:, k * ACH + tc2 * 512: k * ACH + (tc2 + 1) * 512],
                        start=(k == 0), stop=(k == 15)), waits=w, inc=(k == 15))
                tdt = P.cnt["tensor"]
                P.emit("scalar", lambda e, et=et, jd=jd: e.activation(
                    s_spt[:, 0:512], PDT[jd], AF.Exp, bias=s_dtb[:, et:et + 1]),
                    waits=[("tensor", tdt), ("vector", dtx_free[xw])])
                sp_done[jd] = P.emit("scalar", lambda e, et=et, tc2=tc2, xw=xw: e.activation(
                    s_dtT[:, xw * 4 * ACH + et * ACH + tc2 * 512:
                          xw * 4 * ACH + et * ACH + (tc2 + 1) * 512],
                    s_spt[:, 0:512], AF.Ln, bias=1.0))
                if g == 7:
                    chk[ch]["sp"] = P.cnt["scalar"]

            def u_bc(tc2):
                for k in range(16):
                    w = []
                    if k == 0:
                        w = [("scalar", max(sp_done[0], sp_done[1])),
                             ("vector", max(bc_ev[0], bc_ev[1]))]
                    P.emit("tensor", lambda e, tc2=tc2, k=k: e.matmul(
                        PBC[:, 0:512], s_xpw[:, k * 64:(k + 1) * 64],
                        s_xT[:, k * ACH + tc2 * 512: k * ACH + (tc2 + 1) * 512],
                        start=(k == 0), stop=(k == 15)), waits=w, inc=(k == 15))
                tbc = P.cnt["tensor"]
                bc_ev[xw] = P.emit("vector", lambda e, tc2=tc2, xw=xw: e.tensor_scalar_add(
                    s_bc[:, xw * ACH + tc2 * 512: xw * ACH + (tc2 + 1) * 512],
                    PBC[:, 0:512], s_xpb[:, 0:1]), waits=[("tensor", tbc)])
                if tc2 == 1:
                    chk[ch]["bc"] = P.cnt["vector"]

            for tt in range(8):
                units.append(lambda tt=tt: u_xT(tt))
            for tt in range(8):
                units.append(lambda tt=tt: u_xs(tt))
            for g in range(8):
                units.append(lambda g=g: u_dt(g))
            for tc2 in range(2):
                units.append(lambda tc2=tc2: u_bc(tc2))
            return units

        def emit_scan(ch, next_units):
            t0 = ch * ACH
            xw = ch % 2
            for dt4 in range(4):
                dtx_free[xw] = P.emit("vector", lambda e, dt4=dt4, xw=xw: e.tensor_mul(
                    s_dtx[:, xw * 4 * ACH + dt4 * ACH: xw * 4 * ACH + (dt4 + 1) * ACH],
                    s_dtT[:, xw * 4 * ACH + dt4 * ACH: xw * 4 * ACH + (dt4 + 1) * ACH],
                    s_xs[:, dt4 * ACH:(dt4 + 1) * ACH]),
                    waits=[("scalar", chk[ch]["sp"])])
            ui = [0]
            tmp_free = [0, 0]
            yv_last = [0, 0, 0, 0]
            ya_last = 0

            def run_units(kmax):
                while ui[0] < min(kmax, len(next_units)):
                    next_units[ui[0]]()
                    ui[0] += 1

            u_last = 0
            ym_last = 0
            for n in range(NS):
                for tc2 in range(ACH // 512):
                    w = [("vector", max(u_last, chk[ch]["bc"]))] if tc2 == 0 else []
                    P.emit("tensor", lambda e, n=n, tc2=tc2, xw=xw: e.matmul(
                        PB[:, tc2 * 512:(tc2 + 1) * 512],
                        s_sel[:, n * 128:(n + 1) * 128],
                        s_bc[:, xw * ACH + tc2 * 512: xw * ACH + (tc2 + 1) * 512],
                        start=True, stop=True), waits=w, inc=(tc2 == 1))
                repB = P.cnt["tensor"]
                for tc2 in range(ACH // 512):
                    w = [("vector", ym_last)] if tc2 == 0 else []
                    P.emit("tensor", lambda e, n=n, tc2=tc2, xw=xw: e.matmul(
                        PC[:, tc2 * 512:(tc2 + 1) * 512],
                        s_sel[:, (16 + n) * 128:(17 + n) * 128],
                        s_bc[:, xw * ACH + tc2 * 512: xw * ACH + (tc2 + 1) * 512],
                        start=True, stop=True), waits=w, inc=(tc2 == 1))
                repC = P.cnt["tensor"]
                for dt4 in range(4):
                    col = dt4 * 16 + n
                    j2 = (n * 4 + dt4) % 2
                    dA_cnt[j2] = P.emit("scalar", lambda e, dt4=dt4, col=col, j2=j2, xw=xw: e.activation(
                        s_dA[:, j2 * ACH:(j2 + 1) * ACH],
                        s_dtT[:, xw * 4 * ACH + dt4 * ACH: xw * 4 * ACH + (dt4 + 1) * ACH],
                        AF.Exp, scale=s_A[:, col:col + 1]),
                        waits=[("vector", dA_free[j2]), ("scalar", chk[ch]["sp"])])
                    u_last = P.emit("vector", lambda e, dt4=dt4, xw=xw: e.tensor_mul(
                        s_u[:], s_dtx[:, xw * 4 * ACH + dt4 * ACH: xw * 4 * ACH + (dt4 + 1) * ACH],
                        PB[:]), waits=[("tensor", repB)])
                    if ch == 0:
                        scv = P.emit("vector", lambda e, j2=j2: e.tensor_tensor_scan(
                            s_h[:, j2 * ACH:(j2 + 1) * ACH], s_dA[:, j2 * ACH:(j2 + 1) * ACH],
                            s_u[:], 0.0, ALU.mult, ALU.add),
                            waits=[("scalar", dA_cnt[j2]), ("gpsimd", hstc_free[j2])])
                    else:
                        scv = P.emit("vector", lambda e, col=col, j2=j2: e.tensor_tensor_scan(
                            s_h[:, j2 * ACH:(j2 + 1) * ACH], s_dA[:, j2 * ACH:(j2 + 1) * ACH],
                            s_u[:], s_hst[:, col:col + 1], ALU.mult, ALU.add),
                            waits=[("scalar", dA_cnt[j2]), ("gpsimd", hst_done[col]),
                                   ("gpsimd", hstc_free[j2])])
                    dA_free[j2] = scv
                    hst_done[col] = P.emit("gpsimd", lambda e, col=col, j2=j2: e.tensor_copy(
                        s_hst[:, col:col + 1], s_h[:, (j2 + 1) * ACH - 1:(j2 + 1) * ACH]),
                        waits=[("vector", scv)])
                    hstc_free[j2] = hst_done[col]
                    if n == 0:
                        w = [("tensor", repC)]
                        if ch > 0 and dt4 == 0:
                            w.append(("yst", yst_cnt[0]))
                        ym_last = P.emit("vector", lambda e, dt4=dt4, j2=j2: e.tensor_mul(
                            s_y[:, dt4 * ACH:(dt4 + 1) * ACH],
                            s_h[:, j2 * ACH:(j2 + 1) * ACH], PC[:]), waits=w)
                        ya_last = 0
                        yv_last[dt4] = ym_last
                    else:
                        ym_last = P.emit("vector", lambda e, j2=j2: e.tensor_mul(
                            s_tmp[:, j2 * ACH:(j2 + 1) * ACH],
                            s_h[:, j2 * ACH:(j2 + 1) * ACH], PC[:]),
                            waits=[("tensor", repC), ("gpsimd", tmp_free[j2])])
                        ydst = s_ybf if n == NS - 1 else s_y
                        ya_last = P.emit("gpsimd", lambda e, dt4=dt4, j2=j2, ydst=ydst: e.tensor_add(
                            ydst[:, dt4 * ACH:(dt4 + 1) * ACH],
                            s_y[:, dt4 * ACH:(dt4 + 1) * ACH],
                            s_tmp[:, j2 * ACH:(j2 + 1) * ACH]),
                            waits=[("vector", max(ym_last, yv_last[dt4]))])
                        tmp_free[j2] = ya_last
                run_units(min(2 * (n + 1), 16))
            run_units(len(next_units))
            jblk, c0 = (t0 // 2048), (t0 % 2048)
            for dt4 in range(4):
                r0 = jblk * 512 + dt4 * 128
                yst_cnt[0] = P.dma_q("dsync", yTs[r0:r0 + 128, c0:c0 + ACH],
                                     s_ybf[:, dt4 * ACH:(dt4 + 1) * ACH],
                                     waits=[("gpsimd", ya_last)], sem="yst")

        for u in prep_units(0):
            u()
        for ch in range(nch):
            nxt = prep_units(ch + 1) if ch + 1 < nch else []
            emit_scan(ch, nxt)
        P.barrier()

def host_inputs_A(inp):
    x = np.asarray(inp["L0_attn_x"], dtype=np.float32)
    wdtT = np.ascontiguousarray(np.asarray(inp["L0_attn_dt_proj_w"], np.float32).T)
    dtb = np.asarray(inp["L0_attn_dt_proj_b"], np.float32)
    xpwT = np.ascontiguousarray(np.asarray(inp["L0_attn_x_proj_w"], np.float32).T)
    xpb = np.asarray(inp["L0_attn_x_proj_b"], np.float32)
    alog = np.asarray(inp["L0_attn_A_log"], np.float32)
    ident = np.eye(128, dtype=np.float32)
    selm = np.zeros((64, 4096), dtype=np.float32)
    for n in range(16):
        selm[n, n * 128:(n + 1) * 128] = 1.0
        selm[32 + n, (16 + n) * 128:(17 + n) * 128] = 1.0
    xpw_pad = np.zeros((D, 64), dtype=np.float32)
    xpw_pad[:, 0:16] = xpwT[:, 0:16]
    xpw_pad[:, 32:48] = xpwT[:, 16:32]
    xpb_pad = np.zeros((64, 1), dtype=np.float32)
    xpb_pad[0:16, 0] = xpb[0:16]
    xpb_pad[32:48, 0] = xpb[16:32]
    maps = []
    for c in range(8):
        b, ds = c // 4, c % 4
        d0 = ds * 512
        maps.append({
            "xA": np.ascontiguousarray(x[b]),
            "xAs": np.ascontiguousarray(x[b][:, d0:d0 + 512]),
            "wdtT": np.ascontiguousarray(wdtT[:, d0:d0 + 512]),
            "dtb": np.ascontiguousarray(dtb[d0:d0 + 512].reshape(4, 128).T),
            "xpw": xpw_pad,
            "xpb": xpb_pad,
            "alog": np.ascontiguousarray(
                alog[d0:d0 + 512].reshape(4, 128, 16).transpose(1, 0, 2).reshape(128, 64)),
            "ident": ident,
            "sel": selm.astype(BF),
        })
    return maps


def sp_pack(Wt):
    Kf, Mf = Wt.shape
    Kt, Mt = Kf // 128, Mf // 128
    return np.ascontiguousarray(
        Wt.reshape(Kt, 128, Mt, 128).transpose(2, 0, 1, 3).reshape(Mt * Kt * 128, 128))


def load_sp(ser, eng, s_tile, dram_pack, mt, Kt):
    src = dram_pack[mt * Kt * 128:(mt + 1) * Kt * 128, :].rearrange(
        "(kt p) f -> p kt f", p=128)
    dst = s_tile[:, 0:Kt * 128].rearrange("p (kt f) -> p kt f", f=128)
    ser.dma(eng, dst, src)


def emit_load_xT(ser, nc, x_tm, t0, ntt, s_tok, s_dst, p_tp, s_id, ach, ndk=16):
    for tt in range(ntt):
        ser.dma(nc.sync, s_tok[:, 0:ndk * 128],
                x_tm[t0 + tt * 128: t0 + (tt + 1) * 128, :])
        for k in range(ndk):
            ser.step(nc.tensor, lambda e, k=k: e.transpose(
                p_tp[:], s_tok[:, k * 128:(k + 1) * 128], s_id[:]))
            ser.step(nc.vector, lambda e, k=k, tt=tt: e.tensor_copy(
                s_dst[:, k * ach + tt * 128: k * ach + (tt + 1) * 128], p_tp[:]))


def emit_ln(ser, nc, s_s, s_bf, s_st, s_onc, s_onr, p_s, p_rep, s_out, eps, w=512):
    for k in range(16):
        ser.step(nc.scalar, lambda e, k=k: e.copy(
            s_bf[:, k * w:(k + 1) * w], s_s[:, k * w:(k + 1) * w]))
    for k in range(16):
        ser.step(nc.tensor, lambda e, k=k: e.matmul(
            p_s[0:1, 0:w], s_onc[:, 0:1], s_bf[:, k * w:(k + 1) * w],
            start=(k == 0), stop=(k == 15)))
    for k in range(16):
        ser.step(nc.scalar, lambda e, k=k: e.activation(
            s_bf[:, k * w:(k + 1) * w], s_s[:, k * w:(k + 1) * w], AF.Square))
    for k in range(16):
        ser.step(nc.tensor, lambda e, k=k: e.matmul(
            p_s[0:1, w:2 * w], s_onc[:, 0:1], s_bf[:, k * w:(k + 1) * w],
            start=(k == 0), stop=(k == 15)))
    mu, e2, va, rs, nm = (s_st[0:1, i * w:(i + 1) * w] for i in range(5))
    ser.step(nc.vector, lambda e: e.tensor_scalar_mul(mu, p_s[0:1, 0:w], 1.0 / 2048))
    ser.step(nc.vector, lambda e: e.tensor_scalar_mul(e2, p_s[0:1, w:2 * w], 1.0 / 2048))
    ser.step(nc.vector, lambda e: e.tensor_mul(va, mu, mu))
    ser.step(nc.vector, lambda e: e.tensor_sub(va, e2, va))
    ser.step(nc.vector, lambda e: e.tensor_scalar_add(va, va, float(eps)))
    ser.step(nc.scalar, lambda e: e.activation(va, va, AF.Ln))
    ser.step(nc.vector, lambda e: e.tensor_scalar_mul(va, va, -0.5))
    ser.step(nc.scalar, lambda e: e.activation(rs, va, AF.Exp))
    ser.step(nc.vector, lambda e: e.tensor_mul(nm, mu, rs))
    ser.step(nc.vector, lambda e: e.tensor_scalar_mul(nm, nm, -1.0))
    ser.step(nc.tensor, lambda e: e.matmul(
        p_rep[:, 0:w], s_onr[0:1, :], rs, start=True, stop=True))
    ser.step(nc.tensor, lambda e: e.matmul(
        p_rep[:, w:2 * w], s_onr[0:1, :], nm, start=True, stop=True))
    for k in range(16):
        ser.step(nc.vector, lambda e, k=k: e.tensor_mul(
            s_out[:, k * w:(k + 1) * w], s_s[:, k * w:(k + 1) * w], p_rep[:, 0:w]))
        ser.step(nc.vector, lambda e, k=k: e.tensor_add(
            s_out[:, k * w:(k + 1) * w], s_out[:, k * w:(k + 1) * w],
            p_rep[:, w:2 * w]))


def emit_W(nc, P, t):
    wupT = t["wupT_sl"]
    wgT = t["wgT_sl"]
    wdT = t["wdT_sl"]
    h128 = t["h128"]
    up_t = t["up_t"]
    g_t = t["g_t"]
    dn_t = t["dn_t"]

    with ExitStack() as stack:
        def sb(name, shape, dt=F32):
            return stack.enter_context(nc.sbuf_tensor(_uniq(name), shape, dt))
        s_w = sb("s_w", [128, 16 * 1024])
        s_h = sb("s_h", [128, 128])
        s_t = sb("s_t", [128, 1024])
        s_o = sb("s_o", [128, 16 * 1024], BF16)
        p_m = stack.enter_context(nc.psum_tensor(_uniq("p_m"), [128, 512], F32))
        ser = P
        ser.dma(nc.sync, s_h[:], h128)

        def transform(srcT, dst, ntile, width, kind):
            ser.dmas(nc.sync, [
                (s_w[:, i * width:(i + 1) * width], srcT[i * 128:(i + 1) * 128, :])
                for i in range(ntile)])
            for i in range(ntile):
                for cqq in range(0, width, 512):
                    cw = min(512, width - cqq)
                    ser.step(nc.tensor, lambda e, i=i, cq=cqq, cw=cw: e.matmul(
                        p_m[:, 0:cw], s_h[:], s_w[:, i * width + cq: i * width + cq + cw],
                        start=True, stop=True))
                    ser.step(nc.vector, lambda e, i=i, cq=cqq, cw=cw: e.tensor_copy(
                        s_w[:, i * width + cq: i * width + cq + cw], p_m[:, 0:cw]))
            h = 1
            while h < ntile:
                for base in range(0, ntile, 2 * h):
                    for j in range(h):
                        a = (base + j) * width
                        b2 = (base + j + h) * width
                        ser.step(nc.vector, lambda e, a=a, b2=b2: e.tensor_sub(
                            s_t[:, 0:width], s_w[:, a:a + width], s_w[:, b2:b2 + width]))
                        ser.step(nc.vector, lambda e, a=a, b2=b2: e.tensor_add(
                            s_w[:, a:a + width], s_w[:, a:a + width], s_w[:, b2:b2 + width]))
                        ser.step(nc.vector, lambda e, b2=b2: e.tensor_copy(
                            s_w[:, b2:b2 + width], s_t[:, 0:width]))
                h *= 2
            for i in range(ntile):
                ser.step(nc.vector, lambda e, i=i: e.tensor_copy(
                    s_o[:, i * width:(i + 1) * width], s_w[:, i * width:(i + 1) * width]))
            if kind == "up":
                for ft in range(8):
                    ser.dma(nc.sync,
                            dst[ft * 16 * 128:(ft + 1) * 16 * 128, :].rearrange(
                                "(kt p) f -> p kt f", p=128),
                            s_o.rearrange("p (kt g) -> p kt g", g=1024)[
                                :, :, ft * 128:(ft + 1) * 128])
            else:
                for dt_ in range(2):
                    ser.dma(nc.sync,
                            dst[dt_ * 64 * 128:(dt_ + 1) * 64 * 128, :].rearrange(
                                "(fk p) f -> p fk f", p=128),
                            s_o.rearrange("p (fk g) -> p fk g", g=256)[
                                :, :, dt_ * 128:(dt_ + 1) * 128])

        transform(wupT, up_t, 16, 1024, "up")
        transform(wgT, g_t, 16, 1024, "up")
        transform(wdT, dn_t, 64, 256, "dn")
        for name, rows in PACK_SECTIONS:
            cin = t[f"cv_{name}_in"]
            cout = t[f"cv_{name}_out"]
            n = rows // 8
            csz = 4096 if n % 4096 == 0 else n // 4 if n % 4 == 0 else n
            r = 0
            while r < n:
                c = min(csz, n - r)
                ser.dma(nc.gpsimd, cout[r:r + c, :], cin[r:r + c, :])
                r += c
        P.barrier()


PACK_SECTIONS = (("qkv", 98304), ("pr", 32768), ("ot", 32768),
                 ("g", 147456), ("u", 147456), ("d", 147456))


def host_packs_f32(inp):
    pk = {}
    pk["qkv"] = sp_pack(np.ascontiguousarray(np.asarray(inp["L1_attn_W_qkv"], np.float32).T))
    pk["pr"] = sp_pack(np.ascontiguousarray(np.asarray(inp["L1_attn_W_proj"], np.float32).T))
    pk["ot"] = sp_pack(np.ascontiguousarray(np.asarray(inp["L0_attn_out_proj_w"], np.float32).T))
    wg = np.asarray(inp["L1_ffn_w_gate"], np.float32)
    wu = np.asarray(inp["L1_ffn_w_up"], np.float32)
    wd = np.asarray(inp["L1_ffn_w_down"], np.float32)
    pk["g"] = np.concatenate([sp_pack(wg[e]) for e in range(8)]
                             + [sp_pack(np.asarray(inp["L1_ffn_sw_gate"], np.float32))], axis=0)
    pk["u"] = np.concatenate([sp_pack(wu[e]) for e in range(8)]
                             + [sp_pack(np.asarray(inp["L1_ffn_sw_up"], np.float32))], axis=0)
    pk["d"] = np.concatenate([sp_pack(wd[e]) for e in range(8)]
                             + [sp_pack(np.asarray(inp["L1_ffn_sw_down"], np.float32))], axis=0)
    return pk


def host_inputs_W(inp):
    wupT = np.ascontiguousarray(np.asarray(inp["L0_ffn_W_up"], np.float32).T)
    wgT = np.ascontiguousarray(np.asarray(inp["L0_ffn_W_gate"], np.float32).T)
    wdT = np.ascontiguousarray(np.asarray(inp["L0_ffn_W_down"], np.float32).T)
    h = hadamard(128)
    pk = host_packs_f32(inp)
    maps = []
    for c in range(8):
        m = {
            "wupT_sl": np.ascontiguousarray(wupT[:, c * 1024:(c + 1) * 1024]),
            "wgT_sl": np.ascontiguousarray(wgT[:, c * 1024:(c + 1) * 1024]),
            "wdT_sl": np.ascontiguousarray(wdT[:, c * 256:(c + 1) * 256]),
            "h128": h,
        }
        for name, rows in PACK_SECTIONS:
            n = rows // 8
            m[f"cv_{name}_in"] = np.ascontiguousarray(pk[name][c * n:(c + 1) * n])
        maps.append(m)
    return maps


def assemble_packs_bf16(resW):
    out = {}
    for name, rows in PACK_SECTIONS:
        out[name] = np.concatenate(
            [resW.results[c][f"cv_{name}_out"] for c in range(8)], axis=0)
    return out


def unpack_fold(resW):
    wup_pk = np.concatenate([resW.results[c]["up_t"] for c in range(8)], axis=0)
    wg_pk = np.concatenate([resW.results[c]["g_t"] for c in range(8)], axis=0)
    wdn_pk = np.concatenate([resW.results[c]["dn_t"] for c in range(8)], axis=0)
    return wup_pk, wg_pk, wdn_pk


def emit_B(nc, P, t):
    yT = t["yT"]
    wot_pk = t["wot_pk"]
    otb = t["otb"]
    x_tm = t["x_tm"]
    ident = t["ident"]
    x1T = t["x1T"]

    with ExitStack() as stack:
        def sb(name, shape, dt=F32):
            return stack.enter_context(nc.sbuf_tensor(_uniq(name), shape, dt))
        s_y = sb("s_y", [128, 16 * TC], BF16)
        s_wb = sb("s_wb", [128, 4 * 2048], BF16)
        s_otb = sb("s_otb", [128, 16])
        s_id = sb("s_id", [128, 128])
        s_tok = sb("s_tok", [128, 2 * D])
        s_xr = sb("s_xr", [128, 16 * 512])
        s_s = sb("s_s", [128, 16 * 512])
        s_bf = sb("s_bf", [128, 16 * 512], BF16)
        s_st = sb("s_st", [1, 8 * 512])
        s_onc = sb("s_onc", [128, 1], BF16)
        s_onr = sb("s_onr", [1, 128])
        s_x1 = s_xr
        p = stack.enter_context(nc.psum_tensor(_uniq("p"), [128, 4096], F32))
        PTP = [p[:, 0:128], p[:, 512:640]]
        PM = [p[:, 1024:1536], p[:, 1536:2048]]
        p_s = p[0:1, 2048:3072]
        p_rep = p[:, 3072:4096]
        ser = P

        ser.dmas(nc.sync, [(s_otb[:], otb), (s_id[:], ident)])
        ser.step(nc.vector, lambda e: e.memset(s_onc[:, 0:1], 1.0))
        ser.step(nc.vector, lambda e: e.memset(s_onr[0:1, :], 1.0))
        ser.dmas(nc.sync, [
            (s_y[:, k * TC:(k + 1) * TC], yT[k * 128:(k + 1) * 128, :])
            for k in range(16)])
        P.barrier()

        NLD = 16
        NTC = TC // 512
        li = [0]
        ld_done = [0] * (NLD * NTC)
        ld_cnt = [0] * (NLD * NTC)

        def issue_loads(upto):
            while li[0] < min(upto, NLD * NTC):
                i = li[0]
                mt = i % NLD
                slot = i % 4
                srcv = wot_pk[mt * 16 * 128:(mt + 1) * 16 * 128, :].rearrange(
                    "(kt p) f -> p kt f", p=128)
                dstv = s_wb[:, slot * 2048: slot * 2048 + 2048].rearrange(
                    "p (kt f) -> p kt f", f=128)
                w = []
                if i >= 4:
                    w = [("tensor", ld_done[i - 4])]
                qn = "dgp" if (i % 2 == 0) else "dsync"
                ld_cnt[i] = P.dma_q(qn, dstv, srcv, waits=w, sem=f"slw{i % 4}")
                li[0] += 1

        tp_done = [0, 0]
        ev_v = [0, 0]
        xld_cnt = [0, 0]
        xdma_free = [0, 0]
        ev_cnt = [0, 0]
        for tc in range(NTC):
            t0 = tc * 512
            for tt in range(4):
                tb = tt % 2
                xld_cnt[tb] = P.dma_q(
                    "dsync", s_tok[:, tb * D: tb * D + D],
                    x_tm[t0 + tt * 128: t0 + (tt + 1) * 128, :],
                    waits=[("tensor", xdma_free[tb])], sem=f"xld{tb}")
                for k in range(16):
                    j = k % 2
                    w = [(f"xld{tb}", xld_cnt[tb]), ("vector", ev_v[j])]
                    tp_done[j] = P.emit("tensor", lambda e, k=k, tb=tb, j=j: e.transpose(
                        PTP[j], s_tok[:, tb * D + k * 128: tb * D + (k + 1) * 128],
                        s_id[:]), waits=w)
                    ev_v[j] = P.emit("vector", lambda e, k=k, tt=tt, j=j: e.tensor_copy(
                        s_xr[:, k * 512 + tt * 128: k * 512 + (tt + 1) * 128], PTP[j]),
                        waits=[("tensor", tp_done[j])])
                xdma_free[tb] = P.cnt["tensor"]
            xrv = P.cnt["vector"]
            for mt in range(16):
                gi = tc * NLD + mt
                issue_loads(gi + 4)
                slot = gi % 4
                par = mt % 2
                for k in range(16):
                    w = []
                    if k == 0:
                        w = [(f"slw{gi % 4}", ld_cnt[gi]), ("vector", ev_cnt[par])]
                    P.emit("tensor", lambda e, mt=mt, k=k, slot=slot, par=par: e.matmul(
                        PM[par], s_wb[:, slot * 2048 + k * 128: slot * 2048 + (k + 1) * 128],
                        s_y[:, k * TC + t0: k * TC + t0 + 512],
                        start=(k == 0), stop=(k == 15)),
                        waits=w, inc=(k == 15))
                tm = P.cnt["tensor"]
                ld_done[gi] = tm
                ev_cnt[par] = P.emit("vector", lambda e, mt=mt, par=par: e.scalar_tensor_tensor(
                    s_s[:, mt * 512:(mt + 1) * 512], PM[par], s_otb[:, mt:mt + 1],
                    s_xr[:, mt * 512:(mt + 1) * 512], ALU.add, ALU.add),
                    waits=[("tensor", tm), ("vector", xrv)])
            P.barrier()
            emit_ln(ser, nc, s_s, s_bf, s_st, s_onc, s_onr, p_s, p_rep,
                    s_x1, LN_EPS)
            ser.dmas(nc.sync, [
                (x1T[k * 128:(k + 1) * 128, t0:t0 + 512],
                 s_x1[:, k * 512:(k + 1) * 512]) for k in range(16)])
            P.barrier()
        P.barrier()


def emit_C(nc, P, t):
    x1T = t["x1T"]
    wup_pk = t["wup_pk"]
    wg_pk = t["wg_pk"]
    wdn_pk = t["wdn_pk"]
    su_ug = t["su_ug"]
    wsg = t["wsg"]
    wsu = t["wsu"]
    wsd = t["wsd"]
    x2T = t["x2T"]

    with ExitStack() as stack:
        def sb(name, shape, dt=F32):
            return stack.enter_context(nc.sbuf_tensor(_uniq(name), shape, dt))
        s_x1 = sb("s_x1", [128, 16 * 512])
        s_xs = sb("s_xs", [128, 16 * 512], BF16)
        s_su = sb("s_su", [128, 16])
        s_wsg = sb("s_wsg", [128, 64])
        s_wsu = sb("s_wsu", [128, 64])
        s_wsd = sb("s_wsd", [128, 16])
        s_wb = sb("s_wb", [128, 4 * 2048], BF16)
        s_sg = sb("s_sg", [128, 2 * 512])
        s_z = sb("s_z", [128, 64 * 512], BF16)
        s_s = sb("s_s", [128, 16 * 512])
        s_bf = sb("s_bf", [128, 16 * 512], BF16)
        s_st = sb("s_st", [1, 8 * 512])
        s_onc = sb("s_onc", [128, 1], BF16)
        s_onr = sb("s_onr", [1, 128])
        s_x2 = s_x1
        p = stack.enter_context(nc.psum_tensor(_uniq("p"), [128, 4096], F32))
        PG = [p[:, 0:512], p[:, 512:1024]]
        PU = [p[:, 1024:1536], p[:, 1536:2048]]
        PD = [p[:, 2048:2560], p[:, 2560:3072]]
        p_rep = p[:, 3072:4096]
        p_s = p[0:1, 2048:3072]
        ser = P

        ser.dmas(nc.sync, [(s_su[:], su_ug), (s_wsg[:], wsg), (s_wsu[:], wsu),
                           (s_wsd[:], wsd)])
        ser.step(nc.vector, lambda e: e.memset(s_onc[:, 0:1], 1.0))
        ser.step(nc.vector, lambda e: e.memset(s_onr[0:1, :], 1.0))

        loads = []
        for ft in range(64):
            loads.append((wg_pk, ft * 16 * 128, 16))
            loads.append((wup_pk, ft * 16 * 128, 16))
        for dt_ in range(16):
            for sub in range(4):
                loads.append((wdn_pk, dt_ * 64 * 128 + sub * 16 * 128, 16))
        NLD = len(loads)
        NTC = TC // 512
        li = [0]
        ld_done = [0] * (NLD * NTC)
        ld_cnt = [0] * (NLD * NTC)

        def issue_loads(upto):
            while li[0] < min(upto, NLD * NTC):
                i = li[0]
                pk, row0, Kt = loads[i % NLD]
                slot = i % 4
                srcv = pk[row0:row0 + Kt * 128, :].rearrange("(kt p) f -> p kt f", p=128)
                dstv = s_wb[:, slot * 2048: slot * 2048 + Kt * 128].rearrange(
                    "p (kt f) -> p kt f", f=128)
                w = []
                if i >= 4:
                    w = [("tensor", ld_done[i - 4])]
                qn = "dgp" if (i % 2 == 0) else "dsync"
                ld_cnt[i] = P.dma_q(qn, dstv, srcv, waits=w, sem=f"slw{i % 4}")
                li[0] += 1

        for tc in range(NTC):
            t0 = tc * 512
            ser.dmas(nc.sync, [
                (s_x1[:, k * 512:(k + 1) * 512], x1T[k * 128:(k + 1) * 128, t0:t0 + 512])
                for k in range(16)])
            for k in range(16):
                ser.step(nc.vector, lambda e, k=k: e.tensor_scalar_mul(
                    s_xs[:, k * 512:(k + 1) * 512], s_x1[:, k * 512:(k + 1) * 512],
                    s_su[:, k:k + 1]))
            P.barrier()

            silu_cnt = [0, 0]
            sg_free = [0, 0]
            z_cnt = [0, 0]
            evac_cnt = [0, 0]
            z_done = 0
            gi = tc * NLD
            for ft in range(64):
                par = ft % 2
                issue_loads(gi + 4)
                slot = gi % 4
                for k in range(16):
                    w = []
                    if k == 0:
                        w = [(f"slw{gi % 4}", ld_cnt[gi]), ("scalar", silu_cnt[par])]
                    P.emit("tensor", lambda e, k=k, slot=slot, par=par: e.matmul(
                        PG[par], s_wb[:, slot * 2048 + k * 128: slot * 2048 + (k + 1) * 128],
                        s_xs[:, k * 512:(k + 1) * 512],
                        start=(k == 0), stop=(k == 15)),
                        waits=w, inc=(k == 15))
                tg = P.cnt["tensor"]
                ld_done[gi] = tg
                gi += 1
                issue_loads(gi + 4)
                slot = gi % 4
                for k in range(16):
                    w = []
                    if k == 0:
                        w = [(f"slw{gi % 4}", ld_cnt[gi]), ("vector", z_cnt[par])]
                    P.emit("tensor", lambda e, k=k, slot=slot, par=par: e.matmul(
                        PU[par], s_wb[:, slot * 2048 + k * 128: slot * 2048 + (k + 1) * 128],
                        s_xs[:, k * 512:(k + 1) * 512],
                        start=(k == 0), stop=(k == 15)),
                        waits=w, inc=(k == 15))
                tu = P.cnt["tensor"]
                ld_done[gi] = tu
                gi += 1
                silu_cnt[par] = P.emit("scalar", lambda e, par=par, ft=ft: e.activation(
                    s_sg[:, par * 512: par * 512 + 512], PG[par], AF.Silu,
                    scale=s_wsg[:, ft:ft + 1]),
                    waits=[("tensor", tg), ("vector", sg_free[par])])
                z_cnt[par] = P.emit("vector", lambda e, par=par, ft=ft: e.scalar_tensor_tensor(
                    s_z[:, ft * 512:(ft + 1) * 512], PU[par], s_wsu[:, ft:ft + 1],
                    s_sg[:, par * 512: par * 512 + 512], ALU.mult, ALU.mult),
                    waits=[("tensor", tu), ("scalar", silu_cnt[par])])
                sg_free[par] = z_cnt[par]
                if ft == 63:
                    z_done = z_cnt[par]
            for dt_ in range(16):
                par2 = dt_ % 2
                for fk in range(64):
                    sub = fk // 16
                    if fk % 16 == 0:
                        issue_loads(gi + sub + 4)
                    sl = (gi + sub) % 4
                    fo = fk % 16
                    w = []
                    if fk == 0:
                        w = [(f"slw{sl}", ld_cnt[gi]),
                             ("vector", max(z_done, evac_cnt[par2]))]
                    elif fk % 16 == 0:
                        w = [(f"slw{sl}", ld_cnt[gi + sub])]
                    P.emit("tensor", lambda e, fk=fk, sl=sl, fo=fo, par2=par2: e.matmul(
                        PD[par2], s_wb[:, sl * 2048 + fo * 128: sl * 2048 + (fo + 1) * 128],
                        s_z[:, fk * 512:(fk + 1) * 512],
                        start=(fk == 0), stop=(fk == 63)),
                        waits=w, inc=(fk % 16 == 15))
                    if fk % 16 == 15:
                        ld_done[gi + sub] = P.cnt["tensor"]
                td = P.cnt["tensor"]
                gi += 4
                evac_cnt[par2] = P.emit("vector", lambda e, dt_=dt_, par2=par2: e.scalar_tensor_tensor(
                    s_s[:, dt_ * 512:(dt_ + 1) * 512], PD[par2], s_wsd[:, dt_:dt_ + 1],
                    s_x1[:, dt_ * 512:(dt_ + 1) * 512], ALU.mult, ALU.add),
                    waits=[("tensor", td)])
            P.barrier()
            emit_ln(ser, nc, s_s, s_bf, s_st, s_onc, s_onr, p_s, p_rep,
                    s_x2, LN_EPS)
            ser.dmas(nc.sync, [
                (x2T[k * 128:(k + 1) * 128, t0:t0 + 512],
                 s_x2[:, k * 512:(k + 1) * 512]) for k in range(16)])
        P.barrier()


def host_scales_C(inp):
    wsc = np.asarray(inp["L0_ffn_Wscale_ug"], np.float32)
    sudp = np.asarray(inp["L0_ffn_SU_dp"], np.float32)
    wsg = np.ascontiguousarray((wsc[INTER:] / np.sqrt(D)).reshape(64, 128).T)
    wsu = np.ascontiguousarray((wsc[:INTER] * sudp / np.sqrt(D)).reshape(64, 128).T)
    wsd = np.ascontiguousarray(
        (np.asarray(inp["L0_ffn_Wscale_dp"], np.float32) / np.sqrt(INTER)).reshape(16, 128).T)
    suug = np.ascontiguousarray(np.asarray(inp["L0_ffn_SU_ug"], np.float32).reshape(16, 128).T)
    return suug, wsg, wsu, wsd


def emit_D(nc, P, t, on_kind=None):
    x2T = t["x2T"]
    wqkv_pk = t["wqkv_pk"]
    qkvT = t["qkvT"]

    with ExitStack() as stack:
        def sb(name, shape, dt=F32):
            return stack.enter_context(nc.sbuf_tensor(_uniq(name), shape, dt))
        s_x = sb("s_x", [128, 16 * TC], BF16)
        s_wb = sb("s_wb", [128, 4 * 2048], BF16)
        s_o = sb("s_o", [128, 2 * TC], BF16)
        p = stack.enter_context(nc.psum_tensor(_uniq("p"), [128, 4096], F32))
        PM = [p[:, 0:512], p[:, 512:1024]]
        ser = P
        ser.dmas(nc.gpsimd, [
            (s_x[:, k * TC:(k + 1) * TC], x2T[k * 128:(k + 1) * 128, :])
            for k in range(16)])
        P.barrier()

        li = [0]
        ld_done = [0] * 48
        ld_cnt = [0] * 48

        def issue_loads(upto):
            while li[0] < min(upto, 48):
                i = li[0]
                slot = i % 4
                srcv = wqkv_pk[i * 16 * 128:(i + 1) * 16 * 128, :].rearrange(
                    "(kt p) f -> p kt f", p=128)
                dstv = s_wb[:, slot * 2048: slot * 2048 + 2048].rearrange(
                    "p (kt f) -> p kt f", f=128)
                w = []
                if i >= 4:
                    w = [("tensor", ld_done[i - 4])]
                qn = "dgp" if (i % 2 == 0) else "dsync"
                ld_cnt[i] = P.dma_q(qn, dstv, srcv, waits=w, sem=f"slw{i % 4}")
                li[0] += 1

        ev_cnt = [0, 0]
        ost_cnt = [0, 0]
        for mt in range(48):
            issue_loads(mt + 4)
            slot = mt % 4
            oh = mt % 2
            for tc in range(TC // 512):
                par = tc % 2
                for k in range(16):
                    w = []
                    if k == 0:
                        w = [(f"slw{slot}", ld_cnt[mt]), ("scalar", ev_cnt[par])]
                    P.emit("tensor", lambda e, k=k, tc=tc, slot=slot, par=par: e.matmul(
                        PM[par], s_wb[:, slot * 2048 + k * 128: slot * 2048 + (k + 1) * 128],
                        s_x[:, k * TC + tc * 512: k * TC + (tc + 1) * 512],
                        start=(k == 0), stop=(k == 15)),
                        waits=w, inc=(k == 15))
                tm = P.cnt["tensor"]
                if tc == 3:
                    ld_done[mt] = tm
                w = [("tensor", tm)]
                if tc == 0:
                    w.append((f"ost{oh}", ost_cnt[oh]))
                ev_cnt[par] = P.emit("scalar", lambda e, tc=tc, oh=oh, par=par: e.copy(
                    s_o[:, oh * TC + tc * 512: oh * TC + (tc + 1) * 512], PM[par]),
                    waits=w)
            ost_cnt[oh] = P.dma_q("dsync", qkvT[mt * 128:(mt + 1) * 128, :],
                                  s_o[:, oh * TC:(oh + 1) * TC],
                                  waits=[("scalar", ev_cnt[(TC // 512 - 1) % 2])],
                                  sem=f"ost{oh}")
            if on_kind is not None and mt % 16 == 15:
                on_kind(mt // 16, ost_cnt[0], ost_cnt[1])
        P.barrier()


def emit_E(nc, P, t):
    qkvg = t["qkvg"]
    maskw = t["maskw"]
    ident_bf = t["ident_bf"]
    y_tm = t["y_tm"]

    NKV = T // 128
    with ExitStack() as stack:
        def sb(name, shape, dt=F32):
            return stack.enter_context(nc.sbuf_tensor(_uniq(name), shape, dt))
        s_q = sb("s_q", [128, 4 * T], BF16)
        s_k = sb("s_k", [128, 4 * T], BF16)
        s_v = sb("s_v", [128, NKV * 4 * 129], BF16)
        s_vt = sb("s_vt", [128, 2 * 2048], BF16)
        s_m = sb("s_m", [128, 896], BF16)
        s_p = sb("s_p", [128, 2 * 512], BF16)
        s_sqq = s_p[:, 0:512]
        s_st = sb("s_st", [1, 64])
        s_onc = sb("s_onc", [128, 1], BF16)
        s_onr = sb("s_onr", [1, 128])
        s_scl = sb("s_scl", [128, 1])
        s_mh = sb("s_mh", [128, 1])
        s_r = sb("s_r", [128, 4])
        s_yo = sb("s_yo", [128, 2 * 512], BF16)
        s_id = sb("s_id", [128, 128], BF16)
        p = stack.enter_context(nc.psum_tensor(_uniq("p"), [128, 3072], F32))
        PS = [p[:, 0:512], p[:, 512:1024]]
        PO = [p[:, 1024 + j * 512: 1024 + j * 512 + 129] for j in range(4)]
        p_s1 = p[0:1, 1024:1536]
        p_b = p[:, 1536:1537]
        p_bf = stack.enter_context(nc.psum_tensor(_uniq("p_bf"), [128, 2048], BF16))
        PTB = [p_bf[:, 0:128], p_bf[:, 1024:1152]]
        ser = P

        ser.dmas(nc.sync, [(s_m[:], maskw), (s_id[:], ident_bf)])
        ser.step(nc.vector, lambda e: e.memset(s_onc[:, 0:1], 1.0))
        ser.step(nc.vector, lambda e: e.memset(s_onr[0:1, :], 1.0))
        ser.step(nc.vector, lambda e: e.memset(s_scl[:, 0:1], 0.08838834764831845))
        for h in range(4):
            ser.dmas(nc.sync, [
                (s_q[:, h * T + r * 2048: h * T + (r + 1) * 2048],
                 qkvg[r * 1536 + h * 128: r * 1536 + (h + 1) * 128, :]) for r in range(4)])
            ser.dmas(nc.sync, [
                (s_k[:, h * T + r * 2048: h * T + (r + 1) * 2048],
                 qkvg[r * 1536 + 512 + h * 128: r * 1536 + 512 + (h + 1) * 128, :])
                for r in range(4)])
        tpd = [0, 0]
        evd = [0, 0]
        for h in range(4):
            for rp in range(2):
                ser.dmas(nc.sync, [
                    (s_vt[:, r2 * 2048:(r2 + 1) * 2048],
                     qkvg[(rp * 2 + r2) * 1536 + 1024 + h * 128:
                          (rp * 2 + r2) * 1536 + 1024 + (h + 1) * 128, :])
                    for r2 in range(2)])
                for r2 in range(2):
                    r = rp * 2 + r2
                    for kvl in range(16):
                        j = kvl % 2
                        tpd[j] = P.emit("tensor", lambda e, r2=r2, kvl=kvl, j=j: e.transpose(
                            PTB[j], s_vt[:, r2 * 2048 + kvl * 128: r2 * 2048 + (kvl + 1) * 128],
                            s_id[:]),
                            waits=[("scalar", evd[j]), ("dsync", P.cnt["dsync"])])
                        kvb = r * 16 + kvl
                        evd[j] = P.emit("scalar", lambda e, kvb=kvb, h=h, j=j: e.copy(
                            s_v[:, (kvb * 4 + h) * 129:(kvb * 4 + h) * 129 + 128], PTB[j]),
                            waits=[("tensor", tpd[j])])
        ser.step(nc.vector, lambda e: e.memset(
            s_v.rearrange("p (t c) -> p t c", c=129)[:, :, 128:129], 1.0))
        P.barrier()

        exp_free = [0, 0]
        av_free = [0, 0]
        yo_free = [0, 0]
        for h in range(4):
            for which, buf in ((0, s_q), (1, s_k)):
                for tcq in range(T // 512):
                    ser.step(nc.scalar, lambda e, buf=buf, h=h, tcq=tcq: e.activation(
                        s_sqq[:], buf[:, h * T + tcq * 512: h * T + (tcq + 1) * 512], AF.Square))
                    ser.step(nc.tensor, lambda e: e.matmul(
                        p_s1[0:1, :], s_onc[:, 0:1], s_sqq[:], start=True, stop=True))
                    ser.step(nc.vector, lambda e, w=which: e.reduce_max(
                        s_st[0:1, 8 + w:9 + w], p_s1[0:1, :], mybir.AxisListType.X))
                    if tcq == 0:
                        ser.step(nc.vector, lambda e, w=which: e.tensor_copy(
                            s_st[0:1, w:w + 1], s_st[0:1, 8 + w:9 + w]))
                    else:
                        ser.step(nc.vector, lambda e, w=which: e.tensor_max(
                            s_st[0:1, w:w + 1], s_st[0:1, w:w + 1], s_st[0:1, 8 + w:9 + w]))
            ser.step(nc.vector, lambda e: e.tensor_mul(
                s_st[0:1, 2:3], s_st[0:1, 0:1], s_st[0:1, 1:2]))
            ser.step(nc.scalar, lambda e: e.activation(s_st[0:1, 2:3], s_st[0:1, 2:3], AF.Ln))
            ser.step(nc.vector, lambda e: e.tensor_scalar_mul(s_st[0:1, 2:3], s_st[0:1, 2:3], 0.5))
            ser.step(nc.scalar, lambda e: e.activation(s_st[0:1, 2:3], s_st[0:1, 2:3], AF.Exp))
            ser.step(nc.vector, lambda e: e.tensor_scalar_mul(
                s_st[0:1, 2:3], s_st[0:1, 2:3], -0.08838834764831845))
            ser.step(nc.tensor, lambda e: e.matmul(
                p_b[:, 0:1], s_onr[0:1, :], s_st[0:1, 2:3], start=True, stop=True))
            ser.step(nc.vector, lambda e: e.tensor_copy(s_mh[:, 0:1], p_b[:, 0:1]))
            P.barrier()

            po_free = [0, 0, 0, 0]

            for qc in range(T // 512):
                nkv = 4 * qc + 4
                exp_cnt = [0] * nkv
                gate_cnt = [0] * nkv
                gate_eng = [None] * nkv
                av_done = [0] * nkv
                s_done = [0] * nkv
                for i in range(nkv + 1):
                    if i < nkv:
                        par = i % 2
                        w = [("scalar", exp_free[par])]
                        s_done[i] = P.emit("tensor", lambda e, h=h, qc=qc, i=i, par=par: e.matmul(
                            PS[par], s_k[:, h * T + i * 128: h * T + (i + 1) * 128],
                            s_q[:, h * T + qc * 512: h * T + (qc + 1) * 512],
                            start=True, stop=True), waits=w)
                        wexp = [("tensor", s_done[i]), ("tensor", av_free[par])]
                        exp_cnt[i] = P.emit("scalar", lambda e, par=par: e.activation(
                            s_p[:, par * 512: par * 512 + 512], PS[par], AF.Exp,
                            bias=s_mh[:, 0:1], scale=s_scl[:, 0:1]), waits=wexp)
                        exp_free[par] = exp_cnt[i]
                        o = i * 128 - qc * 512
                        if o >= 0:
                            gate_cnt[i] = P.emit("vector", lambda e, par=par, o=o: e.tensor_mul(
                                s_p[:, par * 512: par * 512 + 512],
                                s_p[:, par * 512: par * 512 + 512],
                                s_m[:, 384 - o: 896 - o]),
                                waits=[("scalar", exp_cnt[i])])
                            gate_eng[i] = "vector"
                        else:
                            gate_cnt[i] = exp_cnt[i]
                            gate_eng[i] = "scalar"
                    if i >= 1:
                        kvb = i - 1
                        par = kvb % 2
                        for j in range(4):
                            w = [(gate_eng[kvb], gate_cnt[kvb])]
                            if kvb == 0:
                                w.append(("vector", po_free[j]))
                            P.emit("tensor", lambda e, j=j, kvb=kvb, h=h, par=par,
                                   nkv=nkv: e.matmul(
                                PO[j], s_p[:, par * 512 + j * 128: par * 512 + (j + 1) * 128],
                                s_v[:, (kvb * 4 + h) * 129:(kvb * 4 + h) * 129 + 129],
                                start=(kvb == 0), stop=(kvb == nkv - 1)),
                                waits=w, inc=(j == 3))
                        av_done[kvb] = P.cnt["tensor"]
                        av_free[kvb % 2] = av_done[kvb]
                yh = qc % 2
                for j in range(4):
                    P.emit("vector", lambda e, j=j: e.reciprocal(
                        s_r[:, j:j + 1], PO[j][:, 128:129]),
                        waits=[("tensor", av_done[nkv - 1]), (f"yst{yh}", yo_free[yh])])
                    po_free[j] = P.emit("vector", lambda e, j=j, yh=yh: e.tensor_scalar_mul(
                        s_yo[:, yh * 512 + j * 128: yh * 512 + (j + 1) * 128],
                        PO[j][:, 0:128], s_r[:, j:j + 1]))
                dst = y_tm[qc * 512:(qc + 1) * 512, h * 128:(h + 1) * 128].rearrange(
                    "(j p) c -> p j c", p=128)
                srcv = s_yo[:, yh * 512:(yh + 1) * 512].rearrange("p (j c) -> p j c", c=128)
                yo_free[yh] = P.dma_q("dsync", dst, srcv,
                                      waits=[("vector", po_free[3])], sem=f"yst{yh}")
        P.barrier()


def emit_E2(nc, P, t):
    ytmg = t["ytmg"]
    wpr_pk = t["wpr_pk"]
    x2T = t["x2T"]
    ident_bf = t["ident_bf"]
    x3T = t["x3T"]

    with ExitStack() as stack:
        def sb(name, shape, dt=F32):
            return stack.enter_context(nc.sbuf_tensor(_uniq(name), shape, dt))
        s_tok = sb("s_tok", [128, 2 * D], BF16)
        s_y = sb("s_y", [128, 16 * 512], BF16)
        s_wb = sb("s_wb", [128, 4 * 2048], BF16)
        s_id = sb("s_id", [128, 128], BF16)
        s_xr = sb("s_xr", [128, 16 * 512])
        s_s = sb("s_s", [128, 16 * 512])
        s_bf = sb("s_bf", [128, 16 * 512], BF16)
        s_st = sb("s_st", [1, 8 * 512])
        s_onc = sb("s_onc", [128, 1], BF16)
        s_onr = sb("s_onr", [1, 128])
        p = stack.enter_context(nc.psum_tensor(_uniq("p"), [128, 3072], F32))
        PM = [p[:, 0:512], p[:, 512:1024]]
        p_s = p[0:1, 1024:2048]
        p_rep = p[:, 2048:3072]
        p_bf = stack.enter_context(nc.psum_tensor(_uniq("p_bf"), [128, 2048], BF16))
        PTP = [p_bf[:, 0:128], p_bf[:, 1024:1152]]
        ser = P
        ser.dmas(nc.sync, [(s_id[:], ident_bf)])
        ser.step(nc.vector, lambda e: e.memset(s_onc[:, 0:1], 1.0))
        ser.step(nc.vector, lambda e: e.memset(s_onr[0:1, :], 1.0))
        P.barrier()

        NLD = 16
        NTC = TC // 512
        li = [0]
        ld_done = [0] * (NLD * NTC)
        ld_cnt = [0] * (NLD * NTC)

        def issue_loads(upto):
            while li[0] < min(upto, NLD * NTC):
                i = li[0]
                mt = i % NLD
                slot = i % 4
                srcv = wpr_pk[mt * 16 * 128:(mt + 1) * 16 * 128, :].rearrange(
                    "(kt p) f -> p kt f", p=128)
                dstv = s_wb[:, slot * 2048: slot * 2048 + 2048].rearrange(
                    "p (kt f) -> p kt f", f=128)
                w = []
                if i >= 4:
                    w = [("tensor", ld_done[i - 4])]
                qn = "dgp" if (i % 2 == 0) else "dsync"
                ld_cnt[i] = P.dma_q(qn, dstv, srcv, waits=w, sem=f"slw{i % 4}")
                li[0] += 1

        tp_done = [0, 0]
        ev_s = [0, 0]
        xld_cnt = [0, 0]
        xdma_free = [0, 0]
        ev_cnt = [0, 0]
        for tc in range(NTC):
            t0 = tc * 512
            ser.dmas(nc.sync, [
                (s_xr[:, k * 512:(k + 1) * 512], x2T[k * 128:(k + 1) * 128, t0:t0 + 512])
                for k in range(16)])
            for tt in range(4):
                tb = tt % 2
                for r in range(4):
                    base = r * 2048 + t0 + tt * 128
                    xld_cnt[tb] = P.dma_q(
                        "dscal", s_tok[:, tb * D + r * 512: tb * D + (r + 1) * 512],
                        ytmg[base: base + 128, :],
                        waits=([("tensor", xdma_free[tb])] if r == 0 else []),
                        sem=f"xld{tb}")
                for k in range(16):
                    j = k % 2
                    w = [(f"xld{tb}", xld_cnt[tb]), ("scalar", ev_s[j])]
                    tp_done[j] = P.emit("tensor", lambda e, k=k, tb=tb, j=j: e.transpose(
                        PTP[j], s_tok[:, tb * D + k * 128: tb * D + (k + 1) * 128],
                        s_id[:]), waits=w)
                    ev_s[j] = P.emit("scalar", lambda e, k=k, tt=tt, j=j: e.copy(
                        s_y[:, k * 512 + tt * 128: k * 512 + (tt + 1) * 128], PTP[j]),
                        waits=[("tensor", tp_done[j])])
                xdma_free[tb] = P.cnt["tensor"]
            yev = P.cnt["scalar"]
            for mt in range(16):
                gi = tc * NLD + mt
                issue_loads(gi + 4)
                slot = gi % 4
                par = mt % 2
                for k in range(16):
                    w = []
                    if k == 0:
                        w = [(f"slw{gi % 4}", ld_cnt[gi]), ("vector", ev_cnt[par]),
                             ("scalar", yev)]
                    P.emit("tensor", lambda e, mt=mt, k=k, slot=slot, par=par: e.matmul(
                        PM[par], s_wb[:, slot * 2048 + k * 128: slot * 2048 + (k + 1) * 128],
                        s_y[:, k * 512:(k + 1) * 512],
                        start=(k == 0), stop=(k == 15)),
                        waits=w, inc=(k == 15))
                tm = P.cnt["tensor"]
                ld_done[gi] = tm
                ev_cnt[par] = P.emit("vector", lambda e, mt=mt, par=par: e.tensor_add(
                    s_s[:, mt * 512:(mt + 1) * 512], PM[par],
                    s_xr[:, mt * 512:(mt + 1) * 512]),
                    waits=[("tensor", tm)])
            P.barrier()
            emit_ln(ser, nc, s_s, s_bf, s_st, s_onc, s_onr, p_s, p_rep, s_xr, LN_EPS)
            ser.dmas(nc.sync, [
                (x3T[k * 128:(k + 1) * 128, t0:t0 + 512],
                 s_xr[:, k * 512:(k + 1) * 512]) for k in range(16)])
            P.barrier()
        P.barrier()


def emit_F(nc, P, t, pipe_mode=True):
    x3T = t["x3T"]
    rmsw = t["rmsw"]
    rt_w = t["rt_w"]
    wg_pk = t["wg_pk"]
    wu_pk = t["wu_pk"]
    wd_pk = t["wd_pk"]
    sel8 = t["sel8"]
    ident = t["ident"]
    out_tm = t["out_tm"]

    with ExitStack() as stack:
        def sb(name, shape, dt=F32):
            return stack.enter_context(nc.sbuf_tensor(_uniq(name), shape, dt))
        s_x3 = sb("s_x3", [128, 16 * 512])
        s_ht = sb("s_ht", [128, 16 * 512], BF16)
        s_rw = sb("s_rw", [128, 16])
        s_rt = sb("s_rt", [128, 16 * 8], BF16)
        s_sel = sb("s_sel", [8, 1024])
        s_id = sb("s_id", [128, 128])
        s_weT = sb("s_weT", [8, 512])
        s_we = sb("s_we", [128, 2 * 512])
        s_sg = sb("s_sg", [128, 2 * 512])
        s_uw = sb("s_uw", [128, 2 * 512])
        s_z = sb("s_z", [128, 2 * 8 * 512], BF16)
        s_acc = sb("s_acc", [128, 16 * 512])
        s_bf = sb("s_bf", [128, 16 * 512], BF16)
        s_st = sb("s_st", [1, 8 * 512])
        s_onc = sb("s_onc", [128, 1], BF16)
        s_onr = sb("s_onr", [1, 128])
        s_lg = sb("s_lg", [128, 48])
        s_wb = sb("s_wb", [128, 4 * 2048], BF16)
        s_tm = sb("s_tm", [128, 16 * 128])
        p = stack.enter_context(nc.psum_tensor(_uniq("p"), [128, 4096], F32))
        PG = [p[:, 0:512], p[:, 512:1024]]
        PU = [p[:, 1024:1536], p[:, 1536:2048]]
        PD = [p[:, 2048:2560], p[:, 2560:3072]]
        PWE = p[:, 3072:3584]
        p_rep = p[:, 3072:4096]
        p_s = p[0:1, 2048:3072]
        p_r = p[:, 0:8]
        p_tp = p[:, 0:128]
        ser = P

        ser.dmas(nc.sync, [(s_rw[:], rmsw), (s_sel[:], sel8), (s_id[:], ident)])
        ser.dmas(nc.gpsimd, [
            (s_rt[:, k * 8:(k + 1) * 8], rt_w[k * 128:(k + 1) * 128, :])
            for k in range(16)])
        ser.step(nc.vector, lambda e: e.memset(s_onc[:, 0:1], 1.0))
        ser.step(nc.vector, lambda e: e.memset(s_onr[0:1, :], 1.0))

        loads = []
        for e_ in range(9):
            for ft in range(8):
                loads.append((wg_pk, e_ * 8 + ft, 16))
                loads.append((wu_pk, e_ * 8 + ft, 16))
            for dt_ in range(16):
                loads.append((wd_pk, e_ * 16 + dt_, 8))
        NLD = len(loads)
        NTC = TC // 512
        li = [0]
        ld_done = [0] * (NLD * NTC)
        ld_cnt = [0] * (NLD * NTC)

        def W_(w):
            return w if pipe_mode else P.snap()

        def issue_loads(upto):
            while li[0] < min(upto, NLD * NTC):
                i = li[0]
                pk, mt, Kt = loads[i % NLD]
                slot = i % 4
                srcv = pk[mt * Kt * 128:(mt + 1) * Kt * 128, :].rearrange(
                    "(kt p) f -> p kt f", p=128)
                dstv = s_wb[:, slot * 2048: slot * 2048 + Kt * 128].rearrange(
                    "p (kt f) -> p kt f", f=128)
                w = []
                if i >= 4:
                    w = [("tensor", ld_done[i - 4])]
                qn = "dgp" if (i % 2 == 0) else "dsync"
                ld_cnt[i] = P.dma_q(qn, dstv, srcv, waits=W_(w), sem=f"slw{i % 4}")
                li[0] += 1

        for tc in range(NTC):
            t0 = tc * 512
            ser.dmas(nc.sync, [
                (s_x3[:, k * 512:(k + 1) * 512], x3T[k * 128:(k + 1) * 128, t0:t0 + 512])
                for k in range(16)])
            for k in range(16):
                ser.step(nc.scalar, lambda e, k=k: e.activation(
                    s_bf[:, k * 512:(k + 1) * 512], s_x3[:, k * 512:(k + 1) * 512], AF.Square))
            for k in range(16):
                ser.step(nc.tensor, lambda e, k=k: e.matmul(
                    p_s[0:1, 0:512], s_onc[:, 0:1], s_bf[:, k * 512:(k + 1) * 512],
                    start=(k == 0), stop=(k == 15)))
            ms = s_st[0:1, 0:512]
            ser.step(nc.vector, lambda e: e.tensor_scalar_mul(ms, p_s[0:1, 0:512], 1.0 / 2048))
            ser.step(nc.vector, lambda e: e.tensor_scalar_add(ms, ms, RMS_EPS))
            ser.step(nc.scalar, lambda e: e.activation(ms, ms, AF.Ln))
            ser.step(nc.vector, lambda e: e.tensor_scalar_mul(ms, ms, -0.5))
            ser.step(nc.scalar, lambda e: e.activation(ms, ms, AF.Exp))
            ser.step(nc.tensor, lambda e: e.matmul(
                p_rep[:, 0:512], s_onr[0:1, :], ms, start=True, stop=True))
            for k in range(16):
                ser.step(nc.vector, lambda e, k=k: e.scalar_tensor_tensor(
                    s_ht[:, k * 512:(k + 1) * 512], s_x3[:, k * 512:(k + 1) * 512],
                    s_rw[:, k:k + 1], p_rep[:, 0:512], ALU.mult, ALU.mult))
            for tt in range(4):
                for k in range(16):
                    ser.step(nc.tensor, lambda e, k=k, tt=tt: e.matmul(
                        p_r, s_ht[:, k * 512 + tt * 128: k * 512 + (tt + 1) * 128],
                        s_rt[:, k * 8:(k + 1) * 8],
                        start=(k == 0), stop=(k == 15)))
                lg = s_lg[:, 0:8]; pe = s_lg[:, 8:16]; m1 = s_lg[:, 16:17]
                sm = s_lg[:, 17:18]; m2 = s_lg[:, 18:19]; kp = s_lg[:, 24:32]
                wi = s_lg[:, 19:20]; we8 = s_lg[:, 32:40]; nm1 = s_lg[:, 20:21]
                ser.step(nc.vector, lambda e: e.tensor_copy(lg, p_r))
                ser.step(nc.vector, lambda e: e.reduce_max(m1, lg, mybir.AxisListType.X))
                ser.step(nc.vector, lambda e: e.tensor_scalar_mul(nm1, m1, -1.0))
                ser.step(nc.scalar, lambda e: e.activation(pe, lg, AF.Exp, bias=nm1[:, 0:1]))
                ser.step(nc.vector, lambda e: e.reduce_sum(sm, pe, mybir.AxisListType.X))
                ser.step(nc.vector, lambda e: e.reciprocal(wi, sm))
                ser.step(nc.vector, lambda e: e.tensor_scalar_mul(pe, pe, wi[:, 0:1]))
                ser.step(nc.vector, lambda e: e.reduce_max(m1, pe, mybir.AxisListType.X))
                ser.step(nc.vector, lambda e: e.tensor_single_scalar(
                    kp, pe, m1[:, 0:1], ALU.is_equal))
                ser.step(nc.vector, lambda e: e.scalar_tensor_tensor(
                    we8, kp, -1e30, pe, ALU.mult, ALU.add))
                ser.step(nc.vector, lambda e: e.reduce_max(m2, we8, mybir.AxisListType.X))
                ser.step(nc.vector, lambda e: e.tensor_single_scalar(
                    kp, pe, m2[:, 0:1], ALU.is_ge))
                ser.step(nc.vector, lambda e: e.tensor_add(sm, m1, m2))
                ser.step(nc.vector, lambda e: e.reciprocal(wi, sm))
                ser.step(nc.vector, lambda e: e.tensor_mul(we8, pe, kp))
                ser.step(nc.vector, lambda e: e.tensor_scalar_mul(we8, we8, wi[:, 0:1]))
                ser.step(nc.tensor, lambda e: e.transpose(p_tp[0:8, 0:128], we8, s_id[:]))
                ser.step(nc.vector, lambda e, tt=tt: e.tensor_copy(
                    s_weT[0:8, tt * 128:(tt + 1) * 128], p_tp[0:8, 0:128]))
            P.barrier()

            silu_cnt = [0, 0]
            sg_free = [0, 0]
            uw_cnt = [0, 0]
            z_done = [0] * 9
            acc_cnt = [0, 0]
            we_evac = [0] * 9
            uw_last_e = [0] * 9
            gi = tc * NLD
            for e_ in range(9):
                zh = (e_ % 2) * 8 * 512
                if e_ < 8:
                    twe = P.emit("tensor", lambda e, e_=e_: e.matmul(
                        PWE, s_sel[:, e_ * 128:(e_ + 1) * 128], s_weT[0:8, :],
                        start=True, stop=True),
                        waits=W_([("scalar", we_evac[e_ - 1] if e_ else 0)]))
                    we_evac[e_] = P.emit("scalar", lambda e, e_=e_: e.copy(
                        s_we[:, (e_ % 2) * 512:(e_ % 2) * 512 + 512], PWE),
                        waits=W_([("tensor", twe),
                               ("vector", uw_last_e[e_ - 2] if e_ >= 2 else 0)]))
                for ft in range(8):
                    par = ft % 2
                    issue_loads(gi + 4)
                    slot = gi % 4
                    for k in range(16):
                        w = []
                        if k == 0:
                            w = [(f"slw{gi % 4}", ld_cnt[gi]), ("scalar", silu_cnt[par])]
                        P.emit("tensor", lambda e, k=k, slot=slot, par=par: e.matmul(
                            PG[par], s_wb[:, slot * 2048 + k * 128: slot * 2048 + (k + 1) * 128],
                            s_ht[:, k * 512:(k + 1) * 512],
                            start=(k == 0), stop=(k == 15)),
                            waits=W_(w), inc=(k == 15))
                    tg = P.cnt["tensor"]
                    ld_done[gi] = tg
                    gi += 1
                    issue_loads(gi + 4)
                    slot = gi % 4
                    for k in range(16):
                        w = []
                        if k == 0:
                            w = [(f"slw{gi % 4}", ld_cnt[gi]), ("vector", uw_cnt[par])]
                        P.emit("tensor", lambda e, k=k, slot=slot, par=par: e.matmul(
                            PU[par], s_wb[:, slot * 2048 + k * 128: slot * 2048 + (k + 1) * 128],
                            s_ht[:, k * 512:(k + 1) * 512],
                            start=(k == 0), stop=(k == 15)),
                            waits=W_(w), inc=(k == 15))
                    tu = P.cnt["tensor"]
                    ld_done[gi] = tu
                    gi += 1
                    silu_cnt[par] = P.emit("scalar", lambda e, par=par: e.activation(
                        s_sg[:, par * 512: par * 512 + 512], PG[par], AF.Silu),
                        waits=W_([("tensor", tg), ("vector", sg_free[par])]))
                    if e_ < 8:
                        uw_cnt[par] = P.emit("vector", lambda e, par=par: e.tensor_mul(
                            s_uw[:, par * 512: par * 512 + 512], PU[par],
                            s_we[:, (e_ % 2) * 512:(e_ % 2) * 512 + 512]),
                            waits=W_([("tensor", tu), ("scalar", we_evac[e_])]))
                        zc = P.emit("vector", lambda e, par=par, ft=ft, zh=zh: e.tensor_mul(
                            s_z[:, zh + ft * 512: zh + (ft + 1) * 512],
                            s_uw[:, par * 512: par * 512 + 512],
                            s_sg[:, par * 512: par * 512 + 512]),
                            waits=W_([("scalar", silu_cnt[par])]))
                    else:
                        zc = P.emit("vector", lambda e, par=par, ft=ft, zh=zh: e.tensor_mul(
                            s_z[:, zh + ft * 512: zh + (ft + 1) * 512],
                            PU[par], s_sg[:, par * 512: par * 512 + 512]),
                            waits=W_([("tensor", tu), ("scalar", silu_cnt[par])]))
                        uw_cnt[par] = zc
                    sg_free[par] = zc
                    uw_last_e[e_] = uw_cnt[par]
                    if ft == 7:
                        z_done[e_] = zc
                for dt_ in range(16):
                    par2 = dt_ % 2
                    issue_loads(gi + 4)
                    slot = gi % 4
                    for fk in range(8):
                        w = []
                        if fk == 0:
                            w = [(f"slw{gi % 4}", ld_cnt[gi]),
                                 ("vector", max(z_done[e_], acc_cnt[par2]))]
                        P.emit("tensor", lambda e, fk=fk, slot=slot, par2=par2, zh=zh: e.matmul(
                            PD[par2], s_wb[:, slot * 2048 + fk * 128: slot * 2048 + (fk + 1) * 128],
                            s_z[:, zh + fk * 512: zh + (fk + 1) * 512],
                            start=(fk == 0), stop=(fk == 7)),
                            waits=W_(w), inc=(fk == 7))
                    td = P.cnt["tensor"]
                    ld_done[gi] = td
                    gi += 1
                    if e_ == 0:
                        acc_cnt[par2] = P.emit("vector", lambda e, dt_=dt_, par2=par2: e.tensor_copy(
                            s_acc[:, dt_ * 512:(dt_ + 1) * 512], PD[par2]),
                            waits=W_([("tensor", td)]))
                    else:
                        acc_cnt[par2] = P.emit("vector", lambda e, dt_=dt_, par2=par2: e.tensor_add(
                            s_acc[:, dt_ * 512:(dt_ + 1) * 512],
                            s_acc[:, dt_ * 512:(dt_ + 1) * 512], PD[par2]),
                            waits=W_([("tensor", td)]))
            P.barrier()

            for k in range(16):
                ser.step(nc.vector, lambda e, k=k: e.tensor_add(
                    s_acc[:, k * 512:(k + 1) * 512], s_acc[:, k * 512:(k + 1) * 512],
                    s_x3[:, k * 512:(k + 1) * 512]))
            emit_ln(ser, nc, s_acc, s_bf, s_st, s_onc, s_onr, p_s, p_rep, s_x3, LN_EPS)
            for tt in range(4):
                for k in range(16):
                    ser.step(nc.tensor, lambda e, k=k, tt=tt: e.transpose(
                        p_tp[:, 0:128], s_x3[:, k * 512 + tt * 128: k * 512 + (tt + 1) * 128],
                        s_id[:]))
                    ser.step(nc.vector, lambda e, k=k: e.tensor_copy(
                        s_tm[:, k * 128:(k + 1) * 128], p_tp[:, 0:128]))
                ser.dma(nc.sync, out_tm[t0 + tt * 128: t0 + (tt + 1) * 128, :], s_tm[:])
        P.barrier()
        P.barrier()


FUSED_SEMS = ["slw0", "slw1", "slw2", "slw3", "xld0", "xld1", "yst",
              "ost0", "ost1", "yst0", "yst1", "cc"]


def build_fused():
    nc = bass.Bass("TRN2", target_bir_lowering=False)

    def EI(name, shape, dt=F32):
        return nc.dram_tensor(name, shape, dt, kind="ExternalInput").ap()

    xA = EI("xA", [T, D])
    xAs = EI("xAs", [T, 512])
    wdtT = EI("wdtT", [D, 512])
    dtb = EI("dtb", [128, 4])
    xpw = EI("xpw", [D, 64])
    xpb = EI("xpb", [64, 1])
    alog = EI("alog", [128, 64])
    ident = EI("ident", [128, 128])
    sel = EI("sel", [64, 4096], BF16)
    wupT_sl = EI("wupT_sl", [D, 1024])
    wgT_sl = EI("wgT_sl", [D, 1024])
    wdT_sl = EI("wdT_sl", [INTER, 256])
    h128 = EI("h128", [128, 128])
    cv_in = {name: EI(f"cv_{name}_in", [rows // 8, 128]) for name, rows in PACK_SECTIONS}
    otb = EI("otb", [128, 16])
    x_tm = EI("x_tm", [TC, D])
    su_ug = EI("su_ug", [128, 16])
    wsg = EI("wsg", [128, 64])
    wsu = EI("wsu", [128, 64])
    wsd = EI("wsd", [128, 16])
    maskw = EI("maskw", [128, 896], BF16)
    ident_bf = EI("ident_bf", [128, 128], BF16)
    rmsw = EI("rmsw", [128, 16])
    rt_w = EI("rt_w", [D, 8])
    sel8 = EI("sel8", [8, 1024])
    out_tm = nc.dram_tensor("out_tm", [TC, D], F32, kind="ExternalOutput").ap()

    up_t_loc = nc.dram_tensor("up_t_loc", [8 * 16 * 128, 128], BF16)
    g_t_loc = nc.dram_tensor("g_t_loc", [8 * 16 * 128, 128], BF16)
    dn_t_loc = nc.dram_tensor("dn_t_loc", [2 * 64 * 128, 128], BF16)
    cv_out = {name: nc.dram_tensor(f"cv_{name}_out", [rows // 8, 128], BF16)
              for name, rows in PACK_SECTIONS}
    wup_pk = nc.dram_tensor("wup_pk", [64 * 16 * 128, 128], BF16)
    wg_pk = nc.dram_tensor("wg_pk", [64 * 16 * 128, 128], BF16)
    wdn_pk = nc.dram_tensor("wdn_pk", [16 * 64 * 128, 128], BF16)
    pk = {name: nc.dram_tensor(f"pk_{name}", [rows, 128], BF16)
          for name, rows in PACK_SECTIONS}
    yTs = nc.dram_tensor("yTs", [2048, 2048], BF16)
    YG = nc.dram_tensor("YG", [8192, 2048], BF16)
    yloc = nc.dram_tensor("yloc", [2048, 2048], BF16)
    qloc = nc.dram_tensor("qloc", [6144, 2048], BF16)
    ytloc = nc.dram_tensor("ytloc", [8192, 512], BF16)
    x1T = nc.dram_tensor("x1T", [D, TC], F32)
    x2T = nc.dram_tensor("x2T", [D, TC], F32)
    x3T = nc.dram_tensor("x3T", [D, TC], F32)
    qkvT = nc.dram_tensor("qkvT", [3 * D, TC], BF16)
    QGs = [nc.dram_tensor(f"QG{s}", [4 * D, TC], BF16) for s in range(3)]
    ytm = nc.dram_tensor("ytm", [T, 512], BF16)
    YAG = nc.dram_tensor("YAG", [4 * T, 512], BF16)

    G8 = [list(range(8))]
    G4 = [[0, 1, 2, 3], [4, 5, 6, 7]]

    with ExitStack() as stack:
        P = Pipe(nc, stack)
        for n in FUSED_SEMS:
            P.new_sem(n, stack)

        def ccol(kind, groups, in_h, out_h):
            i = nc.gpsimd.collective_compute(
                kind, ALU.bypass, replica_groups=groups,
                ins=[in_h.ap().opt()], outs=[out_h.ap().opt()])
            i.then_inc(P.sem["cc"], 1)
            P.cnt["cc"] += 1

        def ccbar():
            for e2 in P.ENGS:
                P._wait(e2, getattr(nc, e2), [("cc", P.cnt["cc"])])

        tW = {"wupT_sl": wupT_sl, "wgT_sl": wgT_sl, "wdT_sl": wdT_sl, "h128": h128,
              "up_t": up_t_loc.ap(), "g_t": g_t_loc.ap(), "dn_t": dn_t_loc.ap()}
        for name, _ in PACK_SECTIONS:
            tW[f"cv_{name}_in"] = cv_in[name]
            tW[f"cv_{name}_out"] = cv_out[name].ap()
        emit_W(nc, P, tW)
        emit_A(nc, P, {"xA": xA, "xAs": xAs, "wdtT": wdtT, "dtb": dtb, "xpw": xpw,
                       "xpb": xpb, "alog": alog, "ident": ident, "sel": sel,
                       "yTs": yTs.ap()})
        P.barrier(["gpsimd"])
        ccol("AllGather", G8, up_t_loc, wup_pk)
        ccol("AllGather", G8, g_t_loc, wg_pk)
        ccol("AllGather", G8, dn_t_loc, wdn_pk)
        for name, _ in PACK_SECTIONS:
            ccol("AllGather", G8, cv_out[name], pk[name])
        ccol("AllGather", G4, yTs, YG)
        ccbar()
        d512 = (nc.sync.partition_id() % 4) * 512
        yg3 = YG.ap().rearrange("(rank x) c -> x rank c", x=2048)
        yl3 = yloc.ap().rearrange("(rank x) c -> x rank c", x=512)
        for i in range(2):
            P.dma_q("dsync", yl3[:, i * 2:(i + 1) * 2, :],
                    yg3[bass.ds(d512, 512), i * 2:(i + 1) * 2, :])

        emit_B(nc, P, {"yT": yloc.ap(), "wot_pk": pk["ot"].ap(), "otb": otb,
                       "x_tm": x_tm, "ident": ident, "x1T": x1T.ap()})
        emit_C(nc, P, {"x1T": x1T.ap(), "wup_pk": wup_pk.ap(), "wg_pk": wg_pk.ap(),
                       "wdn_pk": wdn_pk.ap(), "su_ug": su_ug, "wsg": wsg, "wsu": wsu,
                       "wsd": wsd, "x2T": x2T.ap()})
        def on_kind(kind, c0, c1):
            i = nc.gpsimd.collective_compute(
                "AllGather", ALU.bypass, replica_groups=G4,
                ins=[qkvT.ap()[kind * D:(kind + 1) * D, :].opt()],
                outs=[QGs[kind].ap().opt()])
            i.then_inc(P.sem["cc"], 1)
            P.cnt["cc"] += 1

        def on_kind_w(kind, c0, c1):
            P._wait("gpsimd", nc.gpsimd, [("ost0", c0), ("ost1", c1)])
            on_kind(kind, c0, c1)

        emit_D(nc, P, {"x2T": x2T.ap(), "wqkv_pk": pk["qkv"].ap(), "qkvT": qkvT.ap()},
               on_kind=on_kind_w)
        ccbar()
        ql3 = qloc.ap().rearrange("(rs x) c -> x rs c", x=512)
        for s in range(3):
            qg3 = QGs[s].ap().rearrange("(r x) c -> x r c", x=2048)
            P.dma_q("dsync", ql3[:, s::3, :], qg3[bass.ds(d512, 512), :, :])

        emit_E(nc, P, {"qkvg": qloc.ap(), "maskw": maskw, "ident_bf": ident_bf,
                       "y_tm": ytm.ap()})
        P.barrier(["gpsimd"])
        ccol("AllGather", G4, ytm, YAG)
        ccbar()
        d2048 = (nc.sync.partition_id() % 4) * 2048
        ya3 = YAG.ap().rearrange("(r x) c -> x r c", x=8192)
        yt3 = ytloc.ap().rearrange("(r x) c -> x r c", x=2048)
        for i in range(4):
            P.dma_q("dsync", yt3[:, i:i + 1, :],
                    ya3[bass.ds(d2048, 2048), i:i + 1, :])
        P.barrier()

        emit_E2(nc, P, {"ytmg": ytloc.ap(), "wpr_pk": pk["pr"].ap(), "x2T": x2T.ap(),
                        "ident_bf": ident_bf, "x3T": x3T.ap()})
        emit_F(nc, P, {"x3T": x3T.ap(), "rmsw": rmsw, "rt_w": rt_w,
                       "wg_pk": pk["g"].ap(), "wu_pk": pk["u"].ap(),
                       "wd_pk": pk["d"].ap(), "sel8": sel8, "ident": ident,
                       "out_tm": out_tm})
        P.barrier()
    return nc


def host_fused_inputs(inp):
    mapsW = host_inputs_W(inp)
    mapsA = host_inputs_A(inp)
    suug, wsg, wsu, wsd = host_scales_C(inp)
    x = np.asarray(inp["L0_attn_x"], np.float32)
    otb = np.ascontiguousarray(
        np.asarray(inp["L0_attn_out_proj_b"], np.float32).reshape(16, 128).T)
    rmsw = np.ascontiguousarray(
        np.asarray(inp["L1_ffn_rms_weight"], np.float32).reshape(16, 128).T)
    rtw = np.ascontiguousarray(np.asarray(inp["L1_ffn_router_w"], np.float32))
    sel8 = np.zeros((8, 1024), dtype=np.float32)
    for e in range(8):
        sel8[e, e * 128:(e + 1) * 128] = 1.0
    maskw = (np.arange(896)[None, :] - 384 >= np.arange(128)[:, None]).astype(BF)
    ident_bf = np.eye(128, dtype=np.float32).astype(BF)
    maps = []
    for c in range(8):
        b, q = c // 4, c % 4
        m = {}
        m.update(mapsW[c])
        m.update(mapsA[c])
        m.update({
            "otb": otb, "x_tm": np.ascontiguousarray(x[b][q * TC:(q + 1) * TC, :]),
            "su_ug": suug, "wsg": wsg, "wsu": wsu, "wsd": wsd,
            "maskw": maskw, "ident_bf": ident_bf,
            "rmsw": rmsw, "rt_w": rtw, "sel8": sel8,
        })
        maps.append(m)
    return maps


def kernel(**inputs):
    inp = {k: np.asarray(v) for k, v in inputs.items()}
    res = run_bass_kernel_spmd(build_fused(), host_fused_inputs(inp),
                               core_ids=list(range(8)))
    out = np.empty((B, T, D), dtype=np.float32)
    for c in range(8):
        b, q = c // 4, c % 4
        out[b, q * TC:(q + 1) * TC, :] = res.results[c]["out_tm"]
    return out


# version 11
# speedup vs baseline: 1.1339x; 1.0106x over previous
import numpy as np
import ml_dtypes
from contextlib import ExitStack
import concourse.bass as bass
import concourse.mybir as mybir
from concourse.bass_utils import run_bass_kernel_spmd

F32 = mybir.dt.float32
BF16 = mybir.dt.bfloat16
AF = mybir.ActivationFunctionType
ALU = mybir.AluOpType
BF = ml_dtypes.bfloat16

B, T, D = 2, 8192, 2048
NS = 16
INTER = 8192
NH, DH = 16, 128
NE, FE = 8, 1024
SF = 1024
TC = 2048
LN_EPS = 1e-5
RMS_EPS = 1e-6


_NAME_UNIQ = [0]


def _uniq(name):
    _NAME_UNIQ[0] += 1
    return f"{name}_{_NAME_UNIQ[0]}"


def hadamard(n):
    H = np.ones((1, 1), dtype=np.float32)
    while H.shape[0] < n:
        H = np.block([[H, H], [H, -H]])
    return H


class Ser:

    def __init__(self, nc, g, gd):
        self.nc = nc
        self.g = g
        self.gd = gd
        self.gc = 0
        self.gdc = 0

    def _wait(self, eng):
        if self.gc:
            eng.wait_ge(self.g, self.gc)
        if self.gdc:
            eng.wait_ge(self.gd, self.gdc)

    def step(self, eng, fn):
        self._wait(eng)
        fn(eng).then_inc(self.g, 1)
        self.gc += 1

    def steps(self, eng, fns):
        self._wait(eng)
        ins = None
        for fn in fns:
            ins = fn(eng)
        ins.then_inc(self.g, 1)
        self.gc += 1

    def dma(self, eng, out, in_):
        self._wait(eng)
        eng.dma_start(out=out, in_=in_).then_inc(self.gd, 16)
        self.gdc += 16

    def dmas(self, eng, pairs):
        self._wait(eng)
        for out, in_ in pairs:
            eng.dma_start(out=out, in_=in_).then_inc(self.gd, 16)
            self.gdc += 16

    def finish(self, eng):
        self._wait(eng)


class Pipe:

    ENGS = ("tensor", "vector", "scalar", "gpsimd", "sync")

    def __init__(self, nc, stack):
        self.nc = nc
        self.sem = {}
        self.cnt = {}
        self.seen = {e: {} for e in self.ENGS}
        for e in self.ENGS:
            self.sem[e] = stack.enter_context(nc.semaphore("sem_" + e))
            self.cnt[e] = 0
        for q in ("dsync", "dgp", "dscal"):
            self.sem[q] = stack.enter_context(nc.semaphore("sem_" + q))
            self.cnt[q] = 0

    def _wait(self, ename, eng, waits):
        kn = self.seen[ename]
        for s, v in waits:
            if s == ename or v <= 0:
                continue
            if kn.get(s, 0) >= v:
                continue
            eng.wait_ge(self.sem[s], v)
            kn[s] = v

    SELFSYNC = {"vector": True, "scalar": True, "gpsimd": True,
                "tensor": False, "sync": False}

    def emit(self, ename, fn, waits=(), inc=True, selfsync=None):
        eng = getattr(self.nc, ename)
        if selfsync is None:
            selfsync = self.SELFSYNC[ename]
        if selfsync and self.cnt[ename] > self.seen[ename].get(ename, 0):
            eng.wait_ge(self.sem[ename], self.cnt[ename])
            self.seen[ename][ename] = self.cnt[ename]
        self._wait(ename, eng, waits)
        ins = fn(eng)
        if inc:
            ins.then_inc(self.sem[ename], 1)
            self.cnt[ename] += 1
        return self.cnt[ename]

    def dma_q(self, qname, out, in_, waits=(), inc=True, sem=None):
        ename = {"dsync": "sync", "dgp": "gpsimd", "dscal": "scalar"}[qname]
        eng = getattr(self.nc, ename)
        self._wait(ename, eng, waits)
        ins = eng.dma_start(out=out, in_=in_)
        key = sem if sem is not None else qname
        if inc:
            ins.then_inc(self.sem[key], 16)
            self.cnt[key] += 16
        return self.cnt[key]

    def new_sem(self, name, stack):
        self.sem[name] = stack.enter_context(self.nc.semaphore("sem_" + name))
        self.cnt[name] = 0

    def snap(self):
        return [(k, v) for k, v in self.cnt.items() if v > 0]

    def lazy(self, ename, fn):
        return self.emit(ename, fn, waits=self.snap())

    def ldma(self, qname, out, in_):
        return self.dma_q(qname, out, in_, waits=self.snap())

    def barrier(self, enames=None):
        w = self.snap()
        for ename in (enames or self.ENGS):
            eng = getattr(self.nc, ename)
            self._wait(ename, eng, w)

    def _name_of(self, eng):
        if not hasattr(self, "_byid"):
            self._byid = {id(getattr(self.nc, e)): e for e in self.ENGS}
        return self._byid[id(eng)]

    def step(self, eng, fn):
        return self.lazy(self._name_of(eng), fn)

    def steps(self, eng, fns):
        ename = self._name_of(eng)
        w = self.snap()
        e = getattr(self.nc, ename)
        if self.SELFSYNC[ename] and self.cnt[ename] > self.seen[ename].get(ename, 0):
            e.wait_ge(self.sem[ename], self.cnt[ename])
            self.seen[ename][ename] = self.cnt[ename]
        self._wait(ename, e, w)
        ins = None
        for fn in fns:
            ins = fn(e)
        ins.then_inc(self.sem[ename], 1)
        self.cnt[ename] += 1
        return self.cnt[ename]

    _QOF = {"sync": "dsync", "gpsimd": "dgp", "scalar": "dscal"}

    def dma(self, eng, out, in_):
        return self.dma_q(self._QOF[self._name_of(eng)], out, in_, waits=self.snap())

    def dmas(self, eng, pairs):
        ename = self._name_of(eng)
        qname = self._QOF[ename]
        e = getattr(self.nc, ename)
        self._wait(ename, e, self.snap())
        for out, in_ in pairs:
            e.dma_start(out=out, in_=in_).then_inc(self.sem[qname], 16)
            self.cnt[qname] += 16
        return self.cnt[qname]

    def finish(self, eng):
        self._wait(self._name_of(eng), eng, self.snap())


ACH = 1024


def emit_A(nc, P, t):
    xA = t["xA"]
    xAs = t["xAs"]
    wdtT = t["wdtT"]
    dtb = t["dtb"]
    xpw = t["xpw"]
    xpb = t["xpb"]
    alog = t["alog"]
    ident = t["ident"]
    sel = t["sel"]
    yTs = t["yTs"]

    nch = T // ACH
    with ExitStack() as stack:
        def sb(name, shape, dt=F32):
            return stack.enter_context(nc.sbuf_tensor(_uniq(name), shape, dt))
        s_xtok = sb("s_xtok", [128, 2 * D])
        s_xT = sb("s_xT", [128, 16 * ACH], BF16)
        s_xs = sb("s_xs", [128, 4 * ACH], BF16)
        s_wdt = sb("s_wdt", [128, 16 * 512], BF16)
        s_xpw = sb("s_xpw", [128, 16 * 64], BF16)
        s_sel = sb("s_sel", [64, 4096], BF16)
        s_id = sb("s_id", [128, 128])
        s_dtb = sb("s_dtb", [128, 4])
        s_xpb = sb("s_xpb", [64, 1])
        s_A = sb("s_A", [128, 64])
        s_dtT = sb("s_dtT", [128, 2 * 4 * ACH])
        s_dtx = sb("s_dtx", [128, 2 * 4 * ACH])
        s_bc = sb("s_bc", [64, 2 * ACH], BF16)
        s_dA = sb("s_dA", [128, 2 * ACH])
        s_spt = sb("s_spt", [128, 512])
        s_h = sb("s_h", [128, 2 * ACH])
        s_tmp = sb("s_tmp", [128, 2 * ACH])
        s_u = sb("s_u", [128, ACH])
        s_y = sb("s_y", [128, 4 * ACH])
        s_ybf = sb("s_ybf", [128, 4 * ACH], BF16)
        s_hst = sb("s_hst", [128, 64])
        p = stack.enter_context(nc.psum_tensor(_uniq("p"), [128, 4096], F32))
        PTP = [p[:, 0:128], p[:, 512:640]]
        PDT = [p[:, 1024:1536], p[:, 1536:2048]]
        PB = p[:, 2048:3072]
        PBC = p[0:64, 1024:1536]
        PC = p[:, 3072:4096]
        ser = P

        ser.dmas(nc.sync, [
            (s_sel[:], sel), (s_id[:], ident), (s_dtb[:], dtb), (s_xpb[:], xpb),
        ])
        ser.dma(nc.sync, s_A[:], alog)
        ser.step(nc.scalar, lambda e: e.activation(s_A[:], s_A[:], AF.Exp))
        ser.step(nc.vector, lambda e: e.tensor_scalar_mul(s_A[:], s_A[:], -1.0))
        ser.dmas(nc.gpsimd, [
            (s_wdt[:, (k * 4 + et) * 128:(k * 4 + et + 1) * 128],
             wdtT[k * 128:(k + 1) * 128, et * 128:(et + 1) * 128])
            for k in range(16) for et in range(4)
        ])
        ser.dmas(nc.gpsimd, [
            (s_xpw[:, k * 64:(k + 1) * 64], xpw[k * 128:(k + 1) * 128, :])
            for k in range(16)
        ])
        P.barrier()

        tp_done = [0, 0]
        ev_s = [0, 0]
        ev_v = [0, 0]
        xld_cnt = [0, 0]
        xdma_free = [0, 0]
        sp_done = [0, 0]
        dtx_free = [0, 0]
        bc_ev = [0, 0]
        dA_cnt = [0, 0]
        dA_free = [0, 0]
        hst_done = [0] * 64
        hstc_free = [0, 0]
        yst_cnt = [0]
        chk = [dict() for _ in range(nch)]

        def prep_units(ch):
            t0 = ch * ACH
            xw = ch % 2
            units = []

            def u_xT(tt):
                tb = tt % 2
                xld_cnt[tb] = P.dma_q(
                    "dsync", s_xtok[:, tb * D: tb * D + D],
                    xA[t0 + tt * 128: t0 + (tt + 1) * 128, :],
                    waits=[("tensor", xdma_free[tb])], sem=f"xld{tb}")
                for k in range(16):
                    j = k % 2
                    w = [(f"xld{tb}", xld_cnt[tb]), ("scalar", ev_s[j]), ("vector", ev_v[j])]
                    tp_done[j] = P.emit("tensor", lambda e, k=k, tb=tb, j=j: e.transpose(
                        PTP[j], s_xtok[:, tb * D + k * 128: tb * D + (k + 1) * 128],
                        s_id[:]), waits=w)
                    ev_s[j] = P.emit("scalar", lambda e, k=k, tt=tt, j=j: e.copy(
                        s_xT[:, k * ACH + tt * 128: k * ACH + (tt + 1) * 128], PTP[j]),
                        waits=[("tensor", tp_done[j])])
                xdma_free[tb] = P.cnt["tensor"]

            def u_xs(tt):
                tb = tt % 2
                xld_cnt[tb] = P.dma_q(
                    "dsync", s_xtok[:, tb * D: tb * D + 512],
                    xAs[t0 + tt * 128: t0 + (tt + 1) * 128, :],
                    waits=[("tensor", xdma_free[tb])], sem=f"xld{tb}")
                for dt4 in range(4):
                    j = dt4 % 2
                    w = [(f"xld{tb}", xld_cnt[tb]), ("scalar", ev_s[j]), ("vector", ev_v[j])]
                    tp_done[j] = P.emit("tensor", lambda e, dt4=dt4, tb=tb, j=j: e.transpose(
                        PTP[j], s_xtok[:, tb * D + dt4 * 128: tb * D + (dt4 + 1) * 128],
                        s_id[:]), waits=w)
                    ev_v[j] = P.emit("vector", lambda e, dt4=dt4, tt=tt, j=j: e.tensor_copy(
                        s_xs[:, dt4 * ACH + tt * 128: dt4 * ACH + (tt + 1) * 128], PTP[j]),
                        waits=[("tensor", tp_done[j])])
                xdma_free[tb] = P.cnt["tensor"]

            def u_dt(g):
                et, tc2 = g // 2, g % 2
                jd = g % 2
                if g == 0:
                    chk[ch]["xTev"] = P.cnt["scalar"]
                for k in range(16):
                    w = []
                    if k == 0:
                        w = [("scalar", max(chk[ch]["xTev"], sp_done[jd])),
                             ("vector", max(bc_ev[0], bc_ev[1]))]
                    P.emit("tensor", lambda e, et=et, tc2=tc2, k=k, jd=jd: e.matmul(
                        PDT[jd],
                        s_wdt[:, (k * 4 + et) * 128:(k * 4 + et + 1) * 128],
                        s_xT[:, k * ACH + tc2 * 512: k * ACH + (tc2 + 1) * 512],
                        start=(k == 0), stop=(k == 15)), waits=w, inc=(k == 15))
                tdt = P.cnt["tensor"]
                P.emit("scalar", lambda e, et=et, jd=jd: e.activation(
                    s_spt[:, 0:512], PDT[jd], AF.Exp, bias=s_dtb[:, et:et + 1]),
                    waits=[("tensor", tdt), ("vector", dtx_free[xw])])
                sp_done[jd] = P.emit("scalar", lambda e, et=et, tc2=tc2, xw=xw: e.activation(
                    s_dtT[:, xw * 4 * ACH + et * ACH + tc2 * 512:
                          xw * 4 * ACH + et * ACH + (tc2 + 1) * 512],
                    s_spt[:, 0:512], AF.Ln, bias=1.0))
                if g == 7:
                    chk[ch]["sp"] = P.cnt["scalar"]

            def u_bc(tc2):
                for k in range(16):
                    w = []
                    if k == 0:
                        w = [("scalar", max(sp_done[0], sp_done[1])),
                             ("vector", max(bc_ev[0], bc_ev[1]))]
                    P.emit("tensor", lambda e, tc2=tc2, k=k: e.matmul(
                        PBC[:, 0:512], s_xpw[:, k * 64:(k + 1) * 64],
                        s_xT[:, k * ACH + tc2 * 512: k * ACH + (tc2 + 1) * 512],
                        start=(k == 0), stop=(k == 15)), waits=w, inc=(k == 15))
                tbc = P.cnt["tensor"]
                bc_ev[xw] = P.emit("vector", lambda e, tc2=tc2, xw=xw: e.tensor_scalar_add(
                    s_bc[:, xw * ACH + tc2 * 512: xw * ACH + (tc2 + 1) * 512],
                    PBC[:, 0:512], s_xpb[:, 0:1]), waits=[("tensor", tbc)])
                if tc2 == 1:
                    chk[ch]["bc"] = P.cnt["vector"]

            for tt in range(8):
                units.append(lambda tt=tt: u_xT(tt))
            for tt in range(8):
                units.append(lambda tt=tt: u_xs(tt))
            for g in range(8):
                units.append(lambda g=g: u_dt(g))
            for tc2 in range(2):
                units.append(lambda tc2=tc2: u_bc(tc2))
            return units

        def emit_scan(ch, next_units):
            t0 = ch * ACH
            xw = ch % 2
            for dt4 in range(4):
                dtx_free[xw] = P.emit("vector", lambda e, dt4=dt4, xw=xw: e.tensor_mul(
                    s_dtx[:, xw * 4 * ACH + dt4 * ACH: xw * 4 * ACH + (dt4 + 1) * ACH],
                    s_dtT[:, xw * 4 * ACH + dt4 * ACH: xw * 4 * ACH + (dt4 + 1) * ACH],
                    s_xs[:, dt4 * ACH:(dt4 + 1) * ACH]),
                    waits=[("scalar", chk[ch]["sp"])])
            ui = [0]
            tmp_free = [0, 0]
            yv_last = [0, 0, 0, 0]
            ya_last = 0

            def run_units(kmax):
                while ui[0] < min(kmax, len(next_units)):
                    next_units[ui[0]]()
                    ui[0] += 1

            u_last = 0
            ym_last = 0
            for n in range(NS):
                for tc2 in range(ACH // 512):
                    w = [("vector", max(u_last, chk[ch]["bc"]))] if tc2 == 0 else []
                    P.emit("tensor", lambda e, n=n, tc2=tc2, xw=xw: e.matmul(
                        PB[:, tc2 * 512:(tc2 + 1) * 512],
                        s_sel[:, n * 128:(n + 1) * 128],
                        s_bc[:, xw * ACH + tc2 * 512: xw * ACH + (tc2 + 1) * 512],
                        start=True, stop=True), waits=w, inc=(tc2 == 1))
                repB = P.cnt["tensor"]
                for tc2 in range(ACH // 512):
                    w = [("vector", ym_last)] if tc2 == 0 else []
                    P.emit("tensor", lambda e, n=n, tc2=tc2, xw=xw: e.matmul(
                        PC[:, tc2 * 512:(tc2 + 1) * 512],
                        s_sel[:, (16 + n) * 128:(17 + n) * 128],
                        s_bc[:, xw * ACH + tc2 * 512: xw * ACH + (tc2 + 1) * 512],
                        start=True, stop=True), waits=w, inc=(tc2 == 1))
                repC = P.cnt["tensor"]
                for dt4 in range(4):
                    col = dt4 * 16 + n
                    j2 = (n * 4 + dt4) % 2
                    dA_cnt[j2] = P.emit("scalar", lambda e, dt4=dt4, col=col, j2=j2, xw=xw: e.activation(
                        s_dA[:, j2 * ACH:(j2 + 1) * ACH],
                        s_dtT[:, xw * 4 * ACH + dt4 * ACH: xw * 4 * ACH + (dt4 + 1) * ACH],
                        AF.Exp, scale=s_A[:, col:col + 1]),
                        waits=[("vector", dA_free[j2]), ("scalar", chk[ch]["sp"])])
                    u_last = P.emit("vector", lambda e, dt4=dt4, xw=xw: e.tensor_mul(
                        s_u[:], s_dtx[:, xw * 4 * ACH + dt4 * ACH: xw * 4 * ACH + (dt4 + 1) * ACH],
                        PB[:]), waits=[("tensor", repB)])
                    if ch == 0:
                        scv = P.emit("vector", lambda e, j2=j2: e.tensor_tensor_scan(
                            s_h[:, j2 * ACH:(j2 + 1) * ACH], s_dA[:, j2 * ACH:(j2 + 1) * ACH],
                            s_u[:], 0.0, ALU.mult, ALU.add),
                            waits=[("scalar", dA_cnt[j2]), ("gpsimd", hstc_free[j2])])
                    else:
                        scv = P.emit("vector", lambda e, col=col, j2=j2: e.tensor_tensor_scan(
                            s_h[:, j2 * ACH:(j2 + 1) * ACH], s_dA[:, j2 * ACH:(j2 + 1) * ACH],
                            s_u[:], s_hst[:, col:col + 1], ALU.mult, ALU.add),
                            waits=[("scalar", dA_cnt[j2]), ("gpsimd", hst_done[col]),
                                   ("gpsimd", hstc_free[j2])])
                    dA_free[j2] = scv
                    hst_done[col] = P.emit("gpsimd", lambda e, col=col, j2=j2: e.tensor_copy(
                        s_hst[:, col:col + 1], s_h[:, (j2 + 1) * ACH - 1:(j2 + 1) * ACH]),
                        waits=[("vector", scv)])
                    hstc_free[j2] = hst_done[col]
                    if n == 0:
                        w = [("tensor", repC)]
                        if ch > 0 and dt4 == 0:
                            w.append(("yst", yst_cnt[0]))
                        ym_last = P.emit("vector", lambda e, dt4=dt4, j2=j2: e.tensor_mul(
                            s_y[:, dt4 * ACH:(dt4 + 1) * ACH],
                            s_h[:, j2 * ACH:(j2 + 1) * ACH], PC[:]), waits=w)
                        ya_last = 0
                        yv_last[dt4] = ym_last
                    else:
                        ym_last = P.emit("vector", lambda e, j2=j2: e.tensor_mul(
                            s_tmp[:, j2 * ACH:(j2 + 1) * ACH],
                            s_h[:, j2 * ACH:(j2 + 1) * ACH], PC[:]),
                            waits=[("tensor", repC), ("gpsimd", tmp_free[j2])])
                        ydst = s_ybf if n == NS - 1 else s_y
                        ya_last = P.emit("gpsimd", lambda e, dt4=dt4, j2=j2, ydst=ydst: e.tensor_add(
                            ydst[:, dt4 * ACH:(dt4 + 1) * ACH],
                            s_y[:, dt4 * ACH:(dt4 + 1) * ACH],
                            s_tmp[:, j2 * ACH:(j2 + 1) * ACH]),
                            waits=[("vector", max(ym_last, yv_last[dt4]))])
                        tmp_free[j2] = ya_last
                run_units(min(2 * (n + 1), 24))
            run_units(len(next_units))
            jblk, c0 = (t0 // 2048), (t0 % 2048)
            for dt4 in range(4):
                r0 = jblk * 512 + dt4 * 128
                yst_cnt[0] = P.dma_q("dsync", yTs[r0:r0 + 128, c0:c0 + ACH],
                                     s_ybf[:, dt4 * ACH:(dt4 + 1) * ACH],
                                     waits=[("gpsimd", ya_last)], sem="yst")

        for u in prep_units(0):
            u()
        for ch in range(nch):
            nxt = prep_units(ch + 1) if ch + 1 < nch else []
            emit_scan(ch, nxt)
        P.barrier()

def host_inputs_A(inp):
    x = np.asarray(inp["L0_attn_x"], dtype=np.float32)
    wdtT = np.ascontiguousarray(np.asarray(inp["L0_attn_dt_proj_w"], np.float32).T)
    dtb = np.asarray(inp["L0_attn_dt_proj_b"], np.float32)
    xpwT = np.ascontiguousarray(np.asarray(inp["L0_attn_x_proj_w"], np.float32).T)
    xpb = np.asarray(inp["L0_attn_x_proj_b"], np.float32)
    alog = np.asarray(inp["L0_attn_A_log"], np.float32)
    ident = np.eye(128, dtype=np.float32)
    selm = np.zeros((64, 4096), dtype=np.float32)
    for n in range(16):
        selm[n, n * 128:(n + 1) * 128] = 1.0
        selm[32 + n, (16 + n) * 128:(17 + n) * 128] = 1.0
    xpw_pad = np.zeros((D, 64), dtype=np.float32)
    xpw_pad[:, 0:16] = xpwT[:, 0:16]
    xpw_pad[:, 32:48] = xpwT[:, 16:32]
    xpb_pad = np.zeros((64, 1), dtype=np.float32)
    xpb_pad[0:16, 0] = xpb[0:16]
    xpb_pad[32:48, 0] = xpb[16:32]
    maps = []
    for c in range(8):
        b, ds = c // 4, c % 4
        d0 = ds * 512
        maps.append({
            "xA": np.ascontiguousarray(x[b]),
            "xAs": np.ascontiguousarray(x[b][:, d0:d0 + 512]),
            "wdtT": np.ascontiguousarray(wdtT[:, d0:d0 + 512]),
            "dtb": np.ascontiguousarray(dtb[d0:d0 + 512].reshape(4, 128).T),
            "xpw": xpw_pad,
            "xpb": xpb_pad,
            "alog": np.ascontiguousarray(
                alog[d0:d0 + 512].reshape(4, 128, 16).transpose(1, 0, 2).reshape(128, 64)),
            "ident": ident,
            "sel": selm.astype(BF),
        })
    return maps


def sp_pack(Wt):
    Kf, Mf = Wt.shape
    Kt, Mt = Kf // 128, Mf // 128
    return np.ascontiguousarray(
        Wt.reshape(Kt, 128, Mt, 128).transpose(2, 0, 1, 3).reshape(Mt * Kt * 128, 128))


def load_sp(ser, eng, s_tile, dram_pack, mt, Kt):
    src = dram_pack[mt * Kt * 128:(mt + 1) * Kt * 128, :].rearrange(
        "(kt p) f -> p kt f", p=128)
    dst = s_tile[:, 0:Kt * 128].rearrange("p (kt f) -> p kt f", f=128)
    ser.dma(eng, dst, src)


def emit_load_xT(ser, nc, x_tm, t0, ntt, s_tok, s_dst, p_tp, s_id, ach, ndk=16):
    for tt in range(ntt):
        ser.dma(nc.sync, s_tok[:, 0:ndk * 128],
                x_tm[t0 + tt * 128: t0 + (tt + 1) * 128, :])
        for k in range(ndk):
            ser.step(nc.tensor, lambda e, k=k: e.transpose(
                p_tp[:], s_tok[:, k * 128:(k + 1) * 128], s_id[:]))
            ser.step(nc.vector, lambda e, k=k, tt=tt: e.tensor_copy(
                s_dst[:, k * ach + tt * 128: k * ach + (tt + 1) * 128], p_tp[:]))


def emit_ln(ser, nc, s_s, s_bf, s_st, s_onc, s_onr, p_s, p_rep, s_out, eps, w=512):
    for k in range(16):
        ser.step(nc.scalar, lambda e, k=k: e.copy(
            s_bf[:, k * w:(k + 1) * w], s_s[:, k * w:(k + 1) * w]))
    for k in range(16):
        ser.step(nc.tensor, lambda e, k=k: e.matmul(
            p_s[0:1, 0:w], s_onc[:, 0:1], s_bf[:, k * w:(k + 1) * w],
            start=(k == 0), stop=(k == 15)))
    for k in range(16):
        ser.step(nc.scalar, lambda e, k=k: e.activation(
            s_bf[:, k * w:(k + 1) * w], s_s[:, k * w:(k + 1) * w], AF.Square))
    for k in range(16):
        ser.step(nc.tensor, lambda e, k=k: e.matmul(
            p_s[0:1, w:2 * w], s_onc[:, 0:1], s_bf[:, k * w:(k + 1) * w],
            start=(k == 0), stop=(k == 15)))
    mu, e2, va, rs, nm = (s_st[0:1, i * w:(i + 1) * w] for i in range(5))
    ser.step(nc.vector, lambda e: e.tensor_scalar_mul(mu, p_s[0:1, 0:w], 1.0 / 2048))
    ser.step(nc.vector, lambda e: e.tensor_scalar_mul(e2, p_s[0:1, w:2 * w], 1.0 / 2048))
    ser.step(nc.vector, lambda e: e.tensor_mul(va, mu, mu))
    ser.step(nc.vector, lambda e: e.tensor_sub(va, e2, va))
    ser.step(nc.vector, lambda e: e.tensor_scalar_add(va, va, float(eps)))
    ser.step(nc.scalar, lambda e: e.activation(va, va, AF.Ln))
    ser.step(nc.vector, lambda e: e.tensor_scalar_mul(va, va, -0.5))
    ser.step(nc.scalar, lambda e: e.activation(rs, va, AF.Exp))
    ser.step(nc.vector, lambda e: e.tensor_mul(nm, mu, rs))
    ser.step(nc.vector, lambda e: e.tensor_scalar_mul(nm, nm, -1.0))
    ser.step(nc.tensor, lambda e: e.matmul(
        p_rep[:, 0:w], s_onr[0:1, :], rs, start=True, stop=True))
    ser.step(nc.tensor, lambda e: e.matmul(
        p_rep[:, w:2 * w], s_onr[0:1, :], nm, start=True, stop=True))
    for k in range(16):
        ser.step(nc.vector, lambda e, k=k: e.tensor_mul(
            s_out[:, k * w:(k + 1) * w], s_s[:, k * w:(k + 1) * w], p_rep[:, 0:w]))
        ser.step(nc.vector, lambda e, k=k: e.tensor_add(
            s_out[:, k * w:(k + 1) * w], s_out[:, k * w:(k + 1) * w],
            p_rep[:, w:2 * w]))


def emit_W(nc, P, t):
    wupT = t["wupT_sl"]
    wgT = t["wgT_sl"]
    wdT = t["wdT_sl"]
    h128 = t["h128"]
    up_t = t["up_t"]
    g_t = t["g_t"]
    dn_t = t["dn_t"]

    with ExitStack() as stack:
        def sb(name, shape, dt=F32):
            return stack.enter_context(nc.sbuf_tensor(_uniq(name), shape, dt))
        s_w = sb("s_w", [128, 16 * 1024])
        s_h = sb("s_h", [128, 128])
        s_t = sb("s_t", [128, 1024])
        s_o = sb("s_o", [128, 16 * 1024], BF16)
        p_m = stack.enter_context(nc.psum_tensor(_uniq("p_m"), [128, 512], F32))
        ser = P
        ser.dma(nc.sync, s_h[:], h128)

        def transform(srcT, dst, ntile, width, kind):
            ser.dmas(nc.sync, [
                (s_w[:, i * width:(i + 1) * width], srcT[i * 128:(i + 1) * 128, :])
                for i in range(ntile)])
            for i in range(ntile):
                for cqq in range(0, width, 512):
                    cw = min(512, width - cqq)
                    ser.step(nc.tensor, lambda e, i=i, cq=cqq, cw=cw: e.matmul(
                        p_m[:, 0:cw], s_h[:], s_w[:, i * width + cq: i * width + cq + cw],
                        start=True, stop=True))
                    ser.step(nc.vector, lambda e, i=i, cq=cqq, cw=cw: e.tensor_copy(
                        s_w[:, i * width + cq: i * width + cq + cw], p_m[:, 0:cw]))
            h = 1
            while h < ntile:
                for base in range(0, ntile, 2 * h):
                    for j in range(h):
                        a = (base + j) * width
                        b2 = (base + j + h) * width
                        ser.step(nc.vector, lambda e, a=a, b2=b2: e.tensor_sub(
                            s_t[:, 0:width], s_w[:, a:a + width], s_w[:, b2:b2 + width]))
                        ser.step(nc.vector, lambda e, a=a, b2=b2: e.tensor_add(
                            s_w[:, a:a + width], s_w[:, a:a + width], s_w[:, b2:b2 + width]))
                        ser.step(nc.vector, lambda e, b2=b2: e.tensor_copy(
                            s_w[:, b2:b2 + width], s_t[:, 0:width]))
                h *= 2
            for i in range(ntile):
                ser.step(nc.vector, lambda e, i=i: e.tensor_copy(
                    s_o[:, i * width:(i + 1) * width], s_w[:, i * width:(i + 1) * width]))
            if kind == "up":
                for ft in range(8):
                    ser.dma(nc.sync,
                            dst[ft * 16 * 128:(ft + 1) * 16 * 128, :].rearrange(
                                "(kt p) f -> p kt f", p=128),
                            s_o.rearrange("p (kt g) -> p kt g", g=1024)[
                                :, :, ft * 128:(ft + 1) * 128])
            else:
                for dt_ in range(2):
                    ser.dma(nc.sync,
                            dst[dt_ * 64 * 128:(dt_ + 1) * 64 * 128, :].rearrange(
                                "(fk p) f -> p fk f", p=128),
                            s_o.rearrange("p (fk g) -> p fk g", g=256)[
                                :, :, dt_ * 128:(dt_ + 1) * 128])

        transform(wupT, up_t, 16, 1024, "up")
        transform(wgT, g_t, 16, 1024, "up")
        transform(wdT, dn_t, 64, 256, "dn")
        for name, rows in PACK_SECTIONS:
            cin = t[f"cv_{name}_in"]
            cout = t[f"cv_{name}_out"]
            n = rows // 8
            csz = 4096 if n % 4096 == 0 else n // 4 if n % 4 == 0 else n
            r = 0
            while r < n:
                c = min(csz, n - r)
                ser.dma(nc.gpsimd, cout[r:r + c, :], cin[r:r + c, :])
                r += c
        P.barrier()


PACK_SECTIONS = (("qkv", 98304), ("pr", 32768), ("ot", 32768),
                 ("g", 147456), ("u", 147456), ("d", 147456))


def host_packs_f32(inp):
    pk = {}
    pk["qkv"] = sp_pack(np.ascontiguousarray(np.asarray(inp["L1_attn_W_qkv"], np.float32).T))
    pk["pr"] = sp_pack(np.ascontiguousarray(np.asarray(inp["L1_attn_W_proj"], np.float32).T))
    pk["ot"] = sp_pack(np.ascontiguousarray(np.asarray(inp["L0_attn_out_proj_w"], np.float32).T))
    wg = np.asarray(inp["L1_ffn_w_gate"], np.float32)
    wu = np.asarray(inp["L1_ffn_w_up"], np.float32)
    wd = np.asarray(inp["L1_ffn_w_down"], np.float32)
    pk["g"] = np.concatenate([sp_pack(wg[e]) for e in range(8)]
                             + [sp_pack(np.asarray(inp["L1_ffn_sw_gate"], np.float32))], axis=0)
    pk["u"] = np.concatenate([sp_pack(wu[e]) for e in range(8)]
                             + [sp_pack(np.asarray(inp["L1_ffn_sw_up"], np.float32))], axis=0)
    pk["d"] = np.concatenate([sp_pack(wd[e]) for e in range(8)]
                             + [sp_pack(np.asarray(inp["L1_ffn_sw_down"], np.float32))], axis=0)
    return pk


def host_inputs_W(inp):
    wupT = np.ascontiguousarray(np.asarray(inp["L0_ffn_W_up"], np.float32).T)
    wgT = np.ascontiguousarray(np.asarray(inp["L0_ffn_W_gate"], np.float32).T)
    wdT = np.ascontiguousarray(np.asarray(inp["L0_ffn_W_down"], np.float32).T)
    h = hadamard(128)
    pk = host_packs_f32(inp)
    maps = []
    for c in range(8):
        m = {
            "wupT_sl": np.ascontiguousarray(wupT[:, c * 1024:(c + 1) * 1024]),
            "wgT_sl": np.ascontiguousarray(wgT[:, c * 1024:(c + 1) * 1024]),
            "wdT_sl": np.ascontiguousarray(wdT[:, c * 256:(c + 1) * 256]),
            "h128": h,
        }
        for name, rows in PACK_SECTIONS:
            n = rows // 8
            m[f"cv_{name}_in"] = np.ascontiguousarray(pk[name][c * n:(c + 1) * n])
        maps.append(m)
    return maps


def assemble_packs_bf16(resW):
    out = {}
    for name, rows in PACK_SECTIONS:
        out[name] = np.concatenate(
            [resW.results[c][f"cv_{name}_out"] for c in range(8)], axis=0)
    return out


def unpack_fold(resW):
    wup_pk = np.concatenate([resW.results[c]["up_t"] for c in range(8)], axis=0)
    wg_pk = np.concatenate([resW.results[c]["g_t"] for c in range(8)], axis=0)
    wdn_pk = np.concatenate([resW.results[c]["dn_t"] for c in range(8)], axis=0)
    return wup_pk, wg_pk, wdn_pk


def emit_B(nc, P, t):
    yT = t["yT"]
    wot_pk = t["wot_pk"]
    otb = t["otb"]
    x_tm = t["x_tm"]
    ident = t["ident"]
    x1T = t["x1T"]

    with ExitStack() as stack:
        def sb(name, shape, dt=F32):
            return stack.enter_context(nc.sbuf_tensor(_uniq(name), shape, dt))
        s_y = sb("s_y", [128, 16 * TC], BF16)
        s_wb = sb("s_wb", [128, 4 * 2048], BF16)
        s_otb = sb("s_otb", [128, 16])
        s_id = sb("s_id", [128, 128])
        s_tok = sb("s_tok", [128, 2 * D])
        s_xr = sb("s_xr", [128, 16 * 512])
        s_s = sb("s_s", [128, 16 * 512])
        s_bf = sb("s_bf", [128, 16 * 512], BF16)
        s_st = sb("s_st", [1, 8 * 512])
        s_onc = sb("s_onc", [128, 1], BF16)
        s_onr = sb("s_onr", [1, 128])
        s_x1 = s_xr
        p = stack.enter_context(nc.psum_tensor(_uniq("p"), [128, 4096], F32))
        PTP = [p[:, 0:128], p[:, 512:640]]
        PM = [p[:, 1024:1536], p[:, 1536:2048]]
        p_s = p[0:1, 2048:3072]
        p_rep = p[:, 3072:4096]
        ser = P

        ser.dmas(nc.sync, [(s_otb[:], otb), (s_id[:], ident)])
        ser.step(nc.vector, lambda e: e.memset(s_onc[:, 0:1], 1.0))
        ser.step(nc.vector, lambda e: e.memset(s_onr[0:1, :], 1.0))
        ser.dmas(nc.sync, [
            (s_y[:, k * TC:(k + 1) * TC], yT[k * 128:(k + 1) * 128, :])
            for k in range(16)])
        P.barrier()

        NLD = 16
        NTC = TC // 512
        li = [0]
        ld_done = [0] * (NLD * NTC)
        ld_cnt = [0] * (NLD * NTC)

        def issue_loads(upto):
            while li[0] < min(upto, NLD * NTC):
                i = li[0]
                mt = i % NLD
                slot = i % 4
                srcv = wot_pk[mt * 16 * 128:(mt + 1) * 16 * 128, :].rearrange(
                    "(kt p) f -> p kt f", p=128)
                dstv = s_wb[:, slot * 2048: slot * 2048 + 2048].rearrange(
                    "p (kt f) -> p kt f", f=128)
                w = []
                if i >= 4:
                    w = [("tensor", ld_done[i - 4])]
                qn = "dgp" if (i % 2 == 0) else "dsync"
                ld_cnt[i] = P.dma_q(qn, dstv, srcv, waits=w, sem=f"slw{i % 4}")
                li[0] += 1

        tp_done = [0, 0]
        ev_v = [0, 0]
        xld_cnt = [0, 0]
        xdma_free = [0, 0]
        ev_cnt = [0, 0]
        for tc in range(NTC):
            t0 = tc * 512
            for tt in range(4):
                tb = tt % 2
                xld_cnt[tb] = P.dma_q(
                    "dsync", s_tok[:, tb * D: tb * D + D],
                    x_tm[t0 + tt * 128: t0 + (tt + 1) * 128, :],
                    waits=[("tensor", xdma_free[tb])], sem=f"xld{tb}")
                for k in range(16):
                    j = k % 2
                    w = [(f"xld{tb}", xld_cnt[tb]), ("vector", ev_v[j])]
                    tp_done[j] = P.emit("tensor", lambda e, k=k, tb=tb, j=j: e.transpose(
                        PTP[j], s_tok[:, tb * D + k * 128: tb * D + (k + 1) * 128],
                        s_id[:]), waits=w)
                    ev_v[j] = P.emit("vector", lambda e, k=k, tt=tt, j=j: e.tensor_copy(
                        s_xr[:, k * 512 + tt * 128: k * 512 + (tt + 1) * 128], PTP[j]),
                        waits=[("tensor", tp_done[j])])
                xdma_free[tb] = P.cnt["tensor"]
            xrv = P.cnt["vector"]
            for mt in range(16):
                gi = tc * NLD + mt
                issue_loads(gi + 4)
                slot = gi % 4
                par = mt % 2
                for k in range(16):
                    w = []
                    if k == 0:
                        w = [(f"slw{gi % 4}", ld_cnt[gi]), ("vector", ev_cnt[par])]
                    P.emit("tensor", lambda e, mt=mt, k=k, slot=slot, par=par: e.matmul(
                        PM[par], s_wb[:, slot * 2048 + k * 128: slot * 2048 + (k + 1) * 128],
                        s_y[:, k * TC + t0: k * TC + t0 + 512],
                        start=(k == 0), stop=(k == 15)),
                        waits=w, inc=(k == 15))
                tm = P.cnt["tensor"]
                ld_done[gi] = tm
                ev_cnt[par] = P.emit("vector", lambda e, mt=mt, par=par: e.scalar_tensor_tensor(
                    s_s[:, mt * 512:(mt + 1) * 512], PM[par], s_otb[:, mt:mt + 1],
                    s_xr[:, mt * 512:(mt + 1) * 512], ALU.add, ALU.add),
                    waits=[("tensor", tm), ("vector", xrv)])
            P.barrier()
            emit_ln(ser, nc, s_s, s_bf, s_st, s_onc, s_onr, p_s, p_rep,
                    s_x1, LN_EPS)
            ser.dmas(nc.sync, [
                (x1T[k * 128:(k + 1) * 128, t0:t0 + 512],
                 s_x1[:, k * 512:(k + 1) * 512]) for k in range(16)])
            P.barrier()
        P.barrier()


def emit_C(nc, P, t):
    x1T = t["x1T"]
    wup_pk = t["wup_pk"]
    wg_pk = t["wg_pk"]
    wdn_pk = t["wdn_pk"]
    su_ug = t["su_ug"]
    wsg = t["wsg"]
    wsu = t["wsu"]
    wsd = t["wsd"]
    x2T = t["x2T"]

    with ExitStack() as stack:
        def sb(name, shape, dt=F32):
            return stack.enter_context(nc.sbuf_tensor(_uniq(name), shape, dt))
        s_x1 = sb("s_x1", [128, 16 * 512])
        s_xs = sb("s_xs", [128, 16 * 512], BF16)
        s_su = sb("s_su", [128, 16])
        s_wsg = sb("s_wsg", [128, 64])
        s_wsu = sb("s_wsu", [128, 64])
        s_wsd = sb("s_wsd", [128, 16])
        s_wb = sb("s_wb", [128, 4 * 2048], BF16)
        s_sg = sb("s_sg", [128, 2 * 512])
        s_z = sb("s_z", [128, 64 * 512], BF16)
        s_s = sb("s_s", [128, 16 * 512])
        s_bf = sb("s_bf", [128, 16 * 512], BF16)
        s_st = sb("s_st", [1, 8 * 512])
        s_onc = sb("s_onc", [128, 1], BF16)
        s_onr = sb("s_onr", [1, 128])
        s_x2 = s_x1
        p = stack.enter_context(nc.psum_tensor(_uniq("p"), [128, 4096], F32))
        PG = [p[:, 0:512], p[:, 512:1024]]
        PU = [p[:, 1024:1536], p[:, 1536:2048]]
        PD = [p[:, 2048:2560], p[:, 2560:3072]]
        p_rep = p[:, 3072:4096]
        p_s = p[0:1, 2048:3072]
        ser = P

        ser.dmas(nc.sync, [(s_su[:], su_ug), (s_wsg[:], wsg), (s_wsu[:], wsu),
                           (s_wsd[:], wsd)])
        ser.step(nc.vector, lambda e: e.memset(s_onc[:, 0:1], 1.0))
        ser.step(nc.vector, lambda e: e.memset(s_onr[0:1, :], 1.0))

        loads = []
        for ft in range(64):
            loads.append((wg_pk, ft * 16 * 128, 16))
            loads.append((wup_pk, ft * 16 * 128, 16))
        for dt_ in range(16):
            for sub in range(4):
                loads.append((wdn_pk, dt_ * 64 * 128 + sub * 16 * 128, 16))
        NLD = len(loads)
        NTC = TC // 512
        li = [0]
        ld_done = [0] * (NLD * NTC)
        ld_cnt = [0] * (NLD * NTC)

        def issue_loads(upto):
            while li[0] < min(upto, NLD * NTC):
                i = li[0]
                pk, row0, Kt = loads[i % NLD]
                slot = i % 4
                srcv = pk[row0:row0 + Kt * 128, :].rearrange("(kt p) f -> p kt f", p=128)
                dstv = s_wb[:, slot * 2048: slot * 2048 + Kt * 128].rearrange(
                    "p (kt f) -> p kt f", f=128)
                w = []
                if i >= 4:
                    w = [("tensor", ld_done[i - 4])]
                qn = "dgp" if (i % 2 == 0) else "dsync"
                ld_cnt[i] = P.dma_q(qn, dstv, srcv, waits=w, sem=f"slw{i % 4}")
                li[0] += 1

        for tc in range(NTC):
            t0 = tc * 512
            ser.dmas(nc.sync, [
                (s_x1[:, k * 512:(k + 1) * 512], x1T[k * 128:(k + 1) * 128, t0:t0 + 512])
                for k in range(16)])
            for k in range(16):
                ser.step(nc.vector, lambda e, k=k: e.tensor_scalar_mul(
                    s_xs[:, k * 512:(k + 1) * 512], s_x1[:, k * 512:(k + 1) * 512],
                    s_su[:, k:k + 1]))
            P.barrier()

            silu_cnt = [0, 0]
            sg_free = [0, 0]
            z_cnt = [0, 0]
            evac_cnt = [0, 0]
            z_done = 0
            gi = tc * NLD
            for ft in range(64):
                par = ft % 2
                issue_loads(gi + 4)
                slot = gi % 4
                for k in range(16):
                    w = []
                    if k == 0:
                        w = [(f"slw{gi % 4}", ld_cnt[gi]), ("scalar", silu_cnt[par])]
                    P.emit("tensor", lambda e, k=k, slot=slot, par=par: e.matmul(
                        PG[par], s_wb[:, slot * 2048 + k * 128: slot * 2048 + (k + 1) * 128],
                        s_xs[:, k * 512:(k + 1) * 512],
                        start=(k == 0), stop=(k == 15)),
                        waits=w, inc=(k == 15))
                tg = P.cnt["tensor"]
                ld_done[gi] = tg
                gi += 1
                issue_loads(gi + 4)
                slot = gi % 4
                for k in range(16):
                    w = []
                    if k == 0:
                        w = [(f"slw{gi % 4}", ld_cnt[gi]), ("vector", z_cnt[par])]
                    P.emit("tensor", lambda e, k=k, slot=slot, par=par: e.matmul(
                        PU[par], s_wb[:, slot * 2048 + k * 128: slot * 2048 + (k + 1) * 128],
                        s_xs[:, k * 512:(k + 1) * 512],
                        start=(k == 0), stop=(k == 15)),
                        waits=w, inc=(k == 15))
                tu = P.cnt["tensor"]
                ld_done[gi] = tu
                gi += 1
                silu_cnt[par] = P.emit("scalar", lambda e, par=par, ft=ft: e.activation(
                    s_sg[:, par * 512: par * 512 + 512], PG[par], AF.Silu,
                    scale=s_wsg[:, ft:ft + 1]),
                    waits=[("tensor", tg), ("vector", sg_free[par])])
                z_cnt[par] = P.emit("vector", lambda e, par=par, ft=ft: e.scalar_tensor_tensor(
                    s_z[:, ft * 512:(ft + 1) * 512], PU[par], s_wsu[:, ft:ft + 1],
                    s_sg[:, par * 512: par * 512 + 512], ALU.mult, ALU.mult),
                    waits=[("tensor", tu), ("scalar", silu_cnt[par])])
                sg_free[par] = z_cnt[par]
                if ft == 63:
                    z_done = z_cnt[par]
            for dt_ in range(16):
                par2 = dt_ % 2
                for fk in range(64):
                    sub = fk // 16
                    if fk % 16 == 0:
                        issue_loads(gi + sub + 4)
                    sl = (gi + sub) % 4
                    fo = fk % 16
                    w = []
                    if fk == 0:
                        w = [(f"slw{sl}", ld_cnt[gi]),
                             ("vector", max(z_done, evac_cnt[par2]))]
                    elif fk % 16 == 0:
                        w = [(f"slw{sl}", ld_cnt[gi + sub])]
                    P.emit("tensor", lambda e, fk=fk, sl=sl, fo=fo, par2=par2: e.matmul(
                        PD[par2], s_wb[:, sl * 2048 + fo * 128: sl * 2048 + (fo + 1) * 128],
                        s_z[:, fk * 512:(fk + 1) * 512],
                        start=(fk == 0), stop=(fk == 63)),
                        waits=w, inc=(fk % 16 == 15))
                    if fk % 16 == 15:
                        ld_done[gi + sub] = P.cnt["tensor"]
                td = P.cnt["tensor"]
                gi += 4
                evac_cnt[par2] = P.emit("vector", lambda e, dt_=dt_, par2=par2: e.scalar_tensor_tensor(
                    s_s[:, dt_ * 512:(dt_ + 1) * 512], PD[par2], s_wsd[:, dt_:dt_ + 1],
                    s_x1[:, dt_ * 512:(dt_ + 1) * 512], ALU.mult, ALU.add),
                    waits=[("tensor", td)])
            P.barrier()
            emit_ln(ser, nc, s_s, s_bf, s_st, s_onc, s_onr, p_s, p_rep,
                    s_x2, LN_EPS)
            ser.dmas(nc.sync, [
                (x2T[k * 128:(k + 1) * 128, t0:t0 + 512],
                 s_x2[:, k * 512:(k + 1) * 512]) for k in range(16)])
        P.barrier()


def host_scales_C(inp):
    wsc = np.asarray(inp["L0_ffn_Wscale_ug"], np.float32)
    sudp = np.asarray(inp["L0_ffn_SU_dp"], np.float32)
    wsg = np.ascontiguousarray((wsc[INTER:] / np.sqrt(D)).reshape(64, 128).T)
    wsu = np.ascontiguousarray((wsc[:INTER] * sudp / np.sqrt(D)).reshape(64, 128).T)
    wsd = np.ascontiguousarray(
        (np.asarray(inp["L0_ffn_Wscale_dp"], np.float32) / np.sqrt(INTER)).reshape(16, 128).T)
    suug = np.ascontiguousarray(np.asarray(inp["L0_ffn_SU_ug"], np.float32).reshape(16, 128).T)
    return suug, wsg, wsu, wsd


def emit_D(nc, P, t, on_kind=None):
    x2T = t["x2T"]
    wqkv_pk = t["wqkv_pk"]
    qkvT = t["qkvT"]

    with ExitStack() as stack:
        def sb(name, shape, dt=F32):
            return stack.enter_context(nc.sbuf_tensor(_uniq(name), shape, dt))
        s_x = sb("s_x", [128, 16 * TC], BF16)
        s_wb = sb("s_wb", [128, 4 * 2048], BF16)
        s_o = sb("s_o", [128, 2 * TC], BF16)
        p = stack.enter_context(nc.psum_tensor(_uniq("p"), [128, 4096], F32))
        PM = [p[:, 0:512], p[:, 512:1024]]
        ser = P
        ser.dmas(nc.gpsimd, [
            (s_x[:, k * TC:(k + 1) * TC], x2T[k * 128:(k + 1) * 128, :])
            for k in range(16)])
        P.barrier()

        li = [0]
        ld_done = [0] * 48
        ld_cnt = [0] * 48

        def issue_loads(upto):
            while li[0] < min(upto, 48):
                i = li[0]
                slot = i % 4
                srcv = wqkv_pk[i * 16 * 128:(i + 1) * 16 * 128, :].rearrange(
                    "(kt p) f -> p kt f", p=128)
                dstv = s_wb[:, slot * 2048: slot * 2048 + 2048].rearrange(
                    "p (kt f) -> p kt f", f=128)
                w = []
                if i >= 4:
                    w = [("tensor", ld_done[i - 4])]
                qn = "dgp" if (i % 2 == 0) else "dsync"
                ld_cnt[i] = P.dma_q(qn, dstv, srcv, waits=w, sem=f"slw{i % 4}")
                li[0] += 1

        ev_cnt = [0, 0]
        ost_cnt = [0, 0]
        for mt in range(48):
            issue_loads(mt + 4)
            slot = mt % 4
            oh = mt % 2
            for tc in range(TC // 512):
                par = tc % 2
                for k in range(16):
                    w = []
                    if k == 0:
                        w = [(f"slw{slot}", ld_cnt[mt]), ("scalar", ev_cnt[par])]
                    P.emit("tensor", lambda e, k=k, tc=tc, slot=slot, par=par: e.matmul(
                        PM[par], s_wb[:, slot * 2048 + k * 128: slot * 2048 + (k + 1) * 128],
                        s_x[:, k * TC + tc * 512: k * TC + (tc + 1) * 512],
                        start=(k == 0), stop=(k == 15)),
                        waits=w, inc=(k == 15))
                tm = P.cnt["tensor"]
                if tc == 3:
                    ld_done[mt] = tm
                w = [("tensor", tm)]
                if tc == 0:
                    w.append((f"ost{oh}", ost_cnt[oh]))
                ev_cnt[par] = P.emit("scalar", lambda e, tc=tc, oh=oh, par=par: e.copy(
                    s_o[:, oh * TC + tc * 512: oh * TC + (tc + 1) * 512], PM[par]),
                    waits=w)
            ost_cnt[oh] = P.dma_q("dsync", qkvT[mt * 128:(mt + 1) * 128, :],
                                  s_o[:, oh * TC:(oh + 1) * TC],
                                  waits=[("scalar", ev_cnt[(TC // 512 - 1) % 2])],
                                  sem=f"ost{oh}")
            if on_kind is not None and mt % 16 == 15:
                on_kind(mt // 16, ost_cnt[0], ost_cnt[1])
        P.barrier()


def emit_E(nc, P, t):
    qkvg = t["qkvg"]
    maskw = t["maskw"]
    ident_bf = t["ident_bf"]
    y_tm = t["y_tm"]

    NKV = T // 128
    with ExitStack() as stack:
        def sb(name, shape, dt=F32):
            return stack.enter_context(nc.sbuf_tensor(_uniq(name), shape, dt))
        s_q = sb("s_q", [128, 4 * T], BF16)
        s_k = sb("s_k", [128, 4 * T], BF16)
        s_v = sb("s_v", [128, NKV * 4 * 129], BF16)
        s_vt = sb("s_vt", [128, 2 * 2048], BF16)
        s_m = sb("s_m", [128, 896], BF16)
        s_p = sb("s_p", [128, 2 * 512], BF16)
        s_sqq = s_p[:, 0:512]
        s_st = sb("s_st", [1, 64])
        s_onc = sb("s_onc", [128, 1], BF16)
        s_onr = sb("s_onr", [1, 128])
        s_scl = sb("s_scl", [128, 1])
        s_mh = sb("s_mh", [128, 1])
        s_r = sb("s_r", [128, 4])
        s_yo = sb("s_yo", [128, 2 * 512], BF16)
        s_id = sb("s_id", [128, 128], BF16)
        p = stack.enter_context(nc.psum_tensor(_uniq("p"), [128, 3072], F32))
        PS = [p[:, 0:512], p[:, 512:1024]]
        PO = [p[:, 1024 + j * 512: 1024 + j * 512 + 129] for j in range(4)]
        p_s1 = p[0:1, 1024:1536]
        p_b = p[:, 1536:1537]
        p_bf = stack.enter_context(nc.psum_tensor(_uniq("p_bf"), [128, 2048], BF16))
        PTB = [p_bf[:, 0:128], p_bf[:, 1024:1152]]
        ser = P

        ser.dmas(nc.sync, [(s_m[:], maskw), (s_id[:], ident_bf)])
        ser.step(nc.vector, lambda e: e.memset(s_onc[:, 0:1], 1.0))
        ser.step(nc.vector, lambda e: e.memset(s_onr[0:1, :], 1.0))
        ser.step(nc.vector, lambda e: e.memset(s_scl[:, 0:1], 0.08838834764831845))
        for h in range(4):
            ser.dmas(nc.sync, [
                (s_q[:, h * T + r * 2048: h * T + (r + 1) * 2048],
                 qkvg[r * 1536 + h * 128: r * 1536 + (h + 1) * 128, :]) for r in range(4)])
            ser.dmas(nc.sync, [
                (s_k[:, h * T + r * 2048: h * T + (r + 1) * 2048],
                 qkvg[r * 1536 + 512 + h * 128: r * 1536 + 512 + (h + 1) * 128, :])
                for r in range(4)])
        tpd = [0, 0]
        evd = [0, 0]
        for h in range(4):
            for rp in range(2):
                ser.dmas(nc.sync, [
                    (s_vt[:, r2 * 2048:(r2 + 1) * 2048],
                     qkvg[(rp * 2 + r2) * 1536 + 1024 + h * 128:
                          (rp * 2 + r2) * 1536 + 1024 + (h + 1) * 128, :])
                    for r2 in range(2)])
                for r2 in range(2):
                    r = rp * 2 + r2
                    for kvl in range(16):
                        j = kvl % 2
                        tpd[j] = P.emit("tensor", lambda e, r2=r2, kvl=kvl, j=j: e.transpose(
                            PTB[j], s_vt[:, r2 * 2048 + kvl * 128: r2 * 2048 + (kvl + 1) * 128],
                            s_id[:]),
                            waits=[("scalar", evd[j]), ("dsync", P.cnt["dsync"])])
                        kvb = r * 16 + kvl
                        evd[j] = P.emit("scalar", lambda e, kvb=kvb, h=h, j=j: e.copy(
                            s_v[:, (kvb * 4 + h) * 129:(kvb * 4 + h) * 129 + 128], PTB[j]),
                            waits=[("tensor", tpd[j])])
        ser.step(nc.vector, lambda e: e.memset(
            s_v.rearrange("p (t c) -> p t c", c=129)[:, :, 128:129], 1.0))
        P.barrier()

        exp_free = [0, 0]
        av_free = [0, 0]
        yo_free = [0, 0]
        for h in range(4):
            for which, buf in ((0, s_q), (1, s_k)):
                for tcq in range(T // 512):
                    ser.step(nc.scalar, lambda e, buf=buf, h=h, tcq=tcq: e.activation(
                        s_sqq[:], buf[:, h * T + tcq * 512: h * T + (tcq + 1) * 512], AF.Square))
                    ser.step(nc.tensor, lambda e: e.matmul(
                        p_s1[0:1, :], s_onc[:, 0:1], s_sqq[:], start=True, stop=True))
                    ser.step(nc.vector, lambda e, w=which: e.reduce_max(
                        s_st[0:1, 8 + w:9 + w], p_s1[0:1, :], mybir.AxisListType.X))
                    if tcq == 0:
                        ser.step(nc.vector, lambda e, w=which: e.tensor_copy(
                            s_st[0:1, w:w + 1], s_st[0:1, 8 + w:9 + w]))
                    else:
                        ser.step(nc.vector, lambda e, w=which: e.tensor_max(
                            s_st[0:1, w:w + 1], s_st[0:1, w:w + 1], s_st[0:1, 8 + w:9 + w]))
            ser.step(nc.vector, lambda e: e.tensor_mul(
                s_st[0:1, 2:3], s_st[0:1, 0:1], s_st[0:1, 1:2]))
            ser.step(nc.scalar, lambda e: e.activation(s_st[0:1, 2:3], s_st[0:1, 2:3], AF.Ln))
            ser.step(nc.vector, lambda e: e.tensor_scalar_mul(s_st[0:1, 2:3], s_st[0:1, 2:3], 0.5))
            ser.step(nc.scalar, lambda e: e.activation(s_st[0:1, 2:3], s_st[0:1, 2:3], AF.Exp))
            ser.step(nc.vector, lambda e: e.tensor_scalar_mul(
                s_st[0:1, 2:3], s_st[0:1, 2:3], -0.08838834764831845))
            ser.step(nc.tensor, lambda e: e.matmul(
                p_b[:, 0:1], s_onr[0:1, :], s_st[0:1, 2:3], start=True, stop=True))
            ser.step(nc.vector, lambda e: e.tensor_copy(s_mh[:, 0:1], p_b[:, 0:1]))
            P.barrier()

            po_free = [0, 0, 0, 0]

            for qc in range(T // 512):
                nkv = 4 * qc + 4
                exp_cnt = [0] * nkv
                gate_cnt = [0] * nkv
                gate_eng = [None] * nkv
                av_done = [0] * nkv
                s_done = [0] * nkv
                for i in range(nkv + 1):
                    if i < nkv:
                        par = i % 2
                        w = [("scalar", exp_free[par])]
                        s_done[i] = P.emit("tensor", lambda e, h=h, qc=qc, i=i, par=par: e.matmul(
                            PS[par], s_k[:, h * T + i * 128: h * T + (i + 1) * 128],
                            s_q[:, h * T + qc * 512: h * T + (qc + 1) * 512],
                            start=True, stop=True), waits=w)
                        wexp = [("tensor", s_done[i]), ("tensor", av_free[par])]
                        exp_cnt[i] = P.emit("scalar", lambda e, par=par: e.activation(
                            s_p[:, par * 512: par * 512 + 512], PS[par], AF.Exp,
                            bias=s_mh[:, 0:1], scale=s_scl[:, 0:1]), waits=wexp)
                        exp_free[par] = exp_cnt[i]
                        o = i * 128 - qc * 512
                        if o >= 0:
                            gate_cnt[i] = P.emit("vector", lambda e, par=par, o=o: e.tensor_mul(
                                s_p[:, par * 512: par * 512 + 512],
                                s_p[:, par * 512: par * 512 + 512],
                                s_m[:, 384 - o: 896 - o]),
                                waits=[("scalar", exp_cnt[i])])
                            gate_eng[i] = "vector"
                        else:
                            gate_cnt[i] = exp_cnt[i]
                            gate_eng[i] = "scalar"
                    if i >= 1:
                        kvb = i - 1
                        par = kvb % 2
                        for j in range(4):
                            w = [(gate_eng[kvb], gate_cnt[kvb])]
                            if kvb == 0:
                                w.append(("vector", po_free[j]))
                            P.emit("tensor", lambda e, j=j, kvb=kvb, h=h, par=par,
                                   nkv=nkv: e.matmul(
                                PO[j], s_p[:, par * 512 + j * 128: par * 512 + (j + 1) * 128],
                                s_v[:, (kvb * 4 + h) * 129:(kvb * 4 + h) * 129 + 129],
                                start=(kvb == 0), stop=(kvb == nkv - 1)),
                                waits=w, inc=(j == 3))
                        av_done[kvb] = P.cnt["tensor"]
                        av_free[kvb % 2] = av_done[kvb]
                yh = qc % 2
                for j in range(4):
                    P.emit("vector", lambda e, j=j: e.reciprocal(
                        s_r[:, j:j + 1], PO[j][:, 128:129]),
                        waits=[("tensor", av_done[nkv - 1]), (f"yst{yh}", yo_free[yh])])
                    po_free[j] = P.emit("vector", lambda e, j=j, yh=yh: e.tensor_scalar_mul(
                        s_yo[:, yh * 512 + j * 128: yh * 512 + (j + 1) * 128],
                        PO[j][:, 0:128], s_r[:, j:j + 1]))
                dst = y_tm[qc * 512:(qc + 1) * 512, h * 128:(h + 1) * 128].rearrange(
                    "(j p) c -> p j c", p=128)
                srcv = s_yo[:, yh * 512:(yh + 1) * 512].rearrange("p (j c) -> p j c", c=128)
                yo_free[yh] = P.dma_q("dsync", dst, srcv,
                                      waits=[("vector", po_free[3])], sem=f"yst{yh}")
        P.barrier()


def emit_E2(nc, P, t):
    ytmg = t["ytmg"]
    wpr_pk = t["wpr_pk"]
    x2T = t["x2T"]
    ident_bf = t["ident_bf"]
    x3T = t["x3T"]

    with ExitStack() as stack:
        def sb(name, shape, dt=F32):
            return stack.enter_context(nc.sbuf_tensor(_uniq(name), shape, dt))
        s_tok = sb("s_tok", [128, 2 * D], BF16)
        s_y = sb("s_y", [128, 16 * 512], BF16)
        s_wb = sb("s_wb", [128, 4 * 2048], BF16)
        s_id = sb("s_id", [128, 128], BF16)
        s_xr = sb("s_xr", [128, 16 * 512])
        s_s = sb("s_s", [128, 16 * 512])
        s_bf = sb("s_bf", [128, 16 * 512], BF16)
        s_st = sb("s_st", [1, 8 * 512])
        s_onc = sb("s_onc", [128, 1], BF16)
        s_onr = sb("s_onr", [1, 128])
        p = stack.enter_context(nc.psum_tensor(_uniq("p"), [128, 3072], F32))
        PM = [p[:, 0:512], p[:, 512:1024]]
        p_s = p[0:1, 1024:2048]
        p_rep = p[:, 2048:3072]
        p_bf = stack.enter_context(nc.psum_tensor(_uniq("p_bf"), [128, 2048], BF16))
        PTP = [p_bf[:, 0:128], p_bf[:, 1024:1152]]
        ser = P
        ser.dmas(nc.sync, [(s_id[:], ident_bf)])
        ser.step(nc.vector, lambda e: e.memset(s_onc[:, 0:1], 1.0))
        ser.step(nc.vector, lambda e: e.memset(s_onr[0:1, :], 1.0))
        P.barrier()

        NLD = 16
        NTC = TC // 512
        li = [0]
        ld_done = [0] * (NLD * NTC)
        ld_cnt = [0] * (NLD * NTC)

        def issue_loads(upto):
            while li[0] < min(upto, NLD * NTC):
                i = li[0]
                mt = i % NLD
                slot = i % 4
                srcv = wpr_pk[mt * 16 * 128:(mt + 1) * 16 * 128, :].rearrange(
                    "(kt p) f -> p kt f", p=128)
                dstv = s_wb[:, slot * 2048: slot * 2048 + 2048].rearrange(
                    "p (kt f) -> p kt f", f=128)
                w = []
                if i >= 4:
                    w = [("tensor", ld_done[i - 4])]
                qn = "dgp" if (i % 2 == 0) else "dsync"
                ld_cnt[i] = P.dma_q(qn, dstv, srcv, waits=w, sem=f"slw{i % 4}")
                li[0] += 1

        tp_done = [0, 0]
        ev_s = [0, 0]
        xld_cnt = [0, 0]
        xdma_free = [0, 0]
        ev_cnt = [0, 0]
        for tc in range(NTC):
            t0 = tc * 512
            ser.dmas(nc.sync, [
                (s_xr[:, k * 512:(k + 1) * 512], x2T[k * 128:(k + 1) * 128, t0:t0 + 512])
                for k in range(16)])
            for tt in range(4):
                tb = tt % 2
                for r in range(4):
                    base = r * 2048 + t0 + tt * 128
                    xld_cnt[tb] = P.dma_q(
                        "dscal", s_tok[:, tb * D + r * 512: tb * D + (r + 1) * 512],
                        ytmg[base: base + 128, :],
                        waits=([("tensor", xdma_free[tb])] if r == 0 else []),
                        sem=f"xld{tb}")
                for k in range(16):
                    j = k % 2
                    w = [(f"xld{tb}", xld_cnt[tb]), ("scalar", ev_s[j])]
                    tp_done[j] = P.emit("tensor", lambda e, k=k, tb=tb, j=j: e.transpose(
                        PTP[j], s_tok[:, tb * D + k * 128: tb * D + (k + 1) * 128],
                        s_id[:]), waits=w)
                    ev_s[j] = P.emit("scalar", lambda e, k=k, tt=tt, j=j: e.copy(
                        s_y[:, k * 512 + tt * 128: k * 512 + (tt + 1) * 128], PTP[j]),
                        waits=[("tensor", tp_done[j])])
                xdma_free[tb] = P.cnt["tensor"]
            yev = P.cnt["scalar"]
            for mt in range(16):
                gi = tc * NLD + mt
                issue_loads(gi + 4)
                slot = gi % 4
                par = mt % 2
                for k in range(16):
                    w = []
                    if k == 0:
                        w = [(f"slw{gi % 4}", ld_cnt[gi]), ("vector", ev_cnt[par]),
                             ("scalar", yev)]
                    P.emit("tensor", lambda e, mt=mt, k=k, slot=slot, par=par: e.matmul(
                        PM[par], s_wb[:, slot * 2048 + k * 128: slot * 2048 + (k + 1) * 128],
                        s_y[:, k * 512:(k + 1) * 512],
                        start=(k == 0), stop=(k == 15)),
                        waits=w, inc=(k == 15))
                tm = P.cnt["tensor"]
                ld_done[gi] = tm
                ev_cnt[par] = P.emit("vector", lambda e, mt=mt, par=par: e.tensor_add(
                    s_s[:, mt * 512:(mt + 1) * 512], PM[par],
                    s_xr[:, mt * 512:(mt + 1) * 512]),
                    waits=[("tensor", tm)])
            P.barrier()
            emit_ln(ser, nc, s_s, s_bf, s_st, s_onc, s_onr, p_s, p_rep, s_xr, LN_EPS)
            ser.dmas(nc.sync, [
                (x3T[k * 128:(k + 1) * 128, t0:t0 + 512],
                 s_xr[:, k * 512:(k + 1) * 512]) for k in range(16)])
            P.barrier()
        P.barrier()


def emit_F(nc, P, t, pipe_mode=True):
    x3T = t["x3T"]
    rmsw = t["rmsw"]
    rt_w = t["rt_w"]
    wg_pk = t["wg_pk"]
    wu_pk = t["wu_pk"]
    wd_pk = t["wd_pk"]
    sel8 = t["sel8"]
    ident = t["ident"]
    out_tm = t["out_tm"]

    with ExitStack() as stack:
        def sb(name, shape, dt=F32):
            return stack.enter_context(nc.sbuf_tensor(_uniq(name), shape, dt))
        s_x3 = sb("s_x3", [128, 16 * 512])
        s_ht = sb("s_ht", [128, 16 * 512], BF16)
        s_rw = sb("s_rw", [128, 16])
        s_rt = sb("s_rt", [128, 16 * 8], BF16)
        s_sel = sb("s_sel", [8, 1024])
        s_id = sb("s_id", [128, 128])
        s_weT = sb("s_weT", [8, 512])
        s_we = sb("s_we", [128, 2 * 512])
        s_sg = sb("s_sg", [128, 2 * 512])
        s_uw = sb("s_uw", [128, 2 * 512])
        s_z = sb("s_z", [128, 2 * 8 * 512], BF16)
        s_acc = sb("s_acc", [128, 16 * 512])
        s_bf = sb("s_bf", [128, 16 * 512], BF16)
        s_st = sb("s_st", [1, 8 * 512])
        s_onc = sb("s_onc", [128, 1], BF16)
        s_onr = sb("s_onr", [1, 128])
        s_lg = sb("s_lg", [128, 48])
        s_wb = sb("s_wb", [128, 4 * 2048], BF16)
        s_tm = sb("s_tm", [128, 16 * 128])
        p = stack.enter_context(nc.psum_tensor(_uniq("p"), [128, 4096], F32))
        PG = [p[:, 0:512], p[:, 512:1024]]
        PU = [p[:, 1024:1536], p[:, 1536:2048]]
        PD = [p[:, 2048:2560], p[:, 2560:3072]]
        PWE = p[:, 3072:3584]
        p_rep = p[:, 3072:4096]
        p_s = p[0:1, 2048:3072]
        p_r = p[:, 0:8]
        p_tp = p[:, 0:128]
        ser = P

        ser.dmas(nc.sync, [(s_rw[:], rmsw), (s_sel[:], sel8), (s_id[:], ident)])
        ser.dmas(nc.gpsimd, [
            (s_rt[:, k * 8:(k + 1) * 8], rt_w[k * 128:(k + 1) * 128, :])
            for k in range(16)])
        ser.step(nc.vector, lambda e: e.memset(s_onc[:, 0:1], 1.0))
        ser.step(nc.vector, lambda e: e.memset(s_onr[0:1, :], 1.0))

        loads = []
        for e_ in range(9):
            for ft in range(8):
                loads.append((wg_pk, e_ * 8 + ft, 16))
                loads.append((wu_pk, e_ * 8 + ft, 16))
            for dt_ in range(16):
                loads.append((wd_pk, e_ * 16 + dt_, 8))
        NLD = len(loads)
        NTC = TC // 512
        li = [0]
        ld_done = [0] * (NLD * NTC)
        ld_cnt = [0] * (NLD * NTC)

        def W_(w):
            return w if pipe_mode else P.snap()

        def issue_loads(upto):
            while li[0] < min(upto, NLD * NTC):
                i = li[0]
                pk, mt, Kt = loads[i % NLD]
                slot = i % 4
                srcv = pk[mt * Kt * 128:(mt + 1) * Kt * 128, :].rearrange(
                    "(kt p) f -> p kt f", p=128)
                dstv = s_wb[:, slot * 2048: slot * 2048 + Kt * 128].rearrange(
                    "p (kt f) -> p kt f", f=128)
                w = []
                if i >= 4:
                    w = [("tensor", ld_done[i - 4])]
                qn = "dgp" if (i % 2 == 0) else "dsync"
                ld_cnt[i] = P.dma_q(qn, dstv, srcv, waits=W_(w), sem=f"slw{i % 4}")
                li[0] += 1

        for tc in range(NTC):
            t0 = tc * 512
            ser.dmas(nc.sync, [
                (s_x3[:, k * 512:(k + 1) * 512], x3T[k * 128:(k + 1) * 128, t0:t0 + 512])
                for k in range(16)])
            for k in range(16):
                ser.step(nc.scalar, lambda e, k=k: e.activation(
                    s_bf[:, k * 512:(k + 1) * 512], s_x3[:, k * 512:(k + 1) * 512], AF.Square))
            for k in range(16):
                ser.step(nc.tensor, lambda e, k=k: e.matmul(
                    p_s[0:1, 0:512], s_onc[:, 0:1], s_bf[:, k * 512:(k + 1) * 512],
                    start=(k == 0), stop=(k == 15)))
            ms = s_st[0:1, 0:512]
            ser.step(nc.vector, lambda e: e.tensor_scalar_mul(ms, p_s[0:1, 0:512], 1.0 / 2048))
            ser.step(nc.vector, lambda e: e.tensor_scalar_add(ms, ms, RMS_EPS))
            ser.step(nc.scalar, lambda e: e.activation(ms, ms, AF.Ln))
            ser.step(nc.vector, lambda e: e.tensor_scalar_mul(ms, ms, -0.5))
            ser.step(nc.scalar, lambda e: e.activation(ms, ms, AF.Exp))
            ser.step(nc.tensor, lambda e: e.matmul(
                p_rep[:, 0:512], s_onr[0:1, :], ms, start=True, stop=True))
            for k in range(16):
                ser.step(nc.vector, lambda e, k=k: e.scalar_tensor_tensor(
                    s_ht[:, k * 512:(k + 1) * 512], s_x3[:, k * 512:(k + 1) * 512],
                    s_rw[:, k:k + 1], p_rep[:, 0:512], ALU.mult, ALU.mult))
            for tt in range(4):
                for k in range(16):
                    ser.step(nc.tensor, lambda e, k=k, tt=tt: e.matmul(
                        p_r, s_ht[:, k * 512 + tt * 128: k * 512 + (tt + 1) * 128],
                        s_rt[:, k * 8:(k + 1) * 8],
                        start=(k == 0), stop=(k == 15)))
                lg = s_lg[:, 0:8]; pe = s_lg[:, 8:16]; m1 = s_lg[:, 16:17]
                sm = s_lg[:, 17:18]; m2 = s_lg[:, 18:19]; kp = s_lg[:, 24:32]
                wi = s_lg[:, 19:20]; we8 = s_lg[:, 32:40]; nm1 = s_lg[:, 20:21]
                ser.step(nc.vector, lambda e: e.tensor_copy(lg, p_r))
                ser.step(nc.vector, lambda e: e.reduce_max(m1, lg, mybir.AxisListType.X))
                ser.step(nc.vector, lambda e: e.tensor_scalar_mul(nm1, m1, -1.0))
                ser.step(nc.scalar, lambda e: e.activation(pe, lg, AF.Exp, bias=nm1[:, 0:1]))
                ser.step(nc.vector, lambda e: e.reduce_sum(sm, pe, mybir.AxisListType.X))
                ser.step(nc.vector, lambda e: e.reciprocal(wi, sm))
                ser.step(nc.vector, lambda e: e.tensor_scalar_mul(pe, pe, wi[:, 0:1]))
                ser.step(nc.vector, lambda e: e.reduce_max(m1, pe, mybir.AxisListType.X))
                ser.step(nc.vector, lambda e: e.tensor_single_scalar(
                    kp, pe, m1[:, 0:1], ALU.is_equal))
                ser.step(nc.vector, lambda e: e.scalar_tensor_tensor(
                    we8, kp, -1e30, pe, ALU.mult, ALU.add))
                ser.step(nc.vector, lambda e: e.reduce_max(m2, we8, mybir.AxisListType.X))
                ser.step(nc.vector, lambda e: e.tensor_single_scalar(
                    kp, pe, m2[:, 0:1], ALU.is_ge))
                ser.step(nc.vector, lambda e: e.tensor_add(sm, m1, m2))
                ser.step(nc.vector, lambda e: e.reciprocal(wi, sm))
                ser.step(nc.vector, lambda e: e.tensor_mul(we8, pe, kp))
                ser.step(nc.vector, lambda e: e.tensor_scalar_mul(we8, we8, wi[:, 0:1]))
                ser.step(nc.tensor, lambda e: e.transpose(p_tp[0:8, 0:128], we8, s_id[:]))
                ser.step(nc.vector, lambda e, tt=tt: e.tensor_copy(
                    s_weT[0:8, tt * 128:(tt + 1) * 128], p_tp[0:8, 0:128]))
            P.barrier()

            silu_cnt = [0, 0]
            sg_free = [0, 0]
            uw_cnt = [0, 0]
            z_done = [0] * 9
            acc_cnt = [0, 0]
            we_evac = [0] * 9
            uw_last_e = [0] * 9
            gi = tc * NLD
            for e_ in range(9):
                zh = (e_ % 2) * 8 * 512
                if e_ < 8:
                    twe = P.emit("tensor", lambda e, e_=e_: e.matmul(
                        PWE, s_sel[:, e_ * 128:(e_ + 1) * 128], s_weT[0:8, :],
                        start=True, stop=True),
                        waits=W_([("scalar", we_evac[e_ - 1] if e_ else 0)]))
                    we_evac[e_] = P.emit("scalar", lambda e, e_=e_: e.copy(
                        s_we[:, (e_ % 2) * 512:(e_ % 2) * 512 + 512], PWE),
                        waits=W_([("tensor", twe),
                               ("vector", uw_last_e[e_ - 2] if e_ >= 2 else 0)]))
                for ft in range(8):
                    par = ft % 2
                    issue_loads(gi + 4)
                    slot = gi % 4
                    for k in range(16):
                        w = []
                        if k == 0:
                            w = [(f"slw{gi % 4}", ld_cnt[gi]), ("scalar", silu_cnt[par])]
                        P.emit("tensor", lambda e, k=k, slot=slot, par=par: e.matmul(
                            PG[par], s_wb[:, slot * 2048 + k * 128: slot * 2048 + (k + 1) * 128],
                            s_ht[:, k * 512:(k + 1) * 512],
                            start=(k == 0), stop=(k == 15)),
                            waits=W_(w), inc=(k == 15))
                    tg = P.cnt["tensor"]
                    ld_done[gi] = tg
                    gi += 1
                    issue_loads(gi + 4)
                    slot = gi % 4
                    for k in range(16):
                        w = []
                        if k == 0:
                            w = [(f"slw{gi % 4}", ld_cnt[gi]), ("vector", uw_cnt[par])]
                        P.emit("tensor", lambda e, k=k, slot=slot, par=par: e.matmul(
                            PU[par], s_wb[:, slot * 2048 + k * 128: slot * 2048 + (k + 1) * 128],
                            s_ht[:, k * 512:(k + 1) * 512],
                            start=(k == 0), stop=(k == 15)),
                            waits=W_(w), inc=(k == 15))
                    tu = P.cnt["tensor"]
                    ld_done[gi] = tu
                    gi += 1
                    silu_cnt[par] = P.emit("scalar", lambda e, par=par: e.activation(
                        s_sg[:, par * 512: par * 512 + 512], PG[par], AF.Silu),
                        waits=W_([("tensor", tg), ("vector", sg_free[par])]))
                    if e_ < 8:
                        uw_cnt[par] = P.emit("vector", lambda e, par=par: e.tensor_mul(
                            s_uw[:, par * 512: par * 512 + 512], PU[par],
                            s_we[:, (e_ % 2) * 512:(e_ % 2) * 512 + 512]),
                            waits=W_([("tensor", tu), ("scalar", we_evac[e_])]))
                        zc = P.emit("vector", lambda e, par=par, ft=ft, zh=zh: e.tensor_mul(
                            s_z[:, zh + ft * 512: zh + (ft + 1) * 512],
                            s_uw[:, par * 512: par * 512 + 512],
                            s_sg[:, par * 512: par * 512 + 512]),
                            waits=W_([("scalar", silu_cnt[par])]))
                    else:
                        zc = P.emit("vector", lambda e, par=par, ft=ft, zh=zh: e.tensor_mul(
                            s_z[:, zh + ft * 512: zh + (ft + 1) * 512],
                            PU[par], s_sg[:, par * 512: par * 512 + 512]),
                            waits=W_([("tensor", tu), ("scalar", silu_cnt[par])]))
                        uw_cnt[par] = zc
                    sg_free[par] = zc
                    uw_last_e[e_] = uw_cnt[par]
                    if ft == 7:
                        z_done[e_] = zc
                for dt_ in range(16):
                    par2 = dt_ % 2
                    issue_loads(gi + 4)
                    slot = gi % 4
                    for fk in range(8):
                        w = []
                        if fk == 0:
                            w = [(f"slw{gi % 4}", ld_cnt[gi]),
                                 ("vector", max(z_done[e_], acc_cnt[par2]))]
                        P.emit("tensor", lambda e, fk=fk, slot=slot, par2=par2, zh=zh: e.matmul(
                            PD[par2], s_wb[:, slot * 2048 + fk * 128: slot * 2048 + (fk + 1) * 128],
                            s_z[:, zh + fk * 512: zh + (fk + 1) * 512],
                            start=(fk == 0), stop=(fk == 7)),
                            waits=W_(w), inc=(fk == 7))
                    td = P.cnt["tensor"]
                    ld_done[gi] = td
                    gi += 1
                    if e_ == 0:
                        acc_cnt[par2] = P.emit("vector", lambda e, dt_=dt_, par2=par2: e.tensor_copy(
                            s_acc[:, dt_ * 512:(dt_ + 1) * 512], PD[par2]),
                            waits=W_([("tensor", td)]))
                    else:
                        acc_cnt[par2] = P.emit("vector", lambda e, dt_=dt_, par2=par2: e.tensor_add(
                            s_acc[:, dt_ * 512:(dt_ + 1) * 512],
                            s_acc[:, dt_ * 512:(dt_ + 1) * 512], PD[par2]),
                            waits=W_([("tensor", td)]))
            P.barrier()

            for k in range(16):
                ser.step(nc.vector, lambda e, k=k: e.tensor_add(
                    s_acc[:, k * 512:(k + 1) * 512], s_acc[:, k * 512:(k + 1) * 512],
                    s_x3[:, k * 512:(k + 1) * 512]))
            emit_ln(ser, nc, s_acc, s_bf, s_st, s_onc, s_onr, p_s, p_rep, s_x3, LN_EPS)
            for tt in range(4):
                for k in range(16):
                    ser.step(nc.tensor, lambda e, k=k, tt=tt: e.transpose(
                        p_tp[:, 0:128], s_x3[:, k * 512 + tt * 128: k * 512 + (tt + 1) * 128],
                        s_id[:]))
                    ser.step(nc.vector, lambda e, k=k: e.tensor_copy(
                        s_tm[:, k * 128:(k + 1) * 128], p_tp[:, 0:128]))
                ser.dma(nc.sync, out_tm[t0 + tt * 128: t0 + (tt + 1) * 128, :], s_tm[:])
        P.barrier()
        P.barrier()


FUSED_SEMS = ["slw0", "slw1", "slw2", "slw3", "xld0", "xld1", "yst",
              "ost0", "ost1", "yst0", "yst1", "cc"]


def build_fused():
    nc = bass.Bass("TRN2", target_bir_lowering=False)

    def EI(name, shape, dt=F32):
        return nc.dram_tensor(name, shape, dt, kind="ExternalInput").ap()

    xA = EI("xA", [T, D])
    xAs = EI("xAs", [T, 512])
    wdtT = EI("wdtT", [D, 512])
    dtb = EI("dtb", [128, 4])
    xpw = EI("xpw", [D, 64])
    xpb = EI("xpb", [64, 1])
    alog = EI("alog", [128, 64])
    ident = EI("ident", [128, 128])
    sel = EI("sel", [64, 4096], BF16)
    wupT_sl = EI("wupT_sl", [D, 1024])
    wgT_sl = EI("wgT_sl", [D, 1024])
    wdT_sl = EI("wdT_sl", [INTER, 256])
    h128 = EI("h128", [128, 128])
    cv_in = {name: EI(f"cv_{name}_in", [rows // 8, 128]) for name, rows in PACK_SECTIONS}
    otb = EI("otb", [128, 16])
    x_tm = EI("x_tm", [TC, D])
    su_ug = EI("su_ug", [128, 16])
    wsg = EI("wsg", [128, 64])
    wsu = EI("wsu", [128, 64])
    wsd = EI("wsd", [128, 16])
    maskw = EI("maskw", [128, 896], BF16)
    ident_bf = EI("ident_bf", [128, 128], BF16)
    rmsw = EI("rmsw", [128, 16])
    rt_w = EI("rt_w", [D, 8])
    sel8 = EI("sel8", [8, 1024])
    out_tm = nc.dram_tensor("out_tm", [TC, D], F32, kind="ExternalOutput").ap()

    up_t_loc = nc.dram_tensor("up_t_loc", [8 * 16 * 128, 128], BF16)
    g_t_loc = nc.dram_tensor("g_t_loc", [8 * 16 * 128, 128], BF16)
    dn_t_loc = nc.dram_tensor("dn_t_loc", [2 * 64 * 128, 128], BF16)
    cv_out = {name: nc.dram_tensor(f"cv_{name}_out", [rows // 8, 128], BF16)
              for name, rows in PACK_SECTIONS}
    wup_pk = nc.dram_tensor("wup_pk", [64 * 16 * 128, 128], BF16)
    wg_pk = nc.dram_tensor("wg_pk", [64 * 16 * 128, 128], BF16)
    wdn_pk = nc.dram_tensor("wdn_pk", [16 * 64 * 128, 128], BF16)
    pk = {name: nc.dram_tensor(f"pk_{name}", [rows, 128], BF16)
          for name, rows in PACK_SECTIONS}
    yTs = nc.dram_tensor("yTs", [2048, 2048], BF16)
    YG = nc.dram_tensor("YG", [8192, 2048], BF16)
    yloc = nc.dram_tensor("yloc", [2048, 2048], BF16)
    qloc = nc.dram_tensor("qloc", [6144, 2048], BF16)
    ytloc = nc.dram_tensor("ytloc", [8192, 512], BF16)
    x1T = nc.dram_tensor("x1T", [D, TC], F32)
    x2T = nc.dram_tensor("x2T", [D, TC], F32)
    x3T = nc.dram_tensor("x3T", [D, TC], F32)
    qkvT = nc.dram_tensor("qkvT", [3 * D, TC], BF16)
    QGs = [nc.dram_tensor(f"QG{s}", [4 * D, TC], BF16) for s in range(3)]
    ytm = nc.dram_tensor("ytm", [T, 512], BF16)
    YAG = nc.dram_tensor("YAG", [4 * T, 512], BF16)

    G8 = [list(range(8))]
    G4 = [[0, 1, 2, 3], [4, 5, 6, 7]]

    with ExitStack() as stack:
        P = Pipe(nc, stack)
        for n in FUSED_SEMS:
            P.new_sem(n, stack)

        def ccol(kind, groups, in_h, out_h):
            i = nc.gpsimd.collective_compute(
                kind, ALU.bypass, replica_groups=groups,
                ins=[in_h.ap().opt()], outs=[out_h.ap().opt()])
            i.then_inc(P.sem["cc"], 1)
            P.cnt["cc"] += 1

        def ccbar():
            for e2 in P.ENGS:
                P._wait(e2, getattr(nc, e2), [("cc", P.cnt["cc"])])

        tW = {"wupT_sl": wupT_sl, "wgT_sl": wgT_sl, "wdT_sl": wdT_sl, "h128": h128,
              "up_t": up_t_loc.ap(), "g_t": g_t_loc.ap(), "dn_t": dn_t_loc.ap()}
        for name, _ in PACK_SECTIONS:
            tW[f"cv_{name}_in"] = cv_in[name]
            tW[f"cv_{name}_out"] = cv_out[name].ap()
        emit_W(nc, P, tW)
        emit_A(nc, P, {"xA": xA, "xAs": xAs, "wdtT": wdtT, "dtb": dtb, "xpw": xpw,
                       "xpb": xpb, "alog": alog, "ident": ident, "sel": sel,
                       "yTs": yTs.ap()})
        P.barrier(["gpsimd"])
        ccol("AllGather", G8, up_t_loc, wup_pk)
        ccol("AllGather", G8, g_t_loc, wg_pk)
        ccol("AllGather", G8, dn_t_loc, wdn_pk)
        for name, _ in PACK_SECTIONS:
            ccol("AllGather", G8, cv_out[name], pk[name])
        ccol("AllGather", G4, yTs, YG)
        ccbar()
        d512 = (nc.sync.partition_id() % 4) * 512
        yg3 = YG.ap().rearrange("(rank x) c -> x rank c", x=2048)
        yl3 = yloc.ap().rearrange("(rank x) c -> x rank c", x=512)
        for i in range(2):
            P.dma_q("dsync", yl3[:, i * 2:(i + 1) * 2, :],
                    yg3[bass.ds(d512, 512), i * 2:(i + 1) * 2, :])

        emit_B(nc, P, {"yT": yloc.ap(), "wot_pk": pk["ot"].ap(), "otb": otb,
                       "x_tm": x_tm, "ident": ident, "x1T": x1T.ap()})
        emit_C(nc, P, {"x1T": x1T.ap(), "wup_pk": wup_pk.ap(), "wg_pk": wg_pk.ap(),
                       "wdn_pk": wdn_pk.ap(), "su_ug": su_ug, "wsg": wsg, "wsu": wsu,
                       "wsd": wsd, "x2T": x2T.ap()})
        def on_kind(kind, c0, c1):
            i = nc.gpsimd.collective_compute(
                "AllGather", ALU.bypass, replica_groups=G4,
                ins=[qkvT.ap()[kind * D:(kind + 1) * D, :].opt()],
                outs=[QGs[kind].ap().opt()])
            i.then_inc(P.sem["cc"], 1)
            P.cnt["cc"] += 1

        def on_kind_w(kind, c0, c1):
            P._wait("gpsimd", nc.gpsimd, [("ost0", c0), ("ost1", c1)])
            on_kind(kind, c0, c1)

        emit_D(nc, P, {"x2T": x2T.ap(), "wqkv_pk": pk["qkv"].ap(), "qkvT": qkvT.ap()},
               on_kind=on_kind_w)
        ccbar()
        ql3 = qloc.ap().rearrange("(rs x) c -> x rs c", x=512)
        for s in range(3):
            qg3 = QGs[s].ap().rearrange("(r x) c -> x r c", x=2048)
            P.dma_q("dsync", ql3[:, s::3, :], qg3[bass.ds(d512, 512), :, :])

        emit_E(nc, P, {"qkvg": qloc.ap(), "maskw": maskw, "ident_bf": ident_bf,
                       "y_tm": ytm.ap()})
        P.barrier(["gpsimd"])
        ccol("AllGather", G4, ytm, YAG)
        ccbar()
        d2048 = (nc.sync.partition_id() % 4) * 2048
        ya3 = YAG.ap().rearrange("(r x) c -> x r c", x=8192)
        yt3 = ytloc.ap().rearrange("(r x) c -> x r c", x=2048)
        for i in range(4):
            P.dma_q("dsync", yt3[:, i:i + 1, :],
                    ya3[bass.ds(d2048, 2048), i:i + 1, :])
        P.barrier()

        emit_E2(nc, P, {"ytmg": ytloc.ap(), "wpr_pk": pk["pr"].ap(), "x2T": x2T.ap(),
                        "ident_bf": ident_bf, "x3T": x3T.ap()})
        emit_F(nc, P, {"x3T": x3T.ap(), "rmsw": rmsw, "rt_w": rt_w,
                       "wg_pk": pk["g"].ap(), "wu_pk": pk["u"].ap(),
                       "wd_pk": pk["d"].ap(), "sel8": sel8, "ident": ident,
                       "out_tm": out_tm})
        P.barrier()
    return nc


def host_fused_inputs(inp):
    mapsW = host_inputs_W(inp)
    mapsA = host_inputs_A(inp)
    suug, wsg, wsu, wsd = host_scales_C(inp)
    x = np.asarray(inp["L0_attn_x"], np.float32)
    otb = np.ascontiguousarray(
        np.asarray(inp["L0_attn_out_proj_b"], np.float32).reshape(16, 128).T)
    rmsw = np.ascontiguousarray(
        np.asarray(inp["L1_ffn_rms_weight"], np.float32).reshape(16, 128).T)
    rtw = np.ascontiguousarray(np.asarray(inp["L1_ffn_router_w"], np.float32))
    sel8 = np.zeros((8, 1024), dtype=np.float32)
    for e in range(8):
        sel8[e, e * 128:(e + 1) * 128] = 1.0
    maskw = (np.arange(896)[None, :] - 384 >= np.arange(128)[:, None]).astype(BF)
    ident_bf = np.eye(128, dtype=np.float32).astype(BF)
    maps = []
    for c in range(8):
        b, q = c // 4, c % 4
        m = {}
        m.update(mapsW[c])
        m.update(mapsA[c])
        m.update({
            "otb": otb, "x_tm": np.ascontiguousarray(x[b][q * TC:(q + 1) * TC, :]),
            "su_ug": suug, "wsg": wsg, "wsu": wsu, "wsd": wsd,
            "maskw": maskw, "ident_bf": ident_bf,
            "rmsw": rmsw, "rt_w": rtw, "sel8": sel8,
        })
        maps.append(m)
    return maps


def kernel(**inputs):
    inp = {k: np.asarray(v) for k, v in inputs.items()}
    res = run_bass_kernel_spmd(build_fused(), host_fused_inputs(inp),
                               core_ids=list(range(8)))
    out = np.empty((B, T, D), dtype=np.float32)
    for c in range(8):
        b, q = c // 4, c % 4
        out[b, q * TC:(q + 1) * TC, :] = res.results[c]["out_tm"]
    return out
